# version 03
# baseline (speedup 1.0000x reference)
import numpy as np
import ml_dtypes
from contextlib import ExitStack

import concourse.bass as bass
import concourse.mybir as mybir
from concourse.bass_utils import run_bass_kernel_spmd

F32 = mybir.dt.float32
BF16 = mybir.dt.bfloat16
I32 = mybir.dt.int32
AF = mybir.ActivationFunctionType
ALU = mybir.AluOpType
AX = mybir.AxisListType

B, S, D, H, DH = 2, 8192, 2048, 16, 128
HPC = 4
T = 2048
NCH = D // 128
DS, NDS = 2816, 22
DE, NDE = 1408, 11
NE = 8
NTT = S // 512
CSHIFT = -12.0
MASKVAL = -60.0
EPS = 1e-5

ENGS = ("pe", "act", "dve", "sp", "gp")
SEMS = ("pe", "act", "dve", "gpc", "cc",
        "gT", "gX0", "gX1", "gW0", "gW1", "gWo0", "gWo1", "gY",
        "gG0", "gG1", "gD", "gP0", "gP1", "gB0", "gB1",
        "sS0", "sS1", "sR0", "sR1", "sL", "sT0", "sT1", "sT2", "sT3", "sO0", "sO1")


class _Sched:

    def __init__(self):
        self.q = {e: [] for e in ENGS}
        self.cnt = {s: 0 for s in SEMS}
        self.pending = {e: [] for e in ENGS}

    def val(self, sem):
        return self.cnt[sem]

    SELF_SERIAL = ()

    def emit(self, eng, fn, waits=(), inc=None, amt=1):
        w = list(self.pending[eng]) + list(waits)
        self.pending[eng] = []
        if eng in self.SELF_SERIAL:
            if self.cnt[eng]:
                w.append((eng, self.cnt[eng]))
            if inc is None:
                inc = eng
        v = None
        if inc is not None:
            self.cnt[inc] += amt
            v = self.cnt[inc]
        self.q[eng].append((tuple(w), fn, inc, amt))
        return v

    def dma(self, eng, sem, out_ap, in_ap, waits=()):
        return self.emit(eng, lambda e, o=out_ap, i=in_ap: e.dma_start(o, i),
                         waits, inc=sem, amt=16)

    def barrier(self):
        marks = []
        for e in ENGS:
            if not self.q[e]:
                continue
            w, fn, inc, amt = self.q[e][-1]
            if inc is None:
                sem = e if e in ("pe", "act", "dve") else "gpc"
                assert e != "sp", "sp barrier op must carry an inc"
                self.cnt[sem] = self.cnt.get(sem, 0) + 1
                self.q[e][-1] = (w, fn, sem, 1)
                marks.append((sem, self.cnt[sem]))
            else:
                marks.append((inc, self.cnt[inc]))
        for e in ENGS:
            self.pending[e] = list(marks)


class _PB:

    def __init__(self, tile):
        self.t = tile
        self.ready = None
        self.free = {}

    def wait_free(self):
        return list(self.free.items())

    def read_done(self, sem, val):
        self.free[sem] = max(self.free.get(sem, 0), val)


def _mm_group(sc, pb, mms, extra_waits=(), lookahead_ok=True):
    waits = pb.wait_free() + list(extra_waits)
    n = len(mms)
    v = None
    for i, (o, l, r) in enumerate(mms):
        st, sp_ = (i == 0), (i == n - 1)
        v = sc.emit(
            "pe",
            lambda e, o=o, l=l, r=r, st=st, sp_=sp_: e.matmul(o, l, r, start=st, stop=sp_),
            waits if i == 0 else (),
            inc="pe" if sp_ else None,
        )
    pb.ready = ("pe", v)
    pb.free = {}
    return v


def _build(upto="full"):
    nc = bass.Bass("TRN2", target_bir_lowering=False)
    sc = _Sched()

    dt_in = {}
    def din(name, shape, dt=F32):
        dt_in[name] = nc.dram_tensor(name, list(shape), dt, kind="ExternalInput")
        return dt_in[name]

    xT = din("xT", (D, S))
    xres = din("xres", (D, T))
    wqT = din("wqT", (D, HPC * DH))
    wkT = din("wkT", (D, HPC * DH))
    wvT = din("wvT", (D, HPC * DH))
    woT = din("woT", (D, D))
    cosq = din("cosq", (128, S))
    sinq = din("sinq", (128, S))
    cosk = din("cosk", (128, S))
    sink = din("sink", (128, S))
    tri = din("tri", (128, 128))
    sel8 = din("sel8", (8, 8 * 128))
    ident = din("ident", (128, 128))
    swp = din("swp", (128, 128))
    ones1 = din("ones1", (1, 128))
    roff = din("roff", (1, 1), I32)
    wrouter = din("wrouter", (D, NE))
    wsg = din("wsg", (D, DS))
    wsu = din("wsu", (D, DS))
    wsd = din("wsd", (DS, D))
    weg = din("weg", (NE * D, DE))
    weu = din("weu", (NE * D, DE))
    wed = din("wed", (NE * DE, D))

    outfm = nc.dram_tensor("outfm", [D, T], F32, kind="ExternalOutput")

    cc_in_h = [nc.dram_tensor(f"cc_in{h}", [S, DH], BF16, kind="Internal") for h in range(HPC)]
    cc_ag_h = [nc.dram_tensor(f"cc_ag{h}", [4 * S, DH], BF16, kind="Internal") for h in range(HPC)]
    cc_my = nc.dram_tensor("cc_my", [T, D], BF16, kind="Internal")
    x1sp = nc.dram_tensor("x1sp", [D, T], F32, kind="Internal")

    dbg = {}
    if upto == "qkv":
        dbg["q"] = nc.dram_tensor("dbg_q", [128, S], BF16, kind="ExternalOutput")
        dbg["k"] = nc.dram_tensor("dbg_k", [128, S], BF16, kind="ExternalOutput")
        dbg["v"] = nc.dram_tensor("dbg_v", [128, 64 * 132], BF16, kind="ExternalOutput")
        dbg["cos"] = nc.dram_tensor("dbg_cos", [128, S], BF16, kind="ExternalOutput")
        dbg["sin"] = nc.dram_tensor("dbg_sin", [128, S], BF16, kind="ExternalOutput")
        dbg["qtmp"] = nc.dram_tensor("dbg_qtmp", [128, 512], BF16, kind="ExternalOutput")
        dbg["qsw"] = nc.dram_tensor("dbg_qsw", [128, 512], BF16, kind="ExternalOutput")
    if upto == "attn":
        dbg["attn"] = nc.dram_tensor("dbg_attn", [S, HPC * DH], BF16, kind="ExternalOutput")
    if upto == "x1":
        dbg["x1"] = nc.dram_tensor("dbg_x1", [D, T], F32, kind="ExternalOutput")
        dbg["comb"] = nc.dram_tensor("dbg_comb", [8, T], F32, kind="ExternalOutput")

    es = ExitStack()
    sems = {s: es.enter_context(nc.semaphore(s)) for s in SEMS}

    pb = [_PB(es.enter_context(nc.psum_tensor(f"pb{i}", [128, 512], F32))) for i in range(8)]

    ones1_sb = es.enter_context(nc.sbuf_tensor("ones1_sb", [1, 128], F32))
    ones128_sb = es.enter_context(nc.sbuf_tensor("ones128_sb", [128, 1], F32))
    biasC = es.enter_context(nc.sbuf_tensor("biasC", [128, 1], F32))
    epsT = es.enter_context(nc.sbuf_tensor("epsT", [1, 1], F32))
    half_neg = es.enter_context(nc.sbuf_tensor("half_neg", [1, 1], F32))
    neginf8 = es.enter_context(nc.sbuf_tensor("neginf8", [128, 8], F32))
    invD = es.enter_context(nc.sbuf_tensor("invD", [1, 1], F32))
    tri_sb = es.enter_context(nc.sbuf_tensor("tri_sb", [128, 128], F32))
    sel8_sb = es.enter_context(nc.sbuf_tensor("sel8_sb", [8, 8 * 128], F32))
    ident_sb = es.enter_context(nc.sbuf_tensor("ident_sb", [128, 128], F32))
    swp_sb = es.enter_context(nc.sbuf_tensor("swp_sb", [128, 128], BF16))
    roff_sb = es.enter_context(nc.sbuf_tensor("roff_sb", [1, 1], I32))
    comb_sb = es.enter_context(nc.sbuf_tensor("comb_sb", [8, T], F32))
    rg = nc.gpsimd.alloc_register("rg_off")
    rg2 = nc.gpsimd.alloc_register("rg_off2")

    sc.dma("gp", "gT", ones1_sb[:, :], ones1[:, :])
    sc.dma("gp", "gT", tri_sb[:, :], tri[:, :])
    sc.dma("gp", "gT", sel8_sb[:, :], sel8[:, :])
    sc.dma("gp", "gT", ident_sb[:, :], ident[:, :])
    sc.dma("gp", "gT", swp_sb[:, :], swp[:, :])
    v_cst_dma = sc.dma("gp", "gT", roff_sb[:, :], roff[:, :])
    v_tri = v_cst_dma
    sc.emit("gp", lambda e: e.memset(ones128_sb[:, :], 1.0), inc="gpc")
    sc.emit("gp", lambda e: e.memset(biasC[:, :], CSHIFT), inc="gpc")
    sc.emit("gp", lambda e: e.memset(epsT[:, :], EPS), inc="gpc")
    sc.emit("gp", lambda e: e.memset(half_neg[:, :], -0.5), inc="gpc")
    sc.emit("gp", lambda e: e.memset(neginf8[:, :], -1.0e30), inc="gpc")
    v_cst = sc.emit("gp", lambda e: e.memset(invD[:, :], 1.0 / D), inc="gpc")
    CST = [("gT", v_cst_dma), ("gpc", v_cst)]

    wo_outer = ExitStack()
    attn_fm = wo_outer.enter_context(nc.sbuf_tensor("attn_fm", [128, NCH, T], BF16))
    attn_es = ExitStack()
    tb_sb = [attn_es.enter_context(nc.sbuf_tensor(f"tb{i}", [128, 4, 512], BF16)) for i in range(2)]
    TAB = []

    v_tb_load = [0, 0]
    v_tb_used = [0, 0]
    qbf = attn_es.enter_context(nc.sbuf_tensor("qbf", [128, S], BF16))
    kbf = attn_es.enter_context(nc.sbuf_tensor("kbf", [128, S], BF16))
    vsb = attn_es.enter_context(nc.sbuf_tensor("vsb", [128, 64, 132], BF16))
    xt_sb = [attn_es.enter_context(nc.sbuf_tensor(f"xt{i}", [128, NCH, 512], BF16)) for i in range(2)]
    wq_sb = [attn_es.enter_context(nc.sbuf_tensor(f"wq_sb{i}", [128, NCH, DH], BF16)) for i in range(2)]
    wk_sb = [attn_es.enter_context(nc.sbuf_tensor(f"wk_sb{i}", [128, NCH, DH], BF16)) for i in range(2)]
    wv_sb = [attn_es.enter_context(nc.sbuf_tensor(f"wv_sb{i}", [128, NCH, DH], BF16)) for i in range(2)]
    qtmp = attn_es.enter_context(nc.sbuf_tensor("qtmp", [128, 512], BF16))
    qsw = attn_es.enter_context(nc.sbuf_tensor("qsw", [128, 512], BF16))
    ktmp = attn_es.enter_context(nc.sbuf_tensor("ktmp", [128, 512], BF16))
    ksw = attn_es.enter_context(nc.sbuf_tensor("ksw", [128, 512], BF16))
    rope_t1 = attn_es.enter_context(nc.sbuf_tensor("rope_t1", [128, 512], BF16))
    rope_t2 = attn_es.enter_context(nc.sbuf_tensor("rope_t2", [128, 512], BF16))
    probs = [attn_es.enter_context(nc.sbuf_tensor(f"probs{i}", [128, 512], BF16)) for i in range(3)]
    ao_sb = [attn_es.enter_context(nc.sbuf_tensor(f"ao{i}", [128, 4, DH], BF16)) for i in range(2)]
    rc_sb = attn_es.enter_context(nc.sbuf_tensor("rc_sb", [128, 4], F32))

    v_xt_load = [0, 0]
    v_xt_used = [None, None]
    v_ao_dma = [0, 0]

    last_v1 = {}
    last_v2 = {}
    last_v3 = [0]
    v_rc_read = [0, 0, 0, 0]

    v_proj_end = [0, 0]
    v_w_loaded = [0, 0]
    v_pick_h = [0] * HPC
    v_tp_h = [0] * HPC
    rg_loaded = [False]

    def emit_picks(h):
        if not rg_loaded[0]:
            sc.emit("gp", lambda e: e.reg_load(rg, roff_sb[0:1, 0:1]),
                    waits=[("cc", 1), ("gT", v_cst_dma)])
            rg_loaded[0] = True
        v = None
        for j in range(4):
            w0 = [("cc", h + 1)] if j == 0 else []
            if j == 0 and h >= 2:
                w0.append((f"sT{h - 2}", v_tp_h[h - 2]))
            sc.emit("gp", lambda e, j=j: e.reg_add(rg2, rg, j * S * DH), waits=w0)
            v = sc.emit(
                "gp", lambda e, j=j, h=h: e.dma_start(
                    bass.AP(cc_my, j * (HPC * DH) + h * DH, [[D, T], [1, DH]]),
                    bass.AP(cc_ag_h[h], rg2, [[DH, T], [1, DH]])),
                inc=f"gP{h % 2}", amt=16)
        v_pick_h[h] = v

    def emit_tp(h):
        v = None
        for i, fc in enumerate(range(h, NCH, 4)):
            v = sc.emit("sp", lambda e, fc=fc: e.dma_start(
                attn_fm[:, fc, :], cc_my[:, fc * 128:(fc + 1) * 128], transpose=True),
                waits=[(f"gP{h % 2}", v_pick_h[h])] if i == 0 else (), inc=f"sT{h}", amt=16)
        v_tp_h[h] = v
    for hh in range(HPC + 1):
        if hh < HPC:
            hp_ = hh % 2
            vw = None
            for wsb, wt in ((wq_sb[hp_], wqT), (wk_sb[hp_], wkT), (wv_sb[hp_], wvT)):
                vw = sc.dma("gp", f"gW{hp_}", wsb[:, :, :],
                            wt[:, hh * DH:(hh + 1) * DH].rearrange("(c p) o -> p c o", p=128),
                            waits=[("pe", v_proj_end[hp_])] if v_proj_end[hp_] else [])
            v_w_loaded[hp_] = vw
        if hh == 0:
            continue
        h = hh - 1
        hp = h % 2
        v_prev_pe = sc.val("pe")
        W = [(f"gW{hp}", v_w_loaded[hp])]

        vone = sc.emit("gp", lambda e: e.memset(vsb[:, :, 128:132], 1.0),
                       waits=[("pe", sc.val("pe"))], inc="gpc")

        v_q_done = [0] * NTT
        v_k_done = [0] * NTT
        v_v_done = [0] * NTT
        for tt in range(NTT):
            bi = tt % 2
            t0 = tt * 512
            w = [("pe", v_xt_used[bi])] if v_xt_used[bi] else []
            v_xt_load[bi] = sc.dma(
                "gp", f"gX{bi}", xt_sb[bi][:, :, :],
                xT[:, t0:t0 + 512].rearrange("(c p) t -> p c t", p=128), waits=w)
            wtb = [("dve", v_tb_used[bi])] if v_tb_used[bi] else []
            for ti, tsrc in enumerate((cosq, sinq, cosk, sink)):
                v_tb_load[bi] = sc.dma("gp", f"gB{bi}", tb_sb[bi][:, ti, :],
                                       tsrc[:, t0:t0 + 512], waits=wtb if ti == 0 else ())

            pq, pk, pv = pb[0 + (tt % 2)], pb[2 + (tt % 2)], pb[4 + (tt % 2)]
            ld = [(f"gX{bi}", v_xt_load[bi])] + W + TAB + CST
            _mm_group(sc, pq, [(pq.t[:, 0:512], wq_sb[hp][:, c, :], xt_sb[bi][:, c, :])
                              for c in range(NCH)], extra_waits=ld)
            _mm_group(sc, pk, [(pk.t[:, 0:512], wk_sb[hp][:, c, :], xt_sb[bi][:, c, :])
                              for c in range(NCH)], extra_waits=ld)
            vmms = []
            for sub in range(4):
                for c in range(NCH):
                    vmms.append((pv.t[:, sub * 128:(sub + 1) * 128],
                                 xt_sb[bi][:, c, sub * 128:(sub + 1) * 128],
                                 wv_sb[hp][:, c, :]))
            vstop = None
            wv0 = pv.wait_free() + ld
            for i, (o, l, r) in enumerate(vmms):
                st = (i % NCH == 0)
                sp_ = (i % NCH == NCH - 1)
                vstop = sc.emit(
                    "pe", lambda e, o=o, l=l, r=r, st=st, sp_=sp_: e.matmul(o, l, r, start=st, stop=sp_),
                    wv0 if i == 0 else (), inc="pe" if sp_ else None)
            pv.ready = ("pe", vstop)
            pv.free = {}
            v_xt_used[bi] = sc.val("pe")

            def rope(ptile, tmp, sw, ct, st_, dest, t0, bi):
                wva = [ptile.ready] + CST
                tk = id(tmp)
                if tk in last_v1:
                    wva.append(("dve", last_v1[tk]))
                va = sc.emit("act", lambda e: e.copy(tmp[:, :], ptile.t[:, 0:512]),
                             waits=wva, inc="act")
                ptile.read_done("act", va)
                vswp = _mm_group(sc, ptile, [(ptile.t[:, 0:512], swp_sb[:, :], tmp[:, :])],
                                 extra_waits=[("act", va)])
                wsw = [ptile.ready]
                if tk in last_v2:
                    wsw.append(("dve", last_v2[tk]))
                va2 = sc.emit("act", lambda e: e.copy(sw[:, :], ptile.t[:, 0:512]),
                              waits=wsw, inc="act")
                ptile.read_done("act", va2)
                v1 = sc.emit("dve", lambda e: e.tensor_mul(rope_t1[:, :], tmp[:, :], ct),
                             waits=[("act", va), ("dve", last_v3[0]), (f"gB{bi}", v_tb_load[bi])], inc="dve")
                v2 = sc.emit("dve", lambda e: e.tensor_mul(rope_t2[:, :], sw[:, :], st_),
                             waits=[("act", va2)], inc="dve")
                v3 = sc.emit("dve", lambda e, t0=t0: e.tensor_add(dest[:, t0:t0 + 512], rope_t1[:, :], rope_t2[:, :]),
                             waits=[("pe", v_prev_pe), ("dve", v2)], inc="dve")
                last_v1[tk] = v1
                last_v2[tk] = v2
                last_v3[0] = v3
                return v3

            v_q_done[tt] = rope(pq, qtmp, qsw, tb_sb[bi][:, 0, :], tb_sb[bi][:, 1, :], qbf, t0, bi)
            v_k_done[tt] = rope(pk, ktmp, ksw, tb_sb[bi][:, 2, :], tb_sb[bi][:, 3, :], kbf, t0, bi)
            v_tb_used[bi] = v_k_done[tt]
            vv = None
            for sub in range(4):
                vv = sc.emit("act",
                             lambda e, sub=sub, tt=tt, pvt=pv.t: e.copy(
                                 vsb[:, tt * 4 + sub, 0:128], pvt[:, sub * 128:(sub + 1) * 128]),
                             waits=[pv.ready, ("gpc", vone)], inc="act")
            pv.read_done("act", vv)
            v_v_done[tt] = vv
        v_proj_end[hp] = sc.val("pe")

        if upto == "qkv" and h == 0:
            w = [("dve", max(v_q_done)), ("dve", max(v_k_done)), ("act", max(v_v_done))]
            sc.dma("gp", "gT", dbg["q"][:, :], qbf[:, :], waits=w)
            sc.dma("gp", "gT", dbg["k"][:, :], kbf[:, :], waits=w)
            sc.dma("gp", "gT", dbg["v"][:, :], vsb[:, :, :].rearrange("p a b -> p (a b)"), waits=w)
            sc.dma("gp", "gT", dbg["cos"][:, :], tabs["cosq"][:, :], waits=w)
            sc.dma("gp", "gT", dbg["sin"][:, :], tabs["sinq"][:, :], waits=w)
            sc.dma("gp", "gT", dbg["qtmp"][:, :], qtmp[:, :], waits=w)
            vv_ = sc.dma("gp", "gT", dbg["qsw"][:, :], qsw[:, :], waits=w)
            attn_es.close()
            _finish(nc, sc, sems, es, [("gT", vv_)])
            return nc

        for qt in range(NTT):
            q0 = qt * 512
            nkc = qt * 4 + 4
            v_score = [0] * nkc
            v_exp = [0] * nkc
            v_pv = [0] * nkc
            qstart = [0] * 4
            v_pv_stop = [0] * 4

            out_pb = [pb[2], pb[3], pb[4], pb[5]]
            sco_pb = [pb[0], pb[1], pb[6]]

            def emit_score(kc):
                bank = sco_pb[kc % 3]
                dq = kc - qt * 4
                qlo = max(0, dq) * 128
                w = bank.wait_free()
                w += [("dve", v_k_done[kc // 4]), ("dve", v_q_done[qt])]
                v = sc.emit("pe", lambda e, kc=kc, qlo=qlo, bank=bank, q0=q0: e.matmul(
                    bank.t[:, 0:512 - qlo], kbf[:, kc * 128:(kc + 1) * 128],
                    qbf[:, q0 + qlo:q0 + 512], start=True, stop=True),
                    w, inc="pe")
                bank.ready = ("pe", v)
                bank.free = {}
                v_score[kc] = v
                if dq >= 0:
                    vm = sc.emit("dve", lambda e, bank=bank: e.tensor_add(
                        bank.t[:, 0:128], bank.t[:, 0:128], tri_sb[:, :]),
                        [("pe", v), ("gT", v_cst_dma)], inc="dve")
                    bank.ready = ("dve", vm)
                wexp = [bank.ready]
                if kc >= 3 and v_pv[kc - 3]:
                    wexp.append(("pe", v_pv[kc - 3]))
                ve = sc.emit("act", lambda e, kc=kc, qlo=qlo, bank=bank: e.activation(
                    probs[kc % 3][:, 0:512 - qlo], bank.t[:, 0:512 - qlo], AF.Exp,
                    bias=biasC[:, 0:1]), wexp, inc="act")
                bank.read_done("act", ve)
                v_exp[kc] = ve

            def emit_pv(kc):
                dq = kc - qt * 4
                qlo = max(0, dq) * 128
                last = None
                for qc in range(qlo // 128, 4):
                    bank = out_pb[qc]
                    qcg = qt * 4 + qc
                    st = (kc == 0)
                    sp_ = (kc == qcg)
                    w = [("act", v_exp[kc])]
                    if st:
                        w += bank.wait_free()
                    last = sc.emit("pe", lambda e, kc=kc, qc=qc, qlo=qlo, bank=bank, st=st, sp_=sp_: e.matmul(
                        bank.t[:, 0:129],
                        probs[kc % 3][:, qc * 128 - qlo:(qc + 1) * 128 - qlo],
                        vsb[:, kc, 0:129], start=st, stop=sp_),
                        w, inc="pe" if (sp_ or qc == 3) else None)
                    if st:
                        bank.free = {}
                    if sp_:
                        v_pv_stop[qc] = last
                        bank.ready = ("pe", last)
                v_pv[kc] = last

            emit_score(0)
            for kc in range(1, nkc):
                emit_score(kc)
                emit_pv(kc - 1)
            emit_pv(nkc - 1)

            vcp = None
            for qc in range(4):
                bank = out_pb[qc]
                wr_ = [("pe", v_pv_stop[qc]), ("dve", sc.val("dve"))]
                vr = sc.emit("dve", lambda e, qc=qc, bank=bank: e.reciprocal(
                    rc_sb[:, qc:qc + 1], bank.t[:, 128:129]),
                    wr_, inc="dve")
                wao = [("dve", vr)]
                if v_ao_dma[qt % 2]:
                    wao.append((f"sS{qt % 2}", v_ao_dma[qt % 2]))
                vcp = sc.emit("dve", lambda e, qc=qc, bank=bank, qt=qt: e.tensor_scalar(
                    ao_sb[qt % 2][:, qc, :], bank.t[:, 0:128], rc_sb[:, qc:qc + 1], None,
                    ALU.mult), wao, inc="dve")
                bank.read_done("dve", vcp)
                v_rc_read[qc] = vcp
            v_ao_dma[qt % 2] = sc.dma(
                "sp", f"sS{qt % 2}", cc_in_h[h][q0:q0 + 512, :].rearrange("(qc p) f -> p qc f", p=128),
                ao_sb[qt % 2][:, :, :], waits=[("dve", vcp)])

        if upto != "qkv":
            if h >= 1:
                emit_picks(h - 1)
            sc.emit("gp", lambda e, h=h: e.collective_compute(
                "AllGather", ALU.bypass, replica_groups=[[0, 1, 2, 3], [4, 5, 6, 7]],
                ins=[cc_in_h[h][:, :]], outs=[cc_ag_h[h][:, :]]),
                waits=[("sS0", sc.val("sS0")), ("sS1", sc.val("sS1"))], inc="cc")
            if h >= 2:
                emit_tp(h - 2)

    if upto not in ("qkv", "attn"):
        emit_picks(HPC - 1)
        emit_tp(HPC - 2)
        emit_tp(HPC - 1)

    v_attn_pe_end = sc.val("pe")
    attn_es.close()
    sc.barrier()

    if upto == "attn":
        vdbg = None
        for h in range(HPC):
            vdbg = sc.dma("gp", "gT", dbg["attn"][:, h * DH:(h + 1) * DH], cc_in_h[h][:, :],
                          waits=[("sS0", sc.val("sS0")), ("sS1", sc.val("sS1")), ("cc", sc.val("cc"))])
        _finish(nc, sc, sems, es, [("gT", vdbg)])
        return nc


    wo_es = ExitStack()
    wo_sb = [wo_es.enter_context(nc.sbuf_tensor(f"wo{i}", [128, NCH, 128], BF16)) for i in range(2)]
    x1pre = wo_es.enter_context(nc.sbuf_tensor("x1pre", [128, NCH, 512], F32))
    xres_sb = [wo_es.enter_context(nc.sbuf_tensor(f"xres{i}", [128, 512], F32)) for i in range(2)]
    xsq = wo_es.enter_context(nc.sbuf_tensor("xsq", [128, 512], F32))
    stat_sb = wo_es.enter_context(nc.sbuf_tensor("stat_sb", [1, 6 * 512], F32))
    mb_sb = wo_es.enter_context(nc.sbuf_tensor("mb_sb", [128, 512], F32))
    rb_sb = wo_es.enter_context(nc.sbuf_tensor("rb_sb", [128, 512], F32))
    wr_sb = wo_es.enter_context(nc.sbuf_tensor("wr_sb", [128, NCH, NE], F32))
    logits_sb = wo_es.enter_context(nc.sbuf_tensor("logits_sb", [128, NCH, NE], F32))
    pr_sb = wo_es.enter_context(nc.sbuf_tensor("pr_sb", [128, NE], F32))
    eq_sb = wo_es.enter_context(nc.sbuf_tensor("eq_sb", [128, NE], F32))
    pm_sb = wo_es.enter_context(nc.sbuf_tensor("pm_sb", [128, NE], F32))
    sc1_sb = wo_es.enter_context(nc.sbuf_tensor("sc1_sb", [128, 4], F32))



    v_wr = sc.dma("gp", "gT", wr_sb[:, :, :], wrouter.rearrange("(c p) o -> p c o", p=128))

    xsq2 = wo_es.enter_context(nc.sbuf_tensor("xsq2", [128, 2, 512], F32))

    v_spill_tile = [0, 0, 0, 0]
    v_wo_load = [0, 0]
    v_wo_used = [None, None]
    v_xres_used = [None, None]
    v_x1_readers = []
    v_x1sp_last = 0
    v_logit_evac = 0
    for tt in range(4):
        t0 = tt * 512
        sbank, s2bank = pb[6], pb[7]
        v_s2mm = [0] * NCH
        v_s1mm = [0] * NCH
        v_add = [0] * NCH
        v_sq = [0] * NCH

        def emit_smm(oc):
            st, sp_ = (oc == 0), (oc == NCH - 1)
            w1 = [("dve", v_add[oc])]
            if st:
                w1 += sbank.wait_free() + CST
            v_s1mm[oc] = sc.emit("pe", lambda e, oc=oc, st=st, sp_=sp_: e.matmul(
                sbank.t[0:1, 0:512], ones128_sb[:, :], x1pre[:, oc, :], start=st, stop=sp_),
                w1, inc="pe")
            w2 = [("act", v_sq[oc])]
            if st:
                w2 += s2bank.wait_free()
            v_s2mm[oc] = sc.emit("pe", lambda e, oc=oc, st=st, sp_=sp_: e.matmul(
                s2bank.t[0:1, 0:512], ones128_sb[:, :], xsq2[:, oc % 2, :], start=st, stop=sp_),
                w2, inc="pe")

        for oc in range(NCH):
            bi = oc % 2
            w = [("pe", v_wo_used[bi])] if v_wo_used[bi] else []
            v_wo_load[bi] = sc.dma(
                "gp", f"gWo{bi}", wo_sb[bi][:, :, :],
                woT[:, oc * 128:(oc + 1) * 128].rearrange("(c p) o -> p c o", p=128),
                waits=w)
            wbank = pb[oc % 2]
            wf = wbank.wait_free() + [(f"gWo{bi}", v_wo_load[bi])]
            vstop = None
            for c in range(NCH):
                ww = list(wf) if c == 0 else []
                ww.append((f"sT{c % 4}", v_tp_h[c % 4]))
                vstop = sc.emit("pe", lambda e, c=c, bi=bi, wbank=wbank, t0=t0, st=(c == 0), sp_=(c == NCH - 1): e.matmul(
                    wbank.t[:, 0:512], wo_sb[bi][:, c, :], attn_fm[:, c, t0:t0 + 512],
                    start=st, stop=sp_), ww, inc="pe" if c == NCH - 1 else None)
            wbank.ready = ("pe", vstop)
            wbank.free = {}
            v_wo_used[bi] = sc.val("pe")
            xbi = oc % 2
            wx = [("dve", v_xres_used[xbi])] if v_xres_used[xbi] else []
            vxr = sc.dma("sp", f"sR{xbi}", xres_sb[xbi][:, :], xres[oc * 128:(oc + 1) * 128, t0:t0 + 512], waits=wx)
            wadd = [wbank.ready, (f"sR{xbi}", vxr)] + v_x1_readers
            v_add[oc] = sc.emit("dve", lambda e, oc=oc, xbi=xbi, wbank=wbank: e.tensor_add(
                x1pre[:, oc, :], wbank.t[:, 0:512], xres_sb[xbi][:, :]), wadd, inc="dve")
            wbank.read_done("dve", v_add[oc])
            v_xres_used[xbi] = v_add[oc]
            wsq = [("dve", v_add[oc])]
            if oc >= 2:
                wsq.append(("pe", v_s2mm[oc - 2]))
            v_sq[oc] = sc.emit("act", lambda e, oc=oc: e.activation(
                xsq2[:, oc % 2, :], x1pre[:, oc, :], AF.Square), wsq, inc="act")
            if oc >= 1:
                emit_smm(oc - 1)
        emit_smm(NCH - 1)
        v_x1_readers = []
        sbank.ready = ("pe", v_s1mm[NCH - 1])
        sbank.free = {}
        s2bank.ready = ("pe", v_s2mm[NCH - 1])
        s2bank.free = {}

        st_ = stat_sb
        va_m = sc.emit("act", lambda e: e.activation(st_[0:1, 0:512], sbank.t[0:1, 0:512],
                                                     AF.Copy, scale=invD[0:1, 0:1]),
                       [sbank.ready] + CST, inc="act")
        va_e2 = sc.emit("act", lambda e: e.activation(st_[0:1, 512:1024], s2bank.t[0:1, 0:512],
                                                      AF.Copy, scale=invD[0:1, 0:1]),
                        [s2bank.ready], inc="act")
        sbank.read_done("act", va_m)
        s2bank.read_done("act", va_e2)
        vd_mm = sc.emit("dve", lambda e: e.tensor_mul(st_[0:1, 1024:1536], st_[0:1, 0:512], st_[0:1, 0:512]),
                        [("act", va_m)], inc="dve")
        vd_var = sc.emit("dve", lambda e: e.tensor_sub(st_[0:1, 1536:2048], st_[0:1, 512:1024], st_[0:1, 1024:1536]),
                         [("act", va_e2), ("dve", vd_mm)], inc="dve")
        va_ln = sc.emit("act", lambda e: e.activation(st_[0:1, 2048:2560], st_[0:1, 1536:2048],
                                                      AF.Ln, bias=epsT[0:1, 0:1]),
                        [("dve", vd_var)], inc="act")
        va_rstd = sc.emit("act", lambda e: e.activation(st_[0:1, 2560:3072], st_[0:1, 2048:2560],
                                                        AF.Exp, scale=half_neg[0:1, 0:1]),
                          [("act", va_ln)], inc="act")
        mbank, rbank = pb[2], pb[3]
        vb_m = _mm_group(sc, mbank, [(mbank.t[:, 0:512], ones1_sb[:, :], st_[0:1, 0:512])],
                         extra_waits=[("act", va_m)] + CST)
        vb_r = _mm_group(sc, rbank, [(rbank.t[:, 0:512], ones1_sb[:, :], st_[0:1, 2560:3072])],
                         extra_waits=[("act", va_rstd)])
        ve_m = sc.emit("act", lambda e: e.copy(mb_sb[:, :], mbank.t[:, 0:512]), [mbank.ready], inc="act")
        ve_r = sc.emit("act", lambda e: e.copy(rb_sb[:, :], rbank.t[:, 0:512]), [rbank.ready], inc="act")
        mbank.read_done("act", ve_m)
        rbank.read_done("act", ve_r)

        v_nrm = [0] * NCH
        for oc in range(NCH):
            vs = sc.emit("dve", lambda e, oc=oc: e.tensor_sub(
                x1pre[:, oc, :], x1pre[:, oc, :], mb_sb[:, :]),
                [("act", ve_m), ("pe", v_s2mm[NCH - 1]), ("dve", sc.val("dve"))], inc="dve")
            v_nrm[oc] = sc.emit("dve", lambda e, oc=oc: e.tensor_mul(
                x1pre[:, oc, :], x1pre[:, oc, :], rb_sb[:, :]),
                [("act", ve_r), ("dve", vs)], inc="dve")
            v_x1sp_last = sc.dma("sp", "sL", x1sp[oc * 128:(oc + 1) * 128, t0:t0 + 512],
                                 x1pre[:, oc, :], waits=[("dve", v_nrm[oc])])
        v_x1_readers.append(("sL", v_x1sp_last))
        v_spill_tile[tt] = v_x1sp_last

        lbank = pb[4]
        for tc in range(4):
            g = tt * 4 + tc
            mms = [(lbank.t[0:128, 0:NE], x1pre[:, oc, tc * 128:(tc + 1) * 128], wr_sb[:, oc, :])
                   for oc in range(NCH)]
            _mm_group(sc, lbank, mms,
                      extra_waits=[("dve", v_nrm[NCH - 1]), ("gT", v_wr)])
            v_logit_evac = sc.emit("act", lambda e, g=g: e.copy(
                logits_sb[:, g, :], lbank.t[0:128, 0:NE]), [lbank.ready], inc="act")
            lbank.read_done("act", v_logit_evac)
            v_x1_readers.append(("pe", sc.val("pe")))

    tbank = pb[5]
    v_comb_evac = 0
    v_tp_prev = 0
    for g in range(NCH):
        c0 = g * 128
        vm = sc.emit("dve", lambda e, g=g: e.tensor_reduce(
            sc1_sb[:, 0:1], logits_sb[:, g, :], AX.X, ALU.max), [("act", v_logit_evac)], inc="dve")
        wls = [("dve", vm)]
        if v_tp_prev:
            wls.append(("pe", v_tp_prev))
        vls = sc.emit("dve", lambda e, g=g: e.tensor_single_scalar(
            pr_sb[:, :], logits_sb[:, g, :], sc1_sb[:, 0:1], ALU.subtract),
            wls, inc="dve")
        vex = sc.emit("act", lambda e: e.activation(pr_sb[:, :], pr_sb[:, :], AF.Exp),
                      [("dve", vls)], inc="act")
        vsm = sc.emit("dve", lambda e: e.tensor_reduce(
            sc1_sb[:, 1:2], pr_sb[:, :], AX.X, ALU.add), [("act", vex)], inc="dve")
        vrs = sc.emit("dve", lambda e: e.reciprocal(sc1_sb[:, 2:3], sc1_sb[:, 1:2]),
                      [("dve", vsm)], inc="dve")
        vpr = sc.emit("dve", lambda e: e.tensor_single_scalar(
            pr_sb[:, :], pr_sb[:, :], sc1_sb[:, 2:3], ALU.mult), [("dve", vrs)], inc="dve")
        vm1 = sc.emit("dve", lambda e: e.tensor_reduce(
            sc1_sb[:, 0:1], pr_sb[:, :], AX.X, ALU.max), [("dve", vpr)], inc="dve")
        veq = sc.emit("dve", lambda e: e.tensor_single_scalar(
            eq_sb[:, :], pr_sb[:, :], sc1_sb[:, 0:1], ALU.is_ge), [("dve", vm1)], inc="dve")
        vpm = sc.emit("dve", lambda e: e.scalar_tensor_tensor(
            pm_sb[:, :], eq_sb[:, :], -1.0e30, pr_sb[:, :], ALU.mult, ALU.add),
            [("dve", veq)] + CST, inc="dve")
        vm2 = sc.emit("dve", lambda e: e.tensor_reduce(
            sc1_sb[:, 3:4], pm_sb[:, :], AX.X, ALU.max), [("dve", vpm)], inc="dve")
        vge = sc.emit("dve", lambda e: e.tensor_single_scalar(
            eq_sb[:, :], pr_sb[:, :], sc1_sb[:, 3:4], ALU.is_ge), [("dve", vm2)], inc="dve")
        vcb = sc.emit("dve", lambda e: e.tensor_mul(pr_sb[:, :], pr_sb[:, :], eq_sb[:, :]),
                      [("dve", vge)], inc="dve")
        wtp = tbank.wait_free() + [("dve", vcb)]
        vt = sc.emit("pe", lambda e: e.matmul(
            tbank.t[0:NE, 0:128], pr_sb[:, :], ident_sb[:, :],
            start=True, stop=True), wtp, inc="pe")
        tbank.ready = ("pe", vt)
        tbank.free = {}
        v_tp_prev = vt
        v_comb_evac = sc.emit("act", lambda e, c0=c0: e.copy(
            comb_sb[:, c0:c0 + 128], tbank.t[0:NE, 0:128]), [tbank.ready], inc="act")
        tbank.read_done("act", v_comb_evac)

    wo_es.close()
    wo_outer.close()
    sc.barrier()

    if upto == "x1":
        v1 = sc.dma("gp", "gT", dbg["x1"][:, :], x1sp[:, :], waits=[("sL", v_x1sp_last)])
        v2 = sc.dma("sp", "sO0", dbg["comb"][:, :], comb_sb[:, :], waits=[("act", v_comb_evac)])
        _finish(nc, sc, sems, es, [("gT", v1), ("sO0", v2)])
        return nc

    ffn_es = ExitStack()
    y_acc = ffn_es.enter_context(nc.sbuf_tensor("y_acc", [128, NCH, 1024], F32))
    x1bh = ffn_es.enter_context(nc.sbuf_tensor("x1bh", [128, NCH, 1024], BF16))
    h_sb = ffn_es.enter_context(nc.sbuf_tensor("h_sb", [128, NDE, 1024], BF16))
    wg_sb = [ffn_es.enter_context(nc.sbuf_tensor(f"wg{i}", [128, NCH, 128], BF16)) for i in range(2)]
    wu_sb = [ffn_es.enter_context(nc.sbuf_tensor(f"wu{i}", [128, NCH, 128], BF16)) for i in range(2)]
    wd_sb = ffn_es.enter_context(nc.sbuf_tensor("wd_sb", [128, 6, NCH, 128], BF16))
    sg_sb = ffn_es.enter_context(nc.sbuf_tensor("sg_sb", [128, 2, 512], F32))
    cb_sb = ffn_es.enter_context(nc.sbuf_tensor("cb_sb", [128, 1024], F32))
    xq2 = ffn_es.enter_context(nc.sbuf_tensor("xq2", [128, 2, 512], F32))
    st2 = ffn_es.enter_context(nc.sbuf_tensor("st2", [1, 3 * 512], F32))
    mb2 = ffn_es.enter_context(nc.sbuf_tensor("mb2", [128, 512], F32))
    rb2 = ffn_es.enter_context(nc.sbuf_tensor("rb2", [128, 512], F32))
    x2t = [ffn_es.enter_context(nc.sbuf_tensor(f"x2t{i}", [128, 512], F32)) for i in range(2)]

    stages = [(wsg, wsu, wsd, 0, 0, 0, None), (wsg, wsu, wsd, 11, 0, 0, None)] + \
             [(weg, weu, wed, 0, e * D, e * DE, e) for e in range(NE)]
    JGRP = [(0, 6), (6, 11)]

    v_y_read_prev = []
    v_x1b_pe_prev = 0
    v_out_last = 0
    v_store = [0, 0]
    v_prev_s2 = 0
    v_prev_norm = 0
    for hf in range(2):
        t0h = hf * 1024
        v_ya0 = None
        for ts in range(2):
            g = hf * 2 + ts
            wldx = [("sL", v_spill_tile[g])]
            if v_x1b_pe_prev:
                wldx.append(("pe", v_x1b_pe_prev))
            sc.dma("gp", "gY", x1bh[:, :, ts * 512:(ts + 1) * 512],
                   x1sp[:, g * 512:(g + 1) * 512].rearrange("(c p) t -> p c t", p=128), waits=wldx)
            wldy = [("sL", v_spill_tile[g])] + v_y_read_prev
            v_ya0 = sc.dma("gp", "gY", y_acc[:, :, ts * 512:(ts + 1) * 512],
                           x1sp[:, g * 512:(g + 1) * 512].rearrange("(c p) t -> p c t", p=128), waits=wldy)

        v_h_read_pe = sc.val("pe")
        v_cb_read = sc.val("dve")
        v_wd_used = sc.val("pe")
        v_wg_used = [sc.val("pe"), sc.val("pe")]
        v_sg_used = [sc.val("dve"), sc.val("dve")]
        v_ya_last = sc.val("dve")
        for snum, (wgt, wut, wdt, co, rg_, rd_, ex) in enumerate(stages):
            if ex is not None:
                cbank = pb[6]
                v_cb = 0
                for ts in range(2):
                    _mm_group(sc, cbank, [(cbank.t[:, 0:512], sel8_sb[:, ex * 128:(ex + 1) * 128],
                                           comb_sb[:, t0h + ts * 512:t0h + (ts + 1) * 512])],
                              extra_waits=[("act", v_comb_evac)])
                    wcb = [cbank.ready, ("dve", v_cb_read)]
                    v_cb = sc.emit("act", lambda e, ts=ts, cbank=cbank: e.copy(
                        cb_sb[:, ts * 512:(ts + 1) * 512], cbank.t[:, 0:512]), wcb, inc="act")
                    cbank.read_done("act", v_cb)

            for g0, g1 in JGRP:
                v_h_w = 0
                for j in range(g0, g1):
                    jp = j % 2
                    wwl = [("pe", v_wg_used[jp])]
                    vwg = sc.dma("gp", f"gG{jp}", wg_sb[jp][:, :, :],
                                 wgt[rg_:rg_ + D, (co + j) * 128:(co + j + 1) * 128].rearrange("(c p) o -> p c o", p=128),
                                 waits=wwl)
                    vwu = sc.dma("gp", f"gG{jp}", wu_sb[jp][:, :, :],
                                 wut[rg_:rg_ + D, (co + j) * 128:(co + j + 1) * 128].rearrange("(c p) o -> p c o", p=128))
                    for ts in range(2):
                        gbank, ubank = pb[0 + ts], pb[2 + ts]
                        wg_ = [(f"gG{jp}", vwu), ("gY", v_ya0)]
                        _mm_group(sc, gbank, [(gbank.t[:, 0:512], wg_sb[jp][:, c, :],
                                               x1bh[:, c, ts * 512:(ts + 1) * 512]) for c in range(NCH)],
                                  extra_waits=wg_)
                        _mm_group(sc, ubank, [(ubank.t[:, 0:512], wu_sb[jp][:, c, :],
                                               x1bh[:, c, ts * 512:(ts + 1) * 512]) for c in range(NCH)])
                        v_wg_used[jp] = sc.val("pe")
                        wsl = [gbank.ready, ("dve", v_sg_used[ts])]
                        vsl = sc.emit("act", lambda e, ts=ts, gbank=gbank: e.activation(
                            sg_sb[:, ts, :], gbank.t[:, 0:512], AF.Silu), wsl, inc="act")
                        gbank.read_done("act", vsl)
                        whh = [("act", vsl), ubank.ready, ("pe", v_h_read_pe)]
                        if ex is not None:
                            vcbf = sc.emit("dve", lambda e, ts=ts: e.tensor_mul(
                                sg_sb[:, ts, :], sg_sb[:, ts, :], cb_sb[:, ts * 512:(ts + 1) * 512]),
                                [("act", vsl), ("act", v_cb), ("dve", sc.val("dve"))], inc="dve")
                            whh.append(("dve", vcbf))
                        vh = sc.emit("dve", lambda e, j=j, ts=ts, ubank=ubank: e.tensor_mul(
                            h_sb[:, j, ts * 512:(ts + 1) * 512], sg_sb[:, ts, :], ubank.t[:, 0:512]),
                            whh, inc="dve")
                        ubank.read_done("dve", vh)
                        v_sg_used[ts] = vh
                        v_h_w = vh
                v_wd = 0
                for j in range(g0, g1):
                    wwd = [("pe", v_wd_used)] if j == g0 else []
                    v_wd = sc.dma("gp", "gD", wd_sb[:, j - g0, :, :],
                                  wdt[rd_ + (co + j) * 128:rd_ + (co + j + 1) * 128, :].rearrange("p (c o) -> p c o", o=128),
                                  waits=wwd)
                for ts in range(2):
                    for oc in range(NCH):
                        ybank = pb[4 + ((oc + ts) % 2)]
                        wdn = [("dve", v_h_w), ("gD", v_wd), ("gY", v_ya0)]
                        _mm_group(sc, ybank, [(ybank.t[:, 0:512], wd_sb[:, j2 - g0, oc, :],
                                               h_sb[:, j2, ts * 512:(ts + 1) * 512]) for j2 in range(g0, g1)],
                                  extra_waits=wdn)
                        v_ya_last = sc.emit("dve", lambda e, oc=oc, ts=ts, ybank=ybank: e.tensor_add(
                            y_acc[:, oc, ts * 512:(ts + 1) * 512], ybank.t[:, 0:512],
                            y_acc[:, oc, ts * 512:(ts + 1) * 512]),
                            [ybank.ready, ("dve", sc.val("dve")), ("gY", v_ya0)], inc="dve")
                        ybank.read_done("dve", v_ya_last)
                v_wd_used = sc.val("pe")
            v_h_read_pe = sc.val("pe")
            v_cb_read = sc.val("dve")
        v_x1b_pe_prev = sc.val("pe")

        v_y_read_prev = []
        for ts in range(2):
            sbank, s2bank = pb[6], pb[7]
            v_sq2 = [0] * NCH
            v_m1 = [0] * NCH
            v_m2 = [0] * NCH

            for oc in range(NCH):
                wsq = [("dve", v_ya_last), ("pe", v_m2[oc - 2] if oc >= 2 else v_prev_s2)]
                v_sq2[oc] = sc.emit("act", lambda e, oc=oc, ts=ts: e.activation(
                    xq2[:, oc % 2, :], y_acc[:, oc, ts * 512:(ts + 1) * 512], AF.Square),
                    wsq, inc="act")
                w1 = [("dve", v_ya_last)]
                if oc == 0:
                    w1 += sbank.wait_free() + CST
                v_m1[oc] = sc.emit("pe", lambda e, oc=oc, ts=ts, st=(oc == 0), sp_=(oc == NCH - 1): e.matmul(
                    sbank.t[0:1, 0:512], ones128_sb[:, :], y_acc[:, oc, ts * 512:(ts + 1) * 512],
                    start=st, stop=sp_), w1, inc="pe")
                w2 = [("act", v_sq2[oc])]
                if oc == 0:
                    w2 += s2bank.wait_free()
                v_m2[oc] = sc.emit("pe", lambda e, oc=oc, st=(oc == 0), sp_=(oc == NCH - 1): e.matmul(
                    s2bank.t[0:1, 0:512], ones128_sb[:, :], xq2[:, oc % 2, :],
                    start=st, stop=sp_), w2, inc="pe")
            sbank.ready = ("pe", v_m1[NCH - 1])
            sbank.free = {}
            s2bank.ready = ("pe", v_m2[NCH - 1])
            s2bank.free = {}
            v_prev_s2 = v_m2[NCH - 1]

            va_m = sc.emit("act", lambda e: e.activation(st2[0:1, 0:512], sbank.t[0:1, 0:512],
                                                         AF.Copy, scale=invD[0:1, 0:1]),
                           [sbank.ready], inc="act")
            va_e2 = sc.emit("act", lambda e: e.activation(st2[0:1, 512:1024], s2bank.t[0:1, 0:512],
                                                          AF.Copy, scale=invD[0:1, 0:1]),
                            [s2bank.ready], inc="act")
            sbank.read_done("act", va_m)
            s2bank.read_done("act", va_e2)
            vd_mm = sc.emit("dve", lambda e: e.tensor_mul(st2[0:1, 1024:1536], st2[0:1, 0:512], st2[0:1, 0:512]),
                            [("act", va_m), ("dve", sc.val("dve"))], inc="dve")
            vd_var = sc.emit("dve", lambda e: e.tensor_sub(st2[0:1, 512:1024], st2[0:1, 512:1024], st2[0:1, 1024:1536]),
                             [("act", va_e2), ("dve", vd_mm)], inc="dve")
            va_ln = sc.emit("act", lambda e: e.activation(st2[0:1, 1024:1536], st2[0:1, 512:1024],
                                                          AF.Ln, bias=epsT[0:1, 0:1]),
                            [("dve", vd_var)], inc="act")
            va_rstd = sc.emit("act", lambda e: e.activation(st2[0:1, 512:1024], st2[0:1, 1024:1536],
                                                            AF.Exp, scale=half_neg[0:1, 0:1]),
                              [("act", va_ln)], inc="act")
            mbank, rbank = pb[2], pb[3]
            _mm_group(sc, mbank, [(mbank.t[:, 0:512], ones1_sb[:, :], st2[0:1, 0:512])],
                      extra_waits=[("act", va_m)])
            _mm_group(sc, rbank, [(rbank.t[:, 0:512], ones1_sb[:, :], st2[0:1, 512:1024])],
                      extra_waits=[("act", va_rstd)])
            ve_m = sc.emit("act", lambda e: e.copy(mb2[:, :], mbank.t[:, 0:512]),
                           [mbank.ready, ("dve", v_prev_norm)], inc="act")
            ve_r = sc.emit("act", lambda e: e.copy(rb2[:, :], rbank.t[:, 0:512]),
                           [rbank.ready, ("dve", v_prev_norm)], inc="act")
            mbank.read_done("act", ve_m)
            rbank.read_done("act", ve_r)

            for oc in range(NCH):
                op = oc % 2
                wn = [("act", ve_m), ("act", ve_r), ("dve", sc.val("dve"))]
                if v_store[op]:
                    wn.append((f"sO{op}", v_store[op]))
                vs = sc.emit("dve", lambda e, oc=oc, ts=ts, op=op: e.tensor_sub(
                    x2t[op][:, :], y_acc[:, oc, ts * 512:(ts + 1) * 512], mb2[:, :]), wn, inc="dve")
                vn = sc.emit("dve", lambda e, op=op: e.tensor_mul(
                    x2t[op][:, :], x2t[op][:, :], rb2[:, :]), [("dve", vs)], inc="dve")
                v_out_last = sc.dma("sp", f"sO{op}", outfm[oc * 128:(oc + 1) * 128, t0h + ts * 512:t0h + (ts + 1) * 512],
                                    x2t[op][:, :], waits=[("dve", vn)])
                v_store[op] = v_out_last
                v_y_read_prev.append(("dve", vn))
                v_prev_norm = vn
            v_y_read_prev.append(("pe", sc.val("pe")))
            v_y_read_prev.append(("act", v_sq2[NCH - 1]))

    ffn_es.close()
    _finish(nc, sc, sems, es, [("sO0", sc.val("sO0")), ("sO1", sc.val("sO1"))])
    return nc


def _finish(nc, sc, sems, es, final_waits):
    sc.emit("sp", lambda e: e.nop(), waits=list(final_waits))
    for s in SEMS:
        assert sc.cnt.get(s, 0) < 60000, (s, sc.cnt[s])
    with nc.Block() as block:
        def runner(name):
            def _r(eng):
                hw = {}
                for waits, fn, inc, amt in sc.q[name]:
                    for sem, v in waits:
                        if v and v > hw.get(sem, 0):
                            eng.wait_ge(sems[sem], v)
                            hw[sem] = v
                    ins = fn(eng)
                    if inc is not None:
                        ins.then_inc(sems[inc], amt)
            return _r
        block.tensor(runner("pe"))
        block.scalar(runner("act"))
        block.vector(runner("dve"))
        block.sync(runner("sp"))
        block.gpsimd(runner("gp"))
    es.close()


def _rope_tables():
    idx = np.arange(0, DH, 2, dtype=np.float32)
    inv_freq = (np.float32(10000.0) ** (-idx / np.float32(DH))).astype(np.float32)
    angles = (np.arange(S, dtype=np.float32)[:, None] * inv_freq[None, :]).astype(np.float32)
    cos = np.cos(angles).astype(np.float32)
    sin = np.sin(angles).astype(np.float32)
    cos_e = np.concatenate([cos.T, cos.T], axis=0)
    sin_s = np.concatenate([-sin.T, sin.T], axis=0)
    scl = np.float32(1.0 / np.sqrt(np.float32(DH)))
    return (np.ascontiguousarray(cos_e * scl), np.ascontiguousarray(sin_s * scl),
            np.ascontiguousarray(cos_e), np.ascontiguousarray(sin_s))


def _stage(inputs):
    x = np.asarray(inputs["L0_attn_x"], np.float32)
    wq = np.asarray(inputs["L0_attn_Wq"], np.float32)
    wk = np.asarray(inputs["L0_attn_Wk"], np.float32)
    wv = np.asarray(inputs["L0_attn_Wv"], np.float32)
    wo = np.asarray(inputs["L0_attn_Wo"], np.float32)

    perm = np.concatenate([np.arange(0, DH, 2), np.arange(1, DH, 2)])
    wqT, wkT, wvT = wq.T, wk.T, wv.T
    woT = np.ascontiguousarray(wo.T)
    cq, sq, ck, sk = _rope_tables()

    tri = np.where(np.arange(128)[:, None] <= np.arange(128)[None, :], 0.0, MASKVAL).astype(np.float32)
    identm = np.eye(128, dtype=np.float32)
    swpm = np.zeros((128, 128), dtype=np.float32)
    for i in range(128):
        swpm[(i + 64) % 128, i] = 1.0
    sel8 = np.zeros((8, 8 * 128), dtype=np.float32)
    for e in range(8):
        sel8[e, e * 128:(e + 1) * 128] = 1.0
    ones1 = np.ones((1, 128), dtype=np.float32)

    shared = {
        "woT": woT, "cosq": cq, "sinq": sq, "cosk": ck, "sink": sk,
        "tri": tri, "ident": identm, "swp": swpm, "sel8": sel8, "ones1": ones1,
        "wrouter": np.ascontiguousarray(inputs["L0_ffn_W_router"], dtype=np.float32),
        "wsg": np.ascontiguousarray(inputs["L0_ffn_Ws_gate"], dtype=np.float32),
        "wsu": np.ascontiguousarray(inputs["L0_ffn_Ws_up"], dtype=np.float32),
        "wsd": np.ascontiguousarray(inputs["L0_ffn_Ws_down"], dtype=np.float32),
        "weg": np.ascontiguousarray(np.asarray(inputs["L0_ffn_We_gate"], np.float32).reshape(NE * D, DE)),
        "weu": np.ascontiguousarray(np.asarray(inputs["L0_ffn_We_up"], np.float32).reshape(NE * D, DE)),
        "wed": np.ascontiguousarray(np.asarray(inputs["L0_ffn_We_down"], np.float32).reshape(NE * DE, D)),
    }

    xTs = [np.ascontiguousarray(x[b].T) for b in range(B)]
    in_maps = []
    for c in range(8):
        b, r = c // 4, c % 4
        heads = range(4 * r, 4 * r + 4)
        m = dict(shared)
        m["xT"] = xTs[b]
        m["xres"] = np.ascontiguousarray(xTs[b][:, r * T:(r + 1) * T])
        m["wqT"] = np.ascontiguousarray(np.concatenate(
            [wqT[:, h * DH:(h + 1) * DH][:, perm] for h in heads], axis=1))
        m["wkT"] = np.ascontiguousarray(np.concatenate(
            [wkT[:, h * DH:(h + 1) * DH][:, perm] for h in heads], axis=1))
        m["wvT"] = np.ascontiguousarray(np.concatenate(
            [wvT[:, h * DH:(h + 1) * DH] for h in heads], axis=1))
        m["roff"] = np.array([[r * T * DH]], dtype=np.int32)
        in_maps.append(m)
    return in_maps


def kernel(**inputs):
    nc = _build("full")
    in_maps = _stage(inputs)
    res = run_bass_kernel_spmd(nc, in_maps, core_ids=list(range(8)))
    out = np.empty((B, S, D), dtype=np.float32)
    for c in range(8):
        b, r = c // 4, c % 4
        out[b, r * T:(r + 1) * T, :] = np.asarray(res.results[c]["outfm"]).T
    return out


# version 04
# speedup vs baseline: 1.0189x; 1.0189x over previous
import numpy as np
import ml_dtypes
from contextlib import ExitStack

import concourse.bass as bass
import concourse.mybir as mybir
from concourse.bass_utils import run_bass_kernel_spmd

F32 = mybir.dt.float32
BF16 = mybir.dt.bfloat16
I32 = mybir.dt.int32
AF = mybir.ActivationFunctionType
ALU = mybir.AluOpType
AX = mybir.AxisListType

B, S, D, H, DH = 2, 8192, 2048, 16, 128
HPC = 4
T = 2048
NCH = D // 128
DS, NDS = 2816, 22
DE, NDE = 1408, 11
NE = 8
NTT = S // 512
CSHIFT = -12.0
MASKVAL = -60.0
EPS = 1e-5

ENGS = ("pe", "act", "dve", "sp", "gp")
SEMS = ("pe", "act", "dve", "gpc", "cc",
        "gT", "gX0", "gX1", "gW0", "gW1", "gWo0", "gWo1", "gY",
        "gG0", "gG1", "gD", "gP0", "gP1", "gB0", "gB1",
        "sS0", "sS1", "sR0", "sR1", "sL", "sT0", "sT1", "sT2", "sT3", "sO0", "sO1")


class _Sched:

    def __init__(self):
        self.q = {e: [] for e in ENGS}
        self.cnt = {s: 0 for s in SEMS}
        self.pending = {e: [] for e in ENGS}

    def val(self, sem):
        return self.cnt[sem]

    SELF_SERIAL = ()

    def emit(self, eng, fn, waits=(), inc=None, amt=1):
        w = list(self.pending[eng]) + list(waits)
        self.pending[eng] = []
        if eng in self.SELF_SERIAL:
            if self.cnt[eng]:
                w.append((eng, self.cnt[eng]))
            if inc is None:
                inc = eng
        v = None
        if inc is not None:
            self.cnt[inc] += amt
            v = self.cnt[inc]
        self.q[eng].append((tuple(w), fn, inc, amt))
        return v

    def dma(self, eng, sem, out_ap, in_ap, waits=()):
        return self.emit(eng, lambda e, o=out_ap, i=in_ap: e.dma_start(o, i),
                         waits, inc=sem, amt=16)

    def barrier(self, exclude=()):
        for e in ENGS:
            if not self.q[e]:
                continue
            w, fn, inc, amt = self.q[e][-1]
            if inc is None:
                sem = e if e in ("pe", "act", "dve") else "gpc"
                assert e != "sp", "sp barrier op must carry an inc"
                self.cnt[sem] = self.cnt.get(sem, 0) + 1
                self.q[e][-1] = (w, fn, sem, 1)
        marks = [(s, self.cnt[s]) for s in SEMS
                 if self.cnt.get(s, 0) > 0 and s not in exclude]
        for e in ENGS:
            self.pending[e] = list(marks)


class _PB:

    def __init__(self, tile):
        self.t = tile
        self.ready = None
        self.free = {}

    def wait_free(self):
        return list(self.free.items())

    def read_done(self, sem, val):
        self.free[sem] = max(self.free.get(sem, 0), val)


def _mm_group(sc, pb, mms, extra_waits=(), lookahead_ok=True):
    waits = pb.wait_free() + list(extra_waits)
    n = len(mms)
    v = None
    for i, (o, l, r) in enumerate(mms):
        st, sp_ = (i == 0), (i == n - 1)
        v = sc.emit(
            "pe",
            lambda e, o=o, l=l, r=r, st=st, sp_=sp_: e.matmul(o, l, r, start=st, stop=sp_),
            waits if i == 0 else (),
            inc="pe" if sp_ else None,
        )
    pb.ready = ("pe", v)
    pb.free = {}
    return v


def _build(upto="full"):
    nc = bass.Bass("TRN2", target_bir_lowering=False)
    sc = _Sched()

    dt_in = {}
    def din(name, shape, dt=F32):
        dt_in[name] = nc.dram_tensor(name, list(shape), dt, kind="ExternalInput")
        return dt_in[name]

    xT = din("xT", (D, S))
    xres = din("xres", (D, T))
    wqT = din("wqT", (D, HPC * DH))
    wkT = din("wkT", (D, HPC * DH))
    wvT = din("wvT", (D, HPC * DH))
    woT = din("woT", (D, D))
    cosq = din("cosq", (128, S))
    sinq = din("sinq", (128, S))
    cosk = din("cosk", (128, S))
    sink = din("sink", (128, S))
    tri = din("tri", (128, 128))
    sel8 = din("sel8", (8, 8 * 128))
    ident = din("ident", (128, 128))
    swp = din("swp", (128, 128))
    ones1 = din("ones1", (1, 128))
    roff = din("roff", (1, 1), I32)
    wrouter = din("wrouter", (D, NE))
    wsg = din("wsg", (D, DS))
    wsu = din("wsu", (D, DS))
    wsd = din("wsd", (DS, D))
    weg = din("weg", (NE * D, DE))
    weu = din("weu", (NE * D, DE))
    wed = din("wed", (NE * DE, D))

    outfm = nc.dram_tensor("outfm", [D, T], F32, kind="ExternalOutput")

    cc_in_h = [nc.dram_tensor(f"cc_in{h}", [S, DH], BF16, kind="Internal") for h in range(HPC)]
    cc_ag_h = [nc.dram_tensor(f"cc_ag{h}", [4 * S, DH], BF16, kind="Internal") for h in range(HPC)]
    cc_my = nc.dram_tensor("cc_my", [T, D], BF16, kind="Internal")
    x1sp = nc.dram_tensor("x1sp", [D, T], F32, kind="Internal")

    dbg = {}
    if upto == "qkv":
        dbg["q"] = nc.dram_tensor("dbg_q", [128, S], BF16, kind="ExternalOutput")
        dbg["k"] = nc.dram_tensor("dbg_k", [128, S], BF16, kind="ExternalOutput")
        dbg["v"] = nc.dram_tensor("dbg_v", [128, 64 * 132], BF16, kind="ExternalOutput")
        dbg["cos"] = nc.dram_tensor("dbg_cos", [128, S], BF16, kind="ExternalOutput")
        dbg["sin"] = nc.dram_tensor("dbg_sin", [128, S], BF16, kind="ExternalOutput")
        dbg["qtmp"] = nc.dram_tensor("dbg_qtmp", [128, 512], BF16, kind="ExternalOutput")
        dbg["qsw"] = nc.dram_tensor("dbg_qsw", [128, 512], BF16, kind="ExternalOutput")
    if upto == "attn":
        dbg["attn"] = nc.dram_tensor("dbg_attn", [S, HPC * DH], BF16, kind="ExternalOutput")
    if upto == "x1":
        dbg["x1"] = nc.dram_tensor("dbg_x1", [D, T], F32, kind="ExternalOutput")
        dbg["comb"] = nc.dram_tensor("dbg_comb", [8, T], F32, kind="ExternalOutput")

    es = ExitStack()
    sems = {s: es.enter_context(nc.semaphore(s)) for s in SEMS}

    pb = [_PB(es.enter_context(nc.psum_tensor(f"pb{i}", [128, 512], F32))) for i in range(8)]

    ones1_sb = es.enter_context(nc.sbuf_tensor("ones1_sb", [1, 128], F32))
    ones128_sb = es.enter_context(nc.sbuf_tensor("ones128_sb", [128, 1], F32))
    biasC = es.enter_context(nc.sbuf_tensor("biasC", [128, 1], F32))
    epsT = es.enter_context(nc.sbuf_tensor("epsT", [1, 1], F32))
    half_neg = es.enter_context(nc.sbuf_tensor("half_neg", [1, 1], F32))
    neginf8 = es.enter_context(nc.sbuf_tensor("neginf8", [128, 8], F32))
    invD = es.enter_context(nc.sbuf_tensor("invD", [1, 1], F32))
    tri_sb = es.enter_context(nc.sbuf_tensor("tri_sb", [128, 128], F32))
    sel8_sb = es.enter_context(nc.sbuf_tensor("sel8_sb", [8, 8 * 128], F32))
    ident_sb = es.enter_context(nc.sbuf_tensor("ident_sb", [128, 128], F32))
    swp_sb = es.enter_context(nc.sbuf_tensor("swp_sb", [128, 128], BF16))
    roff_sb = es.enter_context(nc.sbuf_tensor("roff_sb", [1, 1], I32))
    comb_sb = es.enter_context(nc.sbuf_tensor("comb_sb", [8, T], F32))
    rg = nc.gpsimd.alloc_register("rg_off")
    rg2 = nc.gpsimd.alloc_register("rg_off2")

    sc.dma("gp", "gT", ones1_sb[:, :], ones1[:, :])
    sc.dma("gp", "gT", tri_sb[:, :], tri[:, :])
    sc.dma("gp", "gT", sel8_sb[:, :], sel8[:, :])
    sc.dma("gp", "gT", ident_sb[:, :], ident[:, :])
    sc.dma("gp", "gT", swp_sb[:, :], swp[:, :])
    v_cst_dma = sc.dma("gp", "gT", roff_sb[:, :], roff[:, :])
    v_tri = v_cst_dma
    sc.emit("gp", lambda e: e.memset(ones128_sb[:, :], 1.0), inc="gpc")
    sc.emit("gp", lambda e: e.memset(biasC[:, :], CSHIFT), inc="gpc")
    sc.emit("gp", lambda e: e.memset(epsT[:, :], EPS), inc="gpc")
    sc.emit("gp", lambda e: e.memset(half_neg[:, :], -0.5), inc="gpc")
    sc.emit("gp", lambda e: e.memset(neginf8[:, :], -1.0e30), inc="gpc")
    v_cst = sc.emit("gp", lambda e: e.memset(invD[:, :], 1.0 / D), inc="gpc")
    CST = [("gT", v_cst_dma), ("gpc", v_cst)]

    wo_outer = ExitStack()
    attn_fm = wo_outer.enter_context(nc.sbuf_tensor("attn_fm", [128, NCH, T], BF16))
    attn_es = ExitStack()
    tb_sb = [attn_es.enter_context(nc.sbuf_tensor(f"tb{i}", [128, 4, 512], BF16)) for i in range(2)]
    TAB = []

    v_tb_load = [0, 0]
    v_tb_used = [0, 0]
    qbf = attn_es.enter_context(nc.sbuf_tensor("qbf", [128, S], BF16))
    kbf = attn_es.enter_context(nc.sbuf_tensor("kbf", [128, S], BF16))
    vsb = attn_es.enter_context(nc.sbuf_tensor("vsb", [128, 64, 132], BF16))
    xt_sb = [attn_es.enter_context(nc.sbuf_tensor(f"xt{i}", [128, NCH, 512], BF16)) for i in range(2)]
    wq_sb = [attn_es.enter_context(nc.sbuf_tensor(f"wq_sb{i}", [128, NCH, DH], BF16)) for i in range(2)]
    wk_sb = [attn_es.enter_context(nc.sbuf_tensor(f"wk_sb{i}", [128, NCH, DH], BF16)) for i in range(2)]
    wv_sb = [attn_es.enter_context(nc.sbuf_tensor(f"wv_sb{i}", [128, NCH, DH], BF16)) for i in range(2)]
    qtmp = attn_es.enter_context(nc.sbuf_tensor("qtmp", [128, 512], BF16))
    qsw = attn_es.enter_context(nc.sbuf_tensor("qsw", [128, 512], BF16))
    ktmp = attn_es.enter_context(nc.sbuf_tensor("ktmp", [128, 512], BF16))
    ksw = attn_es.enter_context(nc.sbuf_tensor("ksw", [128, 512], BF16))
    rope_t1 = attn_es.enter_context(nc.sbuf_tensor("rope_t1", [128, 512], BF16))
    rope_t2 = attn_es.enter_context(nc.sbuf_tensor("rope_t2", [128, 512], BF16))
    probs = [attn_es.enter_context(nc.sbuf_tensor(f"probs{i}", [128, 512], BF16)) for i in range(3)]
    ao_sb = [attn_es.enter_context(nc.sbuf_tensor(f"ao{i}", [128, 4, DH], BF16)) for i in range(2)]
    rc_sb = attn_es.enter_context(nc.sbuf_tensor("rc_sb", [128, 4], F32))

    v_xt_load = [0, 0]
    v_xt_used = [None, None]
    v_ao_dma = [0, 0]

    last_v1 = {}
    last_v2 = {}
    last_v3 = [0]
    v_rc_read = [0, 0, 0, 0]

    v_proj_end = [0, 0]
    v_w_loaded = [0, 0]
    v_pick_h = [0] * HPC
    v_tp_h = [0] * HPC
    rg_loaded = [False]
    pend_cc = [None]

    def emit_cc(h):
        if h >= 1:
            emit_picks(h - 1)
        sc.emit("gp", lambda e, h=h: e.collective_compute(
            "AllGather", ALU.bypass, replica_groups=[[0, 1, 2, 3], [4, 5, 6, 7]],
            ins=[cc_in_h[h][:, :]], outs=[cc_ag_h[h][:, :]]),
            waits=[("sS0", sc.val("sS0")), ("sS1", sc.val("sS1"))], inc="cc")
        if h >= 2:
            emit_tp(h - 2)

    def emit_picks(h):
        if not rg_loaded[0]:
            sc.emit("gp", lambda e: e.reg_load(rg, roff_sb[0:1, 0:1]),
                    waits=[("cc", 1), ("gT", v_cst_dma)])
            rg_loaded[0] = True
        v = None
        for j in range(4):
            w0 = [("cc", h + 1)] if j == 0 else []
            if j == 0 and h >= 2:
                w0.append((f"sT{h - 2}", v_tp_h[h - 2]))
            sc.emit("gp", lambda e, j=j: e.reg_add(rg2, rg, j * S * DH), waits=w0)
            v = sc.emit(
                "gp", lambda e, j=j, h=h: e.dma_start(
                    bass.AP(cc_my, j * (HPC * DH) + h * DH, [[D, T], [1, DH]]),
                    bass.AP(cc_ag_h[h], rg2, [[DH, T], [1, DH]])),
                inc=f"gP{h % 2}", amt=16)
        v_pick_h[h] = v

    def emit_tp(h):
        v = None
        for i, fc in enumerate(range(h, NCH, 4)):
            v = sc.emit("sp", lambda e, fc=fc: e.dma_start(
                attn_fm[:, fc, :], cc_my[:, fc * 128:(fc + 1) * 128], transpose=True),
                waits=[(f"gP{h % 2}", v_pick_h[h])] if i == 0 else (), inc=f"sT{h}", amt=16)
        v_tp_h[h] = v
    for hh in range(HPC + 1):
        if hh < HPC:
            hp_ = hh % 2
            vw = None
            for wsb, wt in ((wq_sb[hp_], wqT), (wk_sb[hp_], wkT), (wv_sb[hp_], wvT)):
                vw = sc.dma("gp", f"gW{hp_}", wsb[:, :, :],
                            wt[:, hh * DH:(hh + 1) * DH].rearrange("(c p) o -> p c o", p=128),
                            waits=[("pe", v_proj_end[hp_])] if v_proj_end[hp_] else [])
            v_w_loaded[hp_] = vw
        if hh == 0:
            continue
        h = hh - 1
        hp = h % 2
        v_prev_pe = sc.val("pe")
        W = [(f"gW{hp}", v_w_loaded[hp])]

        vone = sc.emit("gp", lambda e: e.memset(vsb[:, :, 128:132], 1.0),
                       waits=[("pe", sc.val("pe"))], inc="gpc")

        v_q_done = [0] * NTT
        v_k_done = [0] * NTT
        v_v_done = [0] * NTT
        for tt in range(NTT):
            bi = tt % 2
            t0 = tt * 512
            w = [("pe", v_xt_used[bi])] if v_xt_used[bi] else []
            v_xt_load[bi] = sc.dma(
                "gp", f"gX{bi}", xt_sb[bi][:, :, :],
                xT[:, t0:t0 + 512].rearrange("(c p) t -> p c t", p=128), waits=w)
            wtb = [("dve", v_tb_used[bi])] if v_tb_used[bi] else []
            for ti, tsrc in enumerate((cosq, sinq, cosk, sink)):
                v_tb_load[bi] = sc.dma("gp", f"gB{bi}", tb_sb[bi][:, ti, :],
                                       tsrc[:, t0:t0 + 512], waits=wtb if ti == 0 else ())
            if tt == 1 and pend_cc[0] is not None and upto != "qkv":
                emit_cc(pend_cc[0])
                pend_cc[0] = None

            pq, pk, pv = pb[0 + (tt % 2)], pb[2 + (tt % 2)], pb[4 + (tt % 2)]
            ld = [(f"gX{bi}", v_xt_load[bi])] + W + TAB + CST
            _mm_group(sc, pq, [(pq.t[:, 0:512], wq_sb[hp][:, c, :], xt_sb[bi][:, c, :])
                              for c in range(NCH)], extra_waits=ld)
            _mm_group(sc, pk, [(pk.t[:, 0:512], wk_sb[hp][:, c, :], xt_sb[bi][:, c, :])
                              for c in range(NCH)], extra_waits=ld)
            vmms = []
            for sub in range(4):
                for c in range(NCH):
                    vmms.append((pv.t[:, sub * 128:(sub + 1) * 128],
                                 xt_sb[bi][:, c, sub * 128:(sub + 1) * 128],
                                 wv_sb[hp][:, c, :]))
            vstop = None
            wv0 = pv.wait_free() + ld
            for i, (o, l, r) in enumerate(vmms):
                st = (i % NCH == 0)
                sp_ = (i % NCH == NCH - 1)
                vstop = sc.emit(
                    "pe", lambda e, o=o, l=l, r=r, st=st, sp_=sp_: e.matmul(o, l, r, start=st, stop=sp_),
                    wv0 if i == 0 else (), inc="pe" if sp_ else None)
            pv.ready = ("pe", vstop)
            pv.free = {}
            v_xt_used[bi] = sc.val("pe")

            def rope(ptile, tmp, sw, ct, st_, dest, t0, bi):
                wva = [ptile.ready] + CST
                tk = id(tmp)
                if tk in last_v1:
                    wva.append(("dve", last_v1[tk]))
                va = sc.emit("act", lambda e: e.copy(tmp[:, :], ptile.t[:, 0:512]),
                             waits=wva, inc="act")
                ptile.read_done("act", va)
                vswp = _mm_group(sc, ptile, [(ptile.t[:, 0:512], swp_sb[:, :], tmp[:, :])],
                                 extra_waits=[("act", va)])
                wsw = [ptile.ready]
                if tk in last_v2:
                    wsw.append(("dve", last_v2[tk]))
                va2 = sc.emit("act", lambda e: e.copy(sw[:, :], ptile.t[:, 0:512]),
                              waits=wsw, inc="act")
                ptile.read_done("act", va2)
                v1 = sc.emit("dve", lambda e: e.tensor_mul(rope_t1[:, :], tmp[:, :], ct),
                             waits=[("act", va), ("dve", last_v3[0]), (f"gB{bi}", v_tb_load[bi])], inc="dve")
                v2 = sc.emit("dve", lambda e: e.tensor_mul(rope_t2[:, :], sw[:, :], st_),
                             waits=[("act", va2)], inc="dve")
                v3 = sc.emit("dve", lambda e, t0=t0: e.tensor_add(dest[:, t0:t0 + 512], rope_t1[:, :], rope_t2[:, :]),
                             waits=[("pe", v_prev_pe), ("dve", v2)], inc="dve")
                last_v1[tk] = v1
                last_v2[tk] = v2
                last_v3[0] = v3
                return v3

            v_q_done[tt] = rope(pq, qtmp, qsw, tb_sb[bi][:, 0, :], tb_sb[bi][:, 1, :], qbf, t0, bi)
            v_k_done[tt] = rope(pk, ktmp, ksw, tb_sb[bi][:, 2, :], tb_sb[bi][:, 3, :], kbf, t0, bi)
            v_tb_used[bi] = v_k_done[tt]
            vv = None
            for sub in range(4):
                vv = sc.emit("act",
                             lambda e, sub=sub, tt=tt, pvt=pv.t: e.copy(
                                 vsb[:, tt * 4 + sub, 0:128], pvt[:, sub * 128:(sub + 1) * 128]),
                             waits=[pv.ready, ("gpc", vone)], inc="act")
            pv.read_done("act", vv)
            v_v_done[tt] = vv
        v_proj_end[hp] = sc.val("pe")

        if upto == "qkv" and h == 0:
            w = [("dve", max(v_q_done)), ("dve", max(v_k_done)), ("act", max(v_v_done))]
            sc.dma("gp", "gT", dbg["q"][:, :], qbf[:, :], waits=w)
            sc.dma("gp", "gT", dbg["k"][:, :], kbf[:, :], waits=w)
            sc.dma("gp", "gT", dbg["v"][:, :], vsb[:, :, :].rearrange("p a b -> p (a b)"), waits=w)
            sc.dma("gp", "gT", dbg["cos"][:, :], tabs["cosq"][:, :], waits=w)
            sc.dma("gp", "gT", dbg["sin"][:, :], tabs["sinq"][:, :], waits=w)
            sc.dma("gp", "gT", dbg["qtmp"][:, :], qtmp[:, :], waits=w)
            vv_ = sc.dma("gp", "gT", dbg["qsw"][:, :], qsw[:, :], waits=w)
            attn_es.close()
            _finish(nc, sc, sems, es, [("gT", vv_)])
            return nc

        for qt in range(NTT):
            q0 = qt * 512
            nkc = qt * 4 + 4
            v_score = [0] * nkc
            v_exp = [0] * nkc
            v_pv = [0] * nkc
            qstart = [0] * 4
            v_pv_stop = [0] * 4

            out_pb = [pb[2], pb[3], pb[4], pb[5]]
            sco_pb = [pb[0], pb[1], pb[6]]

            def emit_score(kc):
                bank = sco_pb[kc % 3]
                dq = kc - qt * 4
                qlo = max(0, dq) * 128
                w = bank.wait_free()
                w += [("dve", v_k_done[kc // 4]), ("dve", v_q_done[qt])]
                v = sc.emit("pe", lambda e, kc=kc, qlo=qlo, bank=bank, q0=q0: e.matmul(
                    bank.t[:, 0:512 - qlo], kbf[:, kc * 128:(kc + 1) * 128],
                    qbf[:, q0 + qlo:q0 + 512], start=True, stop=True),
                    w, inc="pe")
                bank.ready = ("pe", v)
                bank.free = {}
                v_score[kc] = v
                if dq >= 0:
                    vm = sc.emit("dve", lambda e, bank=bank: e.tensor_add(
                        bank.t[:, 0:128], bank.t[:, 0:128], tri_sb[:, :]),
                        [("pe", v), ("gT", v_cst_dma)], inc="dve")
                    bank.ready = ("dve", vm)
                wexp = [bank.ready]
                if kc >= 3 and v_pv[kc - 3]:
                    wexp.append(("pe", v_pv[kc - 3]))
                ve = sc.emit("act", lambda e, kc=kc, qlo=qlo, bank=bank: e.activation(
                    probs[kc % 3][:, 0:512 - qlo], bank.t[:, 0:512 - qlo], AF.Exp,
                    bias=biasC[:, 0:1]), wexp, inc="act")
                bank.read_done("act", ve)
                v_exp[kc] = ve

            def emit_pv(kc):
                dq = kc - qt * 4
                qlo = max(0, dq) * 128
                last = None
                for qc in range(qlo // 128, 4):
                    bank = out_pb[qc]
                    qcg = qt * 4 + qc
                    st = (kc == 0)
                    sp_ = (kc == qcg)
                    w = [("act", v_exp[kc])]
                    if st:
                        w += bank.wait_free()
                    last = sc.emit("pe", lambda e, kc=kc, qc=qc, qlo=qlo, bank=bank, st=st, sp_=sp_: e.matmul(
                        bank.t[:, 0:129],
                        probs[kc % 3][:, qc * 128 - qlo:(qc + 1) * 128 - qlo],
                        vsb[:, kc, 0:129], start=st, stop=sp_),
                        w, inc="pe" if (sp_ or qc == 3) else None)
                    if st:
                        bank.free = {}
                    if sp_:
                        v_pv_stop[qc] = last
                        bank.ready = ("pe", last)
                v_pv[kc] = last

            emit_score(0)
            for kc in range(1, nkc):
                emit_score(kc)
                emit_pv(kc - 1)
            emit_pv(nkc - 1)

            vcp = None
            for qc in range(4):
                bank = out_pb[qc]
                wr_ = [("pe", v_pv_stop[qc]), ("dve", sc.val("dve"))]
                vr = sc.emit("dve", lambda e, qc=qc, bank=bank: e.reciprocal(
                    rc_sb[:, qc:qc + 1], bank.t[:, 128:129]),
                    wr_, inc="dve")
                wao = [("dve", vr)]
                if v_ao_dma[qt % 2]:
                    wao.append((f"sS{qt % 2}", v_ao_dma[qt % 2]))
                vcp = sc.emit("dve", lambda e, qc=qc, bank=bank, qt=qt: e.tensor_scalar(
                    ao_sb[qt % 2][:, qc, :], bank.t[:, 0:128], rc_sb[:, qc:qc + 1], None,
                    ALU.mult), wao, inc="dve")
                bank.read_done("dve", vcp)
                v_rc_read[qc] = vcp
            v_ao_dma[qt % 2] = sc.dma(
                "sp", f"sS{qt % 2}", cc_in_h[h][q0:q0 + 512, :].rearrange("(qc p) f -> p qc f", p=128),
                ao_sb[qt % 2][:, :, :], waits=[("dve", vcp)])

        pend_cc[0] = h

    if upto != "qkv":
        emit_cc(pend_cc[0])
        pend_cc[0] = None
    if upto not in ("qkv", "attn"):
        emit_picks(HPC - 1)
        emit_tp(HPC - 2)
        emit_tp(HPC - 1)

    v_attn_pe_end = sc.val("pe")
    attn_es.close()
    sc.barrier(exclude=("cc", "gP0", "gP1", "sT0", "sT1", "sT2", "sT3"))

    if upto == "attn":
        vdbg = None
        for h in range(HPC):
            vdbg = sc.dma("gp", "gT", dbg["attn"][:, h * DH:(h + 1) * DH], cc_in_h[h][:, :],
                          waits=[("sS0", sc.val("sS0")), ("sS1", sc.val("sS1")), ("cc", sc.val("cc"))])
        _finish(nc, sc, sems, es, [("gT", vdbg)])
        return nc


    wo_es = ExitStack()
    wo_sb = [wo_es.enter_context(nc.sbuf_tensor(f"wo{i}", [128, NCH, 128], BF16)) for i in range(2)]
    x1pre = wo_es.enter_context(nc.sbuf_tensor("x1pre", [128, NCH, 512], F32))
    xres_sb = [wo_es.enter_context(nc.sbuf_tensor(f"xres{i}", [128, 512], F32)) for i in range(2)]
    xsq = wo_es.enter_context(nc.sbuf_tensor("xsq", [128, 512], F32))
    stat_sb = wo_es.enter_context(nc.sbuf_tensor("stat_sb", [1, 6 * 512], F32))
    mb_sb = wo_es.enter_context(nc.sbuf_tensor("mb_sb", [128, 512], F32))
    rb_sb = wo_es.enter_context(nc.sbuf_tensor("rb_sb", [128, 512], F32))
    wr_sb = wo_es.enter_context(nc.sbuf_tensor("wr_sb", [128, NCH, NE], F32))
    logits_sb = wo_es.enter_context(nc.sbuf_tensor("logits_sb", [128, NCH, NE], F32))
    pr_sb = wo_es.enter_context(nc.sbuf_tensor("pr_sb", [128, NE], F32))
    eq_sb = wo_es.enter_context(nc.sbuf_tensor("eq_sb", [128, NE], F32))
    pm_sb = wo_es.enter_context(nc.sbuf_tensor("pm_sb", [128, NE], F32))
    sc1_sb = wo_es.enter_context(nc.sbuf_tensor("sc1_sb", [128, 4], F32))



    v_wr = sc.dma("gp", "gT", wr_sb[:, :, :], wrouter.rearrange("(c p) o -> p c o", p=128))

    xsq2 = wo_es.enter_context(nc.sbuf_tensor("xsq2", [128, 2, 512], F32))

    v_spill_tile = [0, 0, 0, 0]
    v_wo_load = [0, 0]
    v_wo_used = [None, None]
    v_xres_used = [None, None]
    v_x1_readers = []
    v_x1sp_last = 0
    v_logit_evac = 0
    for tt in range(4):
        t0 = tt * 512
        sbank, s2bank = pb[6], pb[7]
        v_s2mm = [0] * NCH
        v_s1mm = [0] * NCH
        v_add = [0] * NCH
        v_sq = [0] * NCH

        def emit_smm(oc):
            st, sp_ = (oc == 0), (oc == NCH - 1)
            w1 = [("dve", v_add[oc])]
            if st:
                w1 += sbank.wait_free() + CST
            v_s1mm[oc] = sc.emit("pe", lambda e, oc=oc, st=st, sp_=sp_: e.matmul(
                sbank.t[0:1, 0:512], ones128_sb[:, :], x1pre[:, oc, :], start=st, stop=sp_),
                w1, inc="pe")
            w2 = [("act", v_sq[oc])]
            if st:
                w2 += s2bank.wait_free()
            v_s2mm[oc] = sc.emit("pe", lambda e, oc=oc, st=st, sp_=sp_: e.matmul(
                s2bank.t[0:1, 0:512], ones128_sb[:, :], xsq2[:, oc % 2, :], start=st, stop=sp_),
                w2, inc="pe")

        for oc in range(NCH):
            bi = oc % 2
            w = [("pe", v_wo_used[bi])] if v_wo_used[bi] else []
            v_wo_load[bi] = sc.dma(
                "gp", f"gWo{bi}", wo_sb[bi][:, :, :],
                woT[:, oc * 128:(oc + 1) * 128].rearrange("(c p) o -> p c o", p=128),
                waits=w)
            wbank = pb[oc % 2]
            wf = wbank.wait_free() + [(f"gWo{bi}", v_wo_load[bi])]
            vstop = None
            for c in range(NCH):
                ww = list(wf) if c == 0 else []
                ww.append((f"sT{c % 4}", v_tp_h[c % 4]))
                vstop = sc.emit("pe", lambda e, c=c, bi=bi, wbank=wbank, t0=t0, st=(c == 0), sp_=(c == NCH - 1): e.matmul(
                    wbank.t[:, 0:512], wo_sb[bi][:, c, :], attn_fm[:, c, t0:t0 + 512],
                    start=st, stop=sp_), ww, inc="pe" if c == NCH - 1 else None)
            wbank.ready = ("pe", vstop)
            wbank.free = {}
            v_wo_used[bi] = sc.val("pe")
            xbi = oc % 2
            wx = [("dve", v_xres_used[xbi])] if v_xres_used[xbi] else []
            vxr = sc.dma("sp", f"sR{xbi}", xres_sb[xbi][:, :], xres[oc * 128:(oc + 1) * 128, t0:t0 + 512], waits=wx)
            wadd = [wbank.ready, (f"sR{xbi}", vxr)] + v_x1_readers
            v_add[oc] = sc.emit("dve", lambda e, oc=oc, xbi=xbi, wbank=wbank: e.tensor_add(
                x1pre[:, oc, :], wbank.t[:, 0:512], xres_sb[xbi][:, :]), wadd, inc="dve")
            wbank.read_done("dve", v_add[oc])
            v_xres_used[xbi] = v_add[oc]
            wsq = [("dve", v_add[oc])]
            if oc >= 2:
                wsq.append(("pe", v_s2mm[oc - 2]))
            v_sq[oc] = sc.emit("act", lambda e, oc=oc: e.activation(
                xsq2[:, oc % 2, :], x1pre[:, oc, :], AF.Square), wsq, inc="act")
            if oc >= 1:
                emit_smm(oc - 1)
        emit_smm(NCH - 1)
        v_x1_readers = []
        sbank.ready = ("pe", v_s1mm[NCH - 1])
        sbank.free = {}
        s2bank.ready = ("pe", v_s2mm[NCH - 1])
        s2bank.free = {}

        st_ = stat_sb
        va_m = sc.emit("act", lambda e: e.activation(st_[0:1, 0:512], sbank.t[0:1, 0:512],
                                                     AF.Copy, scale=invD[0:1, 0:1]),
                       [sbank.ready] + CST, inc="act")
        va_e2 = sc.emit("act", lambda e: e.activation(st_[0:1, 512:1024], s2bank.t[0:1, 0:512],
                                                      AF.Copy, scale=invD[0:1, 0:1]),
                        [s2bank.ready], inc="act")
        sbank.read_done("act", va_m)
        s2bank.read_done("act", va_e2)
        vd_mm = sc.emit("dve", lambda e: e.tensor_mul(st_[0:1, 1024:1536], st_[0:1, 0:512], st_[0:1, 0:512]),
                        [("act", va_m)], inc="dve")
        vd_var = sc.emit("dve", lambda e: e.tensor_sub(st_[0:1, 1536:2048], st_[0:1, 512:1024], st_[0:1, 1024:1536]),
                         [("act", va_e2), ("dve", vd_mm)], inc="dve")
        va_ln = sc.emit("act", lambda e: e.activation(st_[0:1, 2048:2560], st_[0:1, 1536:2048],
                                                      AF.Ln, bias=epsT[0:1, 0:1]),
                        [("dve", vd_var)], inc="act")
        va_rstd = sc.emit("act", lambda e: e.activation(st_[0:1, 2560:3072], st_[0:1, 2048:2560],
                                                        AF.Exp, scale=half_neg[0:1, 0:1]),
                          [("act", va_ln)], inc="act")
        mbank, rbank = pb[2], pb[3]
        vb_m = _mm_group(sc, mbank, [(mbank.t[:, 0:512], ones1_sb[:, :], st_[0:1, 0:512])],
                         extra_waits=[("act", va_m)] + CST)
        vb_r = _mm_group(sc, rbank, [(rbank.t[:, 0:512], ones1_sb[:, :], st_[0:1, 2560:3072])],
                         extra_waits=[("act", va_rstd)])
        ve_m = sc.emit("act", lambda e: e.copy(mb_sb[:, :], mbank.t[:, 0:512]), [mbank.ready], inc="act")
        ve_r = sc.emit("act", lambda e: e.copy(rb_sb[:, :], rbank.t[:, 0:512]), [rbank.ready], inc="act")
        mbank.read_done("act", ve_m)
        rbank.read_done("act", ve_r)

        v_nrm = [0] * NCH
        for oc in range(NCH):
            vs = sc.emit("dve", lambda e, oc=oc: e.tensor_sub(
                x1pre[:, oc, :], x1pre[:, oc, :], mb_sb[:, :]),
                [("act", ve_m), ("pe", v_s2mm[NCH - 1]), ("dve", sc.val("dve"))], inc="dve")
            v_nrm[oc] = sc.emit("dve", lambda e, oc=oc: e.tensor_mul(
                x1pre[:, oc, :], x1pre[:, oc, :], rb_sb[:, :]),
                [("act", ve_r), ("dve", vs)], inc="dve")
            v_x1sp_last = sc.dma("sp", "sL", x1sp[oc * 128:(oc + 1) * 128, t0:t0 + 512],
                                 x1pre[:, oc, :], waits=[("dve", v_nrm[oc])])
        v_x1_readers.append(("sL", v_x1sp_last))
        v_spill_tile[tt] = v_x1sp_last

        lbank = pb[4]
        for tc in range(4):
            g = tt * 4 + tc
            mms = [(lbank.t[0:128, 0:NE], x1pre[:, oc, tc * 128:(tc + 1) * 128], wr_sb[:, oc, :])
                   for oc in range(NCH)]
            _mm_group(sc, lbank, mms,
                      extra_waits=[("dve", v_nrm[NCH - 1]), ("gT", v_wr)])
            v_logit_evac = sc.emit("act", lambda e, g=g: e.copy(
                logits_sb[:, g, :], lbank.t[0:128, 0:NE]), [lbank.ready], inc="act")
            lbank.read_done("act", v_logit_evac)
            v_x1_readers.append(("pe", sc.val("pe")))

    tbank = pb[5]
    v_comb_evac = 0
    v_tp_prev = 0
    for g in range(NCH):
        c0 = g * 128
        vm = sc.emit("dve", lambda e, g=g: e.tensor_reduce(
            sc1_sb[:, 0:1], logits_sb[:, g, :], AX.X, ALU.max), [("act", v_logit_evac)], inc="dve")
        wls = [("dve", vm)]
        if v_tp_prev:
            wls.append(("pe", v_tp_prev))
        vls = sc.emit("dve", lambda e, g=g: e.tensor_single_scalar(
            pr_sb[:, :], logits_sb[:, g, :], sc1_sb[:, 0:1], ALU.subtract),
            wls, inc="dve")
        vex = sc.emit("act", lambda e: e.activation(pr_sb[:, :], pr_sb[:, :], AF.Exp),
                      [("dve", vls)], inc="act")
        vsm = sc.emit("dve", lambda e: e.tensor_reduce(
            sc1_sb[:, 1:2], pr_sb[:, :], AX.X, ALU.add), [("act", vex)], inc="dve")
        vrs = sc.emit("dve", lambda e: e.reciprocal(sc1_sb[:, 2:3], sc1_sb[:, 1:2]),
                      [("dve", vsm)], inc="dve")
        vpr = sc.emit("dve", lambda e: e.tensor_single_scalar(
            pr_sb[:, :], pr_sb[:, :], sc1_sb[:, 2:3], ALU.mult), [("dve", vrs)], inc="dve")
        vm1 = sc.emit("dve", lambda e: e.tensor_reduce(
            sc1_sb[:, 0:1], pr_sb[:, :], AX.X, ALU.max), [("dve", vpr)], inc="dve")
        veq = sc.emit("dve", lambda e: e.tensor_single_scalar(
            eq_sb[:, :], pr_sb[:, :], sc1_sb[:, 0:1], ALU.is_ge), [("dve", vm1)], inc="dve")
        vpm = sc.emit("dve", lambda e: e.scalar_tensor_tensor(
            pm_sb[:, :], eq_sb[:, :], -1.0e30, pr_sb[:, :], ALU.mult, ALU.add),
            [("dve", veq)] + CST, inc="dve")
        vm2 = sc.emit("dve", lambda e: e.tensor_reduce(
            sc1_sb[:, 3:4], pm_sb[:, :], AX.X, ALU.max), [("dve", vpm)], inc="dve")
        vge = sc.emit("dve", lambda e: e.tensor_single_scalar(
            eq_sb[:, :], pr_sb[:, :], sc1_sb[:, 3:4], ALU.is_ge), [("dve", vm2)], inc="dve")
        vcb = sc.emit("dve", lambda e: e.tensor_mul(pr_sb[:, :], pr_sb[:, :], eq_sb[:, :]),
                      [("dve", vge)], inc="dve")
        wtp = tbank.wait_free() + [("dve", vcb)]
        vt = sc.emit("pe", lambda e: e.matmul(
            tbank.t[0:NE, 0:128], pr_sb[:, :], ident_sb[:, :],
            start=True, stop=True), wtp, inc="pe")
        tbank.ready = ("pe", vt)
        tbank.free = {}
        v_tp_prev = vt
        v_comb_evac = sc.emit("act", lambda e, c0=c0: e.copy(
            comb_sb[:, c0:c0 + 128], tbank.t[0:NE, 0:128]), [tbank.ready], inc="act")
        tbank.read_done("act", v_comb_evac)

    wo_es.close()
    wo_outer.close()
    sc.barrier()

    if upto == "x1":
        v1 = sc.dma("gp", "gT", dbg["x1"][:, :], x1sp[:, :], waits=[("sL", v_x1sp_last)])
        v2 = sc.dma("sp", "sO0", dbg["comb"][:, :], comb_sb[:, :], waits=[("act", v_comb_evac)])
        _finish(nc, sc, sems, es, [("gT", v1), ("sO0", v2)])
        return nc

    ffn_es = ExitStack()
    y_acc = ffn_es.enter_context(nc.sbuf_tensor("y_acc", [128, NCH, 1024], F32))
    x1bh = ffn_es.enter_context(nc.sbuf_tensor("x1bh", [128, NCH, 1024], BF16))
    h_sb = ffn_es.enter_context(nc.sbuf_tensor("h_sb", [128, NDE, 1024], BF16))
    wg_sb = [ffn_es.enter_context(nc.sbuf_tensor(f"wg{i}", [128, NCH, 128], BF16)) for i in range(2)]
    wu_sb = [ffn_es.enter_context(nc.sbuf_tensor(f"wu{i}", [128, NCH, 128], BF16)) for i in range(2)]
    wd_sb = ffn_es.enter_context(nc.sbuf_tensor("wd_sb", [128, 6, NCH, 128], BF16))
    sg_sb = ffn_es.enter_context(nc.sbuf_tensor("sg_sb", [128, 2, 512], F32))
    cb_sb = ffn_es.enter_context(nc.sbuf_tensor("cb_sb", [128, 1024], F32))
    xq2 = ffn_es.enter_context(nc.sbuf_tensor("xq2", [128, 2, 512], F32))
    st2 = ffn_es.enter_context(nc.sbuf_tensor("st2", [1, 3 * 512], F32))
    mb2 = ffn_es.enter_context(nc.sbuf_tensor("mb2", [128, 512], F32))
    rb2 = ffn_es.enter_context(nc.sbuf_tensor("rb2", [128, 512], F32))
    x2t = [ffn_es.enter_context(nc.sbuf_tensor(f"x2t{i}", [128, 512], F32)) for i in range(2)]

    stages = [(wsg, wsu, wsd, 0, 0, 0, None), (wsg, wsu, wsd, 11, 0, 0, None)] + \
             [(weg, weu, wed, 0, e * D, e * DE, e) for e in range(NE)]
    JGRP = [(0, 6), (6, 11)]

    v_y_read_prev = []
    v_x1b_pe_prev = 0
    v_out_last = 0
    v_store = [0, 0]
    v_prev_s2 = 0
    v_prev_norm = 0
    for hf in range(2):
        t0h = hf * 1024
        v_ya0 = None
        for ts in range(2):
            g = hf * 2 + ts
            wldx = [("sL", v_spill_tile[g])]
            if v_x1b_pe_prev:
                wldx.append(("pe", v_x1b_pe_prev))
            sc.dma("gp", "gY", x1bh[:, :, ts * 512:(ts + 1) * 512],
                   x1sp[:, g * 512:(g + 1) * 512].rearrange("(c p) t -> p c t", p=128), waits=wldx)
            wldy = [("sL", v_spill_tile[g])] + v_y_read_prev
            v_ya0 = sc.dma("gp", "gY", y_acc[:, :, ts * 512:(ts + 1) * 512],
                           x1sp[:, g * 512:(g + 1) * 512].rearrange("(c p) t -> p c t", p=128), waits=wldy)

        v_h_read_pe = sc.val("pe")
        v_cb_read = sc.val("dve")
        v_wd_used = sc.val("pe")
        v_wg_used = [sc.val("pe"), sc.val("pe")]
        v_sg_used = [sc.val("dve"), sc.val("dve")]
        v_ya_last = sc.val("dve")
        for snum, (wgt, wut, wdt, co, rg_, rd_, ex) in enumerate(stages):
            if ex is not None:
                cbank = pb[6]
                v_cb = 0
                for ts in range(2):
                    _mm_group(sc, cbank, [(cbank.t[:, 0:512], sel8_sb[:, ex * 128:(ex + 1) * 128],
                                           comb_sb[:, t0h + ts * 512:t0h + (ts + 1) * 512])],
                              extra_waits=[("act", v_comb_evac)])
                    wcb = [cbank.ready, ("dve", v_cb_read)]
                    v_cb = sc.emit("act", lambda e, ts=ts, cbank=cbank: e.copy(
                        cb_sb[:, ts * 512:(ts + 1) * 512], cbank.t[:, 0:512]), wcb, inc="act")
                    cbank.read_done("act", v_cb)

            for g0, g1 in JGRP:
                v_h_w = 0
                for j in range(g0, g1):
                    jp = j % 2
                    wwl = [("pe", v_wg_used[jp])]
                    vwg = sc.dma("gp", f"gG{jp}", wg_sb[jp][:, :, :],
                                 wgt[rg_:rg_ + D, (co + j) * 128:(co + j + 1) * 128].rearrange("(c p) o -> p c o", p=128),
                                 waits=wwl)
                    vwu = sc.dma("gp", f"gG{jp}", wu_sb[jp][:, :, :],
                                 wut[rg_:rg_ + D, (co + j) * 128:(co + j + 1) * 128].rearrange("(c p) o -> p c o", p=128))
                    for ts in range(2):
                        gbank, ubank = pb[0 + ts], pb[2 + ts]
                        wg_ = [(f"gG{jp}", vwu), ("gY", v_ya0)]
                        _mm_group(sc, gbank, [(gbank.t[:, 0:512], wg_sb[jp][:, c, :],
                                               x1bh[:, c, ts * 512:(ts + 1) * 512]) for c in range(NCH)],
                                  extra_waits=wg_)
                        _mm_group(sc, ubank, [(ubank.t[:, 0:512], wu_sb[jp][:, c, :],
                                               x1bh[:, c, ts * 512:(ts + 1) * 512]) for c in range(NCH)])
                        v_wg_used[jp] = sc.val("pe")
                        wsl = [gbank.ready, ("dve", v_sg_used[ts])]
                        vsl = sc.emit("act", lambda e, ts=ts, gbank=gbank: e.activation(
                            sg_sb[:, ts, :], gbank.t[:, 0:512], AF.Silu), wsl, inc="act")
                        gbank.read_done("act", vsl)
                        whh = [("act", vsl), ubank.ready, ("pe", v_h_read_pe)]
                        if ex is not None:
                            vcbf = sc.emit("dve", lambda e, ts=ts: e.tensor_mul(
                                sg_sb[:, ts, :], sg_sb[:, ts, :], cb_sb[:, ts * 512:(ts + 1) * 512]),
                                [("act", vsl), ("act", v_cb), ("dve", sc.val("dve"))], inc="dve")
                            whh.append(("dve", vcbf))
                        vh = sc.emit("dve", lambda e, j=j, ts=ts, ubank=ubank: e.tensor_mul(
                            h_sb[:, j, ts * 512:(ts + 1) * 512], sg_sb[:, ts, :], ubank.t[:, 0:512]),
                            whh, inc="dve")
                        ubank.read_done("dve", vh)
                        v_sg_used[ts] = vh
                        v_h_w = vh
                v_wd = 0
                for j in range(g0, g1):
                    wwd = [("pe", v_wd_used)] if j == g0 else []
                    v_wd = sc.dma("gp", "gD", wd_sb[:, j - g0, :, :],
                                  wdt[rd_ + (co + j) * 128:rd_ + (co + j + 1) * 128, :].rearrange("p (c o) -> p c o", o=128),
                                  waits=wwd)
                for ts in range(2):
                    for oc in range(NCH):
                        ybank = pb[4 + ((oc + ts) % 2)]
                        wdn = [("dve", v_h_w), ("gD", v_wd), ("gY", v_ya0)]
                        _mm_group(sc, ybank, [(ybank.t[:, 0:512], wd_sb[:, j2 - g0, oc, :],
                                               h_sb[:, j2, ts * 512:(ts + 1) * 512]) for j2 in range(g0, g1)],
                                  extra_waits=wdn)
                        v_ya_last = sc.emit("dve", lambda e, oc=oc, ts=ts, ybank=ybank: e.tensor_add(
                            y_acc[:, oc, ts * 512:(ts + 1) * 512], ybank.t[:, 0:512],
                            y_acc[:, oc, ts * 512:(ts + 1) * 512]),
                            [ybank.ready, ("dve", sc.val("dve")), ("gY", v_ya0)], inc="dve")
                        ybank.read_done("dve", v_ya_last)
                v_wd_used = sc.val("pe")
            v_h_read_pe = sc.val("pe")
            v_cb_read = sc.val("dve")
        v_x1b_pe_prev = sc.val("pe")

        v_y_read_prev = []
        for ts in range(2):
            sbank, s2bank = pb[6], pb[7]
            v_sq2 = [0] * NCH
            v_m1 = [0] * NCH
            v_m2 = [0] * NCH

            for oc in range(NCH):
                wsq = [("dve", v_ya_last), ("pe", v_m2[oc - 2] if oc >= 2 else v_prev_s2)]
                v_sq2[oc] = sc.emit("act", lambda e, oc=oc, ts=ts: e.activation(
                    xq2[:, oc % 2, :], y_acc[:, oc, ts * 512:(ts + 1) * 512], AF.Square),
                    wsq, inc="act")
                w1 = [("dve", v_ya_last)]
                if oc == 0:
                    w1 += sbank.wait_free() + CST
                v_m1[oc] = sc.emit("pe", lambda e, oc=oc, ts=ts, st=(oc == 0), sp_=(oc == NCH - 1): e.matmul(
                    sbank.t[0:1, 0:512], ones128_sb[:, :], y_acc[:, oc, ts * 512:(ts + 1) * 512],
                    start=st, stop=sp_), w1, inc="pe")
                w2 = [("act", v_sq2[oc])]
                if oc == 0:
                    w2 += s2bank.wait_free()
                v_m2[oc] = sc.emit("pe", lambda e, oc=oc, st=(oc == 0), sp_=(oc == NCH - 1): e.matmul(
                    s2bank.t[0:1, 0:512], ones128_sb[:, :], xq2[:, oc % 2, :],
                    start=st, stop=sp_), w2, inc="pe")
            sbank.ready = ("pe", v_m1[NCH - 1])
            sbank.free = {}
            s2bank.ready = ("pe", v_m2[NCH - 1])
            s2bank.free = {}
            v_prev_s2 = v_m2[NCH - 1]

            va_m = sc.emit("act", lambda e: e.activation(st2[0:1, 0:512], sbank.t[0:1, 0:512],
                                                         AF.Copy, scale=invD[0:1, 0:1]),
                           [sbank.ready], inc="act")
            va_e2 = sc.emit("act", lambda e: e.activation(st2[0:1, 512:1024], s2bank.t[0:1, 0:512],
                                                          AF.Copy, scale=invD[0:1, 0:1]),
                            [s2bank.ready], inc="act")
            sbank.read_done("act", va_m)
            s2bank.read_done("act", va_e2)
            vd_mm = sc.emit("dve", lambda e: e.tensor_mul(st2[0:1, 1024:1536], st2[0:1, 0:512], st2[0:1, 0:512]),
                            [("act", va_m), ("dve", sc.val("dve"))], inc="dve")
            vd_var = sc.emit("dve", lambda e: e.tensor_sub(st2[0:1, 512:1024], st2[0:1, 512:1024], st2[0:1, 1024:1536]),
                             [("act", va_e2), ("dve", vd_mm)], inc="dve")
            va_ln = sc.emit("act", lambda e: e.activation(st2[0:1, 1024:1536], st2[0:1, 512:1024],
                                                          AF.Ln, bias=epsT[0:1, 0:1]),
                            [("dve", vd_var)], inc="act")
            va_rstd = sc.emit("act", lambda e: e.activation(st2[0:1, 512:1024], st2[0:1, 1024:1536],
                                                            AF.Exp, scale=half_neg[0:1, 0:1]),
                              [("act", va_ln)], inc="act")
            mbank, rbank = pb[2], pb[3]
            _mm_group(sc, mbank, [(mbank.t[:, 0:512], ones1_sb[:, :], st2[0:1, 0:512])],
                      extra_waits=[("act", va_m)])
            _mm_group(sc, rbank, [(rbank.t[:, 0:512], ones1_sb[:, :], st2[0:1, 512:1024])],
                      extra_waits=[("act", va_rstd)])
            ve_m = sc.emit("act", lambda e: e.copy(mb2[:, :], mbank.t[:, 0:512]),
                           [mbank.ready, ("dve", v_prev_norm)], inc="act")
            ve_r = sc.emit("act", lambda e: e.copy(rb2[:, :], rbank.t[:, 0:512]),
                           [rbank.ready, ("dve", v_prev_norm)], inc="act")
            mbank.read_done("act", ve_m)
            rbank.read_done("act", ve_r)

            for oc in range(NCH):
                op = oc % 2
                wn = [("act", ve_m), ("act", ve_r), ("dve", sc.val("dve"))]
                if v_store[op]:
                    wn.append((f"sO{op}", v_store[op]))
                vs = sc.emit("dve", lambda e, oc=oc, ts=ts, op=op: e.tensor_sub(
                    x2t[op][:, :], y_acc[:, oc, ts * 512:(ts + 1) * 512], mb2[:, :]), wn, inc="dve")
                vn = sc.emit("dve", lambda e, op=op: e.tensor_mul(
                    x2t[op][:, :], x2t[op][:, :], rb2[:, :]), [("dve", vs)], inc="dve")
                v_out_last = sc.dma("sp", f"sO{op}", outfm[oc * 128:(oc + 1) * 128, t0h + ts * 512:t0h + (ts + 1) * 512],
                                    x2t[op][:, :], waits=[("dve", vn)])
                v_store[op] = v_out_last
                v_y_read_prev.append(("dve", vn))
                v_prev_norm = vn
            v_y_read_prev.append(("pe", sc.val("pe")))
            v_y_read_prev.append(("act", v_sq2[NCH - 1]))

    ffn_es.close()
    _finish(nc, sc, sems, es, [("sO0", sc.val("sO0")), ("sO1", sc.val("sO1"))])
    return nc


def _finish(nc, sc, sems, es, final_waits):
    sc.emit("sp", lambda e: e.nop(), waits=list(final_waits))
    for s in SEMS:
        assert sc.cnt.get(s, 0) < 60000, (s, sc.cnt[s])
    with nc.Block() as block:
        def runner(name):
            def _r(eng):
                hw = {}
                for waits, fn, inc, amt in sc.q[name]:
                    for sem, v in waits:
                        if v and v > hw.get(sem, 0):
                            eng.wait_ge(sems[sem], v)
                            hw[sem] = v
                    ins = fn(eng)
                    if inc is not None:
                        ins.then_inc(sems[inc], amt)
            return _r
        block.tensor(runner("pe"))
        block.scalar(runner("act"))
        block.vector(runner("dve"))
        block.sync(runner("sp"))
        block.gpsimd(runner("gp"))
    es.close()


def _rope_tables():
    idx = np.arange(0, DH, 2, dtype=np.float32)
    inv_freq = (np.float32(10000.0) ** (-idx / np.float32(DH))).astype(np.float32)
    angles = (np.arange(S, dtype=np.float32)[:, None] * inv_freq[None, :]).astype(np.float32)
    cos = np.cos(angles).astype(np.float32)
    sin = np.sin(angles).astype(np.float32)
    cos_e = np.concatenate([cos.T, cos.T], axis=0)
    sin_s = np.concatenate([-sin.T, sin.T], axis=0)
    scl = np.float32(1.0 / np.sqrt(np.float32(DH)))
    return (np.ascontiguousarray(cos_e * scl), np.ascontiguousarray(sin_s * scl),
            np.ascontiguousarray(cos_e), np.ascontiguousarray(sin_s))


def _stage(inputs):
    x = np.asarray(inputs["L0_attn_x"], np.float32)
    wq = np.asarray(inputs["L0_attn_Wq"], np.float32)
    wk = np.asarray(inputs["L0_attn_Wk"], np.float32)
    wv = np.asarray(inputs["L0_attn_Wv"], np.float32)
    wo = np.asarray(inputs["L0_attn_Wo"], np.float32)

    perm = np.concatenate([np.arange(0, DH, 2), np.arange(1, DH, 2)])
    wqT, wkT, wvT = wq.T, wk.T, wv.T
    woT = np.ascontiguousarray(wo.T)
    cq, sq, ck, sk = _rope_tables()

    tri = np.where(np.arange(128)[:, None] <= np.arange(128)[None, :], 0.0, MASKVAL).astype(np.float32)
    identm = np.eye(128, dtype=np.float32)
    swpm = np.zeros((128, 128), dtype=np.float32)
    for i in range(128):
        swpm[(i + 64) % 128, i] = 1.0
    sel8 = np.zeros((8, 8 * 128), dtype=np.float32)
    for e in range(8):
        sel8[e, e * 128:(e + 1) * 128] = 1.0
    ones1 = np.ones((1, 128), dtype=np.float32)

    shared = {
        "woT": woT, "cosq": cq, "sinq": sq, "cosk": ck, "sink": sk,
        "tri": tri, "ident": identm, "swp": swpm, "sel8": sel8, "ones1": ones1,
        "wrouter": np.ascontiguousarray(inputs["L0_ffn_W_router"], dtype=np.float32),
        "wsg": np.ascontiguousarray(inputs["L0_ffn_Ws_gate"], dtype=np.float32),
        "wsu": np.ascontiguousarray(inputs["L0_ffn_Ws_up"], dtype=np.float32),
        "wsd": np.ascontiguousarray(inputs["L0_ffn_Ws_down"], dtype=np.float32),
        "weg": np.ascontiguousarray(np.asarray(inputs["L0_ffn_We_gate"], np.float32).reshape(NE * D, DE)),
        "weu": np.ascontiguousarray(np.asarray(inputs["L0_ffn_We_up"], np.float32).reshape(NE * D, DE)),
        "wed": np.ascontiguousarray(np.asarray(inputs["L0_ffn_We_down"], np.float32).reshape(NE * DE, D)),
    }

    xTs = [np.ascontiguousarray(x[b].T) for b in range(B)]
    in_maps = []
    for c in range(8):
        b, r = c // 4, c % 4
        heads = range(4 * r, 4 * r + 4)
        m = dict(shared)
        m["xT"] = xTs[b]
        m["xres"] = np.ascontiguousarray(xTs[b][:, r * T:(r + 1) * T])
        m["wqT"] = np.ascontiguousarray(np.concatenate(
            [wqT[:, h * DH:(h + 1) * DH][:, perm] for h in heads], axis=1))
        m["wkT"] = np.ascontiguousarray(np.concatenate(
            [wkT[:, h * DH:(h + 1) * DH][:, perm] for h in heads], axis=1))
        m["wvT"] = np.ascontiguousarray(np.concatenate(
            [wvT[:, h * DH:(h + 1) * DH] for h in heads], axis=1))
        m["roff"] = np.array([[r * T * DH]], dtype=np.int32)
        in_maps.append(m)
    return in_maps


def kernel(**inputs):
    nc = _build("full")
    in_maps = _stage(inputs)
    res = run_bass_kernel_spmd(nc, in_maps, core_ids=list(range(8)))
    out = np.empty((B, S, D), dtype=np.float32)
    for c in range(8):
        b, r = c // 4, c % 4
        out[b, r * T:(r + 1) * T, :] = np.asarray(res.results[c]["outfm"]).T
    return out


# version 05
# speedup vs baseline: 1.0731x; 1.0532x over previous
import numpy as np
import ml_dtypes
from contextlib import ExitStack

import concourse.bass as bass
import concourse.mybir as mybir
from concourse.bass_utils import run_bass_kernel_spmd

F32 = mybir.dt.float32
BF16 = mybir.dt.bfloat16
I32 = mybir.dt.int32
AF = mybir.ActivationFunctionType
ALU = mybir.AluOpType
AX = mybir.AxisListType

B, S, D, H, DH = 2, 8192, 2048, 16, 128
HPC = 4
T = 2048
NCH = D // 128
DS, NDS = 2816, 22
DE, NDE = 1408, 11
NE = 8
NTT = S // 512
CSHIFT = -12.0
MASKVAL = -60.0
EPS = 1e-5

ENGS = ("pe", "act", "dve", "sp", "gp")
SEMS = ("pe", "act", "dve", "gpc", "cc",
        "gT", "gX0", "gX1", "gW0", "gW1", "gWo0", "gWo1", "gY",
        "gG0", "gG1", "gD", "gP0", "gP1", "gB0", "gB1", "sXC",
        "sS0", "sS1", "sR0", "sR1", "sL", "sT0", "sT1", "sT2", "sT3", "sO0", "sO1")


class _Sched:

    def __init__(self):
        self.q = {e: [] for e in ENGS}
        self.cnt = {s: 0 for s in SEMS}
        self.pending = {e: [] for e in ENGS}

    def val(self, sem):
        return self.cnt[sem]

    SELF_SERIAL = ()

    def emit(self, eng, fn, waits=(), inc=None, amt=1):
        w = list(self.pending[eng]) + list(waits)
        self.pending[eng] = []
        if eng in self.SELF_SERIAL:
            if self.cnt[eng]:
                w.append((eng, self.cnt[eng]))
            if inc is None:
                inc = eng
        v = None
        if inc is not None:
            self.cnt[inc] += amt
            v = self.cnt[inc]
        self.q[eng].append((tuple(w), fn, inc, amt))
        return v

    def dma(self, eng, sem, out_ap, in_ap, waits=()):
        return self.emit(eng, lambda e, o=out_ap, i=in_ap: e.dma_start(o, i),
                         waits, inc=sem, amt=16)

    def barrier(self, exclude=()):
        for e in ENGS:
            if not self.q[e]:
                continue
            w, fn, inc, amt = self.q[e][-1]
            if inc is None:
                sem = e if e in ("pe", "act", "dve") else "gpc"
                assert e != "sp", "sp barrier op must carry an inc"
                self.cnt[sem] = self.cnt.get(sem, 0) + 1
                self.q[e][-1] = (w, fn, sem, 1)
        marks = [(s, self.cnt[s]) for s in SEMS
                 if self.cnt.get(s, 0) > 0 and s not in exclude]
        for e in ENGS:
            self.pending[e] = list(marks)


class _PB:

    def __init__(self, tile):
        self.t = tile
        self.ready = None
        self.free = {}

    def wait_free(self):
        return list(self.free.items())

    def read_done(self, sem, val):
        self.free[sem] = max(self.free.get(sem, 0), val)


def _mm_group(sc, pb, mms, extra_waits=(), lookahead_ok=True):
    waits = pb.wait_free() + list(extra_waits)
    n = len(mms)
    v = None
    for i, (o, l, r) in enumerate(mms):
        st, sp_ = (i == 0), (i == n - 1)
        v = sc.emit(
            "pe",
            lambda e, o=o, l=l, r=r, st=st, sp_=sp_: e.matmul(o, l, r, start=st, stop=sp_),
            waits if i == 0 else (),
            inc="pe" if sp_ else None,
        )
    pb.ready = ("pe", v)
    pb.free = {}
    return v


def _build(upto="full"):
    nc = bass.Bass("TRN2", target_bir_lowering=False)
    sc = _Sched()

    dt_in = {}
    def din(name, shape, dt=F32):
        dt_in[name] = nc.dram_tensor(name, list(shape), dt, kind="ExternalInput")
        return dt_in[name]

    xTt = din("xTt", (NTT * 128, NCH * 512))
    xres = din("xres", (D, T))
    wqT = din("wqT", (HPC * 128, NCH * DH))
    wkT = din("wkT", (HPC * 128, NCH * DH))
    wvT = din("wvT", (HPC * 128, NCH * DH))
    woT = din("woT", (NCH * 128, NCH * 128))
    tbpk = din("tbpk", (128, NTT * 4 * 512), BF16)
    tri = din("tri", (128, 128))
    sel8 = din("sel8", (8, 8 * 128))
    ident = din("ident", (128, 128))
    swp = din("swp", (128, 128))
    ones1 = din("ones1", (1, 128))
    roff = din("roff", (1, 1), I32)
    wrouter = din("wrouter", (128, NCH * NE))
    wsg = din("wsg", (NDS * 128, NCH * 128))
    wsu = din("wsu", (NDS * 128, NCH * 128))
    wsd = din("wsd", (DS, D))
    weg = din("weg", (NE * NDE * 128, NCH * 128))
    weu = din("weu", (NE * NDE * 128, NCH * 128))
    wed = din("wed", (NE * DE, D))
    xbf = nc.dram_tensor("xbf", [NTT * 128, NCH * 512], BF16, kind="Internal")

    outfm = nc.dram_tensor("outfm", [D, T], F32, kind="ExternalOutput")

    cc_in_h = [nc.dram_tensor(f"cc_in{h}", [S, DH], BF16, kind="Internal") for h in range(HPC)]
    cc_ag_h = [nc.dram_tensor(f"cc_ag{h}", [4 * S, DH], BF16, kind="Internal") for h in range(HPC)]
    cc_my = nc.dram_tensor("cc_my", [T, D], BF16, kind="Internal")
    x1sp = nc.dram_tensor("x1sp", [4 * 128, NCH * 512], F32, kind="Internal")

    assert upto == "full"

    es = ExitStack()
    sems = {s: es.enter_context(nc.semaphore(s)) for s in SEMS}

    pb = [_PB(es.enter_context(nc.psum_tensor(f"pb{i}", [128, 512], F32))) for i in range(8)]

    ones1_sb = es.enter_context(nc.sbuf_tensor("ones1_sb", [1, 128], F32))
    ones128_sb = es.enter_context(nc.sbuf_tensor("ones128_sb", [128, 1], F32))
    biasC = es.enter_context(nc.sbuf_tensor("biasC", [128, 1], F32))
    epsT = es.enter_context(nc.sbuf_tensor("epsT", [1, 1], F32))
    half_neg = es.enter_context(nc.sbuf_tensor("half_neg", [1, 1], F32))
    neginf8 = es.enter_context(nc.sbuf_tensor("neginf8", [128, 8], F32))
    invD = es.enter_context(nc.sbuf_tensor("invD", [1, 1], F32))
    tri_sb = es.enter_context(nc.sbuf_tensor("tri_sb", [128, 128], F32))
    ident_sb = es.enter_context(nc.sbuf_tensor("ident_sb", [128, 128], F32))
    swp_sb = es.enter_context(nc.sbuf_tensor("swp_sb", [128, 128], BF16))
    roff_sb = es.enter_context(nc.sbuf_tensor("roff_sb", [1, 1], I32))
    comb_sb = es.enter_context(nc.sbuf_tensor("comb_sb", [8, T], F32))
    rg = nc.gpsimd.alloc_register("rg_off")
    rg2 = nc.gpsimd.alloc_register("rg_off2")

    sc.dma("gp", "gT", ones1_sb[:, :], ones1[:, :])
    sc.dma("gp", "gT", tri_sb[:, :], tri[:, :])
    sc.dma("gp", "gT", ident_sb[:, :], ident[:, :])
    sc.dma("gp", "gT", swp_sb[:, :], swp[:, :])
    v_cst_dma = sc.dma("gp", "gT", roff_sb[:, :], roff[:, :])
    v_tri = v_cst_dma
    sc.emit("gp", lambda e: e.memset(ones128_sb[:, :], 1.0), inc="gpc")
    sc.emit("gp", lambda e: e.memset(biasC[:, :], CSHIFT), inc="gpc")
    sc.emit("gp", lambda e: e.memset(epsT[:, :], EPS), inc="gpc")
    sc.emit("gp", lambda e: e.memset(half_neg[:, :], -0.5), inc="gpc")
    sc.emit("gp", lambda e: e.memset(neginf8[:, :], -1.0e30), inc="gpc")
    v_cst = sc.emit("gp", lambda e: e.memset(invD[:, :], 1.0 / D), inc="gpc")
    CST = [("gT", v_cst_dma), ("gpc", v_cst)]

    wo_outer = ExitStack()
    attn_fm = wo_outer.enter_context(nc.sbuf_tensor("attn_fm", [128, NCH, T], BF16))
    wo_sb0 = wo_outer.enter_context(nc.sbuf_tensor("wo0", [128, NCH, 128], BF16))
    wr_sb = wo_outer.enter_context(nc.sbuf_tensor("wr_sb", [128, NCH, NE], F32))
    pre = {}
    attn_es = ExitStack()
    tb_sb = [attn_es.enter_context(nc.sbuf_tensor(f"tb{i}", [128, 4, 512], BF16)) for i in range(2)]
    TAB = []

    v_tb_load = [0, 0]
    v_tb_used = [0, 0]
    qbf = attn_es.enter_context(nc.sbuf_tensor("qbf", [128, S], BF16))
    kbf = attn_es.enter_context(nc.sbuf_tensor("kbf", [128, S], BF16))
    vsb = attn_es.enter_context(nc.sbuf_tensor("vsb", [128, 64, 132], BF16))
    xt_sb = [attn_es.enter_context(nc.sbuf_tensor(f"xt{i}", [128, NCH, 512], BF16)) for i in range(2)]
    wq_sb = [attn_es.enter_context(nc.sbuf_tensor(f"wq_sb{i}", [128, NCH, DH], BF16)) for i in range(2)]
    wk_sb = [attn_es.enter_context(nc.sbuf_tensor(f"wk_sb{i}", [128, NCH, DH], BF16)) for i in range(2)]
    wv_sb = [attn_es.enter_context(nc.sbuf_tensor(f"wv_sb{i}", [128, NCH, DH], BF16)) for i in range(2)]
    qtmp = attn_es.enter_context(nc.sbuf_tensor("qtmp", [128, 512], BF16))
    qsw = attn_es.enter_context(nc.sbuf_tensor("qsw", [128, 512], BF16))
    ktmp = attn_es.enter_context(nc.sbuf_tensor("ktmp", [128, 512], BF16))
    ksw = attn_es.enter_context(nc.sbuf_tensor("ksw", [128, 512], BF16))
    rope_t1 = attn_es.enter_context(nc.sbuf_tensor("rope_t1", [128, 512], BF16))
    rope_t2 = attn_es.enter_context(nc.sbuf_tensor("rope_t2", [128, 512], BF16))
    probs = [attn_es.enter_context(nc.sbuf_tensor(f"probs{i}", [128, 512], BF16)) for i in range(3)]
    ao_sb = [attn_es.enter_context(nc.sbuf_tensor(f"ao{i}", [128, 4, DH], BF16)) for i in range(2)]
    rc_sb = attn_es.enter_context(nc.sbuf_tensor("rc_sb", [128, 4], F32))

    v_xt_load = [0, 0]
    v_xt_used = [None, None]
    v_ao_dma = [0, 0]

    last_v1 = {}
    last_v2 = {}
    last_v3 = [0]
    v_rc_read = [0, 0, 0, 0]

    v_proj_end = [0, 0]
    v_w_loaded = [0, 0]
    v_pick_h = [0] * HPC
    v_tp_h = [0] * HPC
    v_xc = [0] * NTT
    v_xc_par = [0, 0]
    rg_loaded = [False]

    def emit_ag(h, half):
        HS = S // 2
        sc.emit("gp", lambda e, h=h, half=half: e.collective_compute(
            "AllGather", ALU.bypass, replica_groups=[[0, 1, 2, 3], [4, 5, 6, 7]],
            ins=[cc_in_h[h][half * HS:(half + 1) * HS, :]],
            outs=[cc_ag_h[h][half * 2 * S:half * 2 * S + 4 * HS, :]]),
            waits=[("sS0", sc.val("sS0")), ("sS1", sc.val("sS1"))], inc="cc")

    def emit_picks(h):
        if not rg_loaded[0]:
            sc.emit("gp", lambda e: e.reg_load(rg, roff_sb[0:1, 0:1]),
                    waits=[("gT", v_cst_dma)])
            rg_loaded[0] = True
        v = None
        for j in range(4):
            w0 = [("cc", 2 * (h + 1))] if j == 0 else []
            if j == 0 and h >= 2:
                w0.append((f"sT{h - 2}", v_tp_h[h - 2]))
            sc.emit("gp", lambda e, j=j: e.reg_add(rg2, rg, j * (S // 2) * DH), waits=w0)
            v = sc.emit(
                "gp", lambda e, j=j, h=h: e.dma_start(
                    bass.AP(cc_my, j * (HPC * DH) + h * DH, [[D, T], [1, DH]]),
                    bass.AP(cc_ag_h[h], rg2, [[DH, T], [1, DH]])),
                inc=f"gP{h % 2}", amt=16)
        v_pick_h[h] = v

    def emit_tp(h):
        v = None
        for i, fc in enumerate(range(h, NCH, 4)):
            v = sc.emit("sp", lambda e, fc=fc: e.dma_start(
                attn_fm[:, fc, :], cc_my[:, fc * 128:(fc + 1) * 128], transpose=True),
                waits=[(f"gP{h % 2}", v_pick_h[h])] if i == 0 else (), inc=f"sT{h}", amt=16)
        v_tp_h[h] = v
    for hh in range(HPC + 1):
        if hh < HPC:
            hp_ = hh % 2
            vw = None
            for wsb, wt in ((wq_sb[hp_], wqT), (wk_sb[hp_], wkT), (wv_sb[hp_], wvT)):
                vw = sc.dma("gp", f"gW{hp_}", wsb[:, :, :],
                            wt[hh * 128:(hh + 1) * 128, :].rearrange("p (c o) -> p c o", o=DH),
                            waits=[("pe", v_proj_end[hp_])] if v_proj_end[hp_] else [])
            v_w_loaded[hp_] = vw
        if hh == 0:
            continue
        h = hh - 1
        hp = h % 2
        v_prev_pe = sc.val("pe")
        W = [(f"gW{hp}", v_w_loaded[hp])]

        vone = sc.emit("gp", lambda e: e.memset(vsb[:, :, 128:132], 1.0),
                       waits=[("pe", sc.val("pe"))], inc="gpc")

        v_q_done = [0] * NTT
        v_k_done = [0] * NTT
        v_v_done = [0] * NTT
        for tt in range(NTT):
            bi = tt % 2
            t0 = tt * 512
            w = [("pe", v_xt_used[bi])] if v_xt_used[bi] else []
            if h == 0:
                if v_xc_par[bi]:
                    w.append(("sXC", v_xc_par[bi]))
                v_xt_load[bi] = sc.dma(
                    "gp", f"gX{bi}", xt_sb[bi][:, :, :],
                    xTt[tt * 128:(tt + 1) * 128, :].rearrange("p (c t) -> p c t", t=512),
                    waits=w)
                v_xc[tt] = sc.dma(
                    "sp", "sXC", xbf[tt * 128:(tt + 1) * 128, :],
                    xt_sb[bi][:, :, :].rearrange("p c t -> p (c t)"),
                    waits=[(f"gX{bi}", v_xt_load[bi])])
                v_xc_par[bi] = v_xc[tt]
            else:
                w.append(("sXC", v_xc[tt]))
                v_xt_load[bi] = sc.dma(
                    "sp", f"gX{bi}", xt_sb[bi][:, :, :],
                    xbf[tt * 128:(tt + 1) * 128, :].rearrange("p (c t) -> p c t", t=512),
                    waits=w)
            wtb = [("dve", v_tb_used[bi])] if v_tb_used[bi] else []
            v_tb_load[bi] = sc.dma(
                "sp", f"gB{bi}", tb_sb[bi][:, :, :].rearrange("p a b -> p (a b)"),
                tbpk[:, tt * 2048:(tt + 1) * 2048], waits=wtb)
            if tt == 1 and h >= 1:
                emit_ag(h - 1, 1)
            if tt == 4 and h >= 1:
                emit_picks(h - 1)

            pq, pk, pv = pb[0 + (tt % 2)], pb[2 + (tt % 2)], pb[4 + (tt % 2)]
            ld = [(f"gX{bi}", v_xt_load[bi])] + W + TAB + CST
            _mm_group(sc, pq, [(pq.t[:, 0:512], wq_sb[hp][:, c, :], xt_sb[bi][:, c, :])
                              for c in range(NCH)], extra_waits=ld)
            _mm_group(sc, pk, [(pk.t[:, 0:512], wk_sb[hp][:, c, :], xt_sb[bi][:, c, :])
                              for c in range(NCH)], extra_waits=ld)
            vmms = []
            for sub in range(4):
                for c in range(NCH):
                    vmms.append((pv.t[:, sub * 128:(sub + 1) * 128],
                                 xt_sb[bi][:, c, sub * 128:(sub + 1) * 128],
                                 wv_sb[hp][:, c, :]))
            vstop = None
            wv0 = pv.wait_free() + ld
            for i, (o, l, r) in enumerate(vmms):
                st = (i % NCH == 0)
                sp_ = (i % NCH == NCH - 1)
                vstop = sc.emit(
                    "pe", lambda e, o=o, l=l, r=r, st=st, sp_=sp_: e.matmul(o, l, r, start=st, stop=sp_),
                    wv0 if i == 0 else (), inc="pe" if sp_ else None)
            pv.ready = ("pe", vstop)
            pv.free = {}
            v_xt_used[bi] = sc.val("pe")

            def rope(ptile, tmp, sw, ct, st_, dest, t0, bi):
                wva = [ptile.ready] + CST
                tk = id(tmp)
                if tk in last_v1:
                    wva.append(("dve", last_v1[tk]))
                va = sc.emit("act", lambda e: e.copy(tmp[:, :], ptile.t[:, 0:512]),
                             waits=wva, inc="act")
                ptile.read_done("act", va)
                vswp = _mm_group(sc, ptile, [(ptile.t[:, 0:512], swp_sb[:, :], tmp[:, :])],
                                 extra_waits=[("act", va)])
                wsw = [ptile.ready]
                if tk in last_v2:
                    wsw.append(("dve", last_v2[tk]))
                va2 = sc.emit("act", lambda e: e.copy(sw[:, :], ptile.t[:, 0:512]),
                              waits=wsw, inc="act")
                ptile.read_done("act", va2)
                v1 = sc.emit("dve", lambda e: e.tensor_mul(rope_t1[:, :], tmp[:, :], ct),
                             waits=[("act", va), ("dve", last_v3[0]), (f"gB{bi}", v_tb_load[bi])], inc="dve")
                v2 = sc.emit("dve", lambda e: e.tensor_mul(rope_t2[:, :], sw[:, :], st_),
                             waits=[("act", va2)], inc="dve")
                v3 = sc.emit("dve", lambda e, t0=t0: e.tensor_add(dest[:, t0:t0 + 512], rope_t1[:, :], rope_t2[:, :]),
                             waits=[("pe", v_prev_pe), ("dve", v2)], inc="dve")
                last_v1[tk] = v1
                last_v2[tk] = v2
                last_v3[0] = v3
                return v3

            v_q_done[tt] = rope(pq, qtmp, qsw, tb_sb[bi][:, 0, :], tb_sb[bi][:, 1, :], qbf, t0, bi)
            v_k_done[tt] = rope(pk, ktmp, ksw, tb_sb[bi][:, 2, :], tb_sb[bi][:, 3, :], kbf, t0, bi)
            v_tb_used[bi] = v_k_done[tt]
            vv = None
            for sub in range(4):
                vv = sc.emit("act",
                             lambda e, sub=sub, tt=tt, pvt=pv.t: e.copy(
                                 vsb[:, tt * 4 + sub, 0:128], pvt[:, sub * 128:(sub + 1) * 128]),
                             waits=[pv.ready, ("gpc", vone)], inc="act")
            pv.read_done("act", vv)
            v_v_done[tt] = vv
        v_proj_end[hp] = sc.val("pe")

        for qt in range(NTT):
            if qt == 0 and h >= 1:
                emit_tp(h - 1)
            if qt == 0 and h == HPC - 1:
                pre["wr"] = sc.dma("gp", "gT", wr_sb[:, :, :],
                                   wrouter[:, :].rearrange("p (c e) -> p c e", e=NE))
                pre[0] = sc.dma(
                    "gp", "gWo0", wo_sb0[:, :, :],
                    woT[0:128, :].rearrange("p (c o) -> p c o", o=128))
            q0 = qt * 512
            nkc = qt * 4 + 4
            v_score = [0] * nkc
            v_exp = [0] * nkc
            v_pv = [0] * nkc
            qstart = [0] * 4
            v_pv_stop = [0] * 4

            out_pb = [pb[2], pb[3], pb[4], pb[5]]
            sco_pb = [pb[0], pb[1], pb[6]]

            def emit_score(kc):
                bank = sco_pb[kc % 3]
                dq = kc - qt * 4
                qlo = max(0, dq) * 128
                w = bank.wait_free()
                w += [("dve", v_k_done[kc // 4]), ("dve", v_q_done[qt])]
                v = sc.emit("pe", lambda e, kc=kc, qlo=qlo, bank=bank, q0=q0: e.matmul(
                    bank.t[:, 0:512 - qlo], kbf[:, kc * 128:(kc + 1) * 128],
                    qbf[:, q0 + qlo:q0 + 512], start=True, stop=True),
                    w, inc="pe")
                bank.ready = ("pe", v)
                bank.free = {}
                v_score[kc] = v
                if dq >= 0:
                    vm = sc.emit("dve", lambda e, bank=bank: e.tensor_add(
                        bank.t[:, 0:128], bank.t[:, 0:128], tri_sb[:, :]),
                        [("pe", v), ("gT", v_cst_dma)], inc="dve")
                    bank.ready = ("dve", vm)
                wexp = [bank.ready]
                if kc >= 3 and v_pv[kc - 3]:
                    wexp.append(("pe", v_pv[kc - 3]))
                ve = sc.emit("act", lambda e, kc=kc, qlo=qlo, bank=bank: e.activation(
                    probs[kc % 3][:, 0:512 - qlo], bank.t[:, 0:512 - qlo], AF.Exp,
                    bias=biasC[:, 0:1]), wexp, inc="act")
                bank.read_done("act", ve)
                v_exp[kc] = ve

            def emit_pv(kc):
                dq = kc - qt * 4
                qlo = max(0, dq) * 128
                last = None
                for qc in range(qlo // 128, 4):
                    bank = out_pb[qc]
                    qcg = qt * 4 + qc
                    st = (kc == 0)
                    sp_ = (kc == qcg)
                    w = [("act", v_exp[kc])]
                    if st:
                        w += bank.wait_free()
                    last = sc.emit("pe", lambda e, kc=kc, qc=qc, qlo=qlo, bank=bank, st=st, sp_=sp_: e.matmul(
                        bank.t[:, 0:129],
                        probs[kc % 3][:, qc * 128 - qlo:(qc + 1) * 128 - qlo],
                        vsb[:, kc, 0:129], start=st, stop=sp_),
                        w, inc="pe" if (sp_ or qc == 3) else None)
                    if st:
                        bank.free = {}
                    if sp_:
                        v_pv_stop[qc] = last
                        bank.ready = ("pe", last)
                v_pv[kc] = last

            emit_score(0)
            emit_score(1)
            for kc in range(2, nkc):
                emit_score(kc)
                emit_pv(kc - 2)
            emit_pv(nkc - 2)
            emit_pv(nkc - 1)

            vcp = None
            for qc in range(4):
                bank = out_pb[qc]
                wr_ = [("pe", v_pv_stop[qc]), ("dve", sc.val("dve"))]
                vr = sc.emit("dve", lambda e, qc=qc, bank=bank: e.reciprocal(
                    rc_sb[:, qc:qc + 1], bank.t[:, 128:129]),
                    wr_, inc="dve")
                wao = [("dve", vr)]
                if v_ao_dma[qt % 2]:
                    wao.append((f"sS{qt % 2}", v_ao_dma[qt % 2]))
                vcp = sc.emit("dve", lambda e, qc=qc, bank=bank, qt=qt: e.tensor_scalar(
                    ao_sb[qt % 2][:, qc, :], bank.t[:, 0:128], rc_sb[:, qc:qc + 1], None,
                    ALU.mult), wao, inc="dve")
                bank.read_done("dve", vcp)
                v_rc_read[qc] = vcp
            v_ao_dma[qt % 2] = sc.dma(
                "act", f"sS{qt % 2}", cc_in_h[h][q0:q0 + 512, :].rearrange("(qc p) f -> p qc f", p=128),
                ao_sb[qt % 2][:, :, :], waits=[("dve", vcp)])
            if qt == 7:
                emit_ag(h, 0)

    emit_ag(HPC - 1, 1)
    emit_picks(HPC - 1)
    emit_tp(HPC - 1)

    v_attn_pe_end = sc.val("pe")
    attn_es.close()
    sc.barrier(exclude=("cc", "gP0", "gP1", "sT0", "sT1", "sT2", "sT3"))


    wo_es = ExitStack()
    wo_sb = [wo_sb0, wo_es.enter_context(nc.sbuf_tensor("wo1", [128, NCH, 128], BF16))]
    x1pre = wo_es.enter_context(nc.sbuf_tensor("x1pre", [128, NCH, 512], F32))
    xres_sb = [wo_es.enter_context(nc.sbuf_tensor(f"xres{i}", [128, 512], F32)) for i in range(2)]
    xsq = wo_es.enter_context(nc.sbuf_tensor("xsq", [128, 512], F32))
    stat_sb = wo_es.enter_context(nc.sbuf_tensor("stat_sb", [1, 6 * 512], F32))
    mb_sb = wo_es.enter_context(nc.sbuf_tensor("mb_sb", [128, 512], F32))
    rb_sb = wo_es.enter_context(nc.sbuf_tensor("rb_sb", [128, 512], F32))
    logits_sb = wo_es.enter_context(nc.sbuf_tensor("logits_sb", [128, NCH, NE], F32))
    pr_sb = wo_es.enter_context(nc.sbuf_tensor("pr_sb", [128, NE], F32))
    eq_sb = wo_es.enter_context(nc.sbuf_tensor("eq_sb", [128, NE], F32))
    pm_sb = wo_es.enter_context(nc.sbuf_tensor("pm_sb", [128, NE], F32))
    sc1_sb = wo_es.enter_context(nc.sbuf_tensor("sc1_sb", [128, 4], F32))

    v_wr = pre["wr"]

    xsq2 = wo_es.enter_context(nc.sbuf_tensor("xsq2", [128, 2, 512], F32))

    v_spill_tile = [0, 0, 0, 0]
    v_wo_load = [0, 0]
    v_wo_used = [None, None]
    v_xres_used = [None, None]
    v_x1_readers = []
    v_x1sp_last = 0
    v_logit_evac = 0
    for tt in range(4):
        t0 = tt * 512
        sbank, s2bank = pb[6], pb[7]
        v_s2mm = [0] * NCH
        v_s1mm = [0] * NCH
        v_add = [0] * NCH
        v_sq = [0] * NCH

        def emit_smm(oc):
            st, sp_ = (oc == 0), (oc == NCH - 1)
            w1 = [("dve", v_add[oc])]
            if st:
                w1 += sbank.wait_free() + CST
            v_s1mm[oc] = sc.emit("pe", lambda e, oc=oc, st=st, sp_=sp_: e.matmul(
                sbank.t[0:1, 0:512], ones128_sb[:, :], x1pre[:, oc, :], start=st, stop=sp_),
                w1, inc="pe")
            w2 = [("act", v_sq[oc])]
            if st:
                w2 += s2bank.wait_free()
            v_s2mm[oc] = sc.emit("pe", lambda e, oc=oc, st=st, sp_=sp_: e.matmul(
                s2bank.t[0:1, 0:512], ones128_sb[:, :], xsq2[:, oc % 2, :], start=st, stop=sp_),
                w2, inc="pe")

        corder = [c for m in range(4) for c in range(m, NCH, 4)]
        for oc in range(NCH):
            bi = oc % 2
            if tt == 0 and oc == 0:
                v_wo_load[bi] = pre[0]
            else:
                w = [("pe", v_wo_used[bi])] if v_wo_used[bi] else []
                v_wo_load[bi] = sc.dma(
                    "gp", f"gWo{bi}", wo_sb[bi][:, :, :],
                    woT[oc * 128:(oc + 1) * 128, :].rearrange("p (c o) -> p c o", o=128),
                    waits=w)
            wbank = pb[oc % 2]
            wf = wbank.wait_free() + [(f"gWo{bi}", v_wo_load[bi])]
            vstop = None
            for ci, c in enumerate(corder):
                ww = list(wf) if ci == 0 else []
                ww.append((f"sT{c % 4}", v_tp_h[c % 4]))
                vstop = sc.emit("pe", lambda e, c=c, bi=bi, wbank=wbank, t0=t0, st=(ci == 0), sp_=(ci == NCH - 1): e.matmul(
                    wbank.t[:, 0:512], wo_sb[bi][:, c, :], attn_fm[:, c, t0:t0 + 512],
                    start=st, stop=sp_), ww, inc="pe" if ci == NCH - 1 else None)
            wbank.ready = ("pe", vstop)
            wbank.free = {}
            v_wo_used[bi] = sc.val("pe")
            xbi = oc % 2
            wx = [("dve", v_xres_used[xbi])] if v_xres_used[xbi] else []
            vxr = sc.dma("sp", f"sR{xbi}", xres_sb[xbi][:, :], xres[oc * 128:(oc + 1) * 128, t0:t0 + 512], waits=wx)
            wadd = [wbank.ready, (f"sR{xbi}", vxr)] + v_x1_readers
            v_add[oc] = sc.emit("dve", lambda e, oc=oc, xbi=xbi, wbank=wbank: e.tensor_add(
                x1pre[:, oc, :], wbank.t[:, 0:512], xres_sb[xbi][:, :]), wadd, inc="dve")
            wbank.read_done("dve", v_add[oc])
            v_xres_used[xbi] = v_add[oc]
            wsq = [("dve", v_add[oc])]
            if oc >= 2:
                wsq.append(("pe", v_s2mm[oc - 2]))
            v_sq[oc] = sc.emit("act", lambda e, oc=oc: e.activation(
                xsq2[:, oc % 2, :], x1pre[:, oc, :], AF.Square), wsq, inc="act")
            if oc >= 1:
                emit_smm(oc - 1)
        emit_smm(NCH - 1)
        v_x1_readers = []
        sbank.ready = ("pe", v_s1mm[NCH - 1])
        sbank.free = {}
        s2bank.ready = ("pe", v_s2mm[NCH - 1])
        s2bank.free = {}

        st_ = stat_sb
        va_m = sc.emit("act", lambda e: e.activation(st_[0:1, 0:512], sbank.t[0:1, 0:512],
                                                     AF.Copy, scale=invD[0:1, 0:1]),
                       [sbank.ready] + CST, inc="act")
        va_e2 = sc.emit("act", lambda e: e.activation(st_[0:1, 512:1024], s2bank.t[0:1, 0:512],
                                                      AF.Copy, scale=invD[0:1, 0:1]),
                        [s2bank.ready], inc="act")
        sbank.read_done("act", va_m)
        s2bank.read_done("act", va_e2)
        vd_mm = sc.emit("dve", lambda e: e.tensor_mul(st_[0:1, 1024:1536], st_[0:1, 0:512], st_[0:1, 0:512]),
                        [("act", va_m)], inc="dve")
        vd_var = sc.emit("dve", lambda e: e.tensor_sub(st_[0:1, 1536:2048], st_[0:1, 512:1024], st_[0:1, 1024:1536]),
                         [("act", va_e2), ("dve", vd_mm)], inc="dve")
        va_ln = sc.emit("act", lambda e: e.activation(st_[0:1, 2048:2560], st_[0:1, 1536:2048],
                                                      AF.Ln, bias=epsT[0:1, 0:1]),
                        [("dve", vd_var)], inc="act")
        va_rstd = sc.emit("act", lambda e: e.activation(st_[0:1, 2560:3072], st_[0:1, 2048:2560],
                                                        AF.Exp, scale=half_neg[0:1, 0:1]),
                          [("act", va_ln)], inc="act")
        mbank, rbank = pb[2], pb[3]
        vb_m = _mm_group(sc, mbank, [(mbank.t[:, 0:512], ones1_sb[:, :], st_[0:1, 0:512])],
                         extra_waits=[("act", va_m)] + CST)
        vb_r = _mm_group(sc, rbank, [(rbank.t[:, 0:512], ones1_sb[:, :], st_[0:1, 2560:3072])],
                         extra_waits=[("act", va_rstd)])
        ve_m = sc.emit("act", lambda e: e.copy(mb_sb[:, :], mbank.t[:, 0:512]), [mbank.ready], inc="act")
        ve_r = sc.emit("act", lambda e: e.copy(rb_sb[:, :], rbank.t[:, 0:512]), [rbank.ready], inc="act")
        mbank.read_done("act", ve_m)
        rbank.read_done("act", ve_r)

        v_nrm = [0] * NCH
        for oc in range(NCH):
            vs = sc.emit("dve", lambda e, oc=oc: e.tensor_sub(
                x1pre[:, oc, :], x1pre[:, oc, :], mb_sb[:, :]),
                [("act", ve_m), ("pe", v_s2mm[NCH - 1]), ("dve", sc.val("dve"))], inc="dve")
            v_nrm[oc] = sc.emit("dve", lambda e, oc=oc: e.tensor_mul(
                x1pre[:, oc, :], x1pre[:, oc, :], rb_sb[:, :]),
                [("act", ve_r), ("dve", vs)], inc="dve")
            v_x1sp_last = sc.dma("sp", "sL", x1sp[tt * 128:(tt + 1) * 128, oc * 512:(oc + 1) * 512],
                                 x1pre[:, oc, :], waits=[("dve", v_nrm[oc])])
        v_x1_readers.append(("sL", v_x1sp_last))
        v_spill_tile[tt] = v_x1sp_last

        lbank = pb[4]
        for tc in range(4):
            g = tt * 4 + tc
            mms = [(lbank.t[0:128, 0:NE], x1pre[:, oc, tc * 128:(tc + 1) * 128], wr_sb[:, oc, :])
                   for oc in range(NCH)]
            _mm_group(sc, lbank, mms,
                      extra_waits=[("dve", v_nrm[NCH - 1]), ("gT", v_wr)])
            v_logit_evac = sc.emit("act", lambda e, g=g: e.copy(
                logits_sb[:, g, :], lbank.t[0:128, 0:NE]), [lbank.ready], inc="act")
            lbank.read_done("act", v_logit_evac)
            v_x1_readers.append(("pe", sc.val("pe")))

    tbank = pb[5]
    v_comb_evac = 0
    v_tp_prev = 0
    for g in range(NCH):
        c0 = g * 128
        vm = sc.emit("dve", lambda e, g=g: e.tensor_reduce(
            sc1_sb[:, 0:1], logits_sb[:, g, :], AX.X, ALU.max), [("act", v_logit_evac)], inc="dve")
        wls = [("dve", vm)]
        if v_tp_prev:
            wls.append(("pe", v_tp_prev))
        vls = sc.emit("dve", lambda e, g=g: e.tensor_single_scalar(
            pr_sb[:, :], logits_sb[:, g, :], sc1_sb[:, 0:1], ALU.subtract),
            wls, inc="dve")
        vex = sc.emit("act", lambda e: e.activation(pr_sb[:, :], pr_sb[:, :], AF.Exp),
                      [("dve", vls)], inc="act")
        vsm = sc.emit("dve", lambda e: e.tensor_reduce(
            sc1_sb[:, 1:2], pr_sb[:, :], AX.X, ALU.add), [("act", vex)], inc="dve")
        vrs = sc.emit("dve", lambda e: e.reciprocal(sc1_sb[:, 2:3], sc1_sb[:, 1:2]),
                      [("dve", vsm)], inc="dve")
        vpr = sc.emit("dve", lambda e: e.tensor_single_scalar(
            pr_sb[:, :], pr_sb[:, :], sc1_sb[:, 2:3], ALU.mult), [("dve", vrs)], inc="dve")
        vm1 = sc.emit("dve", lambda e: e.tensor_reduce(
            sc1_sb[:, 0:1], pr_sb[:, :], AX.X, ALU.max), [("dve", vpr)], inc="dve")
        veq = sc.emit("dve", lambda e: e.tensor_single_scalar(
            eq_sb[:, :], pr_sb[:, :], sc1_sb[:, 0:1], ALU.is_ge), [("dve", vm1)], inc="dve")
        vpm = sc.emit("dve", lambda e: e.scalar_tensor_tensor(
            pm_sb[:, :], eq_sb[:, :], -1.0e30, pr_sb[:, :], ALU.mult, ALU.add),
            [("dve", veq)] + CST, inc="dve")
        vm2 = sc.emit("dve", lambda e: e.tensor_reduce(
            sc1_sb[:, 3:4], pm_sb[:, :], AX.X, ALU.max), [("dve", vpm)], inc="dve")
        vge = sc.emit("dve", lambda e: e.tensor_single_scalar(
            eq_sb[:, :], pr_sb[:, :], sc1_sb[:, 3:4], ALU.is_ge), [("dve", vm2)], inc="dve")
        vcb = sc.emit("dve", lambda e: e.tensor_mul(pr_sb[:, :], pr_sb[:, :], eq_sb[:, :]),
                      [("dve", vge)], inc="dve")
        wtp = tbank.wait_free() + [("dve", vcb)]
        vt = sc.emit("pe", lambda e: e.matmul(
            tbank.t[0:NE, 0:128], pr_sb[:, :], ident_sb[:, :],
            start=True, stop=True), wtp, inc="pe")
        tbank.ready = ("pe", vt)
        tbank.free = {}
        v_tp_prev = vt
        v_comb_evac = sc.emit("act", lambda e, c0=c0: e.copy(
            comb_sb[:, c0:c0 + 128], tbank.t[0:NE, 0:128]), [tbank.ready], inc="act")
        tbank.read_done("act", v_comb_evac)

    wo_es.close()
    wo_outer.close()
    sc.barrier()

    ffn_es = ExitStack()
    sel8_sb = ffn_es.enter_context(nc.sbuf_tensor("sel8_sb", [8, 8 * 128], F32))
    v_sel8 = sc.dma("gp", "gT", sel8_sb[:, :], sel8[:, :])
    y_acc = ffn_es.enter_context(nc.sbuf_tensor("y_acc", [128, NCH, 1024], F32))
    x1bh = ffn_es.enter_context(nc.sbuf_tensor("x1bh", [128, NCH, 1024], BF16))
    h_sb = ffn_es.enter_context(nc.sbuf_tensor("h_sb", [128, NDE, 1024], BF16))
    wg_sb = [ffn_es.enter_context(nc.sbuf_tensor(f"wg{i}", [128, NCH, 128], BF16)) for i in range(2)]
    wu_sb = [ffn_es.enter_context(nc.sbuf_tensor(f"wu{i}", [128, NCH, 128], BF16)) for i in range(2)]
    wd_sb = ffn_es.enter_context(nc.sbuf_tensor("wd_sb", [128, 6, NCH, 128], BF16))
    sg_sb = ffn_es.enter_context(nc.sbuf_tensor("sg_sb", [128, 2, 512], F32))
    cb_sb = ffn_es.enter_context(nc.sbuf_tensor("cb_sb", [128, 1024], F32))
    xq2 = ffn_es.enter_context(nc.sbuf_tensor("xq2", [128, 2, 512], F32))
    st2 = ffn_es.enter_context(nc.sbuf_tensor("st2", [1, 3 * 512], F32))
    mb2 = ffn_es.enter_context(nc.sbuf_tensor("mb2", [128, 512], F32))
    rb2 = ffn_es.enter_context(nc.sbuf_tensor("rb2", [128, 512], F32))
    x2t = [ffn_es.enter_context(nc.sbuf_tensor(f"x2t{i}", [128, 512], F32)) for i in range(2)]

    stages = [(wsg, wsu, wsd, 0, 0, 0, None), (wsg, wsu, wsd, 11, 0, 0, None)] + \
             [(weg, weu, wed, 0, e * NDE, e * DE, e) for e in range(NE)]
    JGRP = [(0, 6), (6, 11)]
    ffn_pre = {}

    v_y_read_prev = []
    v_x1b_pe_prev = 0
    v_out_last = 0
    v_store = [0, 0]
    v_prev_s2 = 0
    v_prev_norm = 0
    for hf in range(2):
        t0h = hf * 1024
        v_ya0 = None
        if hf == 0:
            for j_ in range(2):
                ffn_pre[(j_, "g")] = sc.dma(
                    "gp", f"gG{j_}", wg_sb[j_][:, :, :],
                    wsg[j_ * 128:(j_ + 1) * 128, :].rearrange("p (c o) -> p c o", o=128))
                ffn_pre[(j_, "u")] = sc.dma(
                    "gp", f"gG{j_}", wu_sb[j_][:, :, :],
                    wsu[j_ * 128:(j_ + 1) * 128, :].rearrange("p (c o) -> p c o", o=128))
        for ts in range(2):
            g = hf * 2 + ts
            wldx = [("sL", v_spill_tile[g])]
            if v_x1b_pe_prev:
                wldx.append(("pe", v_x1b_pe_prev))
            sc.dma("gp", "gY", x1bh[:, :, ts * 512:(ts + 1) * 512],
                   x1sp[g * 128:(g + 1) * 128, :].rearrange("p (c t) -> p c t", t=512), waits=wldx)
            wldy = [("sL", v_spill_tile[g])] + v_y_read_prev
            v_ya0 = sc.dma("gp", "gY", y_acc[:, :, ts * 512:(ts + 1) * 512],
                           x1sp[g * 128:(g + 1) * 128, :].rearrange("p (c t) -> p c t", t=512), waits=wldy)

        v_h_read_pe = sc.val("pe")
        v_cb_read = sc.val("dve")
        v_wd_used = sc.val("pe")
        v_wg_used = [sc.val("pe"), sc.val("pe")]
        v_sg_used = [sc.val("dve"), sc.val("dve")]
        v_ya_last = sc.val("dve")
        for snum, (wgt, wut, wdt, co, rg_, rd_, ex) in enumerate(stages):
            if ex is not None:
                cbank = pb[6]
                v_cb = 0
                for ts in range(2):
                    _mm_group(sc, cbank, [(cbank.t[:, 0:512], sel8_sb[:, ex * 128:(ex + 1) * 128],
                                           comb_sb[:, t0h + ts * 512:t0h + (ts + 1) * 512])],
                              extra_waits=[("act", v_comb_evac), ("gT", v_sel8)])
                    wcb = [cbank.ready, ("dve", v_cb_read)]
                    v_cb = sc.emit("act", lambda e, ts=ts, cbank=cbank: e.copy(
                        cb_sb[:, ts * 512:(ts + 1) * 512], cbank.t[:, 0:512]), wcb, inc="act")
                    cbank.read_done("act", v_cb)

            for g0, g1 in JGRP:
                v_h_w = 0
                for j in range(g0, g1):
                    jp = j % 2
                    rb = (rg_ + co + j) * 128
                    if hf == 0 and snum == 0 and j < 2:
                        vwg = ffn_pre[(j, "g")]
                        vwu = ffn_pre[(j, "u")]
                    else:
                        wwl = [("pe", v_wg_used[jp])]
                        vwg = sc.dma("gp", f"gG{jp}", wg_sb[jp][:, :, :],
                                     wgt[rb:rb + 128, :].rearrange("p (c o) -> p c o", o=128),
                                     waits=wwl)
                        vwu = sc.dma("gp", f"gG{jp}", wu_sb[jp][:, :, :],
                                     wut[rb:rb + 128, :].rearrange("p (c o) -> p c o", o=128))
                    for ts in range(2):
                        gbank, ubank = pb[0 + ts], pb[2 + ts]
                        wg_ = [(f"gG{jp}", vwu), ("gY", v_ya0)]
                        _mm_group(sc, gbank, [(gbank.t[:, 0:512], wg_sb[jp][:, c, :],
                                               x1bh[:, c, ts * 512:(ts + 1) * 512]) for c in range(NCH)],
                                  extra_waits=wg_)
                        _mm_group(sc, ubank, [(ubank.t[:, 0:512], wu_sb[jp][:, c, :],
                                               x1bh[:, c, ts * 512:(ts + 1) * 512]) for c in range(NCH)])
                        v_wg_used[jp] = sc.val("pe")
                        wsl = [gbank.ready, ("dve", v_sg_used[ts])]
                        vsl = sc.emit("act", lambda e, ts=ts, gbank=gbank: e.activation(
                            sg_sb[:, ts, :], gbank.t[:, 0:512], AF.Silu), wsl, inc="act")
                        gbank.read_done("act", vsl)
                        whh = [("act", vsl), ubank.ready, ("pe", v_h_read_pe)]
                        if ex is not None:
                            vcbf = sc.emit("dve", lambda e, ts=ts: e.tensor_mul(
                                sg_sb[:, ts, :], sg_sb[:, ts, :], cb_sb[:, ts * 512:(ts + 1) * 512]),
                                [("act", vsl), ("act", v_cb), ("dve", sc.val("dve"))], inc="dve")
                            whh.append(("dve", vcbf))
                        vh = sc.emit("dve", lambda e, j=j, ts=ts, ubank=ubank: e.tensor_mul(
                            h_sb[:, j, ts * 512:(ts + 1) * 512], sg_sb[:, ts, :], ubank.t[:, 0:512]),
                            whh, inc="dve")
                        ubank.read_done("dve", vh)
                        v_sg_used[ts] = vh
                        v_h_w = vh
                v_wd = 0
                for j in range(g0, g1):
                    wwd = [("pe", v_wd_used)] if j == g0 else []
                    v_wd = sc.dma("gp", "gD", wd_sb[:, j - g0, :, :],
                                  wdt[rd_ + (co + j) * 128:rd_ + (co + j + 1) * 128, :].rearrange("p (c o) -> p c o", o=128),
                                  waits=wwd)
                for ts in range(2):
                    for oc in range(NCH):
                        ybank = pb[4 + ((oc + ts) % 2)]
                        wdn = [("dve", v_h_w), ("gD", v_wd), ("gY", v_ya0)]
                        _mm_group(sc, ybank, [(ybank.t[:, 0:512], wd_sb[:, j2 - g0, oc, :],
                                               h_sb[:, j2, ts * 512:(ts + 1) * 512]) for j2 in range(g0, g1)],
                                  extra_waits=wdn)
                        v_ya_last = sc.emit("dve", lambda e, oc=oc, ts=ts, ybank=ybank: e.tensor_add(
                            y_acc[:, oc, ts * 512:(ts + 1) * 512], ybank.t[:, 0:512],
                            y_acc[:, oc, ts * 512:(ts + 1) * 512]),
                            [ybank.ready, ("dve", sc.val("dve")), ("gY", v_ya0)], inc="dve")
                        ybank.read_done("dve", v_ya_last)
                v_wd_used = sc.val("pe")
            v_h_read_pe = sc.val("pe")
            v_cb_read = sc.val("dve")
        v_x1b_pe_prev = sc.val("pe")

        v_y_read_prev = []
        for ts in range(2):
            sbank, s2bank = pb[6], pb[7]
            v_sq2 = [0] * NCH
            v_m1 = [0] * NCH
            v_m2 = [0] * NCH

            for oc in range(NCH):
                wsq = [("dve", v_ya_last), ("pe", v_m2[oc - 2] if oc >= 2 else v_prev_s2)]
                v_sq2[oc] = sc.emit("act", lambda e, oc=oc, ts=ts: e.activation(
                    xq2[:, oc % 2, :], y_acc[:, oc, ts * 512:(ts + 1) * 512], AF.Square),
                    wsq, inc="act")
                w1 = [("dve", v_ya_last)]
                if oc == 0:
                    w1 += sbank.wait_free() + CST
                v_m1[oc] = sc.emit("pe", lambda e, oc=oc, ts=ts, st=(oc == 0), sp_=(oc == NCH - 1): e.matmul(
                    sbank.t[0:1, 0:512], ones128_sb[:, :], y_acc[:, oc, ts * 512:(ts + 1) * 512],
                    start=st, stop=sp_), w1, inc="pe")
                w2 = [("act", v_sq2[oc])]
                if oc == 0:
                    w2 += s2bank.wait_free()
                v_m2[oc] = sc.emit("pe", lambda e, oc=oc, st=(oc == 0), sp_=(oc == NCH - 1): e.matmul(
                    s2bank.t[0:1, 0:512], ones128_sb[:, :], xq2[:, oc % 2, :],
                    start=st, stop=sp_), w2, inc="pe")
            sbank.ready = ("pe", v_m1[NCH - 1])
            sbank.free = {}
            s2bank.ready = ("pe", v_m2[NCH - 1])
            s2bank.free = {}
            v_prev_s2 = v_m2[NCH - 1]

            va_m = sc.emit("act", lambda e: e.activation(st2[0:1, 0:512], sbank.t[0:1, 0:512],
                                                         AF.Copy, scale=invD[0:1, 0:1]),
                           [sbank.ready], inc="act")
            va_e2 = sc.emit("act", lambda e: e.activation(st2[0:1, 512:1024], s2bank.t[0:1, 0:512],
                                                          AF.Copy, scale=invD[0:1, 0:1]),
                            [s2bank.ready], inc="act")
            sbank.read_done("act", va_m)
            s2bank.read_done("act", va_e2)
            vd_mm = sc.emit("dve", lambda e: e.tensor_mul(st2[0:1, 1024:1536], st2[0:1, 0:512], st2[0:1, 0:512]),
                            [("act", va_m), ("dve", sc.val("dve"))], inc="dve")
            vd_var = sc.emit("dve", lambda e: e.tensor_sub(st2[0:1, 512:1024], st2[0:1, 512:1024], st2[0:1, 1024:1536]),
                             [("act", va_e2), ("dve", vd_mm)], inc="dve")
            va_ln = sc.emit("act", lambda e: e.activation(st2[0:1, 1024:1536], st2[0:1, 512:1024],
                                                          AF.Ln, bias=epsT[0:1, 0:1]),
                            [("dve", vd_var)], inc="act")
            va_rstd = sc.emit("act", lambda e: e.activation(st2[0:1, 512:1024], st2[0:1, 1024:1536],
                                                            AF.Exp, scale=half_neg[0:1, 0:1]),
                              [("act", va_ln)], inc="act")
            mbank, rbank = pb[2], pb[3]
            _mm_group(sc, mbank, [(mbank.t[:, 0:512], ones1_sb[:, :], st2[0:1, 0:512])],
                      extra_waits=[("act", va_m)])
            _mm_group(sc, rbank, [(rbank.t[:, 0:512], ones1_sb[:, :], st2[0:1, 512:1024])],
                      extra_waits=[("act", va_rstd)])
            ve_m = sc.emit("act", lambda e: e.copy(mb2[:, :], mbank.t[:, 0:512]),
                           [mbank.ready, ("dve", v_prev_norm)], inc="act")
            ve_r = sc.emit("act", lambda e: e.copy(rb2[:, :], rbank.t[:, 0:512]),
                           [rbank.ready, ("dve", v_prev_norm)], inc="act")
            mbank.read_done("act", ve_m)
            rbank.read_done("act", ve_r)

            for oc in range(NCH):
                op = oc % 2
                wn = [("act", ve_m), ("act", ve_r), ("dve", sc.val("dve"))]
                if v_store[op]:
                    wn.append((f"sO{op}", v_store[op]))
                vs = sc.emit("dve", lambda e, oc=oc, ts=ts, op=op: e.tensor_sub(
                    x2t[op][:, :], y_acc[:, oc, ts * 512:(ts + 1) * 512], mb2[:, :]), wn, inc="dve")
                vn = sc.emit("dve", lambda e, op=op: e.tensor_mul(
                    x2t[op][:, :], x2t[op][:, :], rb2[:, :]), [("dve", vs)], inc="dve")
                v_out_last = sc.dma("sp", f"sO{op}", outfm[oc * 128:(oc + 1) * 128, t0h + ts * 512:t0h + (ts + 1) * 512],
                                    x2t[op][:, :], waits=[("dve", vn)])
                v_store[op] = v_out_last
                v_y_read_prev.append(("dve", vn))
                v_prev_norm = vn
            v_y_read_prev.append(("pe", sc.val("pe")))
            v_y_read_prev.append(("act", v_sq2[NCH - 1]))

    ffn_es.close()
    _finish(nc, sc, sems, es, [("sO0", sc.val("sO0")), ("sO1", sc.val("sO1"))])
    return nc


def _finish(nc, sc, sems, es, final_waits):
    sc.emit("sp", lambda e: e.nop(), waits=list(final_waits))
    for s in SEMS:
        assert sc.cnt.get(s, 0) < 60000, (s, sc.cnt[s])
    with nc.Block() as block:
        def runner(name):
            def _r(eng):
                hw = {}
                for waits, fn, inc, amt in sc.q[name]:
                    for sem, v in waits:
                        if v and v > hw.get(sem, 0):
                            eng.wait_ge(sems[sem], v)
                            hw[sem] = v
                    ins = fn(eng)
                    if inc is not None:
                        ins.then_inc(sems[inc], amt)
            return _r
        block.tensor(runner("pe"))
        block.scalar(runner("act"))
        block.vector(runner("dve"))
        block.sync(runner("sp"))
        block.gpsimd(runner("gp"))
    es.close()


def _rope_tables():
    idx = np.arange(0, DH, 2, dtype=np.float32)
    inv_freq = (np.float32(10000.0) ** (-idx / np.float32(DH))).astype(np.float32)
    angles = (np.arange(S, dtype=np.float32)[:, None] * inv_freq[None, :]).astype(np.float32)
    cos = np.cos(angles).astype(np.float32)
    sin = np.sin(angles).astype(np.float32)
    cos_e = np.concatenate([cos.T, cos.T], axis=0)
    sin_s = np.concatenate([-sin.T, sin.T], axis=0)
    scl = np.float32(1.0 / np.sqrt(np.float32(DH)))
    tabs = np.stack([cos_e * scl, sin_s * scl, cos_e, sin_s], axis=1)
    pk = tabs.reshape(128, 4, NTT, 512).transpose(0, 2, 1, 3).reshape(128, NTT * 4 * 512)
    return np.ascontiguousarray(pk.astype(ml_dtypes.bfloat16))


def _tile_w(w, nj, od):
    return np.ascontiguousarray(
        np.asarray(w, np.float32).reshape(NCH, 128, nj, od)
        .transpose(2, 1, 0, 3).reshape(nj * 128, NCH * od))


def _stage(inputs):
    x = np.asarray(inputs["L0_attn_x"], np.float32)
    wq = np.asarray(inputs["L0_attn_Wq"], np.float32)
    wk = np.asarray(inputs["L0_attn_Wk"], np.float32)
    wv = np.asarray(inputs["L0_attn_Wv"], np.float32)
    wo = np.asarray(inputs["L0_attn_Wo"], np.float32)

    perm = np.concatenate([np.arange(0, DH, 2), np.arange(1, DH, 2)])
    wqT, wkT, wvT = wq.T, wk.T, wv.T
    tbpk = _rope_tables()

    tri = np.where(np.arange(128)[:, None] <= np.arange(128)[None, :], 0.0, MASKVAL).astype(np.float32)
    identm = np.eye(128, dtype=np.float32)
    swpm = np.zeros((128, 128), dtype=np.float32)
    for i in range(128):
        swpm[(i + 64) % 128, i] = 1.0
    sel8 = np.zeros((8, 8 * 128), dtype=np.float32)
    for e in range(8):
        sel8[e, e * 128:(e + 1) * 128] = 1.0
    ones1 = np.ones((1, 128), dtype=np.float32)

    weg = np.asarray(inputs["L0_ffn_We_gate"], np.float32)
    weu = np.asarray(inputs["L0_ffn_We_up"], np.float32)
    shared = {
        "woT": _tile_w(wo.T, NCH, 128), "tbpk": tbpk,
        "tri": tri, "ident": identm, "swp": swpm, "sel8": sel8, "ones1": ones1,
        "wrouter": _tile_w(inputs["L0_ffn_W_router"], 1, NE),
        "wsg": _tile_w(inputs["L0_ffn_Ws_gate"], NDS, 128),
        "wsu": _tile_w(inputs["L0_ffn_Ws_up"], NDS, 128),
        "wsd": np.ascontiguousarray(inputs["L0_ffn_Ws_down"], dtype=np.float32),
        "weg": np.ascontiguousarray(np.concatenate(
            [_tile_w(weg[e], NDE, 128) for e in range(NE)], axis=0)),
        "weu": np.ascontiguousarray(np.concatenate(
            [_tile_w(weu[e], NDE, 128) for e in range(NE)], axis=0)),
        "wed": np.ascontiguousarray(np.asarray(inputs["L0_ffn_We_down"], np.float32).reshape(NE * DE, D)),
    }

    xTs = [np.ascontiguousarray(x[b].T) for b in range(B)]
    xTts = [_tile_w(xT, NTT, 512) for xT in xTs]
    in_maps = []
    for c in range(8):
        b, r = c // 4, c % 4
        heads = range(4 * r, 4 * r + 4)
        m = dict(shared)
        m["xTt"] = xTts[b]
        m["xres"] = np.ascontiguousarray(xTs[b][:, r * T:(r + 1) * T])
        m["wqT"] = _tile_w(np.concatenate(
            [wqT[:, h * DH:(h + 1) * DH][:, perm] for h in heads], axis=1), HPC, DH)
        m["wkT"] = _tile_w(np.concatenate(
            [wkT[:, h * DH:(h + 1) * DH][:, perm] for h in heads], axis=1), HPC, DH)
        m["wvT"] = _tile_w(np.concatenate(
            [wvT[:, h * DH:(h + 1) * DH] for h in heads], axis=1), HPC, DH)
        m["roff"] = np.array([[(r // 2) * 2 * S * DH + (r % 2) * T * DH]], dtype=np.int32)
        in_maps.append(m)
    return in_maps


def kernel(**inputs):
    nc = _build("full")
    in_maps = _stage(inputs)
    res = run_bass_kernel_spmd(nc, in_maps, core_ids=list(range(8)))
    out = np.empty((B, S, D), dtype=np.float32)
    for c in range(8):
        b, r = c // 4, c % 4
        out[b, r * T:(r + 1) * T, :] = np.asarray(res.results[c]["outfm"]).T
    return out


# version 06
# speedup vs baseline: 1.1572x; 1.0783x over previous
import numpy as np
import ml_dtypes
from contextlib import ExitStack

import concourse.bass as bass
import concourse.mybir as mybir
from concourse.bass_utils import run_bass_kernel_spmd

F32 = mybir.dt.float32
BF16 = mybir.dt.bfloat16
I32 = mybir.dt.int32
AF = mybir.ActivationFunctionType
ALU = mybir.AluOpType
AX = mybir.AxisListType

B, S, D, H, DH = 2, 8192, 2048, 16, 128
HPC = 4
T = 2048
NCH = D // 128
DS, NDS = 2816, 22
DE, NDE = 1408, 11
NE = 8
NTT = S // 512
CSHIFT = -12.0
MASKVAL = -60.0
EPS = 1e-5

ENGS = ("pe", "act", "dve", "sp", "gp")
SEMS = ("pe", "act", "dve", "gpc", "cc",
        "gT", "gX0", "gX1", "gW0", "gW1", "gWo0", "gWo1", "gY",
        "gG0", "gG1", "gD", "gP0", "gP1", "gB0", "gB1", "sXC",
        "sS0", "sS1", "sR0", "sR1", "sL", "sT0", "sT1", "sT2", "sT3", "sO0", "sO1")


class _Sched:

    def __init__(self):
        self.q = {e: [] for e in ENGS}
        self.cnt = {s: 0 for s in SEMS}
        self.pending = {e: [] for e in ENGS}

    def val(self, sem):
        return self.cnt[sem]

    SELF_SERIAL = ()

    def emit(self, eng, fn, waits=(), inc=None, amt=1):
        w = list(self.pending[eng]) + list(waits)
        self.pending[eng] = []
        if eng in self.SELF_SERIAL:
            if self.cnt[eng]:
                w.append((eng, self.cnt[eng]))
            if inc is None:
                inc = eng
        v = None
        if inc is not None:
            self.cnt[inc] += amt
            v = self.cnt[inc]
        self.q[eng].append((tuple(w), fn, inc, amt))
        return v

    def dma(self, eng, sem, out_ap, in_ap, waits=()):
        return self.emit(eng, lambda e, o=out_ap, i=in_ap: e.dma_start(o, i),
                         waits, inc=sem, amt=16)

    def barrier(self, exclude=()):
        for e in ENGS:
            if not self.q[e]:
                continue
            w, fn, inc, amt = self.q[e][-1]
            if inc is None:
                sem = e if e in ("pe", "act", "dve") else "gpc"
                assert e != "sp", "sp barrier op must carry an inc"
                self.cnt[sem] = self.cnt.get(sem, 0) + 1
                self.q[e][-1] = (w, fn, sem, 1)
        marks = [(s, self.cnt[s]) for s in SEMS
                 if self.cnt.get(s, 0) > 0 and s not in exclude]
        for e in ENGS:
            self.pending[e] = list(marks)


class _PB:

    def __init__(self, tile):
        self.t = tile
        self.ready = None
        self.free = {}

    def wait_free(self):
        return list(self.free.items())

    def read_done(self, sem, val):
        self.free[sem] = max(self.free.get(sem, 0), val)


def _mm_group(sc, pb, mms, extra_waits=(), lookahead_ok=True):
    waits = pb.wait_free() + list(extra_waits)
    n = len(mms)
    v = None
    for i, (o, l, r) in enumerate(mms):
        st, sp_ = (i == 0), (i == n - 1)
        v = sc.emit(
            "pe",
            lambda e, o=o, l=l, r=r, st=st, sp_=sp_: e.matmul(o, l, r, start=st, stop=sp_),
            waits if i == 0 else (),
            inc="pe" if sp_ else None,
        )
    pb.ready = ("pe", v)
    pb.free = {}
    return v


def _build(upto="full"):
    nc = bass.Bass("TRN2", target_bir_lowering=False)
    sc = _Sched()

    dt_in = {}
    def din(name, shape, dt=F32):
        dt_in[name] = nc.dram_tensor(name, list(shape), dt, kind="ExternalInput")
        return dt_in[name]

    xTt = din("xTt", (NTT * 128, NCH * 512))
    xres = din("xres", (D, T))
    wqT = din("wqT", (HPC * 128, NCH * DH))
    wkT = din("wkT", (HPC * 128, NCH * DH))
    wvT = din("wvT", (HPC * 128, NCH * DH))
    woT = din("woT", (NCH * 128, NCH * 128))
    tbpk = din("tbpk", (128, NTT * 4 * 512), BF16)
    tri = din("tri", (128, 128))
    sel8 = din("sel8", (8, 8 * 128))
    ident = din("ident", (128, 128))
    swp = din("swp", (128, 128))
    ones1 = din("ones1", (1, 128))
    roff = din("roff", (1, 1), I32)
    wrouter = din("wrouter", (128, NCH * NE))
    wsg = din("wsg", (NDS * 128, NCH * 128))
    wsu = din("wsu", (NDS * 128, NCH * 128))
    wsd = din("wsd", (DS, D))
    weg = din("weg", (NE * NDE * 128, NCH * 128))
    weu = din("weu", (NE * NDE * 128, NCH * 128))
    wed = din("wed", (NE * DE, D))
    xbf = nc.dram_tensor("xbf", [NTT * 128, NCH * 512], BF16, kind="Internal")

    outfm = nc.dram_tensor("outfm", [D, T], F32, kind="ExternalOutput")

    cc_in_h = [nc.dram_tensor(f"cc_in{h}", [S, DH], BF16, kind="Internal") for h in range(HPC)]
    cc_ag_h = [nc.dram_tensor(f"cc_ag{h}", [4 * S, DH], BF16, kind="Internal") for h in range(HPC)]
    cc_my = nc.dram_tensor("cc_my", [T, D], BF16, kind="Internal")
    x1sp = nc.dram_tensor("x1sp", [4 * 128, NCH * 512], F32, kind="Internal")

    assert upto == "full"

    es = ExitStack()
    sems = {s: es.enter_context(nc.semaphore(s)) for s in SEMS}

    pb = [_PB(es.enter_context(nc.psum_tensor(f"pb{i}", [128, 512], F32))) for i in range(8)]

    ones1_sb = es.enter_context(nc.sbuf_tensor("ones1_sb", [1, 128], F32))
    ones128_sb = es.enter_context(nc.sbuf_tensor("ones128_sb", [128, 1], F32))
    biasC = es.enter_context(nc.sbuf_tensor("biasC", [128, 1], F32))
    epsT = es.enter_context(nc.sbuf_tensor("epsT", [1, 1], F32))
    half_neg = es.enter_context(nc.sbuf_tensor("half_neg", [1, 1], F32))
    neginf8 = es.enter_context(nc.sbuf_tensor("neginf8", [128, 8], F32))
    invD = es.enter_context(nc.sbuf_tensor("invD", [1, 1], F32))
    tri_sb = es.enter_context(nc.sbuf_tensor("tri_sb", [128, 128], F32))
    ident_sb = es.enter_context(nc.sbuf_tensor("ident_sb", [128, 128], F32))
    swp_sb = es.enter_context(nc.sbuf_tensor("swp_sb", [128, 128], BF16))
    roff_sb = es.enter_context(nc.sbuf_tensor("roff_sb", [1, 1], I32))
    comb_sb = es.enter_context(nc.sbuf_tensor("comb_sb", [8, T], F32))
    rg = nc.gpsimd.alloc_register("rg_off")
    rg2 = nc.gpsimd.alloc_register("rg_off2")

    sc.dma("gp", "gT", ones1_sb[:, :], ones1[:, :])
    sc.dma("gp", "gT", tri_sb[:, :], tri[:, :])
    sc.dma("gp", "gT", ident_sb[:, :], ident[:, :])
    sc.dma("gp", "gT", swp_sb[:, :], swp[:, :])
    v_cst_dma = sc.dma("gp", "gT", roff_sb[:, :], roff[:, :])
    v_tri = v_cst_dma
    sc.emit("gp", lambda e: e.memset(ones128_sb[:, :], 1.0), inc="gpc")
    sc.emit("gp", lambda e: e.memset(biasC[:, :], CSHIFT), inc="gpc")
    sc.emit("gp", lambda e: e.memset(epsT[:, :], EPS), inc="gpc")
    sc.emit("gp", lambda e: e.memset(half_neg[:, :], -0.5), inc="gpc")
    sc.emit("gp", lambda e: e.memset(neginf8[:, :], -1.0e30), inc="gpc")
    v_cst = sc.emit("gp", lambda e: e.memset(invD[:, :], 1.0 / D), inc="gpc")
    CST = [("gT", v_cst_dma), ("gpc", v_cst)]

    wo_outer = ExitStack()
    attn_fm = wo_outer.enter_context(nc.sbuf_tensor("attn_fm", [128, NCH, T], BF16))
    wo_sb0 = wo_outer.enter_context(nc.sbuf_tensor("wo0", [128, NCH, 128], BF16))
    wr_sb = wo_outer.enter_context(nc.sbuf_tensor("wr_sb", [128, NCH, NE], F32))
    pre = {}
    attn_es = ExitStack()
    tb_sb = [attn_es.enter_context(nc.sbuf_tensor(f"tb{i}", [128, 4, 512], BF16)) for i in range(2)]
    TAB = []

    v_tb_load = [0, 0]
    v_tb_used = [0, 0]
    qbf = attn_es.enter_context(nc.sbuf_tensor("qbf", [128, S], BF16))
    kbf = attn_es.enter_context(nc.sbuf_tensor("kbf", [128, S], BF16))
    vsb = attn_es.enter_context(nc.sbuf_tensor("vsb", [128, 64, 132], BF16))
    xt_sb = [attn_es.enter_context(nc.sbuf_tensor(f"xt{i}", [128, NCH, 512], BF16)) for i in range(2)]
    wq_sb = [attn_es.enter_context(nc.sbuf_tensor(f"wq_sb{i}", [128, NCH, DH], BF16)) for i in range(2)]
    wk_sb = [attn_es.enter_context(nc.sbuf_tensor(f"wk_sb{i}", [128, NCH, DH], BF16)) for i in range(2)]
    wv_sb = [attn_es.enter_context(nc.sbuf_tensor(f"wv_sb{i}", [128, NCH, DH], BF16)) for i in range(2)]
    qtmp = attn_es.enter_context(nc.sbuf_tensor("qtmp", [128, 512], BF16))
    qsw = attn_es.enter_context(nc.sbuf_tensor("qsw", [128, 512], BF16))
    ktmp = attn_es.enter_context(nc.sbuf_tensor("ktmp", [128, 512], BF16))
    ksw = attn_es.enter_context(nc.sbuf_tensor("ksw", [128, 512], BF16))
    rope_t1 = attn_es.enter_context(nc.sbuf_tensor("rope_t1", [128, 512], BF16))
    rope_t2 = attn_es.enter_context(nc.sbuf_tensor("rope_t2", [128, 512], BF16))
    probs = [attn_es.enter_context(nc.sbuf_tensor(f"probs{i}", [128, 512], BF16)) for i in range(4)]
    ao_sb = [attn_es.enter_context(nc.sbuf_tensor(f"ao{i}", [128, 4, DH], BF16)) for i in range(2)]
    rc_sb = attn_es.enter_context(nc.sbuf_tensor("rc_sb", [128, 4], F32))

    v_xt_load = [0, 0]
    v_xt_used = [None, None]
    v_ao_dma = [0, 0]

    last_v1 = {}
    last_v2 = {}
    last_v3 = [0]
    v_rc_read = [0, 0, 0, 0]

    v_proj_end = [0, 0]
    v_w_loaded = [0, 0]
    v_pick_h = [0] * HPC
    v_tp_h = [0] * HPC
    v_xc = [0] * NTT
    v_xc_par = [0, 0]
    rg_loaded = [False]

    def emit_ag(h, half):
        HS = S // 2
        sc.emit("gp", lambda e, h=h, half=half: e.collective_compute(
            "AllGather", ALU.bypass, replica_groups=[[0, 1, 2, 3], [4, 5, 6, 7]],
            ins=[cc_in_h[h][half * HS:(half + 1) * HS, :]],
            outs=[cc_ag_h[h][half * 2 * S:half * 2 * S + 4 * HS, :]]),
            waits=[("sS0", sc.val("sS0")), ("sS1", sc.val("sS1"))], inc="cc")

    def emit_picks(h):
        if not rg_loaded[0]:
            sc.emit("gp", lambda e: e.reg_load(rg, roff_sb[0:1, 0:1]),
                    waits=[("gT", v_cst_dma)])
            rg_loaded[0] = True
        v = None
        for j in range(4):
            w0 = [("cc", 2 * (h + 1))] if j == 0 else []
            if j == 0 and h >= 2:
                w0.append((f"sT{h - 2}", v_tp_h[h - 2]))
            sc.emit("gp", lambda e, j=j: e.reg_add(rg2, rg, j * (S // 2) * DH), waits=w0)
            v = sc.emit(
                "gp", lambda e, j=j, h=h: e.dma_start(
                    bass.AP(cc_my, j * (HPC * DH) + h * DH, [[D, T], [1, DH]]),
                    bass.AP(cc_ag_h[h], rg2, [[DH, T], [1, DH]])),
                inc=f"gP{h % 2}", amt=16)
        v_pick_h[h] = v

    def emit_tp(h):
        v = None
        for i, fc in enumerate(range(h, NCH, 4)):
            v = sc.emit("sp", lambda e, fc=fc: e.dma_start(
                attn_fm[:, fc, :], cc_my[:, fc * 128:(fc + 1) * 128], transpose=True),
                waits=[(f"gP{h % 2}", v_pick_h[h])] if i == 0 else (), inc=f"sT{h}", amt=16)
        v_tp_h[h] = v
    for hh in range(HPC + 1):
        if hh < HPC:
            hp_ = hh % 2
            vw = None
            for wsb, wt in ((wq_sb[hp_], wqT), (wk_sb[hp_], wkT), (wv_sb[hp_], wvT)):
                vw = sc.dma("gp", f"gW{hp_}", wsb[:, :, :],
                            wt[hh * 128:(hh + 1) * 128, :].rearrange("p (c o) -> p c o", o=DH),
                            waits=[("pe", v_proj_end[hp_])] if v_proj_end[hp_] else [])
            v_w_loaded[hp_] = vw
        if hh == 0:
            continue
        h = hh - 1
        hp = h % 2
        v_prev_pe = sc.val("pe")
        W = [(f"gW{hp}", v_w_loaded[hp])]

        vone = sc.emit("gp", lambda e: e.memset(vsb[:, :, 128:132], 1.0),
                       waits=[("pe", sc.val("pe"))], inc="gpc")

        v_q_done = [0] * NTT
        v_k_done = [0] * NTT
        v_v_done = [0] * NTT
        for tt in range(NTT):
            bi = tt % 2
            t0 = tt * 512
            w = [("pe", v_xt_used[bi])] if v_xt_used[bi] else []
            if h == 0:
                if v_xc_par[bi]:
                    w.append(("sXC", v_xc_par[bi]))
                v_xt_load[bi] = sc.dma(
                    "gp", f"gX{bi}", xt_sb[bi][:, :, :],
                    xTt[tt * 128:(tt + 1) * 128, :].rearrange("p (c t) -> p c t", t=512),
                    waits=w)
                v_xc[tt] = sc.dma(
                    "sp", "sXC", xbf[tt * 128:(tt + 1) * 128, :],
                    xt_sb[bi][:, :, :].rearrange("p c t -> p (c t)"),
                    waits=[(f"gX{bi}", v_xt_load[bi])])
                v_xc_par[bi] = v_xc[tt]
            else:
                w.append(("sXC", v_xc[tt]))
                v_xt_load[bi] = sc.dma(
                    "sp", f"gX{bi}", xt_sb[bi][:, :, :],
                    xbf[tt * 128:(tt + 1) * 128, :].rearrange("p (c t) -> p c t", t=512),
                    waits=w)
            wtb = [("dve", v_tb_used[bi])] if v_tb_used[bi] else []
            v_tb_load[bi] = sc.dma(
                "sp", f"gB{bi}", tb_sb[bi][:, :, :].rearrange("p a b -> p (a b)"),
                tbpk[:, tt * 2048:(tt + 1) * 2048], waits=wtb)
            if tt == 1 and h >= 1:
                emit_ag(h - 1, 1)
            if tt == 4 and h >= 1:
                emit_picks(h - 1)

            pq, pk, pv = pb[0 + (tt % 2)], pb[2 + (tt % 2)], pb[4 + (tt % 2)]
            ld = [(f"gX{bi}", v_xt_load[bi])] + W + TAB + CST
            _mm_group(sc, pq, [(pq.t[:, 0:512], wq_sb[hp][:, c, :], xt_sb[bi][:, c, :])
                              for c in range(NCH)], extra_waits=ld)
            _mm_group(sc, pk, [(pk.t[:, 0:512], wk_sb[hp][:, c, :], xt_sb[bi][:, c, :])
                              for c in range(NCH)], extra_waits=ld)
            vmms = []
            for sub in range(4):
                for c in range(NCH):
                    vmms.append((pv.t[:, sub * 128:(sub + 1) * 128],
                                 xt_sb[bi][:, c, sub * 128:(sub + 1) * 128],
                                 wv_sb[hp][:, c, :]))
            vstop = None
            wv0 = pv.wait_free() + ld
            for i, (o, l, r) in enumerate(vmms):
                st = (i % NCH == 0)
                sp_ = (i % NCH == NCH - 1)
                vstop = sc.emit(
                    "pe", lambda e, o=o, l=l, r=r, st=st, sp_=sp_: e.matmul(o, l, r, start=st, stop=sp_),
                    wv0 if i == 0 else (), inc="pe" if sp_ else None)
            pv.ready = ("pe", vstop)
            pv.free = {}
            v_xt_used[bi] = sc.val("pe")

            def rope(ptile, tmp, sw, ct, st_, dest, t0, bi):
                wva = [ptile.ready] + CST
                tk = id(tmp)
                if tk in last_v1:
                    wva.append(("dve", last_v1[tk]))
                va = sc.emit("act", lambda e: e.copy(tmp[:, :], ptile.t[:, 0:512]),
                             waits=wva, inc="act")
                ptile.read_done("act", va)
                vswp = _mm_group(sc, ptile, [(ptile.t[:, 0:512], swp_sb[:, :], tmp[:, :])],
                                 extra_waits=[("act", va)])
                wsw = [ptile.ready]
                if tk in last_v2:
                    wsw.append(("dve", last_v2[tk]))
                va2 = sc.emit("act", lambda e: e.copy(sw[:, :], ptile.t[:, 0:512]),
                              waits=wsw, inc="act")
                ptile.read_done("act", va2)
                v1 = sc.emit("dve", lambda e: e.tensor_mul(rope_t1[:, :], tmp[:, :], ct),
                             waits=[("act", va), ("dve", last_v3[0]), (f"gB{bi}", v_tb_load[bi])], inc="dve")
                v2 = sc.emit("dve", lambda e: e.tensor_mul(rope_t2[:, :], sw[:, :], st_),
                             waits=[("act", va2)], inc="dve")
                v3 = sc.emit("dve", lambda e, t0=t0: e.tensor_add(dest[:, t0:t0 + 512], rope_t1[:, :], rope_t2[:, :]),
                             waits=[("pe", v_prev_pe), ("dve", v2)], inc="dve")
                last_v1[tk] = v1
                last_v2[tk] = v2
                last_v3[0] = v3
                return v3

            v_q_done[tt] = rope(pq, qtmp, qsw, tb_sb[bi][:, 0, :], tb_sb[bi][:, 1, :], qbf, t0, bi)
            v_k_done[tt] = rope(pk, ktmp, ksw, tb_sb[bi][:, 2, :], tb_sb[bi][:, 3, :], kbf, t0, bi)
            v_tb_used[bi] = v_k_done[tt]
            vv = None
            for sub in range(4):
                vv = sc.emit("act",
                             lambda e, sub=sub, tt=tt, pvt=pv.t: e.copy(
                                 vsb[:, tt * 4 + sub, 0:128], pvt[:, sub * 128:(sub + 1) * 128]),
                             waits=[pv.ready, ("gpc", vone)], inc="act")
            pv.read_done("act", vv)
            v_v_done[tt] = vv
        v_proj_end[hp] = sc.val("pe")

        for qt in range(NTT):
            if qt == 0 and h >= 1:
                emit_tp(h - 1)
            if qt == 0 and h == HPC - 1:
                pre["wr"] = sc.dma("gp", "gT", wr_sb[:, :, :],
                                   wrouter[:, :].rearrange("p (c e) -> p c e", e=NE))
                pre[0] = sc.dma(
                    "gp", "gWo0", wo_sb0[:, :, :],
                    woT[0:128, :].rearrange("p (c o) -> p c o", o=128))
            q0 = qt * 512
            nkc = qt * 4 + 4
            v_score = [0] * nkc
            v_exp = [0] * nkc
            v_pv = [0] * nkc
            qstart = [0] * 4
            v_pv_stop = [0] * 4

            out_pb = [pb[2], pb[3], pb[4], pb[5]]
            sco_pb = [pb[0], pb[1], pb[6], pb[7]]

            def emit_score(kc):
                bank = sco_pb[kc % 4]
                dq = kc - qt * 4
                qlo = max(0, dq) * 128
                w = bank.wait_free()
                w += [("dve", v_k_done[kc // 4]), ("dve", v_q_done[qt])]
                v = sc.emit("pe", lambda e, kc=kc, qlo=qlo, bank=bank, q0=q0: e.matmul(
                    bank.t[:, 0:512 - qlo], kbf[:, kc * 128:(kc + 1) * 128],
                    qbf[:, q0 + qlo:q0 + 512], start=True, stop=True),
                    w, inc="pe")
                bank.ready = ("pe", v)
                bank.free = {}
                v_score[kc] = v
                if dq >= 0:
                    vm = sc.emit("dve", lambda e, bank=bank: e.tensor_add(
                        bank.t[:, 0:128], bank.t[:, 0:128], tri_sb[:, :]),
                        [("pe", v), ("gT", v_cst_dma)], inc="dve")
                    bank.ready = ("dve", vm)
                wexp = [bank.ready]
                if kc >= 4 and v_pv[kc - 4]:
                    wexp.append(("pe", v_pv[kc - 4]))
                ve = sc.emit("act", lambda e, kc=kc, qlo=qlo, bank=bank: e.activation(
                    probs[kc % 4][:, 0:512 - qlo], bank.t[:, 0:512 - qlo], AF.Exp,
                    bias=biasC[:, 0:1]), wexp, inc="act")
                bank.read_done("act", ve)
                v_exp[kc] = ve

            def emit_pv(kc):
                dq = kc - qt * 4
                qlo = max(0, dq) * 128
                last = None
                for qc in range(qlo // 128, 4):
                    bank = out_pb[qc]
                    qcg = qt * 4 + qc
                    st = (kc == 0)
                    sp_ = (kc == qcg)
                    w = [("act", v_exp[kc])]
                    if st:
                        w += bank.wait_free()
                    last = sc.emit("pe", lambda e, kc=kc, qc=qc, qlo=qlo, bank=bank, st=st, sp_=sp_: e.matmul(
                        bank.t[:, 0:129],
                        probs[kc % 4][:, qc * 128 - qlo:(qc + 1) * 128 - qlo],
                        vsb[:, kc, 0:129], start=st, stop=sp_),
                        w, inc="pe" if (sp_ or qc == 3) else None)
                    if st:
                        bank.free = {}
                    if sp_:
                        v_pv_stop[qc] = last
                        bank.ready = ("pe", last)
                v_pv[kc] = last

            emit_score(0)
            emit_score(1)
            emit_score(2)
            for kc in range(3, nkc):
                emit_score(kc)
                emit_pv(kc - 3)
            emit_pv(nkc - 3)
            emit_pv(nkc - 2)
            emit_pv(nkc - 1)

            vcp = None
            for qc in range(4):
                bank = out_pb[qc]
                wr_ = [("pe", v_pv_stop[qc]), ("dve", sc.val("dve"))]
                vr = sc.emit("dve", lambda e, qc=qc, bank=bank: e.reciprocal(
                    rc_sb[:, qc:qc + 1], bank.t[:, 128:129]),
                    wr_, inc="dve")
                wao = [("dve", vr)]
                if v_ao_dma[qt % 2]:
                    wao.append((f"sS{qt % 2}", v_ao_dma[qt % 2]))
                vcp = sc.emit("dve", lambda e, qc=qc, bank=bank, qt=qt: e.tensor_scalar(
                    ao_sb[qt % 2][:, qc, :], bank.t[:, 0:128], rc_sb[:, qc:qc + 1], None,
                    ALU.mult), wao, inc="dve")
                bank.read_done("dve", vcp)
                v_rc_read[qc] = vcp
            v_ao_dma[qt % 2] = sc.dma(
                "gp", f"sS{qt % 2}", cc_in_h[h][q0:q0 + 512, :].rearrange("(qc p) f -> p qc f", p=128),
                ao_sb[qt % 2][:, :, :], waits=[("dve", vcp)])
            if qt == 7:
                emit_ag(h, 0)

    emit_ag(HPC - 1, 1)
    emit_picks(HPC - 1)
    emit_tp(HPC - 1)

    v_attn_pe_end = sc.val("pe")
    attn_es.close()
    sc.barrier(exclude=("cc", "gP0", "gP1", "sT0", "sT1", "sT2", "sT3"))


    wo_es = ExitStack()
    wo_sb = [wo_sb0, wo_es.enter_context(nc.sbuf_tensor("wo1", [128, NCH, 128], BF16))]
    x1pre = wo_es.enter_context(nc.sbuf_tensor("x1pre", [128, NCH, 512], F32))
    xres_sb = [wo_es.enter_context(nc.sbuf_tensor(f"xres{i}", [128, 512], F32)) for i in range(2)]
    xsq = wo_es.enter_context(nc.sbuf_tensor("xsq", [128, 512], F32))
    stat_sb = wo_es.enter_context(nc.sbuf_tensor("stat_sb", [1, 6 * 512], F32))
    mb_sb = wo_es.enter_context(nc.sbuf_tensor("mb_sb", [128, 512], F32))
    rb_sb = wo_es.enter_context(nc.sbuf_tensor("rb_sb", [128, 512], F32))
    logits_sb = wo_es.enter_context(nc.sbuf_tensor("logits_sb", [128, NCH, NE], F32))
    pr_sb = wo_es.enter_context(nc.sbuf_tensor("pr_sb", [128, NE], F32))
    eq_sb = wo_es.enter_context(nc.sbuf_tensor("eq_sb", [128, NE], F32))
    pm_sb = wo_es.enter_context(nc.sbuf_tensor("pm_sb", [128, NE], F32))
    sc1_sb = wo_es.enter_context(nc.sbuf_tensor("sc1_sb", [128, 4], F32))

    v_wr = pre["wr"]

    xsq2 = wo_es.enter_context(nc.sbuf_tensor("xsq2", [128, 2, 512], F32))

    v_spill_tile = [0, 0, 0, 0]
    v_wo_load = [0, 0]
    v_wo_used = [None, None]
    v_xres_used = [None, None]
    v_x1_readers = []
    v_x1sp_last = 0
    v_logit_evac = 0
    for tt in range(4):
        t0 = tt * 512
        sbank, s2bank = pb[6], pb[7]
        v_s2mm = [0] * NCH
        v_s1mm = [0] * NCH
        v_add = [0] * NCH
        v_sq = [0] * NCH

        def emit_smm(oc):
            st, sp_ = (oc == 0), (oc == NCH - 1)
            w1 = [("dve", v_add[oc])]
            if st:
                w1 += sbank.wait_free() + CST
            v_s1mm[oc] = sc.emit("pe", lambda e, oc=oc, st=st, sp_=sp_: e.matmul(
                sbank.t[0:1, 0:512], ones128_sb[:, :], x1pre[:, oc, :], start=st, stop=sp_),
                w1, inc="pe")
            w2 = [("act", v_sq[oc])]
            if st:
                w2 += s2bank.wait_free()
            v_s2mm[oc] = sc.emit("pe", lambda e, oc=oc, st=st, sp_=sp_: e.matmul(
                s2bank.t[0:1, 0:512], ones128_sb[:, :], xsq2[:, oc % 2, :], start=st, stop=sp_),
                w2, inc="pe")

        corder = [c for m in range(4) for c in range(m, NCH, 4)]
        for oc in range(NCH):
            bi = oc % 2
            if tt == 0 and oc == 0:
                v_wo_load[bi] = pre[0]
            else:
                w = [("pe", v_wo_used[bi])] if v_wo_used[bi] else []
                v_wo_load[bi] = sc.dma(
                    "gp", f"gWo{bi}", wo_sb[bi][:, :, :],
                    woT[oc * 128:(oc + 1) * 128, :].rearrange("p (c o) -> p c o", o=128),
                    waits=w)
            wbank = pb[oc % 2]
            wf = wbank.wait_free() + [(f"gWo{bi}", v_wo_load[bi])]
            vstop = None
            for ci, c in enumerate(corder):
                ww = list(wf) if ci == 0 else []
                ww.append((f"sT{c % 4}", v_tp_h[c % 4]))
                vstop = sc.emit("pe", lambda e, c=c, bi=bi, wbank=wbank, t0=t0, st=(ci == 0), sp_=(ci == NCH - 1): e.matmul(
                    wbank.t[:, 0:512], wo_sb[bi][:, c, :], attn_fm[:, c, t0:t0 + 512],
                    start=st, stop=sp_), ww, inc="pe" if ci == NCH - 1 else None)
            wbank.ready = ("pe", vstop)
            wbank.free = {}
            v_wo_used[bi] = sc.val("pe")
            xbi = oc % 2
            wx = [("dve", v_xres_used[xbi])] if v_xres_used[xbi] else []
            vxr = sc.dma("sp", f"sR{xbi}", xres_sb[xbi][:, :], xres[oc * 128:(oc + 1) * 128, t0:t0 + 512], waits=wx)
            wadd = [wbank.ready, (f"sR{xbi}", vxr)] + v_x1_readers
            v_add[oc] = sc.emit("dve", lambda e, oc=oc, xbi=xbi, wbank=wbank: e.tensor_add(
                x1pre[:, oc, :], wbank.t[:, 0:512], xres_sb[xbi][:, :]), wadd, inc="dve")
            wbank.read_done("dve", v_add[oc])
            v_xres_used[xbi] = v_add[oc]
            wsq = [("dve", v_add[oc])]
            if oc >= 2:
                wsq.append(("pe", v_s2mm[oc - 2]))
            v_sq[oc] = sc.emit("act", lambda e, oc=oc: e.activation(
                xsq2[:, oc % 2, :], x1pre[:, oc, :], AF.Square), wsq, inc="act")
            if oc >= 1:
                emit_smm(oc - 1)
        emit_smm(NCH - 1)
        v_x1_readers = []
        sbank.ready = ("pe", v_s1mm[NCH - 1])
        sbank.free = {}
        s2bank.ready = ("pe", v_s2mm[NCH - 1])
        s2bank.free = {}

        st_ = stat_sb
        va_m = sc.emit("act", lambda e: e.activation(st_[0:1, 0:512], sbank.t[0:1, 0:512],
                                                     AF.Copy, scale=invD[0:1, 0:1]),
                       [sbank.ready] + CST, inc="act")
        va_e2 = sc.emit("act", lambda e: e.activation(st_[0:1, 512:1024], s2bank.t[0:1, 0:512],
                                                      AF.Copy, scale=invD[0:1, 0:1]),
                        [s2bank.ready], inc="act")
        sbank.read_done("act", va_m)
        s2bank.read_done("act", va_e2)
        vd_mm = sc.emit("dve", lambda e: e.tensor_mul(st_[0:1, 1024:1536], st_[0:1, 0:512], st_[0:1, 0:512]),
                        [("act", va_m)], inc="dve")
        vd_var = sc.emit("dve", lambda e: e.tensor_sub(st_[0:1, 1536:2048], st_[0:1, 512:1024], st_[0:1, 1024:1536]),
                         [("act", va_e2), ("dve", vd_mm)], inc="dve")
        va_ln = sc.emit("act", lambda e: e.activation(st_[0:1, 2048:2560], st_[0:1, 1536:2048],
                                                      AF.Ln, bias=epsT[0:1, 0:1]),
                        [("dve", vd_var)], inc="act")
        va_rstd = sc.emit("act", lambda e: e.activation(st_[0:1, 2560:3072], st_[0:1, 2048:2560],
                                                        AF.Exp, scale=half_neg[0:1, 0:1]),
                          [("act", va_ln)], inc="act")
        mbank, rbank = pb[2], pb[3]
        vb_m = _mm_group(sc, mbank, [(mbank.t[:, 0:512], ones1_sb[:, :], st_[0:1, 0:512])],
                         extra_waits=[("act", va_m)] + CST)
        vb_r = _mm_group(sc, rbank, [(rbank.t[:, 0:512], ones1_sb[:, :], st_[0:1, 2560:3072])],
                         extra_waits=[("act", va_rstd)])
        ve_m = sc.emit("act", lambda e: e.copy(mb_sb[:, :], mbank.t[:, 0:512]), [mbank.ready], inc="act")
        ve_r = sc.emit("act", lambda e: e.copy(rb_sb[:, :], rbank.t[:, 0:512]), [rbank.ready], inc="act")
        mbank.read_done("act", ve_m)
        rbank.read_done("act", ve_r)

        v_nrm = [0] * NCH
        for oc in range(NCH):
            vs = sc.emit("dve", lambda e, oc=oc: e.tensor_sub(
                x1pre[:, oc, :], x1pre[:, oc, :], mb_sb[:, :]),
                [("act", ve_m), ("pe", v_s2mm[NCH - 1]), ("dve", sc.val("dve"))], inc="dve")
            v_nrm[oc] = sc.emit("dve", lambda e, oc=oc: e.tensor_mul(
                x1pre[:, oc, :], x1pre[:, oc, :], rb_sb[:, :]),
                [("act", ve_r), ("dve", vs)], inc="dve")
            v_x1sp_last = sc.dma("sp", "sL", x1sp[tt * 128:(tt + 1) * 128, oc * 512:(oc + 1) * 512],
                                 x1pre[:, oc, :], waits=[("dve", v_nrm[oc])])
        v_x1_readers.append(("sL", v_x1sp_last))
        v_spill_tile[tt] = v_x1sp_last

        lbank = pb[4]
        for tc in range(4):
            g = tt * 4 + tc
            mms = [(lbank.t[0:128, 0:NE], x1pre[:, oc, tc * 128:(tc + 1) * 128], wr_sb[:, oc, :])
                   for oc in range(NCH)]
            _mm_group(sc, lbank, mms,
                      extra_waits=[("dve", v_nrm[NCH - 1]), ("gT", v_wr)])
            v_logit_evac = sc.emit("act", lambda e, g=g: e.copy(
                logits_sb[:, g, :], lbank.t[0:128, 0:NE]), [lbank.ready], inc="act")
            lbank.read_done("act", v_logit_evac)
            v_x1_readers.append(("pe", sc.val("pe")))

    tbank = pb[5]
    v_comb_evac = 0
    v_tp_prev = 0
    for g in range(NCH):
        c0 = g * 128
        vm = sc.emit("dve", lambda e, g=g: e.tensor_reduce(
            sc1_sb[:, 0:1], logits_sb[:, g, :], AX.X, ALU.max), [("act", v_logit_evac)], inc="dve")
        wls = [("dve", vm)]
        if v_tp_prev:
            wls.append(("pe", v_tp_prev))
        vls = sc.emit("dve", lambda e, g=g: e.tensor_single_scalar(
            pr_sb[:, :], logits_sb[:, g, :], sc1_sb[:, 0:1], ALU.subtract),
            wls, inc="dve")
        vex = sc.emit("act", lambda e: e.activation(pr_sb[:, :], pr_sb[:, :], AF.Exp),
                      [("dve", vls)], inc="act")
        vsm = sc.emit("dve", lambda e: e.tensor_reduce(
            sc1_sb[:, 1:2], pr_sb[:, :], AX.X, ALU.add), [("act", vex)], inc="dve")
        vrs = sc.emit("dve", lambda e: e.reciprocal(sc1_sb[:, 2:3], sc1_sb[:, 1:2]),
                      [("dve", vsm)], inc="dve")
        vpr = sc.emit("dve", lambda e: e.tensor_single_scalar(
            pr_sb[:, :], pr_sb[:, :], sc1_sb[:, 2:3], ALU.mult), [("dve", vrs)], inc="dve")
        vm1 = sc.emit("dve", lambda e: e.tensor_reduce(
            sc1_sb[:, 0:1], pr_sb[:, :], AX.X, ALU.max), [("dve", vpr)], inc="dve")
        veq = sc.emit("dve", lambda e: e.tensor_single_scalar(
            eq_sb[:, :], pr_sb[:, :], sc1_sb[:, 0:1], ALU.is_ge), [("dve", vm1)], inc="dve")
        vpm = sc.emit("dve", lambda e: e.scalar_tensor_tensor(
            pm_sb[:, :], eq_sb[:, :], -1.0e30, pr_sb[:, :], ALU.mult, ALU.add),
            [("dve", veq)] + CST, inc="dve")
        vm2 = sc.emit("dve", lambda e: e.tensor_reduce(
            sc1_sb[:, 3:4], pm_sb[:, :], AX.X, ALU.max), [("dve", vpm)], inc="dve")
        vge = sc.emit("dve", lambda e: e.tensor_single_scalar(
            eq_sb[:, :], pr_sb[:, :], sc1_sb[:, 3:4], ALU.is_ge), [("dve", vm2)], inc="dve")
        vcb = sc.emit("dve", lambda e: e.tensor_mul(pr_sb[:, :], pr_sb[:, :], eq_sb[:, :]),
                      [("dve", vge)], inc="dve")
        wtp = tbank.wait_free() + [("dve", vcb)]
        vt = sc.emit("pe", lambda e: e.matmul(
            tbank.t[0:NE, 0:128], pr_sb[:, :], ident_sb[:, :],
            start=True, stop=True), wtp, inc="pe")
        tbank.ready = ("pe", vt)
        tbank.free = {}
        v_tp_prev = vt
        v_comb_evac = sc.emit("act", lambda e, c0=c0: e.copy(
            comb_sb[:, c0:c0 + 128], tbank.t[0:NE, 0:128]), [tbank.ready], inc="act")
        tbank.read_done("act", v_comb_evac)

    wo_es.close()
    wo_outer.close()
    sc.barrier()

    ffn_es = ExitStack()
    sel8_sb = ffn_es.enter_context(nc.sbuf_tensor("sel8_sb", [8, 8 * 128], F32))
    v_sel8 = sc.dma("gp", "gT", sel8_sb[:, :], sel8[:, :])
    y_acc = ffn_es.enter_context(nc.sbuf_tensor("y_acc", [128, NCH, 1024], F32))
    x1bh = ffn_es.enter_context(nc.sbuf_tensor("x1bh", [128, NCH, 1024], BF16))
    h_sb = ffn_es.enter_context(nc.sbuf_tensor("h_sb", [128, NDE, 1024], BF16))
    wg_sb = [ffn_es.enter_context(nc.sbuf_tensor(f"wg{i}", [128, NCH, 128], BF16)) for i in range(2)]
    wu_sb = [ffn_es.enter_context(nc.sbuf_tensor(f"wu{i}", [128, NCH, 128], BF16)) for i in range(2)]
    wd_sb = ffn_es.enter_context(nc.sbuf_tensor("wd_sb", [128, 6, NCH, 128], BF16))
    sg_sb = ffn_es.enter_context(nc.sbuf_tensor("sg_sb", [128, 2, 512], F32))
    cb_sb = ffn_es.enter_context(nc.sbuf_tensor("cb_sb", [128, 1024], F32))
    xq2 = ffn_es.enter_context(nc.sbuf_tensor("xq2", [128, 2, 512], F32))
    st2 = ffn_es.enter_context(nc.sbuf_tensor("st2", [1, 3 * 512], F32))
    mb2 = ffn_es.enter_context(nc.sbuf_tensor("mb2", [128, 512], F32))
    rb2 = ffn_es.enter_context(nc.sbuf_tensor("rb2", [128, 512], F32))
    x2t = [ffn_es.enter_context(nc.sbuf_tensor(f"x2t{i}", [128, 512], F32)) for i in range(2)]

    stages = [(wsg, wsu, wsd, 0, 0, 0, None), (wsg, wsu, wsd, 11, 0, 0, None)] + \
             [(weg, weu, wed, 0, e * NDE, e * DE, e) for e in range(NE)]
    JGRP = [(0, 6), (6, 11)]
    ffn_pre = {}

    v_y_read_prev = []
    v_x1b_pe_prev = 0
    v_out_last = 0
    v_store = [0, 0]
    v_prev_s2 = 0
    v_prev_norm = 0
    for hf in range(2):
        t0h = hf * 1024
        v_ya0 = None
        if hf == 0:
            for j_ in range(2):
                ffn_pre[(j_, "g")] = sc.dma(
                    "gp", f"gG{j_}", wg_sb[j_][:, :, :],
                    wsg[j_ * 128:(j_ + 1) * 128, :].rearrange("p (c o) -> p c o", o=128))
                ffn_pre[(j_, "u")] = sc.dma(
                    "gp", f"gG{j_}", wu_sb[j_][:, :, :],
                    wsu[j_ * 128:(j_ + 1) * 128, :].rearrange("p (c o) -> p c o", o=128))
        for ts in range(2):
            g = hf * 2 + ts
            wldx = [("sL", v_spill_tile[g])]
            if v_x1b_pe_prev:
                wldx.append(("pe", v_x1b_pe_prev))
            sc.dma("gp", "gY", x1bh[:, :, ts * 512:(ts + 1) * 512],
                   x1sp[g * 128:(g + 1) * 128, :].rearrange("p (c t) -> p c t", t=512), waits=wldx)
            wldy = [("sL", v_spill_tile[g])] + v_y_read_prev
            v_ya0 = sc.dma("gp", "gY", y_acc[:, :, ts * 512:(ts + 1) * 512],
                           x1sp[g * 128:(g + 1) * 128, :].rearrange("p (c t) -> p c t", t=512), waits=wldy)

        v_h_read_pe = sc.val("pe")
        v_cb_read = sc.val("dve")
        v_wd_used = sc.val("pe")
        v_wg_used = [sc.val("pe"), sc.val("pe")]
        v_sg_used = [sc.val("dve"), sc.val("dve")]
        v_ya_last = sc.val("dve")
        v_ya_chunk = {}
        for snum, (wgt, wut, wdt, co, rg_, rd_, ex) in enumerate(stages):
            if ex is not None:
                cbank = pb[6]
                v_cb = 0
                for ts in range(2):
                    _mm_group(sc, cbank, [(cbank.t[:, 0:512], sel8_sb[:, ex * 128:(ex + 1) * 128],
                                           comb_sb[:, t0h + ts * 512:t0h + (ts + 1) * 512])],
                              extra_waits=[("act", v_comb_evac), ("gT", v_sel8)])
                    wcb = [cbank.ready, ("dve", v_cb_read)]
                    v_cb = sc.emit("act", lambda e, ts=ts, cbank=cbank: e.copy(
                        cb_sb[:, ts * 512:(ts + 1) * 512], cbank.t[:, 0:512]), wcb, inc="act")
                    cbank.read_done("act", v_cb)

            for g0, g1 in JGRP:
                v_h_w = 0
                for j in range(g0, g1):
                    jp = j % 2
                    rb = (rg_ + co + j) * 128
                    if hf == 0 and snum == 0 and j < 2:
                        vwg = ffn_pre[(j, "g")]
                        vwu = ffn_pre[(j, "u")]
                    else:
                        wwl = [("pe", v_wg_used[jp])]
                        vwg = sc.dma("gp", f"gG{jp}", wg_sb[jp][:, :, :],
                                     wgt[rb:rb + 128, :].rearrange("p (c o) -> p c o", o=128),
                                     waits=wwl)
                        vwu = sc.dma("gp", f"gG{jp}", wu_sb[jp][:, :, :],
                                     wut[rb:rb + 128, :].rearrange("p (c o) -> p c o", o=128))
                    for ts in range(2):
                        gbank, ubank = pb[0 + ts], pb[2 + ts]
                        wg_ = [(f"gG{jp}", vwu), ("gY", v_ya0)]
                        _mm_group(sc, gbank, [(gbank.t[:, 0:512], wg_sb[jp][:, c, :],
                                               x1bh[:, c, ts * 512:(ts + 1) * 512]) for c in range(NCH)],
                                  extra_waits=wg_)
                        _mm_group(sc, ubank, [(ubank.t[:, 0:512], wu_sb[jp][:, c, :],
                                               x1bh[:, c, ts * 512:(ts + 1) * 512]) for c in range(NCH)])
                        v_wg_used[jp] = sc.val("pe")
                        wsl = [gbank.ready, ("dve", v_sg_used[ts])]
                        vsl = sc.emit("act", lambda e, ts=ts, gbank=gbank: e.activation(
                            sg_sb[:, ts, :], gbank.t[:, 0:512], AF.Silu), wsl, inc="act")
                        gbank.read_done("act", vsl)
                        whh = [("act", vsl), ubank.ready, ("pe", v_h_read_pe)]
                        if ex is not None:
                            vcbf = sc.emit("dve", lambda e, ts=ts: e.tensor_mul(
                                sg_sb[:, ts, :], sg_sb[:, ts, :], cb_sb[:, ts * 512:(ts + 1) * 512]),
                                [("act", vsl), ("act", v_cb), ("dve", v_sg_used[ts])], inc="dve")
                            whh.append(("dve", vcbf))
                        vh = sc.emit("dve", lambda e, j=j, ts=ts, ubank=ubank: e.tensor_mul(
                            h_sb[:, j, ts * 512:(ts + 1) * 512], sg_sb[:, ts, :], ubank.t[:, 0:512]),
                            whh, inc="dve")
                        ubank.read_done("dve", vh)
                        v_sg_used[ts] = vh
                        v_h_w = vh
                v_wd = 0
                for j in range(g0, g1):
                    wwd = [("pe", v_wd_used)] if j == g0 else []
                    v_wd = sc.dma("gp", "gD", wd_sb[:, j - g0, :, :],
                                  wdt[rd_ + (co + j) * 128:rd_ + (co + j + 1) * 128, :].rearrange("p (c o) -> p c o", o=128),
                                  waits=wwd)
                for ts in range(2):
                    for oc in range(NCH):
                        ybank = pb[4 + ((oc + ts) % 2)]
                        wdn = [("dve", v_h_w), ("gD", v_wd), ("gY", v_ya0)]
                        _mm_group(sc, ybank, [(ybank.t[:, 0:512], wd_sb[:, j2 - g0, oc, :],
                                               h_sb[:, j2, ts * 512:(ts + 1) * 512]) for j2 in range(g0, g1)],
                                  extra_waits=wdn)
                        v_ya_last = sc.emit("dve", lambda e, oc=oc, ts=ts, ybank=ybank: e.tensor_add(
                            y_acc[:, oc, ts * 512:(ts + 1) * 512], ybank.t[:, 0:512],
                            y_acc[:, oc, ts * 512:(ts + 1) * 512]),
                            [ybank.ready, ("dve", v_ya_chunk.get((oc, ts), 0)), ("gY", v_ya0)], inc="dve")
                        v_ya_chunk[(oc, ts)] = v_ya_last
                        ybank.read_done("dve", v_ya_last)
                v_wd_used = sc.val("pe")
            v_h_read_pe = sc.val("pe")
            v_cb_read = sc.val("dve")
        v_x1b_pe_prev = sc.val("pe")

        v_y_read_prev = []
        for ts in range(2):
            sbank, s2bank = pb[6], pb[7]
            v_sq2 = [0] * NCH
            v_m1 = [0] * NCH
            v_m2 = [0] * NCH

            for oc in range(NCH):
                wsq = [("dve", v_ya_last), ("pe", v_m2[oc - 2] if oc >= 2 else v_prev_s2)]
                v_sq2[oc] = sc.emit("act", lambda e, oc=oc, ts=ts: e.activation(
                    xq2[:, oc % 2, :], y_acc[:, oc, ts * 512:(ts + 1) * 512], AF.Square),
                    wsq, inc="act")
                w1 = [("dve", v_ya_last)]
                if oc == 0:
                    w1 += sbank.wait_free() + CST
                v_m1[oc] = sc.emit("pe", lambda e, oc=oc, ts=ts, st=(oc == 0), sp_=(oc == NCH - 1): e.matmul(
                    sbank.t[0:1, 0:512], ones128_sb[:, :], y_acc[:, oc, ts * 512:(ts + 1) * 512],
                    start=st, stop=sp_), w1, inc="pe")
                w2 = [("act", v_sq2[oc])]
                if oc == 0:
                    w2 += s2bank.wait_free()
                v_m2[oc] = sc.emit("pe", lambda e, oc=oc, st=(oc == 0), sp_=(oc == NCH - 1): e.matmul(
                    s2bank.t[0:1, 0:512], ones128_sb[:, :], xq2[:, oc % 2, :],
                    start=st, stop=sp_), w2, inc="pe")
            sbank.ready = ("pe", v_m1[NCH - 1])
            sbank.free = {}
            s2bank.ready = ("pe", v_m2[NCH - 1])
            s2bank.free = {}
            v_prev_s2 = v_m2[NCH - 1]

            va_m = sc.emit("act", lambda e: e.activation(st2[0:1, 0:512], sbank.t[0:1, 0:512],
                                                         AF.Copy, scale=invD[0:1, 0:1]),
                           [sbank.ready], inc="act")
            va_e2 = sc.emit("act", lambda e: e.activation(st2[0:1, 512:1024], s2bank.t[0:1, 0:512],
                                                          AF.Copy, scale=invD[0:1, 0:1]),
                            [s2bank.ready], inc="act")
            sbank.read_done("act", va_m)
            s2bank.read_done("act", va_e2)
            vd_mm = sc.emit("dve", lambda e: e.tensor_mul(st2[0:1, 1024:1536], st2[0:1, 0:512], st2[0:1, 0:512]),
                            [("act", va_m), ("dve", sc.val("dve"))], inc="dve")
            vd_var = sc.emit("dve", lambda e: e.tensor_sub(st2[0:1, 512:1024], st2[0:1, 512:1024], st2[0:1, 1024:1536]),
                             [("act", va_e2), ("dve", vd_mm)], inc="dve")
            va_ln = sc.emit("act", lambda e: e.activation(st2[0:1, 1024:1536], st2[0:1, 512:1024],
                                                          AF.Ln, bias=epsT[0:1, 0:1]),
                            [("dve", vd_var)], inc="act")
            va_rstd = sc.emit("act", lambda e: e.activation(st2[0:1, 512:1024], st2[0:1, 1024:1536],
                                                            AF.Exp, scale=half_neg[0:1, 0:1]),
                              [("act", va_ln)], inc="act")
            mbank, rbank = pb[2], pb[3]
            _mm_group(sc, mbank, [(mbank.t[:, 0:512], ones1_sb[:, :], st2[0:1, 0:512])],
                      extra_waits=[("act", va_m)])
            _mm_group(sc, rbank, [(rbank.t[:, 0:512], ones1_sb[:, :], st2[0:1, 512:1024])],
                      extra_waits=[("act", va_rstd)])
            ve_m = sc.emit("act", lambda e: e.copy(mb2[:, :], mbank.t[:, 0:512]),
                           [mbank.ready, ("dve", v_prev_norm)], inc="act")
            ve_r = sc.emit("act", lambda e: e.copy(rb2[:, :], rbank.t[:, 0:512]),
                           [rbank.ready, ("dve", v_prev_norm)], inc="act")
            mbank.read_done("act", ve_m)
            rbank.read_done("act", ve_r)

            for oc in range(NCH):
                op = oc % 2
                wn = [("act", ve_m), ("act", ve_r), ("dve", sc.val("dve"))]
                if v_store[op]:
                    wn.append((f"sO{op}", v_store[op]))
                vs = sc.emit("dve", lambda e, oc=oc, ts=ts, op=op: e.tensor_sub(
                    x2t[op][:, :], y_acc[:, oc, ts * 512:(ts + 1) * 512], mb2[:, :]), wn, inc="dve")
                vn = sc.emit("dve", lambda e, op=op: e.tensor_mul(
                    x2t[op][:, :], x2t[op][:, :], rb2[:, :]), [("dve", vs)], inc="dve")
                v_out_last = sc.dma("sp", f"sO{op}", outfm[oc * 128:(oc + 1) * 128, t0h + ts * 512:t0h + (ts + 1) * 512],
                                    x2t[op][:, :], waits=[("dve", vn)])
                v_store[op] = v_out_last
                v_y_read_prev.append(("dve", vn))
                v_prev_norm = vn
            v_y_read_prev.append(("pe", sc.val("pe")))
            v_y_read_prev.append(("act", v_sq2[NCH - 1]))

    ffn_es.close()
    _finish(nc, sc, sems, es, [("sO0", sc.val("sO0")), ("sO1", sc.val("sO1"))])
    return nc


def _finish(nc, sc, sems, es, final_waits):
    sc.emit("sp", lambda e: e.nop(), waits=list(final_waits))
    for s in SEMS:
        assert sc.cnt.get(s, 0) < 60000, (s, sc.cnt[s])
    with nc.Block() as block:
        def runner(name):
            def _r(eng):
                hw = {}
                for waits, fn, inc, amt in sc.q[name]:
                    for sem, v in waits:
                        if v and v > hw.get(sem, 0):
                            eng.wait_ge(sems[sem], v)
                            hw[sem] = v
                    ins = fn(eng)
                    if inc is not None:
                        ins.then_inc(sems[inc], amt)
            return _r
        block.tensor(runner("pe"))
        block.scalar(runner("act"))
        block.vector(runner("dve"))
        block.sync(runner("sp"))
        block.gpsimd(runner("gp"))
    es.close()


def _rope_tables():
    idx = np.arange(0, DH, 2, dtype=np.float32)
    inv_freq = (np.float32(10000.0) ** (-idx / np.float32(DH))).astype(np.float32)
    angles = (np.arange(S, dtype=np.float32)[:, None] * inv_freq[None, :]).astype(np.float32)
    cos = np.cos(angles).astype(np.float32)
    sin = np.sin(angles).astype(np.float32)
    cos_e = np.concatenate([cos.T, cos.T], axis=0)
    sin_s = np.concatenate([-sin.T, sin.T], axis=0)
    scl = np.float32(1.0 / np.sqrt(np.float32(DH)))
    tabs = np.stack([cos_e * scl, sin_s * scl, cos_e, sin_s], axis=1)
    pk = tabs.reshape(128, 4, NTT, 512).transpose(0, 2, 1, 3).reshape(128, NTT * 4 * 512)
    return np.ascontiguousarray(pk.astype(ml_dtypes.bfloat16))


def _tile_w(w, nj, od):
    return np.ascontiguousarray(
        np.asarray(w, np.float32).reshape(NCH, 128, nj, od)
        .transpose(2, 1, 0, 3).reshape(nj * 128, NCH * od))


def _stage(inputs):
    x = np.asarray(inputs["L0_attn_x"], np.float32)
    wq = np.asarray(inputs["L0_attn_Wq"], np.float32)
    wk = np.asarray(inputs["L0_attn_Wk"], np.float32)
    wv = np.asarray(inputs["L0_attn_Wv"], np.float32)
    wo = np.asarray(inputs["L0_attn_Wo"], np.float32)

    perm = np.concatenate([np.arange(0, DH, 2), np.arange(1, DH, 2)])
    wqT, wkT, wvT = wq.T, wk.T, wv.T
    tbpk = _rope_tables()

    tri = np.where(np.arange(128)[:, None] <= np.arange(128)[None, :], 0.0, MASKVAL).astype(np.float32)
    identm = np.eye(128, dtype=np.float32)
    swpm = np.zeros((128, 128), dtype=np.float32)
    for i in range(128):
        swpm[(i + 64) % 128, i] = 1.0
    sel8 = np.zeros((8, 8 * 128), dtype=np.float32)
    for e in range(8):
        sel8[e, e * 128:(e + 1) * 128] = 1.0
    ones1 = np.ones((1, 128), dtype=np.float32)

    weg = np.asarray(inputs["L0_ffn_We_gate"], np.float32)
    weu = np.asarray(inputs["L0_ffn_We_up"], np.float32)
    shared = {
        "woT": _tile_w(wo.T, NCH, 128), "tbpk": tbpk,
        "tri": tri, "ident": identm, "swp": swpm, "sel8": sel8, "ones1": ones1,
        "wrouter": _tile_w(inputs["L0_ffn_W_router"], 1, NE),
        "wsg": _tile_w(inputs["L0_ffn_Ws_gate"], NDS, 128),
        "wsu": _tile_w(inputs["L0_ffn_Ws_up"], NDS, 128),
        "wsd": np.ascontiguousarray(inputs["L0_ffn_Ws_down"], dtype=np.float32),
        "weg": np.ascontiguousarray(np.concatenate(
            [_tile_w(weg[e], NDE, 128) for e in range(NE)], axis=0)),
        "weu": np.ascontiguousarray(np.concatenate(
            [_tile_w(weu[e], NDE, 128) for e in range(NE)], axis=0)),
        "wed": np.ascontiguousarray(np.asarray(inputs["L0_ffn_We_down"], np.float32).reshape(NE * DE, D)),
    }

    xTs = [np.ascontiguousarray(x[b].T) for b in range(B)]
    xTts = [_tile_w(xT, NTT, 512) for xT in xTs]
    in_maps = []
    for c in range(8):
        b, r = c // 4, c % 4
        heads = range(4 * r, 4 * r + 4)
        m = dict(shared)
        m["xTt"] = xTts[b]
        m["xres"] = np.ascontiguousarray(xTs[b][:, r * T:(r + 1) * T])
        m["wqT"] = _tile_w(np.concatenate(
            [wqT[:, h * DH:(h + 1) * DH][:, perm] for h in heads], axis=1), HPC, DH)
        m["wkT"] = _tile_w(np.concatenate(
            [wkT[:, h * DH:(h + 1) * DH][:, perm] for h in heads], axis=1), HPC, DH)
        m["wvT"] = _tile_w(np.concatenate(
            [wvT[:, h * DH:(h + 1) * DH] for h in heads], axis=1), HPC, DH)
        m["roff"] = np.array([[(r // 2) * 2 * S * DH + (r % 2) * T * DH]], dtype=np.int32)
        in_maps.append(m)
    return in_maps


def kernel(**inputs):
    nc = _build("full")
    in_maps = _stage(inputs)
    res = run_bass_kernel_spmd(nc, in_maps, core_ids=list(range(8)))
    out = np.empty((B, S, D), dtype=np.float32)
    for c in range(8):
        b, r = c // 4, c % 4
        out[b, r * T:(r + 1) * T, :] = np.asarray(res.results[c]["outfm"]).T
    return out


# version 07
# speedup vs baseline: 1.1668x; 1.0083x over previous
import numpy as np
import ml_dtypes
from contextlib import ExitStack

import concourse.bass as bass
import concourse.mybir as mybir
from concourse.bass_utils import run_bass_kernel_spmd

F32 = mybir.dt.float32
BF16 = mybir.dt.bfloat16
I32 = mybir.dt.int32
AF = mybir.ActivationFunctionType
ALU = mybir.AluOpType
AX = mybir.AxisListType

B, S, D, H, DH = 2, 8192, 2048, 16, 128
HPC = 4
T = 2048
NCH = D // 128
DS, NDS = 2816, 22
DE, NDE = 1408, 11
NE = 8
NTT = S // 512
CSHIFT = -12.0
MASKVAL = -60.0
EPS = 1e-5

ENGS = ("pe", "act", "dve", "sp", "gp")
SEMS = ("pe", "act", "dve", "gpc", "cc",
        "gT", "gX0", "gX1", "gW0", "gW1", "gWo0", "gWo1", "gY",
        "gG0", "gG1", "gD", "gP0", "gP1", "gB0", "gB1", "sXC", "sY",
        "sS0", "sS1", "sR0", "sR1", "sL", "sT0", "sT1", "sT2", "sT3", "sO0", "sO1")


class _Sched:

    def __init__(self):
        self.q = {e: [] for e in ENGS}
        self.cnt = {s: 0 for s in SEMS}
        self.pending = {e: [] for e in ENGS}

    def val(self, sem):
        return self.cnt[sem]

    SELF_SERIAL = ()

    def emit(self, eng, fn, waits=(), inc=None, amt=1):
        w = list(self.pending[eng]) + list(waits)
        self.pending[eng] = []
        if eng in self.SELF_SERIAL:
            if self.cnt[eng]:
                w.append((eng, self.cnt[eng]))
            if inc is None:
                inc = eng
        v = None
        if inc is not None:
            self.cnt[inc] += amt
            v = self.cnt[inc]
        self.q[eng].append((tuple(w), fn, inc, amt))
        return v

    def dma(self, eng, sem, out_ap, in_ap, waits=()):
        return self.emit(eng, lambda e, o=out_ap, i=in_ap: e.dma_start(o, i),
                         waits, inc=sem, amt=16)

    def barrier(self, exclude=()):
        for e in ENGS:
            if not self.q[e]:
                continue
            w, fn, inc, amt = self.q[e][-1]
            if inc is None:
                sem = e if e in ("pe", "act", "dve") else "gpc"
                assert e != "sp", "sp barrier op must carry an inc"
                self.cnt[sem] = self.cnt.get(sem, 0) + 1
                self.q[e][-1] = (w, fn, sem, 1)
        marks = [(s, self.cnt[s]) for s in SEMS
                 if self.cnt.get(s, 0) > 0 and s not in exclude]
        for e in ENGS:
            self.pending[e] = list(marks)


class _PB:

    def __init__(self, tile):
        self.t = tile
        self.ready = None
        self.free = {}

    def wait_free(self):
        return list(self.free.items())

    def read_done(self, sem, val):
        self.free[sem] = max(self.free.get(sem, 0), val)


def _mm_group(sc, pb, mms, extra_waits=(), lookahead_ok=True):
    waits = pb.wait_free() + list(extra_waits)
    n = len(mms)
    v = None
    for i, (o, l, r) in enumerate(mms):
        st, sp_ = (i == 0), (i == n - 1)
        v = sc.emit(
            "pe",
            lambda e, o=o, l=l, r=r, st=st, sp_=sp_: e.matmul(o, l, r, start=st, stop=sp_),
            waits if i == 0 else (),
            inc="pe" if sp_ else None,
        )
    pb.ready = ("pe", v)
    pb.free = {}
    return v


def _build(upto="full"):
    nc = bass.Bass("TRN2", target_bir_lowering=False)
    sc = _Sched()

    dt_in = {}
    def din(name, shape, dt=F32):
        dt_in[name] = nc.dram_tensor(name, list(shape), dt, kind="ExternalInput")
        return dt_in[name]

    xTt = din("xTt", (NTT * 128, NCH * 512))
    xres = din("xres", (D, T))
    wqT = din("wqT", (HPC * 128, NCH * DH))
    wkT = din("wkT", (HPC * 128, NCH * DH))
    wvT = din("wvT", (HPC * 128, NCH * DH))
    woT = din("woT", (NCH * 128, NCH * 128))
    tbpk = din("tbpk", (128, NTT * 4 * 512), BF16)
    tri = din("tri", (128, 128))
    sel8 = din("sel8", (8, 8 * 128))
    ident = din("ident", (128, 128))
    swp = din("swp", (128, 128))
    ones1 = din("ones1", (1, 128))
    roff = din("roff", (1, 1), I32)
    wrouter = din("wrouter", (128, NCH * NE))
    wsg = din("wsg", (NDS * 128, NCH * 128))
    wsu = din("wsu", (NDS * 128, NCH * 128))
    wsd = din("wsd", (DS, D))
    weg = din("weg", (NE * NDE * 128, NCH * 128))
    weu = din("weu", (NE * NDE * 128, NCH * 128))
    wed = din("wed", (NE * DE, D))
    xbf = nc.dram_tensor("xbf", [NTT * 128, NCH * 512], BF16, kind="Internal")

    outfm = nc.dram_tensor("outfm", [D, T], F32, kind="ExternalOutput")

    cc_in_h = [nc.dram_tensor(f"cc_in{h}", [S, DH], BF16, kind="Internal") for h in range(HPC)]
    cc_ag_h = [nc.dram_tensor(f"cc_ag{h}", [4 * S, DH], BF16, kind="Internal") for h in range(HPC)]
    cc_my = nc.dram_tensor("cc_my", [T, D], BF16, kind="Internal")
    x1sp = nc.dram_tensor("x1sp", [4 * 128, NCH * 512], F32, kind="Internal")

    assert upto == "full"

    es = ExitStack()
    sems = {s: es.enter_context(nc.semaphore(s)) for s in SEMS}

    pb = [_PB(es.enter_context(nc.psum_tensor(f"pb{i}", [128, 512], F32))) for i in range(8)]

    ones1_sb = es.enter_context(nc.sbuf_tensor("ones1_sb", [1, 128], F32))
    ones128_sb = es.enter_context(nc.sbuf_tensor("ones128_sb", [128, 1], F32))
    biasC = es.enter_context(nc.sbuf_tensor("biasC", [128, 1], F32))
    epsT = es.enter_context(nc.sbuf_tensor("epsT", [1, 1], F32))
    half_neg = es.enter_context(nc.sbuf_tensor("half_neg", [1, 1], F32))
    neginf8 = es.enter_context(nc.sbuf_tensor("neginf8", [128, 8], F32))
    invD = es.enter_context(nc.sbuf_tensor("invD", [1, 1], F32))
    tri_sb = es.enter_context(nc.sbuf_tensor("tri_sb", [128, 128], F32))
    ident_sb = es.enter_context(nc.sbuf_tensor("ident_sb", [128, 128], F32))
    swp_sb = es.enter_context(nc.sbuf_tensor("swp_sb", [128, 128], BF16))
    roff_sb = es.enter_context(nc.sbuf_tensor("roff_sb", [1, 1], I32))
    comb_sb = es.enter_context(nc.sbuf_tensor("comb_sb", [8, T], F32))
    rg = nc.gpsimd.alloc_register("rg_off")
    rg2 = nc.gpsimd.alloc_register("rg_off2")

    sc.dma("gp", "gT", ones1_sb[:, :], ones1[:, :])
    sc.dma("gp", "gT", tri_sb[:, :], tri[:, :])
    sc.dma("gp", "gT", ident_sb[:, :], ident[:, :])
    sc.dma("gp", "gT", swp_sb[:, :], swp[:, :])
    v_cst_dma = sc.dma("gp", "gT", roff_sb[:, :], roff[:, :])
    v_tri = v_cst_dma
    sc.emit("gp", lambda e: e.memset(ones128_sb[:, :], 1.0), inc="gpc")
    sc.emit("gp", lambda e: e.memset(biasC[:, :], CSHIFT), inc="gpc")
    sc.emit("gp", lambda e: e.memset(epsT[:, :], EPS), inc="gpc")
    sc.emit("gp", lambda e: e.memset(half_neg[:, :], -0.5), inc="gpc")
    sc.emit("gp", lambda e: e.memset(neginf8[:, :], -1.0e30), inc="gpc")
    v_cst = sc.emit("gp", lambda e: e.memset(invD[:, :], 1.0 / D), inc="gpc")
    CST = [("gT", v_cst_dma), ("gpc", v_cst)]

    wo_outer = ExitStack()
    attn_fm = wo_outer.enter_context(nc.sbuf_tensor("attn_fm", [128, NCH, T], BF16))
    wo_sb0 = wo_outer.enter_context(nc.sbuf_tensor("wo0", [128, NCH, 128], BF16))
    wr_sb = wo_outer.enter_context(nc.sbuf_tensor("wr_sb", [128, NCH, NE], F32))
    pre = {}
    attn_es = ExitStack()
    tb_sb = [attn_es.enter_context(nc.sbuf_tensor(f"tb{i}", [128, 4, 512], BF16)) for i in range(2)]
    TAB = []

    v_tb_load = [0, 0]
    v_tb_used = [0, 0]
    qbf = attn_es.enter_context(nc.sbuf_tensor("qbf", [128, S], BF16))
    kbf = attn_es.enter_context(nc.sbuf_tensor("kbf", [128, S], BF16))
    vsb = attn_es.enter_context(nc.sbuf_tensor("vsb", [128, 64, 132], BF16))
    xt_sb = [attn_es.enter_context(nc.sbuf_tensor(f"xt{i}", [128, NCH, 512], BF16)) for i in range(2)]
    wq_sb = [attn_es.enter_context(nc.sbuf_tensor(f"wq_sb{i}", [128, NCH, DH], BF16)) for i in range(2)]
    wk_sb = [attn_es.enter_context(nc.sbuf_tensor(f"wk_sb{i}", [128, NCH, DH], BF16)) for i in range(2)]
    wv_sb = [attn_es.enter_context(nc.sbuf_tensor(f"wv_sb{i}", [128, NCH, DH], BF16)) for i in range(2)]
    qtmp = attn_es.enter_context(nc.sbuf_tensor("qtmp", [128, 512], BF16))
    qsw = attn_es.enter_context(nc.sbuf_tensor("qsw", [128, 512], BF16))
    ktmp = attn_es.enter_context(nc.sbuf_tensor("ktmp", [128, 512], BF16))
    ksw = attn_es.enter_context(nc.sbuf_tensor("ksw", [128, 512], BF16))
    rope_t1 = attn_es.enter_context(nc.sbuf_tensor("rope_t1", [128, 512], BF16))
    rope_t2 = attn_es.enter_context(nc.sbuf_tensor("rope_t2", [128, 512], BF16))
    probs = [attn_es.enter_context(nc.sbuf_tensor(f"probs{i}", [128, 512], BF16)) for i in range(4)]
    ao_sb = [attn_es.enter_context(nc.sbuf_tensor(f"ao{i}", [128, 4, DH], BF16)) for i in range(2)]
    rc_sb = attn_es.enter_context(nc.sbuf_tensor("rc_sb", [128, 4], F32))

    v_xt_load = [0, 0]
    v_xt_used = [None, None]
    v_ao_dma = [0, 0]

    last_v1 = {}
    last_v2 = {}
    last_v3 = [0]
    v_rc_read = [0, 0, 0, 0]

    v_proj_end = [0, 0]
    v_w_loaded = [0, 0]
    v_pick_h = [0] * HPC
    v_tp_h = [0] * HPC
    v_xc = [0] * NTT
    v_xc_par = [0, 0]
    rg_loaded = [False]

    def emit_ag(h, half):
        HS = S // 2
        sc.emit("gp", lambda e, h=h, half=half: e.collective_compute(
            "AllGather", ALU.bypass, replica_groups=[[0, 1, 2, 3], [4, 5, 6, 7]],
            ins=[cc_in_h[h][half * HS:(half + 1) * HS, :]],
            outs=[cc_ag_h[h][half * 2 * S:half * 2 * S + 4 * HS, :]]),
            waits=[("sS0", sc.val("sS0")), ("sS1", sc.val("sS1"))], inc="cc")

    def emit_picks(h):
        if not rg_loaded[0]:
            sc.emit("gp", lambda e: e.reg_load(rg, roff_sb[0:1, 0:1]),
                    waits=[("gT", v_cst_dma)])
            rg_loaded[0] = True
        v = None
        for j in range(4):
            w0 = [("cc", 2 * (h + 1))] if j == 0 else []
            if j == 0 and h >= 2:
                w0.append((f"sT{h - 2}", v_tp_h[h - 2]))
            sc.emit("gp", lambda e, j=j: e.reg_add(rg2, rg, j * (S // 2) * DH), waits=w0)
            v = sc.emit(
                "gp", lambda e, j=j, h=h: e.dma_start(
                    bass.AP(cc_my, j * (HPC * DH) + h * DH, [[D, T], [1, DH]]),
                    bass.AP(cc_ag_h[h], rg2, [[DH, T], [1, DH]])),
                inc=f"gP{h % 2}", amt=16)
        v_pick_h[h] = v

    def emit_tp(h):
        v = None
        for i, fc in enumerate(range(h, NCH, 4)):
            v = sc.emit("sp", lambda e, fc=fc: e.dma_start(
                attn_fm[:, fc, :], cc_my[:, fc * 128:(fc + 1) * 128], transpose=True),
                waits=[(f"gP{h % 2}", v_pick_h[h])] if i == 0 else (), inc=f"sT{h}", amt=16)
        v_tp_h[h] = v
    for hh in range(HPC + 1):
        if hh < HPC:
            hp_ = hh % 2
            vw = None
            for wsb, wt in ((wq_sb[hp_], wqT), (wk_sb[hp_], wkT), (wv_sb[hp_], wvT)):
                vw = sc.dma("gp", f"gW{hp_}", wsb[:, :, :],
                            wt[hh * 128:(hh + 1) * 128, :].rearrange("p (c o) -> p c o", o=DH),
                            waits=[("pe", v_proj_end[hp_])] if v_proj_end[hp_] else [])
            v_w_loaded[hp_] = vw
        if hh == 0:
            continue
        h = hh - 1
        hp = h % 2
        v_prev_pe = sc.val("pe")
        W = [(f"gW{hp}", v_w_loaded[hp])]

        vone = sc.emit("gp", lambda e: e.memset(vsb[:, :, 128:132], 1.0),
                       waits=[("pe", sc.val("pe"))], inc="gpc")

        v_q_done = [0] * NTT
        v_k_done = [0] * NTT
        v_v_done = [0] * NTT
        for tt in range(NTT):
            bi = tt % 2
            t0 = tt * 512
            w = [("pe", v_xt_used[bi])] if v_xt_used[bi] else []
            if h == 0:
                if v_xc_par[bi]:
                    w.append(("sXC", v_xc_par[bi]))
                v_xt_load[bi] = sc.dma(
                    "gp", f"gX{bi}", xt_sb[bi][:, :, :],
                    xTt[tt * 128:(tt + 1) * 128, :].rearrange("p (c t) -> p c t", t=512),
                    waits=w)
                v_xc[tt] = sc.dma(
                    "sp", "sXC", xbf[tt * 128:(tt + 1) * 128, :],
                    xt_sb[bi][:, :, :].rearrange("p c t -> p (c t)"),
                    waits=[(f"gX{bi}", v_xt_load[bi])])
                v_xc_par[bi] = v_xc[tt]
            else:
                w.append(("sXC", v_xc[tt]))
                v_xt_load[bi] = sc.dma(
                    "sp", f"gX{bi}", xt_sb[bi][:, :, :],
                    xbf[tt * 128:(tt + 1) * 128, :].rearrange("p (c t) -> p c t", t=512),
                    waits=w)
            wtb = [("dve", v_tb_used[bi])] if v_tb_used[bi] else []
            v_tb_load[bi] = sc.dma(
                "sp", f"gB{bi}", tb_sb[bi][:, :, :].rearrange("p a b -> p (a b)"),
                tbpk[:, tt * 2048:(tt + 1) * 2048], waits=wtb)
            if tt == 1 and h >= 1:
                emit_ag(h - 1, 1)
            if tt == 4 and h >= 1:
                emit_picks(h - 1)

            pq, pk, pv = pb[0 + (tt % 2)], pb[2 + (tt % 2)], pb[4 + (tt % 2)]
            ld = [(f"gX{bi}", v_xt_load[bi])] + W + TAB + CST
            _mm_group(sc, pq, [(pq.t[:, 0:512], wq_sb[hp][:, c, :], xt_sb[bi][:, c, :])
                              for c in range(NCH)], extra_waits=ld)
            _mm_group(sc, pk, [(pk.t[:, 0:512], wk_sb[hp][:, c, :], xt_sb[bi][:, c, :])
                              for c in range(NCH)], extra_waits=ld)
            vmms = []
            for sub in range(4):
                for c in range(NCH):
                    vmms.append((pv.t[:, sub * 128:(sub + 1) * 128],
                                 xt_sb[bi][:, c, sub * 128:(sub + 1) * 128],
                                 wv_sb[hp][:, c, :]))
            vstop = None
            wv0 = pv.wait_free() + ld
            for i, (o, l, r) in enumerate(vmms):
                st = (i % NCH == 0)
                sp_ = (i % NCH == NCH - 1)
                vstop = sc.emit(
                    "pe", lambda e, o=o, l=l, r=r, st=st, sp_=sp_: e.matmul(o, l, r, start=st, stop=sp_),
                    wv0 if i == 0 else (), inc="pe" if sp_ else None)
            pv.ready = ("pe", vstop)
            pv.free = {}
            v_xt_used[bi] = sc.val("pe")

            def rope(ptile, tmp, sw, ct, st_, dest, t0, bi):
                wva = [ptile.ready] + CST
                tk = id(tmp)
                if tk in last_v1:
                    wva.append(("dve", last_v1[tk]))
                va = sc.emit("act", lambda e: e.copy(tmp[:, :], ptile.t[:, 0:512]),
                             waits=wva, inc="act")
                ptile.read_done("act", va)
                vswp = _mm_group(sc, ptile, [(ptile.t[:, 0:512], swp_sb[:, :], tmp[:, :])],
                                 extra_waits=[("act", va)])
                wsw = [ptile.ready]
                if tk in last_v2:
                    wsw.append(("dve", last_v2[tk]))
                va2 = sc.emit("act", lambda e: e.copy(sw[:, :], ptile.t[:, 0:512]),
                              waits=wsw, inc="act")
                ptile.read_done("act", va2)
                v1 = sc.emit("dve", lambda e: e.tensor_mul(rope_t1[:, :], tmp[:, :], ct),
                             waits=[("act", va), ("dve", last_v3[0]), (f"gB{bi}", v_tb_load[bi])], inc="dve")
                v2 = sc.emit("dve", lambda e: e.tensor_mul(rope_t2[:, :], sw[:, :], st_),
                             waits=[("act", va2)], inc="dve")
                v3 = sc.emit("dve", lambda e, t0=t0: e.tensor_add(dest[:, t0:t0 + 512], rope_t1[:, :], rope_t2[:, :]),
                             waits=[("pe", v_prev_pe), ("dve", v2)], inc="dve")
                last_v1[tk] = v1
                last_v2[tk] = v2
                last_v3[0] = v3
                return v3

            v_q_done[tt] = rope(pq, qtmp, qsw, tb_sb[bi][:, 0, :], tb_sb[bi][:, 1, :], qbf, t0, bi)
            v_k_done[tt] = rope(pk, ktmp, ksw, tb_sb[bi][:, 2, :], tb_sb[bi][:, 3, :], kbf, t0, bi)
            v_tb_used[bi] = v_k_done[tt]
            vv = None
            for sub in range(4):
                vv = sc.emit("act",
                             lambda e, sub=sub, tt=tt, pvt=pv.t: e.copy(
                                 vsb[:, tt * 4 + sub, 0:128], pvt[:, sub * 128:(sub + 1) * 128]),
                             waits=[pv.ready, ("gpc", vone)], inc="act")
            pv.read_done("act", vv)
            v_v_done[tt] = vv
        v_proj_end[hp] = sc.val("pe")

        for qt in range(NTT):
            if qt == 0 and h >= 1:
                emit_tp(h - 1)
            if qt == 0 and h == HPC - 1:
                pre["wr"] = sc.dma("gp", "gT", wr_sb[:, :, :],
                                   wrouter[:, :].rearrange("p (c e) -> p c e", e=NE))
                pre[0] = sc.dma(
                    "gp", "gWo0", wo_sb0[:, :, :],
                    woT[0:128, :].rearrange("p (c o) -> p c o", o=128))
            q0 = qt * 512
            nkc = qt * 4 + 4
            v_score = [0] * nkc
            v_exp = [0] * nkc
            v_pv = [0] * nkc
            qstart = [0] * 4
            v_pv_stop = [0] * 4

            out_pb = [pb[2], pb[3], pb[4], pb[5]]
            sco_pb = [pb[0], pb[1], pb[6], pb[7]]

            def emit_score(kc):
                bank = sco_pb[kc % 4]
                dq = kc - qt * 4
                qlo = max(0, dq) * 128
                w = bank.wait_free()
                w += [("dve", v_k_done[kc // 4]), ("dve", v_q_done[qt])]
                v = sc.emit("pe", lambda e, kc=kc, qlo=qlo, bank=bank, q0=q0: e.matmul(
                    bank.t[:, 0:512 - qlo], kbf[:, kc * 128:(kc + 1) * 128],
                    qbf[:, q0 + qlo:q0 + 512], start=True, stop=True),
                    w, inc="pe")
                bank.ready = ("pe", v)
                bank.free = {}
                v_score[kc] = v
                if dq >= 0:
                    vm = sc.emit("dve", lambda e, bank=bank: e.tensor_add(
                        bank.t[:, 0:128], bank.t[:, 0:128], tri_sb[:, :]),
                        [("pe", v), ("gT", v_cst_dma)], inc="dve")
                    bank.ready = ("dve", vm)
                wexp = [bank.ready]
                if kc >= 4 and v_pv[kc - 4]:
                    wexp.append(("pe", v_pv[kc - 4]))
                ve = sc.emit("act", lambda e, kc=kc, qlo=qlo, bank=bank: e.activation(
                    probs[kc % 4][:, 0:512 - qlo], bank.t[:, 0:512 - qlo], AF.Exp,
                    bias=biasC[:, 0:1]), wexp, inc="act")
                bank.read_done("act", ve)
                v_exp[kc] = ve

            def emit_pv(kc):
                dq = kc - qt * 4
                qlo = max(0, dq) * 128
                last = None
                for qc in range(qlo // 128, 4):
                    bank = out_pb[qc]
                    qcg = qt * 4 + qc
                    st = (kc == 0)
                    sp_ = (kc == qcg)
                    w = [("act", v_exp[kc])]
                    if st:
                        w += bank.wait_free()
                    last = sc.emit("pe", lambda e, kc=kc, qc=qc, qlo=qlo, bank=bank, st=st, sp_=sp_: e.matmul(
                        bank.t[:, 0:129],
                        probs[kc % 4][:, qc * 128 - qlo:(qc + 1) * 128 - qlo],
                        vsb[:, kc, 0:129], start=st, stop=sp_),
                        w, inc="pe" if (sp_ or qc == 3) else None)
                    if st:
                        bank.free = {}
                    if sp_:
                        v_pv_stop[qc] = last
                        bank.ready = ("pe", last)
                v_pv[kc] = last

            emit_score(0)
            emit_score(1)
            emit_score(2)
            for kc in range(3, nkc):
                emit_score(kc)
                emit_pv(kc - 3)
            emit_pv(nkc - 3)
            emit_pv(nkc - 2)
            emit_pv(nkc - 1)

            vcp = None
            for qc in range(4):
                bank = out_pb[qc]
                wr_ = [("pe", v_pv_stop[qc]), ("dve", sc.val("dve"))]
                vr = sc.emit("dve", lambda e, qc=qc, bank=bank: e.reciprocal(
                    rc_sb[:, qc:qc + 1], bank.t[:, 128:129]),
                    wr_, inc="dve")
                wao = [("dve", vr)]
                if v_ao_dma[qt % 2]:
                    wao.append((f"sS{qt % 2}", v_ao_dma[qt % 2]))
                vcp = sc.emit("act", lambda e, qc=qc, bank=bank, qt=qt: e.activation(
                    ao_sb[qt % 2][:, qc, :], bank.t[:, 0:128], AF.Copy,
                    scale=rc_sb[:, qc:qc + 1]), wao, inc="act")
                bank.read_done("act", vcp)
                v_rc_read[qc] = vcp
            v_ao_dma[qt % 2] = sc.dma(
                "gp", f"sS{qt % 2}", cc_in_h[h][q0:q0 + 512, :].rearrange("(qc p) f -> p qc f", p=128),
                ao_sb[qt % 2][:, :, :], waits=[("act", vcp)])
            if qt == 7:
                emit_ag(h, 0)

    emit_ag(HPC - 1, 1)
    emit_picks(HPC - 1)
    emit_tp(HPC - 1)

    v_attn_pe_end = sc.val("pe")
    attn_es.close()
    sc.barrier(exclude=("cc", "gP0", "gP1", "sT0", "sT1", "sT2", "sT3"))


    wo_es = ExitStack()
    wo_sb = [wo_sb0, wo_es.enter_context(nc.sbuf_tensor("wo1", [128, NCH, 128], BF16))]
    x1pre = wo_es.enter_context(nc.sbuf_tensor("x1pre", [128, NCH, 512], F32))
    xres_sb = [wo_es.enter_context(nc.sbuf_tensor(f"xres{i}", [128, 512], F32)) for i in range(2)]
    xsq = wo_es.enter_context(nc.sbuf_tensor("xsq", [128, 512], F32))
    stat_sb = wo_es.enter_context(nc.sbuf_tensor("stat_sb", [1, 6 * 512], F32))
    mb_sb = wo_es.enter_context(nc.sbuf_tensor("mb_sb", [128, 512], F32))
    rb_sb = wo_es.enter_context(nc.sbuf_tensor("rb_sb", [128, 512], F32))
    logits_sb = wo_es.enter_context(nc.sbuf_tensor("logits_sb", [128, NCH, NE], F32))
    pr_sb = wo_es.enter_context(nc.sbuf_tensor("pr_sb", [128, NE], F32))
    eq_sb = wo_es.enter_context(nc.sbuf_tensor("eq_sb", [128, NE], F32))
    pm_sb = wo_es.enter_context(nc.sbuf_tensor("pm_sb", [128, NE], F32))
    sc1_sb = wo_es.enter_context(nc.sbuf_tensor("sc1_sb", [128, 4], F32))

    v_wr = pre["wr"]

    xsq2 = wo_es.enter_context(nc.sbuf_tensor("xsq2", [128, 2, 512], F32))

    v_spill_tile = [0, 0, 0, 0]
    v_wo_load = [0, 0]
    v_wo_used = [None, None]
    v_xres_used = [None, None]
    v_x1_readers = []
    v_x1sp_last = 0
    v_logit_evac = 0
    for tt in range(4):
        t0 = tt * 512
        sbank, s2bank = pb[6], pb[7]
        v_s2mm = [0] * NCH
        v_s1mm = [0] * NCH
        v_add = [0] * NCH
        v_sq = [0] * NCH

        def emit_smm(oc):
            st, sp_ = (oc == 0), (oc == NCH - 1)
            w1 = [("dve", v_add[oc])]
            if st:
                w1 += sbank.wait_free() + CST
            v_s1mm[oc] = sc.emit("pe", lambda e, oc=oc, st=st, sp_=sp_: e.matmul(
                sbank.t[0:1, 0:512], ones128_sb[:, :], x1pre[:, oc, :], start=st, stop=sp_),
                w1, inc="pe")
            w2 = [("act", v_sq[oc])]
            if st:
                w2 += s2bank.wait_free()
            v_s2mm[oc] = sc.emit("pe", lambda e, oc=oc, st=st, sp_=sp_: e.matmul(
                s2bank.t[0:1, 0:512], ones128_sb[:, :], xsq2[:, oc % 2, :], start=st, stop=sp_),
                w2, inc="pe")

        corder = [c for m in range(4) for c in range(m, NCH, 4)]
        for oc in range(NCH):
            bi = oc % 2
            if tt == 0 and oc == 0:
                v_wo_load[bi] = pre[0]
            else:
                w = [("pe", v_wo_used[bi])] if v_wo_used[bi] else []
                v_wo_load[bi] = sc.dma(
                    "gp", f"gWo{bi}", wo_sb[bi][:, :, :],
                    woT[oc * 128:(oc + 1) * 128, :].rearrange("p (c o) -> p c o", o=128),
                    waits=w)
            wbank = pb[oc % 2]
            wf = wbank.wait_free() + [(f"gWo{bi}", v_wo_load[bi])]
            vstop = None
            for ci, c in enumerate(corder):
                ww = list(wf) if ci == 0 else []
                ww.append((f"sT{c % 4}", v_tp_h[c % 4]))
                vstop = sc.emit("pe", lambda e, c=c, bi=bi, wbank=wbank, t0=t0, st=(ci == 0), sp_=(ci == NCH - 1): e.matmul(
                    wbank.t[:, 0:512], wo_sb[bi][:, c, :], attn_fm[:, c, t0:t0 + 512],
                    start=st, stop=sp_), ww, inc="pe" if ci == NCH - 1 else None)
            wbank.ready = ("pe", vstop)
            wbank.free = {}
            v_wo_used[bi] = sc.val("pe")
            xbi = oc % 2
            wx = [("dve", v_xres_used[xbi])] if v_xres_used[xbi] else []
            vxr = sc.dma("sp", f"sR{xbi}", xres_sb[xbi][:, :], xres[oc * 128:(oc + 1) * 128, t0:t0 + 512], waits=wx)
            wadd = [wbank.ready, (f"sR{xbi}", vxr)] + v_x1_readers
            v_add[oc] = sc.emit("dve", lambda e, oc=oc, xbi=xbi, wbank=wbank: e.tensor_add(
                x1pre[:, oc, :], wbank.t[:, 0:512], xres_sb[xbi][:, :]), wadd, inc="dve")
            wbank.read_done("dve", v_add[oc])
            v_xres_used[xbi] = v_add[oc]
            wsq = [("dve", v_add[oc])]
            if oc >= 2:
                wsq.append(("pe", v_s2mm[oc - 2]))
            v_sq[oc] = sc.emit("act", lambda e, oc=oc: e.activation(
                xsq2[:, oc % 2, :], x1pre[:, oc, :], AF.Square), wsq, inc="act")
            if oc >= 1:
                emit_smm(oc - 1)
        emit_smm(NCH - 1)
        v_x1_readers = []
        sbank.ready = ("pe", v_s1mm[NCH - 1])
        sbank.free = {}
        s2bank.ready = ("pe", v_s2mm[NCH - 1])
        s2bank.free = {}

        st_ = stat_sb
        va_m = sc.emit("act", lambda e: e.activation(st_[0:1, 0:512], sbank.t[0:1, 0:512],
                                                     AF.Copy, scale=invD[0:1, 0:1]),
                       [sbank.ready] + CST, inc="act")
        va_e2 = sc.emit("act", lambda e: e.activation(st_[0:1, 512:1024], s2bank.t[0:1, 0:512],
                                                      AF.Copy, scale=invD[0:1, 0:1]),
                        [s2bank.ready], inc="act")
        sbank.read_done("act", va_m)
        s2bank.read_done("act", va_e2)
        vd_mm = sc.emit("dve", lambda e: e.tensor_mul(st_[0:1, 1024:1536], st_[0:1, 0:512], st_[0:1, 0:512]),
                        [("act", va_m)], inc="dve")
        vd_var = sc.emit("dve", lambda e: e.tensor_sub(st_[0:1, 1536:2048], st_[0:1, 512:1024], st_[0:1, 1024:1536]),
                         [("act", va_e2), ("dve", vd_mm)], inc="dve")
        va_ln = sc.emit("act", lambda e: e.activation(st_[0:1, 2048:2560], st_[0:1, 1536:2048],
                                                      AF.Ln, bias=epsT[0:1, 0:1]),
                        [("dve", vd_var)], inc="act")
        va_rstd = sc.emit("act", lambda e: e.activation(st_[0:1, 2560:3072], st_[0:1, 2048:2560],
                                                        AF.Exp, scale=half_neg[0:1, 0:1]),
                          [("act", va_ln)], inc="act")
        mbank, rbank = pb[2], pb[3]
        vb_m = _mm_group(sc, mbank, [(mbank.t[:, 0:512], ones1_sb[:, :], st_[0:1, 0:512])],
                         extra_waits=[("act", va_m)] + CST)
        vb_r = _mm_group(sc, rbank, [(rbank.t[:, 0:512], ones1_sb[:, :], st_[0:1, 2560:3072])],
                         extra_waits=[("act", va_rstd)])
        ve_m = sc.emit("act", lambda e: e.copy(mb_sb[:, :], mbank.t[:, 0:512]), [mbank.ready], inc="act")
        ve_r = sc.emit("act", lambda e: e.copy(rb_sb[:, :], rbank.t[:, 0:512]), [rbank.ready], inc="act")
        mbank.read_done("act", ve_m)
        rbank.read_done("act", ve_r)

        v_nrm = [0] * NCH
        for oc in range(NCH):
            vs = sc.emit("dve", lambda e, oc=oc: e.tensor_sub(
                x1pre[:, oc, :], x1pre[:, oc, :], mb_sb[:, :]),
                [("act", ve_m), ("pe", v_s2mm[NCH - 1]), ("dve", sc.val("dve"))], inc="dve")
            v_nrm[oc] = sc.emit("dve", lambda e, oc=oc: e.tensor_mul(
                x1pre[:, oc, :], x1pre[:, oc, :], rb_sb[:, :]),
                [("act", ve_r), ("dve", vs)], inc="dve")
            v_x1sp_last = sc.dma("sp", "sL", x1sp[tt * 128:(tt + 1) * 128, oc * 512:(oc + 1) * 512],
                                 x1pre[:, oc, :], waits=[("dve", v_nrm[oc])])
        v_x1_readers.append(("sL", v_x1sp_last))
        v_spill_tile[tt] = v_x1sp_last

        lbank = pb[4]
        for tc in range(4):
            g = tt * 4 + tc
            mms = [(lbank.t[0:128, 0:NE], x1pre[:, oc, tc * 128:(tc + 1) * 128], wr_sb[:, oc, :])
                   for oc in range(NCH)]
            _mm_group(sc, lbank, mms,
                      extra_waits=[("dve", v_nrm[NCH - 1]), ("gT", v_wr)])
            v_logit_evac = sc.emit("act", lambda e, g=g: e.copy(
                logits_sb[:, g, :], lbank.t[0:128, 0:NE]), [lbank.ready], inc="act")
            lbank.read_done("act", v_logit_evac)
            v_x1_readers.append(("pe", sc.val("pe")))

    tbank = pb[5]
    v_comb_evac = 0
    v_tp_prev = 0
    for g in range(NCH):
        c0 = g * 128
        vm = sc.emit("dve", lambda e, g=g: e.tensor_reduce(
            sc1_sb[:, 0:1], logits_sb[:, g, :], AX.X, ALU.max), [("act", v_logit_evac)], inc="dve")
        wls = [("dve", vm)]
        if v_tp_prev:
            wls.append(("pe", v_tp_prev))
        vls = sc.emit("dve", lambda e, g=g: e.tensor_single_scalar(
            pr_sb[:, :], logits_sb[:, g, :], sc1_sb[:, 0:1], ALU.subtract),
            wls, inc="dve")
        vex = sc.emit("act", lambda e: e.activation(pr_sb[:, :], pr_sb[:, :], AF.Exp),
                      [("dve", vls)], inc="act")
        vsm = sc.emit("dve", lambda e: e.tensor_reduce(
            sc1_sb[:, 1:2], pr_sb[:, :], AX.X, ALU.add), [("act", vex)], inc="dve")
        vrs = sc.emit("dve", lambda e: e.reciprocal(sc1_sb[:, 2:3], sc1_sb[:, 1:2]),
                      [("dve", vsm)], inc="dve")
        vpr = sc.emit("dve", lambda e: e.tensor_single_scalar(
            pr_sb[:, :], pr_sb[:, :], sc1_sb[:, 2:3], ALU.mult), [("dve", vrs)], inc="dve")
        vm1 = sc.emit("dve", lambda e: e.tensor_reduce(
            sc1_sb[:, 0:1], pr_sb[:, :], AX.X, ALU.max), [("dve", vpr)], inc="dve")
        veq = sc.emit("dve", lambda e: e.tensor_single_scalar(
            eq_sb[:, :], pr_sb[:, :], sc1_sb[:, 0:1], ALU.is_ge), [("dve", vm1)], inc="dve")
        vpm = sc.emit("dve", lambda e: e.scalar_tensor_tensor(
            pm_sb[:, :], eq_sb[:, :], -1.0e30, pr_sb[:, :], ALU.mult, ALU.add),
            [("dve", veq)] + CST, inc="dve")
        vm2 = sc.emit("dve", lambda e: e.tensor_reduce(
            sc1_sb[:, 3:4], pm_sb[:, :], AX.X, ALU.max), [("dve", vpm)], inc="dve")
        vge = sc.emit("dve", lambda e: e.tensor_single_scalar(
            eq_sb[:, :], pr_sb[:, :], sc1_sb[:, 3:4], ALU.is_ge), [("dve", vm2)], inc="dve")
        vcb = sc.emit("dve", lambda e: e.tensor_mul(pr_sb[:, :], pr_sb[:, :], eq_sb[:, :]),
                      [("dve", vge)], inc="dve")
        wtp = tbank.wait_free() + [("dve", vcb)]
        vt = sc.emit("pe", lambda e: e.matmul(
            tbank.t[0:NE, 0:128], pr_sb[:, :], ident_sb[:, :],
            start=True, stop=True), wtp, inc="pe")
        tbank.ready = ("pe", vt)
        tbank.free = {}
        v_tp_prev = vt
        v_comb_evac = sc.emit("act", lambda e, c0=c0: e.copy(
            comb_sb[:, c0:c0 + 128], tbank.t[0:NE, 0:128]), [tbank.ready], inc="act")
        tbank.read_done("act", v_comb_evac)

    wo_es.close()
    wo_outer.close()
    sc.barrier()

    ffn_es = ExitStack()
    sel8_sb = ffn_es.enter_context(nc.sbuf_tensor("sel8_sb", [8, 8 * 128], F32))
    v_sel8 = sc.dma("gp", "gT", sel8_sb[:, :], sel8[:, :])
    y_acc = ffn_es.enter_context(nc.sbuf_tensor("y_acc", [128, NCH, 1024], F32))
    x1bh = ffn_es.enter_context(nc.sbuf_tensor("x1bh", [128, NCH, 1024], BF16))
    h_sb = ffn_es.enter_context(nc.sbuf_tensor("h_sb", [128, NDE, 1024], BF16))
    wg_sb = [ffn_es.enter_context(nc.sbuf_tensor(f"wg{i}", [128, NCH, 128], BF16)) for i in range(2)]
    wu_sb = [ffn_es.enter_context(nc.sbuf_tensor(f"wu{i}", [128, NCH, 128], BF16)) for i in range(2)]
    wd_sb = ffn_es.enter_context(nc.sbuf_tensor("wd_sb", [128, 6, NCH, 128], BF16))
    sg_sb = ffn_es.enter_context(nc.sbuf_tensor("sg_sb", [128, 2, 512], F32))
    cb_sb = ffn_es.enter_context(nc.sbuf_tensor("cb_sb", [128, 1024], F32))
    xq2 = ffn_es.enter_context(nc.sbuf_tensor("xq2", [128, 2, 512], F32))
    st2 = ffn_es.enter_context(nc.sbuf_tensor("st2", [1, 3 * 512], F32))
    mb2 = ffn_es.enter_context(nc.sbuf_tensor("mb2", [128, 512], F32))
    rb2 = ffn_es.enter_context(nc.sbuf_tensor("rb2", [128, 512], F32))
    x2t = [ffn_es.enter_context(nc.sbuf_tensor(f"x2t{i}", [128, 512], F32)) for i in range(2)]

    stages = [(wsg, wsu, wsd, 0, 0, 0, None), (wsg, wsu, wsd, 11, 0, 0, None)] + \
             [(weg, weu, wed, 0, e * NDE, e * DE, e) for e in range(NE)]
    JGRP = [(0, 6), (6, 11)]
    ffn_pre = {}

    v_y_read_prev = []
    v_x1b_pe_prev = 0
    v_out_last = 0
    v_store = [0, 0]
    v_prev_s2 = 0
    v_prev_norm = 0
    for hf in range(2):
        t0h = hf * 1024
        v_x1b = [0, 0]
        v_ya0 = None
        for ts in range(2):
            g = hf * 2 + ts
            wldx = [("sL", v_spill_tile[g])]
            if v_x1b_pe_prev:
                wldx.append(("pe", v_x1b_pe_prev))
            v_x1b[ts] = sc.dma(
                "gp", "gY", x1bh[:, :, ts * 512:(ts + 1) * 512],
                x1sp[g * 128:(g + 1) * 128, :].rearrange("p (c t) -> p c t", t=512), waits=wldx)
            if hf == 0 and ts == 0:
                for j_ in range(2):
                    ffn_pre[(j_, "g")] = sc.dma(
                        "gp", f"gG{j_}", wg_sb[j_][:, :, :],
                        wsg[j_ * 128:(j_ + 1) * 128, :].rearrange("p (c o) -> p c o", o=128))
                    ffn_pre[(j_, "u")] = sc.dma(
                        "gp", f"gG{j_}", wu_sb[j_][:, :, :],
                        wsu[j_ * 128:(j_ + 1) * 128, :].rearrange("p (c o) -> p c o", o=128))
            wldy = [("sL", v_spill_tile[g])] + v_y_read_prev
            v_ya0 = sc.dma("sp", "sY", y_acc[:, :, ts * 512:(ts + 1) * 512],
                           x1sp[g * 128:(g + 1) * 128, :].rearrange("p (c t) -> p c t", t=512), waits=wldy)

        v_h_read_pe = sc.val("pe")
        v_cb_read = sc.val("dve")
        v_wd_used = sc.val("pe")
        v_wg_used = [sc.val("pe"), sc.val("pe")]
        v_sg_used = [sc.val("dve"), sc.val("dve")]
        v_ya_last = sc.val("dve")
        v_ya_chunk = {}
        for snum, (wgt, wut, wdt, co, rg_, rd_, ex) in enumerate(stages):
            if ex is not None:
                cbank = pb[6]
                v_cb = 0
                for ts in range(2):
                    _mm_group(sc, cbank, [(cbank.t[:, 0:512], sel8_sb[:, ex * 128:(ex + 1) * 128],
                                           comb_sb[:, t0h + ts * 512:t0h + (ts + 1) * 512])],
                              extra_waits=[("act", v_comb_evac), ("gT", v_sel8)])
                    wcb = [cbank.ready, ("dve", v_cb_read)]
                    v_cb = sc.emit("act", lambda e, ts=ts, cbank=cbank: e.copy(
                        cb_sb[:, ts * 512:(ts + 1) * 512], cbank.t[:, 0:512]), wcb, inc="act")
                    cbank.read_done("act", v_cb)

            for g0, g1 in JGRP:
                v_h_w = 0
                for j in range(g0, g1):
                    jp = j % 2
                    rb = (rg_ + co + j) * 128
                    if hf == 0 and snum == 0 and j < 2:
                        vwg = ffn_pre[(j, "g")]
                        vwu = ffn_pre[(j, "u")]
                    else:
                        wwl = [("pe", v_wg_used[jp])]
                        vwg = sc.dma("gp", f"gG{jp}", wg_sb[jp][:, :, :],
                                     wgt[rb:rb + 128, :].rearrange("p (c o) -> p c o", o=128),
                                     waits=wwl)
                        vwu = sc.dma("gp", f"gG{jp}", wu_sb[jp][:, :, :],
                                     wut[rb:rb + 128, :].rearrange("p (c o) -> p c o", o=128))
                    for ts in range(2):
                        gbank, ubank = pb[0 + ts], pb[2 + ts]
                        wg_ = [(f"gG{jp}", vwu), ("gY", v_x1b[ts])]
                        _mm_group(sc, gbank, [(gbank.t[:, 0:512], wg_sb[jp][:, c, :],
                                               x1bh[:, c, ts * 512:(ts + 1) * 512]) for c in range(NCH)],
                                  extra_waits=wg_)
                        _mm_group(sc, ubank, [(ubank.t[:, 0:512], wu_sb[jp][:, c, :],
                                               x1bh[:, c, ts * 512:(ts + 1) * 512]) for c in range(NCH)])
                        v_wg_used[jp] = sc.val("pe")
                        wsl = [gbank.ready, ("dve", v_sg_used[ts])]
                        vsl = sc.emit("act", lambda e, ts=ts, gbank=gbank: e.activation(
                            sg_sb[:, ts, :], gbank.t[:, 0:512], AF.Silu), wsl, inc="act")
                        gbank.read_done("act", vsl)
                        whh = [("act", vsl), ubank.ready, ("pe", v_h_read_pe)]
                        if ex is not None:
                            vcbf = sc.emit("dve", lambda e, ts=ts: e.tensor_mul(
                                sg_sb[:, ts, :], sg_sb[:, ts, :], cb_sb[:, ts * 512:(ts + 1) * 512]),
                                [("act", vsl), ("act", v_cb), ("dve", v_sg_used[ts])], inc="dve")
                            whh.append(("dve", vcbf))
                        vh = sc.emit("dve", lambda e, j=j, ts=ts, ubank=ubank: e.tensor_mul(
                            h_sb[:, j, ts * 512:(ts + 1) * 512], sg_sb[:, ts, :], ubank.t[:, 0:512]),
                            whh, inc="dve")
                        ubank.read_done("dve", vh)
                        v_sg_used[ts] = vh
                        v_h_w = vh
                v_wd = 0
                for j in range(g0, g1):
                    wwd = [("pe", v_wd_used)] if j == g0 else []
                    v_wd = sc.dma("gp", "gD", wd_sb[:, j - g0, :, :],
                                  wdt[rd_ + (co + j) * 128:rd_ + (co + j + 1) * 128, :].rearrange("p (c o) -> p c o", o=128),
                                  waits=wwd)
                for ts in range(2):
                    for oc in range(NCH):
                        ybank = (pb[4], pb[5], pb[7])[(ts * NCH + oc) % 3]
                        wdn = [("dve", v_h_w), ("gD", v_wd)]
                        _mm_group(sc, ybank, [(ybank.t[:, 0:512], wd_sb[:, j2 - g0, oc, :],
                                               h_sb[:, j2, ts * 512:(ts + 1) * 512]) for j2 in range(g0, g1)],
                                  extra_waits=wdn)
                        v_ya_last = sc.emit("dve", lambda e, oc=oc, ts=ts, ybank=ybank: e.tensor_add(
                            y_acc[:, oc, ts * 512:(ts + 1) * 512], ybank.t[:, 0:512],
                            y_acc[:, oc, ts * 512:(ts + 1) * 512]),
                            [ybank.ready, ("dve", v_ya_chunk.get((oc, ts), 0)), ("sY", v_ya0)], inc="dve")
                        v_ya_chunk[(oc, ts)] = v_ya_last
                        ybank.read_done("dve", v_ya_last)
                v_wd_used = sc.val("pe")
            v_h_read_pe = sc.val("pe")
            v_cb_read = sc.val("dve")
        v_x1b_pe_prev = sc.val("pe")

        v_y_read_prev = []
        for ts in range(2):
            sbank, s2bank = pb[6], pb[7]
            v_sq2 = [0] * NCH
            v_m1 = [0] * NCH
            v_m2 = [0] * NCH

            for oc in range(NCH):
                wsq = [("dve", v_ya_last), ("pe", v_m2[oc - 2] if oc >= 2 else v_prev_s2)]
                v_sq2[oc] = sc.emit("act", lambda e, oc=oc, ts=ts: e.activation(
                    xq2[:, oc % 2, :], y_acc[:, oc, ts * 512:(ts + 1) * 512], AF.Square),
                    wsq, inc="act")
                w1 = [("dve", v_ya_last)]
                if oc == 0:
                    w1 += sbank.wait_free() + CST
                v_m1[oc] = sc.emit("pe", lambda e, oc=oc, ts=ts, st=(oc == 0), sp_=(oc == NCH - 1): e.matmul(
                    sbank.t[0:1, 0:512], ones128_sb[:, :], y_acc[:, oc, ts * 512:(ts + 1) * 512],
                    start=st, stop=sp_), w1, inc="pe")
                w2 = [("act", v_sq2[oc])]
                if oc == 0:
                    w2 += s2bank.wait_free()
                v_m2[oc] = sc.emit("pe", lambda e, oc=oc, st=(oc == 0), sp_=(oc == NCH - 1): e.matmul(
                    s2bank.t[0:1, 0:512], ones128_sb[:, :], xq2[:, oc % 2, :],
                    start=st, stop=sp_), w2, inc="pe")
            sbank.ready = ("pe", v_m1[NCH - 1])
            sbank.free = {}
            s2bank.ready = ("pe", v_m2[NCH - 1])
            s2bank.free = {}
            v_prev_s2 = v_m2[NCH - 1]

            va_m = sc.emit("act", lambda e: e.activation(st2[0:1, 0:512], sbank.t[0:1, 0:512],
                                                         AF.Copy, scale=invD[0:1, 0:1]),
                           [sbank.ready], inc="act")
            va_e2 = sc.emit("act", lambda e: e.activation(st2[0:1, 512:1024], s2bank.t[0:1, 0:512],
                                                          AF.Copy, scale=invD[0:1, 0:1]),
                            [s2bank.ready], inc="act")
            sbank.read_done("act", va_m)
            s2bank.read_done("act", va_e2)
            vd_mm = sc.emit("dve", lambda e: e.tensor_mul(st2[0:1, 1024:1536], st2[0:1, 0:512], st2[0:1, 0:512]),
                            [("act", va_m), ("dve", sc.val("dve"))], inc="dve")
            vd_var = sc.emit("dve", lambda e: e.tensor_sub(st2[0:1, 512:1024], st2[0:1, 512:1024], st2[0:1, 1024:1536]),
                             [("act", va_e2), ("dve", vd_mm)], inc="dve")
            va_ln = sc.emit("act", lambda e: e.activation(st2[0:1, 1024:1536], st2[0:1, 512:1024],
                                                          AF.Ln, bias=epsT[0:1, 0:1]),
                            [("dve", vd_var)], inc="act")
            va_rstd = sc.emit("act", lambda e: e.activation(st2[0:1, 512:1024], st2[0:1, 1024:1536],
                                                            AF.Exp, scale=half_neg[0:1, 0:1]),
                              [("act", va_ln)], inc="act")
            mbank, rbank = pb[2], pb[3]
            _mm_group(sc, mbank, [(mbank.t[:, 0:512], ones1_sb[:, :], st2[0:1, 0:512])],
                      extra_waits=[("act", va_m)])
            _mm_group(sc, rbank, [(rbank.t[:, 0:512], ones1_sb[:, :], st2[0:1, 512:1024])],
                      extra_waits=[("act", va_rstd)])
            ve_m = sc.emit("act", lambda e: e.copy(mb2[:, :], mbank.t[:, 0:512]),
                           [mbank.ready, ("dve", v_prev_norm)], inc="act")
            ve_r = sc.emit("act", lambda e: e.copy(rb2[:, :], rbank.t[:, 0:512]),
                           [rbank.ready, ("dve", v_prev_norm)], inc="act")
            mbank.read_done("act", ve_m)
            rbank.read_done("act", ve_r)

            for oc in range(NCH):
                op = oc % 2
                wn = [("act", ve_m), ("act", ve_r), ("dve", sc.val("dve"))]
                if v_store[op]:
                    wn.append((f"sO{op}", v_store[op]))
                vs = sc.emit("dve", lambda e, oc=oc, ts=ts, op=op: e.tensor_sub(
                    x2t[op][:, :], y_acc[:, oc, ts * 512:(ts + 1) * 512], mb2[:, :]), wn, inc="dve")
                vn = sc.emit("dve", lambda e, op=op: e.tensor_mul(
                    x2t[op][:, :], x2t[op][:, :], rb2[:, :]), [("dve", vs)], inc="dve")
                v_out_last = sc.dma("sp", f"sO{op}", outfm[oc * 128:(oc + 1) * 128, t0h + ts * 512:t0h + (ts + 1) * 512],
                                    x2t[op][:, :], waits=[("dve", vn)])
                v_store[op] = v_out_last
                v_y_read_prev.append(("dve", vn))
                v_prev_norm = vn
            v_y_read_prev.append(("pe", sc.val("pe")))
            v_y_read_prev.append(("act", v_sq2[NCH - 1]))

    ffn_es.close()
    _finish(nc, sc, sems, es, [("sO0", sc.val("sO0")), ("sO1", sc.val("sO1"))])
    return nc


def _finish(nc, sc, sems, es, final_waits):
    sc.emit("sp", lambda e: e.nop(), waits=list(final_waits))
    for s in SEMS:
        assert sc.cnt.get(s, 0) < 60000, (s, sc.cnt[s])
    with nc.Block() as block:
        def runner(name):
            def _r(eng):
                hw = {}
                for waits, fn, inc, amt in sc.q[name]:
                    for sem, v in waits:
                        if v and v > hw.get(sem, 0):
                            eng.wait_ge(sems[sem], v)
                            hw[sem] = v
                    ins = fn(eng)
                    if inc is not None:
                        ins.then_inc(sems[inc], amt)
            return _r
        block.tensor(runner("pe"))
        block.scalar(runner("act"))
        block.vector(runner("dve"))
        block.sync(runner("sp"))
        block.gpsimd(runner("gp"))
    es.close()


def _rope_tables():
    idx = np.arange(0, DH, 2, dtype=np.float32)
    inv_freq = (np.float32(10000.0) ** (-idx / np.float32(DH))).astype(np.float32)
    angles = (np.arange(S, dtype=np.float32)[:, None] * inv_freq[None, :]).astype(np.float32)
    cos = np.cos(angles).astype(np.float32)
    sin = np.sin(angles).astype(np.float32)
    cos_e = np.concatenate([cos.T, cos.T], axis=0)
    sin_s = np.concatenate([-sin.T, sin.T], axis=0)
    scl = np.float32(1.0 / np.sqrt(np.float32(DH)))
    tabs = np.stack([cos_e * scl, sin_s * scl, cos_e, sin_s], axis=1)
    pk = tabs.reshape(128, 4, NTT, 512).transpose(0, 2, 1, 3).reshape(128, NTT * 4 * 512)
    return np.ascontiguousarray(pk.astype(ml_dtypes.bfloat16))


def _tile_w(w, nj, od):
    return np.ascontiguousarray(
        np.asarray(w, np.float32).reshape(NCH, 128, nj, od)
        .transpose(2, 1, 0, 3).reshape(nj * 128, NCH * od))


def _stage(inputs):
    x = np.asarray(inputs["L0_attn_x"], np.float32)
    wq = np.asarray(inputs["L0_attn_Wq"], np.float32)
    wk = np.asarray(inputs["L0_attn_Wk"], np.float32)
    wv = np.asarray(inputs["L0_attn_Wv"], np.float32)
    wo = np.asarray(inputs["L0_attn_Wo"], np.float32)

    perm = np.concatenate([np.arange(0, DH, 2), np.arange(1, DH, 2)])
    wqT, wkT, wvT = wq.T, wk.T, wv.T
    tbpk = _rope_tables()

    tri = np.where(np.arange(128)[:, None] <= np.arange(128)[None, :], 0.0, MASKVAL).astype(np.float32)
    identm = np.eye(128, dtype=np.float32)
    swpm = np.zeros((128, 128), dtype=np.float32)
    for i in range(128):
        swpm[(i + 64) % 128, i] = 1.0
    sel8 = np.zeros((8, 8 * 128), dtype=np.float32)
    for e in range(8):
        sel8[e, e * 128:(e + 1) * 128] = 1.0
    ones1 = np.ones((1, 128), dtype=np.float32)

    weg = np.asarray(inputs["L0_ffn_We_gate"], np.float32)
    weu = np.asarray(inputs["L0_ffn_We_up"], np.float32)
    shared = {
        "woT": _tile_w(wo.T, NCH, 128), "tbpk": tbpk,
        "tri": tri, "ident": identm, "swp": swpm, "sel8": sel8, "ones1": ones1,
        "wrouter": _tile_w(inputs["L0_ffn_W_router"], 1, NE),
        "wsg": _tile_w(inputs["L0_ffn_Ws_gate"], NDS, 128),
        "wsu": _tile_w(inputs["L0_ffn_Ws_up"], NDS, 128),
        "wsd": np.ascontiguousarray(inputs["L0_ffn_Ws_down"], dtype=np.float32),
        "weg": np.ascontiguousarray(np.concatenate(
            [_tile_w(weg[e], NDE, 128) for e in range(NE)], axis=0)),
        "weu": np.ascontiguousarray(np.concatenate(
            [_tile_w(weu[e], NDE, 128) for e in range(NE)], axis=0)),
        "wed": np.ascontiguousarray(np.asarray(inputs["L0_ffn_We_down"], np.float32).reshape(NE * DE, D)),
    }

    xTs = [np.ascontiguousarray(x[b].T) for b in range(B)]
    xTts = [_tile_w(xT, NTT, 512) for xT in xTs]
    in_maps = []
    for c in range(8):
        b, r = c // 4, c % 4
        heads = range(4 * r, 4 * r + 4)
        m = dict(shared)
        m["xTt"] = xTts[b]
        m["xres"] = np.ascontiguousarray(xTs[b][:, r * T:(r + 1) * T])
        m["wqT"] = _tile_w(np.concatenate(
            [wqT[:, h * DH:(h + 1) * DH][:, perm] for h in heads], axis=1), HPC, DH)
        m["wkT"] = _tile_w(np.concatenate(
            [wkT[:, h * DH:(h + 1) * DH][:, perm] for h in heads], axis=1), HPC, DH)
        m["wvT"] = _tile_w(np.concatenate(
            [wvT[:, h * DH:(h + 1) * DH] for h in heads], axis=1), HPC, DH)
        m["roff"] = np.array([[(r // 2) * 2 * S * DH + (r % 2) * T * DH]], dtype=np.int32)
        in_maps.append(m)
    return in_maps


def kernel(**inputs):
    nc = _build("full")
    in_maps = _stage(inputs)
    res = run_bass_kernel_spmd(nc, in_maps, core_ids=list(range(8)))
    out = np.empty((B, S, D), dtype=np.float32)
    for c in range(8):
        b, r = c // 4, c % 4
        out[b, r * T:(r + 1) * T, :] = np.asarray(res.results[c]["outfm"]).T
    return out


# version 08
# speedup vs baseline: 1.1741x; 1.0063x over previous
import numpy as np
import ml_dtypes
from contextlib import ExitStack

import concourse.bass as bass
import concourse.mybir as mybir
from concourse.bass_utils import run_bass_kernel_spmd

F32 = mybir.dt.float32
BF16 = mybir.dt.bfloat16
I32 = mybir.dt.int32
AF = mybir.ActivationFunctionType
ALU = mybir.AluOpType
AX = mybir.AxisListType

B, S, D, H, DH = 2, 8192, 2048, 16, 128
HPC = 4
T = 2048
NCH = D // 128
DS, NDS = 2816, 22
DE, NDE = 1408, 11
NE = 8
NTT = S // 512
CSHIFT = -12.0
MASKVAL = -60.0
EPS = 1e-5

ENGS = ("pe", "act", "dve", "sp", "gp")
SEMS = ("pe", "act", "dve", "gpc", "cc",
        "gT", "gX0", "gX1", "gW0", "gW1", "gWo0", "gWo1", "gY",
        "gG0", "gG1", "gD", "gP0", "gP1", "gB0", "gB1", "sXC", "sY",
        "sS0", "sS1", "sR0", "sR1", "sL", "sT0", "sT1", "sT2", "sT3", "sO0", "sO1")


class _Sched:

    def __init__(self):
        self.q = {e: [] for e in ENGS}
        self.cnt = {s: 0 for s in SEMS}
        self.pending = {e: [] for e in ENGS}

    def val(self, sem):
        return self.cnt[sem]

    SELF_SERIAL = ()

    def emit(self, eng, fn, waits=(), inc=None, amt=1):
        w = list(self.pending[eng]) + list(waits)
        self.pending[eng] = []
        if eng in self.SELF_SERIAL:
            if self.cnt[eng]:
                w.append((eng, self.cnt[eng]))
            if inc is None:
                inc = eng
        v = None
        if inc is not None:
            self.cnt[inc] += amt
            v = self.cnt[inc]
        self.q[eng].append((tuple(w), fn, inc, amt))
        return v

    def dma(self, eng, sem, out_ap, in_ap, waits=()):
        return self.emit(eng, lambda e, o=out_ap, i=in_ap: e.dma_start(o, i),
                         waits, inc=sem, amt=16)

    def barrier(self, exclude=()):
        for e in ENGS:
            if not self.q[e]:
                continue
            w, fn, inc, amt = self.q[e][-1]
            if inc is None:
                sem = e if e in ("pe", "act", "dve") else "gpc"
                assert e != "sp", "sp barrier op must carry an inc"
                self.cnt[sem] = self.cnt.get(sem, 0) + 1
                self.q[e][-1] = (w, fn, sem, 1)
        marks = [(s, self.cnt[s]) for s in SEMS
                 if self.cnt.get(s, 0) > 0 and s not in exclude]
        for e in ENGS:
            self.pending[e] = list(marks)


class _PB:

    def __init__(self, tile):
        self.t = tile
        self.ready = None
        self.free = {}

    def wait_free(self):
        return list(self.free.items())

    def read_done(self, sem, val):
        self.free[sem] = max(self.free.get(sem, 0), val)


def _mm_group(sc, pb, mms, extra_waits=(), lookahead_ok=True):
    waits = pb.wait_free() + list(extra_waits)
    n = len(mms)
    v = None
    for i, (o, l, r) in enumerate(mms):
        st, sp_ = (i == 0), (i == n - 1)
        v = sc.emit(
            "pe",
            lambda e, o=o, l=l, r=r, st=st, sp_=sp_: e.matmul(o, l, r, start=st, stop=sp_),
            waits if i == 0 else (),
            inc="pe" if sp_ else None,
        )
    pb.ready = ("pe", v)
    pb.free = {}
    return v


def _build(upto="full"):
    nc = bass.Bass("TRN2", target_bir_lowering=False)
    sc = _Sched()

    dt_in = {}
    def din(name, shape, dt=F32):
        dt_in[name] = nc.dram_tensor(name, list(shape), dt, kind="ExternalInput")
        return dt_in[name]

    xTt = din("xTt", (NTT * 128, NCH * 512))
    xres = din("xres", (D, T))
    wqT = din("wqT", (HPC * 128, NCH * DH))
    wkT = din("wkT", (HPC * 128, NCH * DH))
    wvT = din("wvT", (HPC * 128, NCH * DH))
    woT = din("woT", (NCH * 128, NCH * 128))
    tbpk = din("tbpk", (128, NTT * 4 * 512), BF16)
    tri = din("tri", (128, 128))
    sel8 = din("sel8", (8, 8 * 128))
    ident = din("ident", (128, 128))
    swp = din("swp", (128, 128))
    ones1 = din("ones1", (1, 128))
    roff = din("roff", (1, 1), I32)
    roff3 = din("roff3", (1, 1), I32)
    wrouter = din("wrouter", (128, NCH * NE))
    wsg = din("wsg", (NDS * 128, NCH * 128))
    wsu = din("wsu", (NDS * 128, NCH * 128))
    wsd = din("wsd", (DS, D))
    weg = din("weg", (NE * NDE * 128, NCH * 128))
    weu = din("weu", (NE * NDE * 128, NCH * 128))
    wed = din("wed", (NE * DE, D))
    xbf = nc.dram_tensor("xbf", [NTT * 128, NCH * 512], BF16, kind="Internal")

    outfm = nc.dram_tensor("outfm", [D, T], F32, kind="ExternalOutput")

    cc_in_h = [nc.dram_tensor(f"cc_in{h}", [S, DH], BF16, kind="Internal") for h in range(HPC)]
    cc_ag_h = [nc.dram_tensor(f"cc_ag{h}", [4 * S, DH], BF16, kind="Internal") for h in range(HPC)]
    cc_my = nc.dram_tensor("cc_my", [T, D], BF16, kind="Internal")
    x1sp = nc.dram_tensor("x1sp", [4 * 128, NCH * 512], F32, kind="Internal")

    assert upto == "full"

    es = ExitStack()
    sems = {s: es.enter_context(nc.semaphore(s)) for s in SEMS}

    pb = [_PB(es.enter_context(nc.psum_tensor(f"pb{i}", [128, 512], F32))) for i in range(8)]

    ones1_sb = es.enter_context(nc.sbuf_tensor("ones1_sb", [1, 128], F32))
    ones128_sb = es.enter_context(nc.sbuf_tensor("ones128_sb", [128, 1], F32))
    biasC = es.enter_context(nc.sbuf_tensor("biasC", [128, 1], F32))
    epsT = es.enter_context(nc.sbuf_tensor("epsT", [1, 1], F32))
    half_neg = es.enter_context(nc.sbuf_tensor("half_neg", [1, 1], F32))
    neginf8 = es.enter_context(nc.sbuf_tensor("neginf8", [128, 8], F32))
    invD = es.enter_context(nc.sbuf_tensor("invD", [1, 1], F32))
    tri_sb = es.enter_context(nc.sbuf_tensor("tri_sb", [128, 128], F32))
    ident_sb = es.enter_context(nc.sbuf_tensor("ident_sb", [128, 128], F32))
    swp_sb = es.enter_context(nc.sbuf_tensor("swp_sb", [128, 128], BF16))
    roff_sb = es.enter_context(nc.sbuf_tensor("roff_sb", [1, 1], I32))
    roff3_sb = es.enter_context(nc.sbuf_tensor("roff3_sb", [1, 1], I32))
    comb_sb = es.enter_context(nc.sbuf_tensor("comb_sb", [8, T], F32))
    rg = nc.gpsimd.alloc_register("rg_off")
    rg2 = nc.gpsimd.alloc_register("rg_off2")
    rg3 = nc.gpsimd.alloc_register("rg_off3")

    sc.dma("gp", "gT", ones1_sb[:, :], ones1[:, :])
    sc.dma("gp", "gT", tri_sb[:, :], tri[:, :])
    sc.dma("gp", "gT", ident_sb[:, :], ident[:, :])
    sc.dma("gp", "gT", swp_sb[:, :], swp[:, :])
    sc.dma("gp", "gT", roff3_sb[:, :], roff3[:, :])
    v_cst_dma = sc.dma("gp", "gT", roff_sb[:, :], roff[:, :])
    v_tri = v_cst_dma
    sc.emit("gp", lambda e: e.memset(ones128_sb[:, :], 1.0), inc="gpc")
    sc.emit("gp", lambda e: e.memset(biasC[:, :], CSHIFT), inc="gpc")
    sc.emit("gp", lambda e: e.memset(epsT[:, :], EPS), inc="gpc")
    sc.emit("gp", lambda e: e.memset(half_neg[:, :], -0.5), inc="gpc")
    sc.emit("gp", lambda e: e.memset(neginf8[:, :], -1.0e30), inc="gpc")
    v_cst = sc.emit("gp", lambda e: e.memset(invD[:, :], 1.0 / D), inc="gpc")
    CST = [("gT", v_cst_dma), ("gpc", v_cst)]

    wo_outer = ExitStack()
    attn_fm = wo_outer.enter_context(nc.sbuf_tensor("attn_fm", [128, NCH, T], BF16))
    wo_sb0 = wo_outer.enter_context(nc.sbuf_tensor("wo0", [128, NCH, 128], BF16))
    wr_sb = wo_outer.enter_context(nc.sbuf_tensor("wr_sb", [128, NCH, NE], F32))
    pre = {}
    attn_es = ExitStack()
    tb_sb = [attn_es.enter_context(nc.sbuf_tensor(f"tb{i}", [128, 4, 512], BF16)) for i in range(2)]
    TAB = []

    v_tb_load = [0, 0]
    v_tb_used = [0, 0]
    qbf = attn_es.enter_context(nc.sbuf_tensor("qbf", [128, S], BF16))
    kbf = attn_es.enter_context(nc.sbuf_tensor("kbf", [128, S], BF16))
    vsb = attn_es.enter_context(nc.sbuf_tensor("vsb", [128, 64, 132], BF16))
    xt_sb = [attn_es.enter_context(nc.sbuf_tensor(f"xt{i}", [128, NCH, 512], BF16)) for i in range(2)]
    wq_sb = [attn_es.enter_context(nc.sbuf_tensor(f"wq_sb{i}", [128, NCH, DH], BF16)) for i in range(2)]
    wk_sb = [attn_es.enter_context(nc.sbuf_tensor(f"wk_sb{i}", [128, NCH, DH], BF16)) for i in range(2)]
    wv_sb = [attn_es.enter_context(nc.sbuf_tensor(f"wv_sb{i}", [128, NCH, DH], BF16)) for i in range(2)]
    qtmp = attn_es.enter_context(nc.sbuf_tensor("qtmp", [128, 512], BF16))
    qsw = attn_es.enter_context(nc.sbuf_tensor("qsw", [128, 512], BF16))
    ktmp = attn_es.enter_context(nc.sbuf_tensor("ktmp", [128, 512], BF16))
    ksw = attn_es.enter_context(nc.sbuf_tensor("ksw", [128, 512], BF16))
    rope_t1 = attn_es.enter_context(nc.sbuf_tensor("rope_t1", [128, 512], BF16))
    rope_t2 = attn_es.enter_context(nc.sbuf_tensor("rope_t2", [128, 512], BF16))
    probs = [attn_es.enter_context(nc.sbuf_tensor(f"probs{i}", [128, 512], BF16)) for i in range(4)]
    ao_sb = [attn_es.enter_context(nc.sbuf_tensor(f"ao{i}", [128, 4, DH], BF16)) for i in range(2)]
    rc_sb = attn_es.enter_context(nc.sbuf_tensor("rc_sb", [128, 4], F32))

    v_xt_load = [0, 0]
    v_xt_used = [None, None]
    v_ao_dma = [0, 0]

    last_v1 = {}
    last_v2 = {}
    last_v3 = [0]
    v_rc_read = [0, 0, 0, 0]

    v_proj_end = [0, 0]
    v_w_loaded = [0, 0]
    v_pick_h = [0] * HPC
    v_tp_h = [0] * HPC
    v_xc = [0] * NTT
    v_xc_par = [0, 0]
    rg_loaded = [False]

    def emit_ag(h, piece, npieces=2):
        PS = S // npieces
        sc.emit("gp", lambda e, h=h, piece=piece, PS=PS: e.collective_compute(
            "AllGather", ALU.bypass, replica_groups=[[0, 1, 2, 3], [4, 5, 6, 7]],
            ins=[cc_in_h[h][piece * PS:(piece + 1) * PS, :]],
            outs=[cc_ag_h[h][piece * 4 * PS:(piece + 1) * 4 * PS, :]]),
            waits=[("sS0", sc.val("sS0")), ("sS1", sc.val("sS1"))], inc="cc")

    def emit_picks(h):
        if not rg_loaded[0]:
            sc.emit("gp", lambda e: e.reg_load(rg, roff_sb[0:1, 0:1]),
                    waits=[("gT", v_cst_dma)])
            sc.emit("gp", lambda e: e.reg_load(rg3, roff3_sb[0:1, 0:1]))
            rg_loaded[0] = True
        ccw = 2 * (h + 1) if h < HPC - 1 else 2 * HPC + 2
        base, stride = (rg, (S // 2) * DH) if h < HPC - 1 else (rg3, T * DH)
        v = None
        for j in range(4):
            w0 = [("cc", ccw)] if j == 0 else []
            if j == 0 and h >= 2:
                w0.append((f"sT{h - 2}", v_tp_h[h - 2]))
            sc.emit("gp", lambda e, j=j, base=base, stride=stride: e.reg_add(rg2, base, j * stride), waits=w0)
            v = sc.emit(
                "gp", lambda e, j=j, h=h: e.dma_start(
                    bass.AP(cc_my, j * (HPC * DH) + h * DH, [[D, T], [1, DH]]),
                    bass.AP(cc_ag_h[h], rg2, [[DH, T], [1, DH]])),
                inc=f"gP{h % 2}", amt=16)
        v_pick_h[h] = v

    def emit_tp(h):
        v = None
        for i, fc in enumerate(range(h, NCH, 4)):
            v = sc.emit("sp", lambda e, fc=fc: e.dma_start(
                attn_fm[:, fc, :], cc_my[:, fc * 128:(fc + 1) * 128], transpose=True),
                waits=[(f"gP{h % 2}", v_pick_h[h])] if i == 0 else (), inc=f"sT{h}", amt=16)
        v_tp_h[h] = v
    for hh in range(HPC + 1):
        if hh < HPC:
            hp_ = hh % 2
            vw = None
            for wsb, wt in ((wq_sb[hp_], wqT), (wk_sb[hp_], wkT), (wv_sb[hp_], wvT)):
                vw = sc.dma("gp", f"gW{hp_}", wsb[:, :, :],
                            wt[hh * 128:(hh + 1) * 128, :].rearrange("p (c o) -> p c o", o=DH),
                            waits=[("pe", v_proj_end[hp_])] if v_proj_end[hp_] else [])
            v_w_loaded[hp_] = vw
        if hh == 0:
            continue
        h = hh - 1
        hp = h % 2
        v_prev_pe = sc.val("pe")
        W = [(f"gW{hp}", v_w_loaded[hp])]

        vone = sc.emit("gp", lambda e: e.memset(vsb[:, :, 128:132], 1.0),
                       waits=[("pe", sc.val("pe"))], inc="gpc")

        v_q_done = [0] * NTT
        v_k_done = [0] * NTT
        v_v_done = [0] * NTT
        for tt in range(NTT):
            bi = tt % 2
            t0 = tt * 512
            w = [("pe", v_xt_used[bi])] if v_xt_used[bi] else []
            if h == 0:
                if v_xc_par[bi]:
                    w.append(("sXC", v_xc_par[bi]))
                v_xt_load[bi] = sc.dma(
                    "gp", f"gX{bi}", xt_sb[bi][:, :, :],
                    xTt[tt * 128:(tt + 1) * 128, :].rearrange("p (c t) -> p c t", t=512),
                    waits=w)
                v_xc[tt] = sc.dma(
                    "sp", "sXC", xbf[tt * 128:(tt + 1) * 128, :],
                    xt_sb[bi][:, :, :].rearrange("p c t -> p (c t)"),
                    waits=[(f"gX{bi}", v_xt_load[bi])])
                v_xc_par[bi] = v_xc[tt]
            else:
                w.append(("sXC", v_xc[tt]))
                v_xt_load[bi] = sc.dma(
                    "sp", f"gX{bi}", xt_sb[bi][:, :, :],
                    xbf[tt * 128:(tt + 1) * 128, :].rearrange("p (c t) -> p c t", t=512),
                    waits=w)
            wtb = [("dve", v_tb_used[bi])] if v_tb_used[bi] else []
            v_tb_load[bi] = sc.dma(
                "sp", f"gB{bi}", tb_sb[bi][:, :, :].rearrange("p a b -> p (a b)"),
                tbpk[:, tt * 2048:(tt + 1) * 2048], waits=wtb)
            if tt == 1 and h >= 1:
                emit_ag(h - 1, 1)
            if tt == 4 and h >= 1:
                emit_picks(h - 1)

            pq, pk, pv = pb[0 + (tt % 2)], pb[2 + (tt % 2)], pb[4 + (tt % 2)]
            ld = [(f"gX{bi}", v_xt_load[bi])] + W + TAB + CST
            _mm_group(sc, pq, [(pq.t[:, 0:512], wq_sb[hp][:, c, :], xt_sb[bi][:, c, :])
                              for c in range(NCH)], extra_waits=ld)
            _mm_group(sc, pk, [(pk.t[:, 0:512], wk_sb[hp][:, c, :], xt_sb[bi][:, c, :])
                              for c in range(NCH)], extra_waits=ld)
            vmms = []
            for sub in range(4):
                for c in range(NCH):
                    vmms.append((pv.t[:, sub * 128:(sub + 1) * 128],
                                 xt_sb[bi][:, c, sub * 128:(sub + 1) * 128],
                                 wv_sb[hp][:, c, :]))
            vstop = None
            wv0 = pv.wait_free() + ld
            for i, (o, l, r) in enumerate(vmms):
                st = (i % NCH == 0)
                sp_ = (i % NCH == NCH - 1)
                vstop = sc.emit(
                    "pe", lambda e, o=o, l=l, r=r, st=st, sp_=sp_: e.matmul(o, l, r, start=st, stop=sp_),
                    wv0 if i == 0 else (), inc="pe" if sp_ else None)
            pv.ready = ("pe", vstop)
            pv.free = {}
            v_xt_used[bi] = sc.val("pe")

            def rope(ptile, tmp, sw, ct, st_, dest, t0, bi):
                wva = [ptile.ready] + CST
                tk = id(tmp)
                if tk in last_v1:
                    wva.append(("dve", last_v1[tk]))
                va = sc.emit("act", lambda e: e.copy(tmp[:, :], ptile.t[:, 0:512]),
                             waits=wva, inc="act")
                ptile.read_done("act", va)
                vswp = _mm_group(sc, ptile, [(ptile.t[:, 0:512], swp_sb[:, :], tmp[:, :])],
                                 extra_waits=[("act", va)])
                wsw = [ptile.ready]
                if tk in last_v2:
                    wsw.append(("dve", last_v2[tk]))
                va2 = sc.emit("act", lambda e: e.copy(sw[:, :], ptile.t[:, 0:512]),
                              waits=wsw, inc="act")
                ptile.read_done("act", va2)
                v1 = sc.emit("dve", lambda e: e.tensor_mul(rope_t1[:, :], tmp[:, :], ct),
                             waits=[("act", va), ("dve", last_v3[0]), (f"gB{bi}", v_tb_load[bi])], inc="dve")
                v2 = sc.emit("dve", lambda e: e.tensor_mul(rope_t2[:, :], sw[:, :], st_),
                             waits=[("act", va2)], inc="dve")
                v3 = sc.emit("dve", lambda e, t0=t0: e.tensor_add(dest[:, t0:t0 + 512], rope_t1[:, :], rope_t2[:, :]),
                             waits=[("pe", v_prev_pe), ("dve", v2)], inc="dve")
                last_v1[tk] = v1
                last_v2[tk] = v2
                last_v3[0] = v3
                return v3

            v_q_done[tt] = rope(pq, qtmp, qsw, tb_sb[bi][:, 0, :], tb_sb[bi][:, 1, :], qbf, t0, bi)
            v_k_done[tt] = rope(pk, ktmp, ksw, tb_sb[bi][:, 2, :], tb_sb[bi][:, 3, :], kbf, t0, bi)
            v_tb_used[bi] = v_k_done[tt]
            vv = None
            for sub in range(4):
                vv = sc.emit("act",
                             lambda e, sub=sub, tt=tt, pvt=pv.t: e.copy(
                                 vsb[:, tt * 4 + sub, 0:128], pvt[:, sub * 128:(sub + 1) * 128]),
                             waits=[pv.ready, ("gpc", vone)], inc="act")
            pv.read_done("act", vv)
            v_v_done[tt] = vv
        v_proj_end[hp] = sc.val("pe")

        for qt in range(NTT):
            if qt == 0 and h >= 1:
                emit_tp(h - 1)
            if qt == 0 and h == HPC - 1:
                pre["wr"] = sc.dma("gp", "gT", wr_sb[:, :, :],
                                   wrouter[:, :].rearrange("p (c e) -> p c e", e=NE))
                pre[0] = sc.dma(
                    "gp", "gWo0", wo_sb0[:, :, :],
                    woT[0:128, :].rearrange("p (c o) -> p c o", o=128))
            q0 = qt * 512
            nkc = qt * 4 + 4
            v_score = [0] * nkc
            v_exp = [0] * nkc
            v_pv = [0] * nkc
            qstart = [0] * 4
            v_pv_stop = [0] * 4

            out_pb = [pb[2], pb[3], pb[4], pb[5]]
            sco_pb = [pb[0], pb[1], pb[6], pb[7]]

            def emit_score(kc):
                bank = sco_pb[kc % 4]
                dq = kc - qt * 4
                qlo = max(0, dq) * 128
                w = bank.wait_free()
                w += [("dve", v_k_done[kc // 4]), ("dve", v_q_done[qt])]
                v = sc.emit("pe", lambda e, kc=kc, qlo=qlo, bank=bank, q0=q0: e.matmul(
                    bank.t[:, 0:512 - qlo], kbf[:, kc * 128:(kc + 1) * 128],
                    qbf[:, q0 + qlo:q0 + 512], start=True, stop=True),
                    w, inc="pe")
                bank.ready = ("pe", v)
                bank.free = {}
                v_score[kc] = v
                if dq >= 0:
                    vm = sc.emit("dve", lambda e, bank=bank: e.tensor_add(
                        bank.t[:, 0:128], bank.t[:, 0:128], tri_sb[:, :]),
                        [("pe", v), ("gT", v_cst_dma)], inc="dve")
                    bank.ready = ("dve", vm)
                wexp = [bank.ready]
                if kc >= 4 and v_pv[kc - 4]:
                    wexp.append(("pe", v_pv[kc - 4]))
                ve = sc.emit("act", lambda e, kc=kc, qlo=qlo, bank=bank: e.activation(
                    probs[kc % 4][:, 0:512 - qlo], bank.t[:, 0:512 - qlo], AF.Exp,
                    bias=biasC[:, 0:1]), wexp, inc="act")
                bank.read_done("act", ve)
                v_exp[kc] = ve

            def emit_pv(kc):
                dq = kc - qt * 4
                qlo = max(0, dq) * 128
                last = None
                for qc in range(qlo // 128, 4):
                    bank = out_pb[qc]
                    qcg = qt * 4 + qc
                    st = (kc == 0)
                    sp_ = (kc == qcg)
                    w = [("act", v_exp[kc])]
                    if st:
                        w += bank.wait_free()
                    last = sc.emit("pe", lambda e, kc=kc, qc=qc, qlo=qlo, bank=bank, st=st, sp_=sp_: e.matmul(
                        bank.t[:, 0:129],
                        probs[kc % 4][:, qc * 128 - qlo:(qc + 1) * 128 - qlo],
                        vsb[:, kc, 0:129], start=st, stop=sp_),
                        w, inc="pe" if (sp_ or qc == 3) else None)
                    if st:
                        bank.free = {}
                    if sp_:
                        v_pv_stop[qc] = last
                        bank.ready = ("pe", last)
                v_pv[kc] = last

            emit_score(0)
            emit_score(1)
            emit_score(2)
            for kc in range(3, nkc):
                emit_score(kc)
                emit_pv(kc - 3)
            emit_pv(nkc - 3)
            emit_pv(nkc - 2)
            emit_pv(nkc - 1)

            vcp = None
            for qc in range(4):
                bank = out_pb[qc]
                wr_ = [("pe", v_pv_stop[qc]), ("dve", sc.val("dve"))]
                vr = sc.emit("dve", lambda e, qc=qc, bank=bank: e.reciprocal(
                    rc_sb[:, qc:qc + 1], bank.t[:, 128:129]),
                    wr_, inc="dve")
                wao = [("dve", vr)]
                if v_ao_dma[qt % 2]:
                    wao.append((f"sS{qt % 2}", v_ao_dma[qt % 2]))
                vcp = sc.emit("act", lambda e, qc=qc, bank=bank, qt=qt: e.activation(
                    ao_sb[qt % 2][:, qc, :], bank.t[:, 0:128], AF.Copy,
                    scale=rc_sb[:, qc:qc + 1]), wao, inc="act")
                bank.read_done("act", vcp)
                v_rc_read[qc] = vcp
            v_ao_dma[qt % 2] = sc.dma(
                "gp", f"sS{qt % 2}", cc_in_h[h][q0:q0 + 512, :].rearrange("(qc p) f -> p qc f", p=128),
                ao_sb[qt % 2][:, :, :], waits=[("act", vcp)])
            if h < HPC - 1:
                if qt == 7:
                    emit_ag(h, 0)
            elif qt % 4 == 3 and qt < NTT - 1:
                emit_ag(h, qt // 4, npieces=4)

    emit_ag(HPC - 1, 3, npieces=4)
    emit_picks(HPC - 1)
    emit_tp(HPC - 1)

    v_attn_pe_end = sc.val("pe")
    attn_es.close()
    sc.barrier(exclude=("cc", "gP0", "gP1", "sT0", "sT1", "sT2", "sT3"))


    wo_es = ExitStack()
    wo_sb = [wo_sb0, wo_es.enter_context(nc.sbuf_tensor("wo1", [128, NCH, 128], BF16))]
    x1pre_b = [wo_es.enter_context(nc.sbuf_tensor(f"x1pre{i}", [128, NCH, 512], F32)) for i in range(2)]
    xres_sb = [wo_es.enter_context(nc.sbuf_tensor(f"xres{i}", [128, 512], F32)) for i in range(2)]
    xsq = wo_es.enter_context(nc.sbuf_tensor("xsq", [128, 512], F32))
    stat_sb = wo_es.enter_context(nc.sbuf_tensor("stat_sb", [1, 6 * 512], F32))
    mb_sb = wo_es.enter_context(nc.sbuf_tensor("mb_sb", [128, 512], F32))
    rb_sb = wo_es.enter_context(nc.sbuf_tensor("rb_sb", [128, 512], F32))
    logits_sb = wo_es.enter_context(nc.sbuf_tensor("logits_sb", [128, NCH, NE], F32))
    pr_sb = wo_es.enter_context(nc.sbuf_tensor("pr_sb", [128, NE], F32))
    eq_sb = wo_es.enter_context(nc.sbuf_tensor("eq_sb", [128, NE], F32))
    pm_sb = wo_es.enter_context(nc.sbuf_tensor("pm_sb", [128, NE], F32))
    sc1_sb = wo_es.enter_context(nc.sbuf_tensor("sc1_sb", [128, 4], F32))

    v_wr = pre["wr"]

    xsq2 = wo_es.enter_context(nc.sbuf_tensor("xsq2", [128, 2, 512], F32))

    v_spill_tile = [0, 0, 0, 0]
    v_wo_load = [0, 0]
    v_wo_used = [None, None]
    v_xres_used = [None, None]
    rd_x1 = {0: [], 1: []}
    pend_lg = [None]
    v_x1sp_last = 0
    v_logit_evac = 0
    v_nrm_prev = 0
    for tt in range(4):
        t0 = tt * 512
        x1p = x1pre_b[tt % 2]
        sbank, s2bank = pb[6], pb[7]
        v_s2mm = [0] * NCH
        v_s1mm = [0] * NCH
        v_add = [0] * NCH
        v_sq = [0] * NCH

        def emit_smm(oc, x1p=x1p):
            st, sp_ = (oc == 0), (oc == NCH - 1)
            w1 = [("dve", v_add[oc])]
            if st:
                w1 += sbank.wait_free() + CST
            v_s1mm[oc] = sc.emit("pe", lambda e, oc=oc, st=st, sp_=sp_: e.matmul(
                sbank.t[0:1, 0:512], ones128_sb[:, :], x1p[:, oc, :], start=st, stop=sp_),
                w1, inc="pe")
            w2 = [("act", v_sq[oc])]
            if st:
                w2 += s2bank.wait_free()
            v_s2mm[oc] = sc.emit("pe", lambda e, oc=oc, st=st, sp_=sp_: e.matmul(
                s2bank.t[0:1, 0:512], ones128_sb[:, :], xsq2[:, oc % 2, :], start=st, stop=sp_),
                w2, inc="pe")

        corder = [c for m in range(4) for c in range(m, NCH, 4)]
        for oc in range(NCH):
            bi = oc % 2
            if tt == 0 and oc == 0:
                v_wo_load[bi] = pre[0]
            else:
                w = [("pe", v_wo_used[bi])] if v_wo_used[bi] else []
                v_wo_load[bi] = sc.dma(
                    "gp", f"gWo{bi}", wo_sb[bi][:, :, :],
                    woT[oc * 128:(oc + 1) * 128, :].rearrange("p (c o) -> p c o", o=128),
                    waits=w)
            wbank = pb[oc % 2]
            wf = wbank.wait_free() + [(f"gWo{bi}", v_wo_load[bi])]
            vstop = None
            for ci, c in enumerate(corder):
                ww = list(wf) if ci == 0 else []
                ww.append((f"sT{c % 4}", v_tp_h[c % 4]))
                vstop = sc.emit("pe", lambda e, c=c, bi=bi, wbank=wbank, t0=t0, st=(ci == 0), sp_=(ci == NCH - 1): e.matmul(
                    wbank.t[:, 0:512], wo_sb[bi][:, c, :], attn_fm[:, c, t0:t0 + 512],
                    start=st, stop=sp_), ww, inc="pe" if ci == NCH - 1 else None)
            wbank.ready = ("pe", vstop)
            wbank.free = {}
            v_wo_used[bi] = sc.val("pe")
            xbi = oc % 2
            wx = [("dve", v_xres_used[xbi])] if v_xres_used[xbi] else []
            vxr = sc.dma("sp", f"sR{xbi}", xres_sb[xbi][:, :], xres[oc * 128:(oc + 1) * 128, t0:t0 + 512], waits=wx)
            wadd = [wbank.ready, (f"sR{xbi}", vxr)] + rd_x1[tt % 2]
            v_add[oc] = sc.emit("dve", lambda e, oc=oc, xbi=xbi, wbank=wbank, x1p=x1p: e.tensor_add(
                x1p[:, oc, :], wbank.t[:, 0:512], xres_sb[xbi][:, :]), wadd, inc="dve")
            wbank.read_done("dve", v_add[oc])
            v_xres_used[xbi] = v_add[oc]
            wsq = [("dve", v_add[oc])]
            if oc >= 2:
                wsq.append(("pe", v_s2mm[oc - 2]))
            v_sq[oc] = sc.emit("act", lambda e, oc=oc, x1p=x1p: e.activation(
                xsq2[:, oc % 2, :], x1p[:, oc, :], AF.Square), wsq, inc="act")
            if oc >= 1:
                emit_smm(oc - 1)
            if oc == 3 and pend_lg[0] is not None:
                pend_lg[0]()
                pend_lg[0] = None
        emit_smm(NCH - 1)
        rd_x1[tt % 2] = []
        sbank.ready = ("pe", v_s1mm[NCH - 1])
        sbank.free = {}
        s2bank.ready = ("pe", v_s2mm[NCH - 1])
        s2bank.free = {}

        st_ = stat_sb
        va_m = sc.emit("act", lambda e: e.activation(st_[0:1, 0:512], sbank.t[0:1, 0:512],
                                                     AF.Copy, scale=invD[0:1, 0:1]),
                       [sbank.ready] + CST, inc="act")
        va_e2 = sc.emit("act", lambda e: e.activation(st_[0:1, 512:1024], s2bank.t[0:1, 0:512],
                                                      AF.Copy, scale=invD[0:1, 0:1]),
                        [s2bank.ready], inc="act")
        sbank.read_done("act", va_m)
        s2bank.read_done("act", va_e2)
        vd_mm = sc.emit("dve", lambda e: e.tensor_mul(st_[0:1, 1024:1536], st_[0:1, 0:512], st_[0:1, 0:512]),
                        [("act", va_m)], inc="dve")
        vd_var = sc.emit("dve", lambda e: e.tensor_sub(st_[0:1, 1536:2048], st_[0:1, 512:1024], st_[0:1, 1024:1536]),
                         [("act", va_e2), ("dve", vd_mm)], inc="dve")
        va_ln = sc.emit("act", lambda e: e.activation(st_[0:1, 2048:2560], st_[0:1, 1536:2048],
                                                      AF.Ln, bias=epsT[0:1, 0:1]),
                        [("dve", vd_var)], inc="act")
        va_rstd = sc.emit("act", lambda e: e.activation(st_[0:1, 2560:3072], st_[0:1, 2048:2560],
                                                        AF.Exp, scale=half_neg[0:1, 0:1]),
                          [("act", va_ln)], inc="act")
        mbank, rbank = pb[2], pb[3]
        vb_m = _mm_group(sc, mbank, [(mbank.t[:, 0:512], ones1_sb[:, :], st_[0:1, 0:512])],
                         extra_waits=[("act", va_m)] + CST)
        vb_r = _mm_group(sc, rbank, [(rbank.t[:, 0:512], ones1_sb[:, :], st_[0:1, 2560:3072])],
                         extra_waits=[("act", va_rstd)])
        wvem = [mbank.ready] + ([("dve", v_nrm_prev)] if v_nrm_prev else [])
        wver = [rbank.ready] + ([("dve", v_nrm_prev)] if v_nrm_prev else [])
        ve_m = sc.emit("act", lambda e: e.copy(mb_sb[:, :], mbank.t[:, 0:512]), wvem, inc="act")
        ve_r = sc.emit("act", lambda e: e.copy(rb_sb[:, :], rbank.t[:, 0:512]), wver, inc="act")
        mbank.read_done("act", ve_m)
        rbank.read_done("act", ve_r)

        v_nrm = [0] * NCH
        for oc in range(NCH):
            vs = sc.emit("dve", lambda e, oc=oc, x1p=x1p: e.tensor_sub(
                x1p[:, oc, :], x1p[:, oc, :], mb_sb[:, :]),
                [("act", ve_m), ("pe", v_s2mm[NCH - 1]), ("dve", sc.val("dve"))], inc="dve")
            v_nrm[oc] = sc.emit("dve", lambda e, oc=oc, x1p=x1p: e.tensor_mul(
                x1p[:, oc, :], x1p[:, oc, :], rb_sb[:, :]),
                [("act", ve_r), ("dve", vs)], inc="dve")
            v_x1sp_last = sc.dma("sp", "sL", x1sp[tt * 128:(tt + 1) * 128, oc * 512:(oc + 1) * 512],
                                 x1p[:, oc, :], waits=[("dve", v_nrm[oc])])
        rd_x1[tt % 2].append(("sL", v_x1sp_last))
        v_spill_tile[tt] = v_x1sp_last
        v_nrm_prev = v_nrm[NCH - 1]

        def emit_logits(tt=tt, x1p=x1p, v_n=v_nrm[NCH - 1]):
            nonlocal v_logit_evac
            lbank = pb[4]
            for tc in range(4):
                g = tt * 4 + tc
                mms = [(lbank.t[0:128, 0:NE], x1p[:, oc, tc * 128:(tc + 1) * 128], wr_sb[:, oc, :])
                       for oc in range(NCH)]
                _mm_group(sc, lbank, mms,
                          extra_waits=[("dve", v_n), ("gT", v_wr)])
                v_logit_evac = sc.emit("act", lambda e, g=g: e.copy(
                    logits_sb[:, g, :], lbank.t[0:128, 0:NE]), [lbank.ready], inc="act")
                lbank.read_done("act", v_logit_evac)
            rd_x1[tt % 2].append(("pe", sc.val("pe")))
        if tt < 3:
            pend_lg[0] = emit_logits
        else:
            emit_logits()

    tbank = pb[5]
    v_comb_evac = 0
    v_tp_prev = 0
    for g in range(NCH):
        c0 = g * 128
        vm = sc.emit("dve", lambda e, g=g: e.tensor_reduce(
            sc1_sb[:, 0:1], logits_sb[:, g, :], AX.X, ALU.max), [("act", v_logit_evac)], inc="dve")
        wls = [("dve", vm)]
        if v_tp_prev:
            wls.append(("pe", v_tp_prev))
        vls = sc.emit("dve", lambda e, g=g: e.tensor_single_scalar(
            pr_sb[:, :], logits_sb[:, g, :], sc1_sb[:, 0:1], ALU.subtract),
            wls, inc="dve")
        vex = sc.emit("act", lambda e: e.activation(pr_sb[:, :], pr_sb[:, :], AF.Exp),
                      [("dve", vls)], inc="act")
        vsm = sc.emit("dve", lambda e: e.tensor_reduce(
            sc1_sb[:, 1:2], pr_sb[:, :], AX.X, ALU.add), [("act", vex)], inc="dve")
        vrs = sc.emit("dve", lambda e: e.reciprocal(sc1_sb[:, 2:3], sc1_sb[:, 1:2]),
                      [("dve", vsm)], inc="dve")
        vpr = sc.emit("dve", lambda e: e.tensor_single_scalar(
            pr_sb[:, :], pr_sb[:, :], sc1_sb[:, 2:3], ALU.mult), [("dve", vrs)], inc="dve")
        vm1 = sc.emit("dve", lambda e: e.tensor_reduce(
            sc1_sb[:, 0:1], pr_sb[:, :], AX.X, ALU.max), [("dve", vpr)], inc="dve")
        veq = sc.emit("dve", lambda e: e.tensor_single_scalar(
            eq_sb[:, :], pr_sb[:, :], sc1_sb[:, 0:1], ALU.is_ge), [("dve", vm1)], inc="dve")
        vpm = sc.emit("dve", lambda e: e.scalar_tensor_tensor(
            pm_sb[:, :], eq_sb[:, :], -1.0e30, pr_sb[:, :], ALU.mult, ALU.add),
            [("dve", veq)] + CST, inc="dve")
        vm2 = sc.emit("dve", lambda e: e.tensor_reduce(
            sc1_sb[:, 3:4], pm_sb[:, :], AX.X, ALU.max), [("dve", vpm)], inc="dve")
        vge = sc.emit("dve", lambda e: e.tensor_single_scalar(
            eq_sb[:, :], pr_sb[:, :], sc1_sb[:, 3:4], ALU.is_ge), [("dve", vm2)], inc="dve")
        vcb = sc.emit("dve", lambda e: e.tensor_mul(pr_sb[:, :], pr_sb[:, :], eq_sb[:, :]),
                      [("dve", vge)], inc="dve")
        wtp = tbank.wait_free() + [("dve", vcb)]
        vt = sc.emit("pe", lambda e: e.matmul(
            tbank.t[0:NE, 0:128], pr_sb[:, :], ident_sb[:, :],
            start=True, stop=True), wtp, inc="pe")
        tbank.ready = ("pe", vt)
        tbank.free = {}
        v_tp_prev = vt
        v_comb_evac = sc.emit("act", lambda e, c0=c0: e.copy(
            comb_sb[:, c0:c0 + 128], tbank.t[0:NE, 0:128]), [tbank.ready], inc="act")
        tbank.read_done("act", v_comb_evac)

    wo_es.close()
    wo_outer.close()
    sc.barrier()

    ffn_es = ExitStack()
    sel8_sb = ffn_es.enter_context(nc.sbuf_tensor("sel8_sb", [8, 8 * 128], F32))
    v_sel8 = sc.dma("gp", "gT", sel8_sb[:, :], sel8[:, :])
    y_acc = ffn_es.enter_context(nc.sbuf_tensor("y_acc", [128, NCH, 1024], F32))
    x1bh = ffn_es.enter_context(nc.sbuf_tensor("x1bh", [128, NCH, 1024], BF16))
    h_sb = ffn_es.enter_context(nc.sbuf_tensor("h_sb", [128, NDE, 1024], BF16))
    wg_sb = [ffn_es.enter_context(nc.sbuf_tensor(f"wg{i}", [128, NCH, 128], BF16)) for i in range(2)]
    wu_sb = [ffn_es.enter_context(nc.sbuf_tensor(f"wu{i}", [128, NCH, 128], BF16)) for i in range(2)]
    wd_sb = ffn_es.enter_context(nc.sbuf_tensor("wd_sb", [128, 6, NCH, 128], BF16))
    sg_sb = ffn_es.enter_context(nc.sbuf_tensor("sg_sb", [128, 2, 512], F32))
    cb_sb = ffn_es.enter_context(nc.sbuf_tensor("cb_sb", [128, 1024], F32))
    xq2 = ffn_es.enter_context(nc.sbuf_tensor("xq2", [128, 2, 512], F32))
    st2 = ffn_es.enter_context(nc.sbuf_tensor("st2", [1, 3 * 512], F32))
    mb2 = ffn_es.enter_context(nc.sbuf_tensor("mb2", [128, 512], F32))
    rb2 = ffn_es.enter_context(nc.sbuf_tensor("rb2", [128, 512], F32))
    x2t = [ffn_es.enter_context(nc.sbuf_tensor(f"x2t{i}", [128, 512], F32)) for i in range(2)]

    stages = [(wsg, wsu, wsd, 0, 0, 0, None), (wsg, wsu, wsd, 11, 0, 0, None)] + \
             [(weg, weu, wed, 0, e * NDE, e * DE, e) for e in range(NE)]
    JGRP = [(0, 6), (6, 11)]
    ffn_pre = {}

    v_y_read_prev = []
    v_x1b_pe_prev = 0
    v_out_last = 0
    v_store = [0, 0]
    v_prev_s2 = 0
    v_prev_norm = 0
    for hf in range(2):
        t0h = hf * 1024
        v_x1b = [0, 0]
        v_ya0 = None
        for ts in range(2):
            g = hf * 2 + ts
            wldx = [("sL", v_spill_tile[g])]
            if v_x1b_pe_prev:
                wldx.append(("pe", v_x1b_pe_prev))
            v_x1b[ts] = sc.dma(
                "gp", "gY", x1bh[:, :, ts * 512:(ts + 1) * 512],
                x1sp[g * 128:(g + 1) * 128, :].rearrange("p (c t) -> p c t", t=512), waits=wldx)
            if hf == 0 and ts == 0:
                for j_ in range(2):
                    ffn_pre[(j_, "g")] = sc.dma(
                        "gp", f"gG{j_}", wg_sb[j_][:, :, :],
                        wsg[j_ * 128:(j_ + 1) * 128, :].rearrange("p (c o) -> p c o", o=128))
                    ffn_pre[(j_, "u")] = sc.dma(
                        "gp", f"gG{j_}", wu_sb[j_][:, :, :],
                        wsu[j_ * 128:(j_ + 1) * 128, :].rearrange("p (c o) -> p c o", o=128))
            wldy = [("sL", v_spill_tile[g])] + v_y_read_prev
            v_ya0 = sc.dma("sp", "sY", y_acc[:, :, ts * 512:(ts + 1) * 512],
                           x1sp[g * 128:(g + 1) * 128, :].rearrange("p (c t) -> p c t", t=512), waits=wldy)

        v_h_read_pe = sc.val("pe")
        v_cb_read = sc.val("dve")
        v_wd_used = sc.val("pe")
        v_wg_used = [sc.val("pe"), sc.val("pe")]
        v_sg_used = [sc.val("dve"), sc.val("dve")]
        v_ya_last = sc.val("dve")
        v_ya_chunk = {}
        for snum, (wgt, wut, wdt, co, rg_, rd_, ex) in enumerate(stages):
            if ex is not None:
                cbank = pb[6]
                v_cb = 0
                for ts in range(2):
                    _mm_group(sc, cbank, [(cbank.t[:, 0:512], sel8_sb[:, ex * 128:(ex + 1) * 128],
                                           comb_sb[:, t0h + ts * 512:t0h + (ts + 1) * 512])],
                              extra_waits=[("act", v_comb_evac), ("gT", v_sel8)])
                    wcb = [cbank.ready, ("dve", v_cb_read)]
                    v_cb = sc.emit("act", lambda e, ts=ts, cbank=cbank: e.copy(
                        cb_sb[:, ts * 512:(ts + 1) * 512], cbank.t[:, 0:512]), wcb, inc="act")
                    cbank.read_done("act", v_cb)

            for g0, g1 in JGRP:
                v_h_w = 0
                for j in range(g0, g1):
                    jp = j % 2
                    rb = (rg_ + co + j) * 128
                    if hf == 0 and snum == 0 and j < 2:
                        vwg = ffn_pre[(j, "g")]
                        vwu = ffn_pre[(j, "u")]
                    else:
                        wwl = [("pe", v_wg_used[jp])]
                        vwg = sc.dma("gp", f"gG{jp}", wg_sb[jp][:, :, :],
                                     wgt[rb:rb + 128, :].rearrange("p (c o) -> p c o", o=128),
                                     waits=wwl)
                        vwu = sc.dma("gp", f"gG{jp}", wu_sb[jp][:, :, :],
                                     wut[rb:rb + 128, :].rearrange("p (c o) -> p c o", o=128))
                    for ts in range(2):
                        gbank, ubank = pb[0 + ts], pb[2 + ts]
                        wg_ = [(f"gG{jp}", vwu), ("gY", v_x1b[ts])]
                        _mm_group(sc, gbank, [(gbank.t[:, 0:512], wg_sb[jp][:, c, :],
                                               x1bh[:, c, ts * 512:(ts + 1) * 512]) for c in range(NCH)],
                                  extra_waits=wg_)
                        _mm_group(sc, ubank, [(ubank.t[:, 0:512], wu_sb[jp][:, c, :],
                                               x1bh[:, c, ts * 512:(ts + 1) * 512]) for c in range(NCH)])
                        v_wg_used[jp] = sc.val("pe")
                        wsl = [gbank.ready, ("dve", v_sg_used[ts])]
                        vsl = sc.emit("act", lambda e, ts=ts, gbank=gbank: e.activation(
                            sg_sb[:, ts, :], gbank.t[:, 0:512], AF.Silu), wsl, inc="act")
                        gbank.read_done("act", vsl)
                        whh = [("act", vsl), ubank.ready, ("pe", v_h_read_pe)]
                        if ex is not None:
                            vcbf = sc.emit("dve", lambda e, ts=ts: e.tensor_mul(
                                sg_sb[:, ts, :], sg_sb[:, ts, :], cb_sb[:, ts * 512:(ts + 1) * 512]),
                                [("act", vsl), ("act", v_cb), ("dve", v_sg_used[ts])], inc="dve")
                            whh.append(("dve", vcbf))
                        vh = sc.emit("dve", lambda e, j=j, ts=ts, ubank=ubank: e.tensor_mul(
                            h_sb[:, j, ts * 512:(ts + 1) * 512], sg_sb[:, ts, :], ubank.t[:, 0:512]),
                            whh, inc="dve")
                        ubank.read_done("dve", vh)
                        v_sg_used[ts] = vh
                        v_h_w = vh
                v_wd = 0
                for j in range(g0, g1):
                    wwd = [("pe", v_wd_used)] if j == g0 else []
                    v_wd = sc.dma("gp", "gD", wd_sb[:, j - g0, :, :],
                                  wdt[rd_ + (co + j) * 128:rd_ + (co + j + 1) * 128, :].rearrange("p (c o) -> p c o", o=128),
                                  waits=wwd)
                for ts in range(2):
                    for oc in range(NCH):
                        ybank = (pb[4], pb[5], pb[7])[(ts * NCH + oc) % 3]
                        wdn = [("dve", v_h_w), ("gD", v_wd)]
                        _mm_group(sc, ybank, [(ybank.t[:, 0:512], wd_sb[:, j2 - g0, oc, :],
                                               h_sb[:, j2, ts * 512:(ts + 1) * 512]) for j2 in range(g0, g1)],
                                  extra_waits=wdn)
                        v_ya_last = sc.emit("dve", lambda e, oc=oc, ts=ts, ybank=ybank: e.tensor_add(
                            y_acc[:, oc, ts * 512:(ts + 1) * 512], ybank.t[:, 0:512],
                            y_acc[:, oc, ts * 512:(ts + 1) * 512]),
                            [ybank.ready, ("dve", v_ya_chunk.get((oc, ts), 0)), ("sY", v_ya0)], inc="dve")
                        v_ya_chunk[(oc, ts)] = v_ya_last
                        ybank.read_done("dve", v_ya_last)
                v_wd_used = sc.val("pe")
            v_h_read_pe = sc.val("pe")
            v_cb_read = sc.val("dve")
        v_x1b_pe_prev = sc.val("pe")

        v_y_read_prev = []
        for ts in range(2):
            sbank, s2bank = pb[6], pb[7]
            v_sq2 = [0] * NCH
            v_m1 = [0] * NCH
            v_m2 = [0] * NCH

            for oc in range(NCH):
                wsq = [("dve", v_ya_last), ("pe", v_m2[oc - 2] if oc >= 2 else v_prev_s2)]
                v_sq2[oc] = sc.emit("act", lambda e, oc=oc, ts=ts: e.activation(
                    xq2[:, oc % 2, :], y_acc[:, oc, ts * 512:(ts + 1) * 512], AF.Square),
                    wsq, inc="act")
                w1 = [("dve", v_ya_last)]
                if oc == 0:
                    w1 += sbank.wait_free() + CST
                v_m1[oc] = sc.emit("pe", lambda e, oc=oc, ts=ts, st=(oc == 0), sp_=(oc == NCH - 1): e.matmul(
                    sbank.t[0:1, 0:512], ones128_sb[:, :], y_acc[:, oc, ts * 512:(ts + 1) * 512],
                    start=st, stop=sp_), w1, inc="pe")
                w2 = [("act", v_sq2[oc])]
                if oc == 0:
                    w2 += s2bank.wait_free()
                v_m2[oc] = sc.emit("pe", lambda e, oc=oc, st=(oc == 0), sp_=(oc == NCH - 1): e.matmul(
                    s2bank.t[0:1, 0:512], ones128_sb[:, :], xq2[:, oc % 2, :],
                    start=st, stop=sp_), w2, inc="pe")
            sbank.ready = ("pe", v_m1[NCH - 1])
            sbank.free = {}
            s2bank.ready = ("pe", v_m2[NCH - 1])
            s2bank.free = {}
            v_prev_s2 = v_m2[NCH - 1]

            va_m = sc.emit("act", lambda e: e.activation(st2[0:1, 0:512], sbank.t[0:1, 0:512],
                                                         AF.Copy, scale=invD[0:1, 0:1]),
                           [sbank.ready], inc="act")
            va_e2 = sc.emit("act", lambda e: e.activation(st2[0:1, 512:1024], s2bank.t[0:1, 0:512],
                                                          AF.Copy, scale=invD[0:1, 0:1]),
                            [s2bank.ready], inc="act")
            sbank.read_done("act", va_m)
            s2bank.read_done("act", va_e2)
            vd_mm = sc.emit("dve", lambda e: e.tensor_mul(st2[0:1, 1024:1536], st2[0:1, 0:512], st2[0:1, 0:512]),
                            [("act", va_m), ("dve", sc.val("dve"))], inc="dve")
            vd_var = sc.emit("dve", lambda e: e.tensor_sub(st2[0:1, 512:1024], st2[0:1, 512:1024], st2[0:1, 1024:1536]),
                             [("act", va_e2), ("dve", vd_mm)], inc="dve")
            va_ln = sc.emit("act", lambda e: e.activation(st2[0:1, 1024:1536], st2[0:1, 512:1024],
                                                          AF.Ln, bias=epsT[0:1, 0:1]),
                            [("dve", vd_var)], inc="act")
            va_rstd = sc.emit("act", lambda e: e.activation(st2[0:1, 512:1024], st2[0:1, 1024:1536],
                                                            AF.Exp, scale=half_neg[0:1, 0:1]),
                              [("act", va_ln)], inc="act")
            mbank, rbank = pb[2], pb[3]
            _mm_group(sc, mbank, [(mbank.t[:, 0:512], ones1_sb[:, :], st2[0:1, 0:512])],
                      extra_waits=[("act", va_m)])
            _mm_group(sc, rbank, [(rbank.t[:, 0:512], ones1_sb[:, :], st2[0:1, 512:1024])],
                      extra_waits=[("act", va_rstd)])
            ve_m = sc.emit("act", lambda e: e.copy(mb2[:, :], mbank.t[:, 0:512]),
                           [mbank.ready, ("dve", v_prev_norm)], inc="act")
            ve_r = sc.emit("act", lambda e: e.copy(rb2[:, :], rbank.t[:, 0:512]),
                           [rbank.ready, ("dve", v_prev_norm)], inc="act")
            mbank.read_done("act", ve_m)
            rbank.read_done("act", ve_r)

            for oc in range(NCH):
                op = oc % 2
                wn = [("act", ve_m), ("act", ve_r), ("dve", sc.val("dve"))]
                if v_store[op]:
                    wn.append((f"sO{op}", v_store[op]))
                vs = sc.emit("dve", lambda e, oc=oc, ts=ts, op=op: e.tensor_sub(
                    x2t[op][:, :], y_acc[:, oc, ts * 512:(ts + 1) * 512], mb2[:, :]), wn, inc="dve")
                vn = sc.emit("dve", lambda e, op=op: e.tensor_mul(
                    x2t[op][:, :], x2t[op][:, :], rb2[:, :]), [("dve", vs)], inc="dve")
                v_out_last = sc.dma("sp", f"sO{op}", outfm[oc * 128:(oc + 1) * 128, t0h + ts * 512:t0h + (ts + 1) * 512],
                                    x2t[op][:, :], waits=[("dve", vn)])
                v_store[op] = v_out_last
                v_y_read_prev.append(("dve", vn))
                v_prev_norm = vn
            v_y_read_prev.append(("pe", sc.val("pe")))
            v_y_read_prev.append(("act", v_sq2[NCH - 1]))

    ffn_es.close()
    _finish(nc, sc, sems, es, [("sO0", sc.val("sO0")), ("sO1", sc.val("sO1"))])
    return nc


def _finish(nc, sc, sems, es, final_waits):
    sc.emit("sp", lambda e: e.nop(), waits=list(final_waits))
    for s in SEMS:
        assert sc.cnt.get(s, 0) < 60000, (s, sc.cnt[s])
    with nc.Block() as block:
        def runner(name):
            def _r(eng):
                hw = {}
                for waits, fn, inc, amt in sc.q[name]:
                    for sem, v in waits:
                        if v and v > hw.get(sem, 0):
                            eng.wait_ge(sems[sem], v)
                            hw[sem] = v
                    ins = fn(eng)
                    if inc is not None:
                        ins.then_inc(sems[inc], amt)
            return _r
        block.tensor(runner("pe"))
        block.scalar(runner("act"))
        block.vector(runner("dve"))
        block.sync(runner("sp"))
        block.gpsimd(runner("gp"))
    es.close()


def _rope_tables():
    idx = np.arange(0, DH, 2, dtype=np.float32)
    inv_freq = (np.float32(10000.0) ** (-idx / np.float32(DH))).astype(np.float32)
    angles = (np.arange(S, dtype=np.float32)[:, None] * inv_freq[None, :]).astype(np.float32)
    cos = np.cos(angles).astype(np.float32)
    sin = np.sin(angles).astype(np.float32)
    cos_e = np.concatenate([cos.T, cos.T], axis=0)
    sin_s = np.concatenate([-sin.T, sin.T], axis=0)
    scl = np.float32(1.0 / np.sqrt(np.float32(DH)))
    tabs = np.stack([cos_e * scl, sin_s * scl, cos_e, sin_s], axis=1)
    pk = tabs.reshape(128, 4, NTT, 512).transpose(0, 2, 1, 3).reshape(128, NTT * 4 * 512)
    return np.ascontiguousarray(pk.astype(ml_dtypes.bfloat16))


def _tile_w(w, nj, od):
    return np.ascontiguousarray(
        np.asarray(w, np.float32).reshape(NCH, 128, nj, od)
        .transpose(2, 1, 0, 3).reshape(nj * 128, NCH * od))


def _stage(inputs):
    x = np.asarray(inputs["L0_attn_x"], np.float32)
    wq = np.asarray(inputs["L0_attn_Wq"], np.float32)
    wk = np.asarray(inputs["L0_attn_Wk"], np.float32)
    wv = np.asarray(inputs["L0_attn_Wv"], np.float32)
    wo = np.asarray(inputs["L0_attn_Wo"], np.float32)

    perm = np.concatenate([np.arange(0, DH, 2), np.arange(1, DH, 2)])
    wqT, wkT, wvT = wq.T, wk.T, wv.T
    tbpk = _rope_tables()

    tri = np.where(np.arange(128)[:, None] <= np.arange(128)[None, :], 0.0, MASKVAL).astype(np.float32)
    identm = np.eye(128, dtype=np.float32)
    swpm = np.zeros((128, 128), dtype=np.float32)
    for i in range(128):
        swpm[(i + 64) % 128, i] = 1.0
    sel8 = np.zeros((8, 8 * 128), dtype=np.float32)
    for e in range(8):
        sel8[e, e * 128:(e + 1) * 128] = 1.0
    ones1 = np.ones((1, 128), dtype=np.float32)

    weg = np.asarray(inputs["L0_ffn_We_gate"], np.float32)
    weu = np.asarray(inputs["L0_ffn_We_up"], np.float32)
    shared = {
        "woT": _tile_w(wo.T, NCH, 128), "tbpk": tbpk,
        "tri": tri, "ident": identm, "swp": swpm, "sel8": sel8, "ones1": ones1,
        "wrouter": _tile_w(inputs["L0_ffn_W_router"], 1, NE),
        "wsg": _tile_w(inputs["L0_ffn_Ws_gate"], NDS, 128),
        "wsu": _tile_w(inputs["L0_ffn_Ws_up"], NDS, 128),
        "wsd": np.ascontiguousarray(inputs["L0_ffn_Ws_down"], dtype=np.float32),
        "weg": np.ascontiguousarray(np.concatenate(
            [_tile_w(weg[e], NDE, 128) for e in range(NE)], axis=0)),
        "weu": np.ascontiguousarray(np.concatenate(
            [_tile_w(weu[e], NDE, 128) for e in range(NE)], axis=0)),
        "wed": np.ascontiguousarray(np.asarray(inputs["L0_ffn_We_down"], np.float32).reshape(NE * DE, D)),
    }

    xTs = [np.ascontiguousarray(x[b].T) for b in range(B)]
    xTts = [_tile_w(xT, NTT, 512) for xT in xTs]
    in_maps = []
    for c in range(8):
        b, r = c // 4, c % 4
        heads = range(4 * r, 4 * r + 4)
        m = dict(shared)
        m["xTt"] = xTts[b]
        m["xres"] = np.ascontiguousarray(xTs[b][:, r * T:(r + 1) * T])
        m["wqT"] = _tile_w(np.concatenate(
            [wqT[:, h * DH:(h + 1) * DH][:, perm] for h in heads], axis=1), HPC, DH)
        m["wkT"] = _tile_w(np.concatenate(
            [wkT[:, h * DH:(h + 1) * DH][:, perm] for h in heads], axis=1), HPC, DH)
        m["wvT"] = _tile_w(np.concatenate(
            [wvT[:, h * DH:(h + 1) * DH] for h in heads], axis=1), HPC, DH)
        m["roff"] = np.array([[(r // 2) * 2 * S * DH + (r % 2) * T * DH]], dtype=np.int32)
        m["roff3"] = np.array([[r * 4 * T * DH]], dtype=np.int32)
        in_maps.append(m)
    return in_maps


def kernel(**inputs):
    nc = _build("full")
    in_maps = _stage(inputs)
    res = run_bass_kernel_spmd(nc, in_maps, core_ids=list(range(8)))
    out = np.empty((B, S, D), dtype=np.float32)
    for c in range(8):
        b, r = c // 4, c % 4
        out[b, r * T:(r + 1) * T, :] = np.asarray(res.results[c]["outfm"]).T
    return out


# version 09
# speedup vs baseline: 1.1853x; 1.0096x over previous
import numpy as np
import ml_dtypes
from contextlib import ExitStack

import concourse.bass as bass
import concourse.mybir as mybir
from concourse.bass_utils import run_bass_kernel_spmd

F32 = mybir.dt.float32
BF16 = mybir.dt.bfloat16
I32 = mybir.dt.int32
AF = mybir.ActivationFunctionType
ALU = mybir.AluOpType
AX = mybir.AxisListType

B, S, D, H, DH = 2, 8192, 2048, 16, 128
HPC = 4
T = 2048
NCH = D // 128
DS, NDS = 2816, 22
DE, NDE = 1408, 11
NE = 8
NTT = S // 512
CSHIFT = -12.0
MASKVAL = -60.0
EPS = 1e-5

ENGS = ("pe", "act", "dve", "sp", "gp")
SEMS = ("pe", "act", "dve", "gpc", "cc",
        "gT", "gX0", "gX1", "gW0", "gW1", "gWo0", "gWo1", "gY",
        "gG0", "gG1", "gD", "gP0", "gP1", "gB0", "gB1", "sXC", "sY",
        "sS0", "sS1", "sR0", "sR1", "sL", "sT0", "sT1", "sT2", "sT3", "sO0", "sO1")


class _Sched:

    def __init__(self):
        self.q = {e: [] for e in ENGS}
        self.cnt = {s: 0 for s in SEMS}
        self.pending = {e: [] for e in ENGS}

    def val(self, sem):
        return self.cnt[sem]

    SELF_SERIAL = ()

    def emit(self, eng, fn, waits=(), inc=None, amt=1):
        w = list(self.pending[eng]) + list(waits)
        self.pending[eng] = []
        if eng in self.SELF_SERIAL:
            if self.cnt[eng]:
                w.append((eng, self.cnt[eng]))
            if inc is None:
                inc = eng
        v = None
        if inc is not None:
            self.cnt[inc] += amt
            v = self.cnt[inc]
        self.q[eng].append((tuple(w), fn, inc, amt))
        return v

    def dma(self, eng, sem, out_ap, in_ap, waits=()):
        return self.emit(eng, lambda e, o=out_ap, i=in_ap: e.dma_start(o, i),
                         waits, inc=sem, amt=16)

    def barrier(self, exclude=()):
        for e in ENGS:
            if not self.q[e]:
                continue
            w, fn, inc, amt = self.q[e][-1]
            if inc is None:
                sem = e if e in ("pe", "act", "dve") else "gpc"
                assert e != "sp", "sp barrier op must carry an inc"
                self.cnt[sem] = self.cnt.get(sem, 0) + 1
                self.q[e][-1] = (w, fn, sem, 1)
        marks = [(s, self.cnt[s]) for s in SEMS
                 if self.cnt.get(s, 0) > 0 and s not in exclude]
        for e in ENGS:
            self.pending[e] = list(marks)


class _PB:

    def __init__(self, tile):
        self.t = tile
        self.ready = None
        self.free = {}

    def wait_free(self):
        return list(self.free.items())

    def read_done(self, sem, val):
        self.free[sem] = max(self.free.get(sem, 0), val)


def _mm_group(sc, pb, mms, extra_waits=(), lookahead_ok=True):
    waits = pb.wait_free() + list(extra_waits)
    n = len(mms)
    v = None
    for i, (o, l, r) in enumerate(mms):
        st, sp_ = (i == 0), (i == n - 1)
        v = sc.emit(
            "pe",
            lambda e, o=o, l=l, r=r, st=st, sp_=sp_: e.matmul(o, l, r, start=st, stop=sp_),
            waits if i == 0 else (),
            inc="pe" if sp_ else None,
        )
    pb.ready = ("pe", v)
    pb.free = {}
    return v


def _build(upto="full"):
    nc = bass.Bass("TRN2", target_bir_lowering=False)
    sc = _Sched()

    dt_in = {}
    def din(name, shape, dt=F32):
        dt_in[name] = nc.dram_tensor(name, list(shape), dt, kind="ExternalInput")
        return dt_in[name]

    xTt = din("xTt", (NTT * 128, NCH * 512))
    xres = din("xres", (D, T))
    wqT = din("wqT", (HPC * 128, NCH * DH))
    wkT = din("wkT", (HPC * 128, NCH * DH))
    wvT = din("wvT", (HPC * 128, NCH * DH))
    woT = din("woT", (NCH * 128, NCH * 128))
    tbpk = din("tbpk", (128, NTT * 4 * 512), BF16)
    tri = din("tri", (128, 128), BF16)
    sel8 = din("sel8", (8, 8 * 128))
    ident = din("ident", (128, 128))
    identb = din("identb", (128, 128), BF16)
    swp = din("swp", (128, 128))
    ones1 = din("ones1", (1, 128))
    roff = din("roff", (1, 1), I32)
    roff3 = din("roff3", (1, 1), I32)
    wrouter = din("wrouter", (128, NCH * NE))
    wsg = din("wsg", (NDS * 128, NCH * 128))
    wsu = din("wsu", (NDS * 128, NCH * 128))
    wsd = din("wsd", (DS, D))
    weg = din("weg", (NE * NDE * 128, NCH * 128))
    weu = din("weu", (NE * NDE * 128, NCH * 128))
    wed = din("wed", (NE * DE, D))
    xbf = nc.dram_tensor("xbf", [NTT * 128, NCH * 512], BF16, kind="Internal")

    outfm = nc.dram_tensor("outfm", [D, T], F32, kind="ExternalOutput")

    cc_in_h = [nc.dram_tensor(f"cc_in{h}", [S, DH], BF16, kind="Internal") for h in range(HPC)]
    cc_ag_h = [nc.dram_tensor(f"cc_ag{h}", [4 * S, DH], BF16, kind="Internal") for h in range(HPC)]
    cc_my = nc.dram_tensor("cc_my", [T, D], BF16, kind="Internal")
    x1sp = nc.dram_tensor("x1sp", [4 * 128, NCH * 512], F32, kind="Internal")

    assert upto == "full"

    es = ExitStack()
    sems = {s: es.enter_context(nc.semaphore(s)) for s in SEMS}

    pb = [_PB(es.enter_context(nc.psum_tensor(f"pb{i}", [128, 512], F32))) for i in range(8)]

    ones1_sb = es.enter_context(nc.sbuf_tensor("ones1_sb", [1, 128], F32))
    ones128_sb = es.enter_context(nc.sbuf_tensor("ones128_sb", [128, 1], F32))
    biasC = es.enter_context(nc.sbuf_tensor("biasC", [128, 1], F32))
    epsT = es.enter_context(nc.sbuf_tensor("epsT", [1, 1], F32))
    half_neg = es.enter_context(nc.sbuf_tensor("half_neg", [1, 1], F32))
    neginf8 = es.enter_context(nc.sbuf_tensor("neginf8", [128, 8], F32))
    invD = es.enter_context(nc.sbuf_tensor("invD", [1, 1], F32))
    tri_sb = es.enter_context(nc.sbuf_tensor("tri_sb", [128, 128], BF16))
    identb_sb = es.enter_context(nc.sbuf_tensor("identb_sb", [128, 128], BF16))
    ident_sb = es.enter_context(nc.sbuf_tensor("ident_sb", [128, 128], F32))
    swp_sb = es.enter_context(nc.sbuf_tensor("swp_sb", [128, 128], BF16))
    roff_sb = es.enter_context(nc.sbuf_tensor("roff_sb", [1, 1], I32))
    roff3_sb = es.enter_context(nc.sbuf_tensor("roff3_sb", [1, 1], I32))
    comb_sb = es.enter_context(nc.sbuf_tensor("comb_sb", [8, T], F32))
    rg = nc.gpsimd.alloc_register("rg_off")
    rg2 = nc.gpsimd.alloc_register("rg_off2")
    rg3 = nc.gpsimd.alloc_register("rg_off3")

    sc.dma("gp", "gT", ones1_sb[:, :], ones1[:, :])
    sc.dma("gp", "gT", tri_sb[:, :], tri[:, :])
    sc.dma("gp", "gT", ident_sb[:, :], ident[:, :])
    sc.dma("gp", "gT", identb_sb[:, :], identb[:, :])
    sc.dma("gp", "gT", swp_sb[:, :], swp[:, :])
    sc.dma("gp", "gT", roff3_sb[:, :], roff3[:, :])
    v_cst_dma = sc.dma("gp", "gT", roff_sb[:, :], roff[:, :])
    v_tri = v_cst_dma
    sc.emit("gp", lambda e: e.memset(ones128_sb[:, :], 1.0), inc="gpc")
    sc.emit("gp", lambda e: e.memset(biasC[:, :], CSHIFT), inc="gpc")
    sc.emit("gp", lambda e: e.memset(epsT[:, :], EPS), inc="gpc")
    sc.emit("gp", lambda e: e.memset(half_neg[:, :], -0.5), inc="gpc")
    sc.emit("gp", lambda e: e.memset(neginf8[:, :], -1.0e30), inc="gpc")
    v_cst = sc.emit("gp", lambda e: e.memset(invD[:, :], 1.0 / D), inc="gpc")
    CST = [("gT", v_cst_dma), ("gpc", v_cst)]

    wo_outer = ExitStack()
    attn_fm = wo_outer.enter_context(nc.sbuf_tensor("attn_fm", [128, NCH, T], BF16))
    wo_sb0 = wo_outer.enter_context(nc.sbuf_tensor("wo0", [128, NCH, 128], BF16))
    wr_sb = wo_outer.enter_context(nc.sbuf_tensor("wr_sb", [128, NCH, NE], F32))
    pre = {}
    attn_es = ExitStack()
    tb_sb = [attn_es.enter_context(nc.sbuf_tensor(f"tb{i}", [128, 4, 512], BF16)) for i in range(2)]
    TAB = []

    v_tb_load = [0, 0]
    v_tb_used = [0, 0]
    qbf = attn_es.enter_context(nc.sbuf_tensor("qbf", [128, S], BF16))
    kbf = attn_es.enter_context(nc.sbuf_tensor("kbf", [128, S], BF16))
    vsb = attn_es.enter_context(nc.sbuf_tensor("vsb", [128, 64, 132], BF16))
    xt_sb = [attn_es.enter_context(nc.sbuf_tensor(f"xt{i}", [128, NCH, 512], BF16)) for i in range(2)]
    wq_sb = [attn_es.enter_context(nc.sbuf_tensor(f"wq_sb{i}", [128, NCH, DH], BF16)) for i in range(2)]
    wk_sb = [attn_es.enter_context(nc.sbuf_tensor(f"wk_sb{i}", [128, NCH, DH], BF16)) for i in range(2)]
    wv_sb = [attn_es.enter_context(nc.sbuf_tensor(f"wv_sb{i}", [128, NCH, DH], BF16)) for i in range(2)]
    qtmp = attn_es.enter_context(nc.sbuf_tensor("qtmp", [128, 512], BF16))
    qsw = attn_es.enter_context(nc.sbuf_tensor("qsw", [128, 512], BF16))
    ktmp = attn_es.enter_context(nc.sbuf_tensor("ktmp", [128, 512], BF16))
    ksw = attn_es.enter_context(nc.sbuf_tensor("ksw", [128, 512], BF16))
    rope_t1 = attn_es.enter_context(nc.sbuf_tensor("rope_t1", [128, 512], BF16))
    rope_t2 = attn_es.enter_context(nc.sbuf_tensor("rope_t2", [128, 512], BF16))
    probs = [attn_es.enter_context(nc.sbuf_tensor(f"probs{i}", [128, 512], BF16)) for i in range(4)]
    ao_sb = [attn_es.enter_context(nc.sbuf_tensor(f"ao{i}", [128, 4, DH], BF16)) for i in range(2)]
    rc_sb = attn_es.enter_context(nc.sbuf_tensor("rc_sb", [128, 4], F32))

    v_xt_load = [0, 0]
    v_xt_used = [None, None]
    v_ao_dma = [0, 0]

    last_v1 = {}
    last_v2 = {}
    last_v3 = [0]
    v_rc_read = [0, 0, 0, 0]

    v_proj_end = [0, 0]
    v_w_loaded = [0, 0]
    v_pick_h = [0] * HPC
    v_tp_h = [0] * HPC
    v_xc = [0] * NTT
    v_xc_par = [0, 0]
    rg_loaded = [False]

    def emit_ag(h, piece, npieces=2):
        PS = S // npieces
        sc.emit("gp", lambda e, h=h, piece=piece, PS=PS: e.collective_compute(
            "AllGather", ALU.bypass, replica_groups=[[0, 1, 2, 3], [4, 5, 6, 7]],
            ins=[cc_in_h[h][piece * PS:(piece + 1) * PS, :]],
            outs=[cc_ag_h[h][piece * 4 * PS:(piece + 1) * 4 * PS, :]]),
            waits=[("sS0", sc.val("sS0")), ("sS1", sc.val("sS1"))], inc="cc")

    def emit_picks(h):
        if not rg_loaded[0]:
            sc.emit("gp", lambda e: e.reg_load(rg, roff_sb[0:1, 0:1]),
                    waits=[("gT", v_cst_dma)])
            sc.emit("gp", lambda e: e.reg_load(rg3, roff3_sb[0:1, 0:1]))
            rg_loaded[0] = True
        ccw = 2 * (h + 1) if h < HPC - 1 else 2 * HPC + 2
        base, stride = (rg, (S // 2) * DH) if h < HPC - 1 else (rg3, T * DH)
        v = None
        for j in range(4):
            w0 = [("cc", ccw)] if j == 0 else []
            if j == 0 and h >= 2:
                w0.append((f"sT{h - 2}", v_tp_h[h - 2]))
            sc.emit("gp", lambda e, j=j, base=base, stride=stride: e.reg_add(rg2, base, j * stride), waits=w0)
            v = sc.emit(
                "gp", lambda e, j=j, h=h: e.dma_start(
                    bass.AP(cc_my, j * (HPC * DH) + h * DH, [[D, T], [1, DH]]),
                    bass.AP(cc_ag_h[h], rg2, [[DH, T], [1, DH]])),
                inc=f"gP{h % 2}", amt=16)
        v_pick_h[h] = v

    def emit_tp(h):
        v = None
        for i, fc in enumerate(range(h, NCH, 4)):
            v = sc.emit("sp", lambda e, fc=fc: e.dma_start(
                attn_fm[:, fc, :], cc_my[:, fc * 128:(fc + 1) * 128], transpose=True),
                waits=[(f"gP{h % 2}", v_pick_h[h])] if i == 0 else (), inc=f"sT{h}", amt=16)
        v_tp_h[h] = v
    for hh in range(HPC + 1):
        if hh < HPC:
            hp_ = hh % 2
            vw = None
            for wsb, wt in ((wq_sb[hp_], wqT), (wk_sb[hp_], wkT), (wv_sb[hp_], wvT)):
                vw = sc.dma("gp", f"gW{hp_}", wsb[:, :, :],
                            wt[hh * 128:(hh + 1) * 128, :].rearrange("p (c o) -> p c o", o=DH),
                            waits=[("pe", v_proj_end[hp_])] if v_proj_end[hp_] else [])
            v_w_loaded[hp_] = vw
        if hh == 0:
            continue
        h = hh - 1
        hp = h % 2
        v_prev_pe = sc.val("pe")
        W = [(f"gW{hp}", v_w_loaded[hp])]

        vone = sc.emit("gp", lambda e: e.memset(vsb[:, :, 128:132], 1.0),
                       waits=[("pe", sc.val("pe"))], inc="gpc")

        v_q_done = [0] * NTT
        v_k_done = [0] * NTT
        v_v_done = [0] * NTT
        for tt in range(NTT):
            bi = tt % 2
            t0 = tt * 512
            w = [("pe", v_xt_used[bi])] if v_xt_used[bi] else []
            if h == 0:
                if v_xc_par[bi]:
                    w.append(("sXC", v_xc_par[bi]))
                v_xt_load[bi] = sc.dma(
                    "gp", f"gX{bi}", xt_sb[bi][:, :, :],
                    xTt[tt * 128:(tt + 1) * 128, :].rearrange("p (c t) -> p c t", t=512),
                    waits=w)
                v_xc[tt] = sc.dma(
                    "sp", "sXC", xbf[tt * 128:(tt + 1) * 128, :],
                    xt_sb[bi][:, :, :].rearrange("p c t -> p (c t)"),
                    waits=[(f"gX{bi}", v_xt_load[bi])])
                v_xc_par[bi] = v_xc[tt]
            else:
                w.append(("sXC", v_xc[tt]))
                v_xt_load[bi] = sc.dma(
                    "sp", f"gX{bi}", xt_sb[bi][:, :, :],
                    xbf[tt * 128:(tt + 1) * 128, :].rearrange("p (c t) -> p c t", t=512),
                    waits=w)
            wtb = [("dve", v_tb_used[bi])] if v_tb_used[bi] else []
            v_tb_load[bi] = sc.dma(
                "sp", f"gB{bi}", tb_sb[bi][:, :, :].rearrange("p a b -> p (a b)"),
                tbpk[:, tt * 2048:(tt + 1) * 2048], waits=wtb)
            if tt == 1 and h >= 1:
                emit_ag(h - 1, 1)
            if tt == 4 and h >= 1:
                emit_picks(h - 1)

            pq, pk, pv = pb[0 + (tt % 2)], pb[2 + (tt % 2)], pb[4 + (tt % 2)]
            ld = [(f"gX{bi}", v_xt_load[bi])] + W + TAB + CST
            _mm_group(sc, pq, [(pq.t[:, 0:512], wq_sb[hp][:, c, :], xt_sb[bi][:, c, :])
                              for c in range(NCH)], extra_waits=ld)
            _mm_group(sc, pk, [(pk.t[:, 0:512], wk_sb[hp][:, c, :], xt_sb[bi][:, c, :])
                              for c in range(NCH)], extra_waits=ld)
            vmms = []
            for sub in range(4):
                for c in range(NCH):
                    vmms.append((pv.t[:, sub * 128:(sub + 1) * 128],
                                 xt_sb[bi][:, c, sub * 128:(sub + 1) * 128],
                                 wv_sb[hp][:, c, :]))
            vstop = None
            wv0 = pv.wait_free() + ld
            for i, (o, l, r) in enumerate(vmms):
                st = (i % NCH == 0)
                sp_ = (i % NCH == NCH - 1)
                vstop = sc.emit(
                    "pe", lambda e, o=o, l=l, r=r, st=st, sp_=sp_: e.matmul(o, l, r, start=st, stop=sp_),
                    wv0 if i == 0 else (), inc="pe" if sp_ else None)
            pv.ready = ("pe", vstop)
            pv.free = {}
            v_xt_used[bi] = sc.val("pe")

            def rope(ptile, tmp, sw, ct, st_, dest, t0, bi):
                wva = [ptile.ready] + CST
                tk = id(tmp)
                if tk in last_v1:
                    wva.append(("dve", last_v1[tk]))
                va = sc.emit("act", lambda e: e.copy(tmp[:, :], ptile.t[:, 0:512]),
                             waits=wva, inc="act")
                ptile.read_done("act", va)
                vswp = _mm_group(sc, ptile, [(ptile.t[:, 0:512], swp_sb[:, :], tmp[:, :])],
                                 extra_waits=[("act", va)])
                wsw = [ptile.ready]
                if tk in last_v2:
                    wsw.append(("dve", last_v2[tk]))
                va2 = sc.emit("act", lambda e: e.copy(sw[:, :], ptile.t[:, 0:512]),
                              waits=wsw, inc="act")
                ptile.read_done("act", va2)
                v1 = sc.emit("dve", lambda e: e.tensor_mul(rope_t1[:, :], tmp[:, :], ct),
                             waits=[("act", va), ("dve", last_v3[0]), (f"gB{bi}", v_tb_load[bi])], inc="dve")
                v2 = sc.emit("dve", lambda e: e.tensor_mul(rope_t2[:, :], sw[:, :], st_),
                             waits=[("act", va2)], inc="dve")
                v3 = sc.emit("dve", lambda e, t0=t0: e.tensor_add(dest[:, t0:t0 + 512], rope_t1[:, :], rope_t2[:, :]),
                             waits=[("pe", v_prev_pe), ("dve", v2)], inc="dve")
                last_v1[tk] = v1
                last_v2[tk] = v2
                last_v3[0] = v3
                return v3

            v_q_done[tt] = rope(pq, qtmp, qsw, tb_sb[bi][:, 0, :], tb_sb[bi][:, 1, :], qbf, t0, bi)
            v_k_done[tt] = rope(pk, ktmp, ksw, tb_sb[bi][:, 2, :], tb_sb[bi][:, 3, :], kbf, t0, bi)
            v_tb_used[bi] = v_k_done[tt]
            vv = None
            for sub in range(4):
                vv = sc.emit("act",
                             lambda e, sub=sub, tt=tt, pvt=pv.t: e.copy(
                                 vsb[:, tt * 4 + sub, 0:128], pvt[:, sub * 128:(sub + 1) * 128]),
                             waits=[pv.ready, ("gpc", vone)], inc="act")
            pv.read_done("act", vv)
            v_v_done[tt] = vv
        v_proj_end[hp] = sc.val("pe")

        for qt in range(NTT):
            if qt == 0 and h >= 1:
                emit_tp(h - 1)
            if qt == 0 and h == HPC - 1:
                pre["wr"] = sc.dma("gp", "gT", wr_sb[:, :, :],
                                   wrouter[:, :].rearrange("p (c e) -> p c e", e=NE))
                pre[0] = sc.dma(
                    "gp", "gWo0", wo_sb0[:, :, :],
                    woT[0:128, :].rearrange("p (c o) -> p c o", o=128))
            q0 = qt * 512
            nkc = qt * 4 + 4
            v_score = [0] * nkc
            v_exp = [0] * nkc
            v_pv = [0] * nkc
            qstart = [0] * 4
            v_pv_stop = [0] * 4

            out_pb = [pb[2], pb[3], pb[4], pb[5]]
            sco_pb = [pb[0], pb[1], pb[6], pb[7]]

            def emit_score(kc):
                bank = sco_pb[kc % 4]
                dq = kc - qt * 4
                qlo = max(0, dq) * 128
                w = bank.wait_free()
                w += [("dve", v_k_done[kc // 4]), ("dve", v_q_done[qt])]
                diag = dq >= 0
                v = sc.emit("pe", lambda e, kc=kc, qlo=qlo, bank=bank, q0=q0, diag=diag: e.matmul(
                    bank.t[:, 0:512 - qlo], kbf[:, kc * 128:(kc + 1) * 128],
                    qbf[:, q0 + qlo:q0 + 512], start=True, stop=not diag),
                    w, inc=None if diag else "pe")
                if diag:
                    v = sc.emit("pe", lambda e, bank=bank: e.matmul(
                        bank.t[:, 0:128], identb_sb[:, :], tri_sb[:, :],
                        start=False, stop=True),
                        [("gT", v_cst_dma)], inc="pe")
                bank.ready = ("pe", v)
                bank.free = {}
                v_score[kc] = v
                wexp = [bank.ready]
                if kc >= 4 and v_pv[kc - 4]:
                    wexp.append(("pe", v_pv[kc - 4]))
                ve = sc.emit("act", lambda e, kc=kc, qlo=qlo, bank=bank: e.activation(
                    probs[kc % 4][:, 0:512 - qlo], bank.t[:, 0:512 - qlo], AF.Exp,
                    bias=biasC[:, 0:1]), wexp, inc="act")
                bank.read_done("act", ve)
                v_exp[kc] = ve

            def emit_pv(kc):
                dq = kc - qt * 4
                qlo = max(0, dq) * 128
                last = None
                for qc in range(qlo // 128, 4):
                    bank = out_pb[qc]
                    qcg = qt * 4 + qc
                    st = (kc == 0)
                    sp_ = (kc == qcg)
                    w = [("act", v_exp[kc])]
                    if st:
                        w += bank.wait_free()
                    last = sc.emit("pe", lambda e, kc=kc, qc=qc, qlo=qlo, bank=bank, st=st, sp_=sp_: e.matmul(
                        bank.t[:, 0:129],
                        probs[kc % 4][:, qc * 128 - qlo:(qc + 1) * 128 - qlo],
                        vsb[:, kc, 0:129], start=st, stop=sp_),
                        w, inc="pe" if (sp_ or qc == 3) else None)
                    if st:
                        bank.free = {}
                    if sp_:
                        v_pv_stop[qc] = last
                        bank.ready = ("pe", last)
                v_pv[kc] = last

            emit_score(0)
            emit_score(1)
            emit_score(2)
            for kc in range(3, nkc):
                emit_score(kc)
                emit_pv(kc - 3)
            emit_pv(nkc - 3)
            emit_pv(nkc - 2)
            emit_pv(nkc - 1)

            vcp = None
            for qc in range(4):
                bank = out_pb[qc]
                wr_ = [("pe", v_pv_stop[qc]), ("dve", sc.val("dve"))]
                vr = sc.emit("dve", lambda e, qc=qc, bank=bank: e.reciprocal(
                    rc_sb[:, qc:qc + 1], bank.t[:, 128:129]),
                    wr_, inc="dve")
                wao = [("dve", vr)]
                if v_ao_dma[qt % 2]:
                    wao.append((f"sS{qt % 2}", v_ao_dma[qt % 2]))
                vcp = sc.emit("act", lambda e, qc=qc, bank=bank, qt=qt: e.activation(
                    ao_sb[qt % 2][:, qc, :], bank.t[:, 0:128], AF.Copy,
                    scale=rc_sb[:, qc:qc + 1]), wao, inc="act")
                bank.read_done("act", vcp)
                v_rc_read[qc] = vcp
            v_ao_dma[qt % 2] = sc.dma(
                "gp", f"sS{qt % 2}", cc_in_h[h][q0:q0 + 512, :].rearrange("(qc p) f -> p qc f", p=128),
                ao_sb[qt % 2][:, :, :], waits=[("act", vcp)])
            if h < HPC - 1:
                if qt == 7:
                    emit_ag(h, 0)
            elif qt % 4 == 3 and qt < NTT - 1:
                emit_ag(h, qt // 4, npieces=4)

    emit_ag(HPC - 1, 3, npieces=4)
    emit_picks(HPC - 1)
    emit_tp(HPC - 1)

    v_attn_pe_end = sc.val("pe")
    attn_es.close()
    sc.barrier(exclude=("cc", "gP0", "gP1", "sT0", "sT1", "sT2", "sT3"))


    wo_es = ExitStack()
    wo_sb = [wo_sb0, wo_es.enter_context(nc.sbuf_tensor("wo1", [128, NCH, 128], BF16))]
    x1pre_b = [wo_es.enter_context(nc.sbuf_tensor(f"x1pre{i}", [128, NCH, 512], F32)) for i in range(2)]
    xres_sb = [wo_es.enter_context(nc.sbuf_tensor(f"xres{i}", [128, 512], F32)) for i in range(2)]
    xsq = wo_es.enter_context(nc.sbuf_tensor("xsq", [128, 512], F32))
    stat_sb = wo_es.enter_context(nc.sbuf_tensor("stat_sb", [1, 6 * 512], F32))
    mb_sb = wo_es.enter_context(nc.sbuf_tensor("mb_sb", [128, 512], F32))
    rb_sb = wo_es.enter_context(nc.sbuf_tensor("rb_sb", [128, 512], F32))
    logits_sb = wo_es.enter_context(nc.sbuf_tensor("logits_sb", [128, NCH, NE], F32))
    pr_sb = wo_es.enter_context(nc.sbuf_tensor("pr_sb", [128, NE], F32))
    eq_sb = wo_es.enter_context(nc.sbuf_tensor("eq_sb", [128, NE], F32))
    pm_sb = wo_es.enter_context(nc.sbuf_tensor("pm_sb", [128, NE], F32))
    sc1_sb = wo_es.enter_context(nc.sbuf_tensor("sc1_sb", [128, 4], F32))

    v_wr = pre["wr"]

    xsq2 = wo_es.enter_context(nc.sbuf_tensor("xsq2", [128, 2, 512], F32))

    v_spill_tile = [0, 0, 0, 0]
    v_wo_load = [0, 0]
    v_wo_used = [None, None]
    v_xres_used = [None, None]
    rd_x1 = {0: [], 1: []}
    pend_lg = [None]
    pend_bc = [None]
    v_x1sp_last = 0
    v_logit_evac = 0
    v_nrm_prev = 0
    for tt in range(4):
        t0 = tt * 512
        x1p = x1pre_b[tt % 2]
        sbank, s2bank = pb[6], pb[7]
        v_s2mm = [0] * NCH
        v_s1mm = [0] * NCH
        v_add = [0] * NCH
        v_sq = [0] * NCH

        def emit_smm(oc, x1p=x1p):
            st, sp_ = (oc == 0), (oc == NCH - 1)
            w1 = [("dve", v_add[oc])]
            if st:
                w1 += sbank.wait_free() + CST
            v_s1mm[oc] = sc.emit("pe", lambda e, oc=oc, st=st, sp_=sp_: e.matmul(
                sbank.t[0:1, 0:512], ones128_sb[:, :], x1p[:, oc, :], start=st, stop=sp_),
                w1, inc="pe")
            w2 = [("act", v_sq[oc])]
            if st:
                w2 += s2bank.wait_free()
            v_s2mm[oc] = sc.emit("pe", lambda e, oc=oc, st=st, sp_=sp_: e.matmul(
                s2bank.t[0:1, 0:512], ones128_sb[:, :], xsq2[:, oc % 2, :], start=st, stop=sp_),
                w2, inc="pe")

        corder = [c for m in range(4) for c in range(m, NCH, 4)]
        for oc in range(NCH):
            bi = oc % 2
            if tt == 0 and oc == 0:
                v_wo_load[bi] = pre[0]
            else:
                w = [("pe", v_wo_used[bi])] if v_wo_used[bi] else []
                v_wo_load[bi] = sc.dma(
                    "gp", f"gWo{bi}", wo_sb[bi][:, :, :],
                    woT[oc * 128:(oc + 1) * 128, :].rearrange("p (c o) -> p c o", o=128),
                    waits=w)
            wbank = pb[oc % 2]
            wf = wbank.wait_free() + [(f"gWo{bi}", v_wo_load[bi])]
            vstop = None
            for ci, c in enumerate(corder):
                ww = list(wf) if ci == 0 else []
                ww.append((f"sT{c % 4}", v_tp_h[c % 4]))
                vstop = sc.emit("pe", lambda e, c=c, bi=bi, wbank=wbank, t0=t0, st=(ci == 0), sp_=(ci == NCH - 1): e.matmul(
                    wbank.t[:, 0:512], wo_sb[bi][:, c, :], attn_fm[:, c, t0:t0 + 512],
                    start=st, stop=sp_), ww, inc="pe" if ci == NCH - 1 else None)
            wbank.ready = ("pe", vstop)
            wbank.free = {}
            v_wo_used[bi] = sc.val("pe")
            xbi = oc % 2
            wx = [("dve", v_xres_used[xbi])] if v_xres_used[xbi] else []
            vxr = sc.dma("sp", f"sR{xbi}", xres_sb[xbi][:, :], xres[oc * 128:(oc + 1) * 128, t0:t0 + 512], waits=wx)
            wadd = [wbank.ready, (f"sR{xbi}", vxr)] + rd_x1[tt % 2]
            v_add[oc] = sc.emit("dve", lambda e, oc=oc, xbi=xbi, wbank=wbank, x1p=x1p: e.tensor_add(
                x1p[:, oc, :], wbank.t[:, 0:512], xres_sb[xbi][:, :]), wadd, inc="dve")
            wbank.read_done("dve", v_add[oc])
            v_xres_used[xbi] = v_add[oc]
            wsq = [("dve", v_add[oc])]
            if oc >= 2:
                wsq.append(("pe", v_s2mm[oc - 2]))
            v_sq[oc] = sc.emit("act", lambda e, oc=oc, x1p=x1p: e.activation(
                xsq2[:, oc % 2, :], x1p[:, oc, :], AF.Square), wsq, inc="act")
            if oc >= 1:
                emit_smm(oc - 1)
            if oc == 1 and pend_bc[0] is not None:
                pend_bc[0]()
                pend_bc[0] = None
            if oc == 7 and pend_lg[0] is not None:
                pend_lg[0]()
                pend_lg[0] = None
        emit_smm(NCH - 1)
        rd_x1[tt % 2] = []
        sbank.ready = ("pe", v_s1mm[NCH - 1])
        sbank.free = {}
        s2bank.ready = ("pe", v_s2mm[NCH - 1])
        s2bank.free = {}

        st_ = stat_sb
        va_m = sc.emit("act", lambda e: e.activation(st_[0:1, 0:512], sbank.t[0:1, 0:512],
                                                     AF.Copy, scale=invD[0:1, 0:1]),
                       [sbank.ready] + CST, inc="act")
        va_e2 = sc.emit("act", lambda e: e.activation(st_[0:1, 512:1024], s2bank.t[0:1, 0:512],
                                                      AF.Copy, scale=invD[0:1, 0:1]),
                        [s2bank.ready], inc="act")
        sbank.read_done("act", va_m)
        s2bank.read_done("act", va_e2)
        vd_mm = sc.emit("dve", lambda e: e.tensor_mul(st_[0:1, 1024:1536], st_[0:1, 0:512], st_[0:1, 0:512]),
                        [("act", va_m)], inc="dve")
        vd_var = sc.emit("dve", lambda e: e.tensor_sub(st_[0:1, 1536:2048], st_[0:1, 512:1024], st_[0:1, 1024:1536]),
                         [("act", va_e2), ("dve", vd_mm)], inc="dve")
        va_ln = sc.emit("act", lambda e: e.activation(st_[0:1, 2048:2560], st_[0:1, 1536:2048],
                                                      AF.Ln, bias=epsT[0:1, 0:1]),
                        [("dve", vd_var)], inc="act")
        va_rstd = sc.emit("act", lambda e: e.activation(st_[0:1, 2560:3072], st_[0:1, 2048:2560],
                                                        AF.Exp, scale=half_neg[0:1, 0:1]),
                          [("act", va_ln)], inc="act")
        def emit_bcast(tt=tt, x1p=x1p, va_m=va_m, va_rstd=va_rstd, v_s2=v_s2mm[NCH - 1]):
            nonlocal v_x1sp_last, v_nrm_prev
            mbank, rbank = pb[2], pb[3]
            _mm_group(sc, mbank, [(mbank.t[:, 0:512], ones1_sb[:, :], stat_sb[0:1, 0:512])],
                      extra_waits=[("act", va_m)] + CST)
            _mm_group(sc, rbank, [(rbank.t[:, 0:512], ones1_sb[:, :], stat_sb[0:1, 2560:3072])],
                      extra_waits=[("act", va_rstd)])
            wvem = [mbank.ready] + ([("dve", v_nrm_prev)] if v_nrm_prev else [])
            wver = [rbank.ready] + ([("dve", v_nrm_prev)] if v_nrm_prev else [])
            ve_m = sc.emit("act", lambda e: e.copy(mb_sb[:, :], mbank.t[:, 0:512]), wvem, inc="act")
            ve_r = sc.emit("act", lambda e: e.copy(rb_sb[:, :], rbank.t[:, 0:512]), wver, inc="act")
            mbank.read_done("act", ve_m)
            rbank.read_done("act", ve_r)
            v_nrm = [0] * NCH
            for oc in range(NCH):
                vs = sc.emit("dve", lambda e, oc=oc, x1p=x1p: e.tensor_sub(
                    x1p[:, oc, :], x1p[:, oc, :], mb_sb[:, :]),
                    [("act", ve_m), ("pe", v_s2), ("dve", sc.val("dve"))], inc="dve")
                v_nrm[oc] = sc.emit("dve", lambda e, oc=oc, x1p=x1p: e.tensor_mul(
                    x1p[:, oc, :], x1p[:, oc, :], rb_sb[:, :]),
                    [("act", ve_r), ("dve", vs)], inc="dve")
                v_x1sp_last = sc.dma("sp", "sL", x1sp[tt * 128:(tt + 1) * 128, oc * 512:(oc + 1) * 512],
                                     x1p[:, oc, :], waits=[("dve", v_nrm[oc])])
            rd_x1[tt % 2].append(("sL", v_x1sp_last))
            v_spill_tile[tt] = v_x1sp_last
            v_nrm_prev = v_nrm[NCH - 1]

            def emit_logits(tt=tt, x1p=x1p, v_n=v_nrm[NCH - 1]):
                nonlocal v_logit_evac
                lbank = pb[4]
                for tc in range(4):
                    g = tt * 4 + tc
                    mms = [(lbank.t[0:128, 0:NE], x1p[:, oc, tc * 128:(tc + 1) * 128], wr_sb[:, oc, :])
                           for oc in range(NCH)]
                    _mm_group(sc, lbank, mms,
                              extra_waits=[("dve", v_n), ("gT", v_wr)])
                    v_logit_evac = sc.emit("act", lambda e, g=g: e.copy(
                        logits_sb[:, g, :], lbank.t[0:128, 0:NE]), [lbank.ready], inc="act")
                    lbank.read_done("act", v_logit_evac)
                rd_x1[tt % 2].append(("pe", sc.val("pe")))
            pend_lg[0] = emit_logits
        if tt < 3:
            pend_bc[0] = emit_bcast
        else:
            emit_bcast()
            pend_lg[0]()
            pend_lg[0] = None

    tbank = pb[5]
    v_comb_evac = 0
    v_tp_prev = 0
    for g in range(NCH):
        c0 = g * 128
        vm = sc.emit("dve", lambda e, g=g: e.tensor_reduce(
            sc1_sb[:, 0:1], logits_sb[:, g, :], AX.X, ALU.max), [("act", v_logit_evac)], inc="dve")
        wls = [("dve", vm)]
        if v_tp_prev:
            wls.append(("pe", v_tp_prev))
        vls = sc.emit("dve", lambda e, g=g: e.tensor_single_scalar(
            pr_sb[:, :], logits_sb[:, g, :], sc1_sb[:, 0:1], ALU.subtract),
            wls, inc="dve")
        vex = sc.emit("act", lambda e: e.activation(pr_sb[:, :], pr_sb[:, :], AF.Exp),
                      [("dve", vls)], inc="act")
        vsm = sc.emit("dve", lambda e: e.tensor_reduce(
            sc1_sb[:, 1:2], pr_sb[:, :], AX.X, ALU.add), [("act", vex)], inc="dve")
        vrs = sc.emit("dve", lambda e: e.reciprocal(sc1_sb[:, 2:3], sc1_sb[:, 1:2]),
                      [("dve", vsm)], inc="dve")
        vpr = sc.emit("dve", lambda e: e.tensor_single_scalar(
            pr_sb[:, :], pr_sb[:, :], sc1_sb[:, 2:3], ALU.mult), [("dve", vrs)], inc="dve")
        vm1 = sc.emit("dve", lambda e: e.tensor_reduce(
            sc1_sb[:, 0:1], pr_sb[:, :], AX.X, ALU.max), [("dve", vpr)], inc="dve")
        veq = sc.emit("dve", lambda e: e.tensor_single_scalar(
            eq_sb[:, :], pr_sb[:, :], sc1_sb[:, 0:1], ALU.is_ge), [("dve", vm1)], inc="dve")
        vpm = sc.emit("dve", lambda e: e.scalar_tensor_tensor(
            pm_sb[:, :], eq_sb[:, :], -1.0e30, pr_sb[:, :], ALU.mult, ALU.add),
            [("dve", veq)] + CST, inc="dve")
        vm2 = sc.emit("dve", lambda e: e.tensor_reduce(
            sc1_sb[:, 3:4], pm_sb[:, :], AX.X, ALU.max), [("dve", vpm)], inc="dve")
        vge = sc.emit("dve", lambda e: e.tensor_single_scalar(
            eq_sb[:, :], pr_sb[:, :], sc1_sb[:, 3:4], ALU.is_ge), [("dve", vm2)], inc="dve")
        vcb = sc.emit("dve", lambda e: e.tensor_mul(pr_sb[:, :], pr_sb[:, :], eq_sb[:, :]),
                      [("dve", vge)], inc="dve")
        wtp = tbank.wait_free() + [("dve", vcb)]
        vt = sc.emit("pe", lambda e: e.matmul(
            tbank.t[0:NE, 0:128], pr_sb[:, :], ident_sb[:, :],
            start=True, stop=True), wtp, inc="pe")
        tbank.ready = ("pe", vt)
        tbank.free = {}
        v_tp_prev = vt
        v_comb_evac = sc.emit("act", lambda e, c0=c0: e.copy(
            comb_sb[:, c0:c0 + 128], tbank.t[0:NE, 0:128]), [tbank.ready], inc="act")
        tbank.read_done("act", v_comb_evac)

    wo_es.close()
    wo_outer.close()
    sc.barrier()

    ffn_es = ExitStack()
    sel8_sb = ffn_es.enter_context(nc.sbuf_tensor("sel8_sb", [8, 8 * 128], F32))
    v_sel8 = sc.dma("gp", "gT", sel8_sb[:, :], sel8[:, :])
    y_acc = ffn_es.enter_context(nc.sbuf_tensor("y_acc", [128, NCH, 1024], F32))
    x1bh = ffn_es.enter_context(nc.sbuf_tensor("x1bh", [128, NCH, 1024], BF16))
    h_sb = ffn_es.enter_context(nc.sbuf_tensor("h_sb", [128, NDE, 1024], BF16))
    wg_sb = [ffn_es.enter_context(nc.sbuf_tensor(f"wg{i}", [128, NCH, 128], BF16)) for i in range(2)]
    wu_sb = [ffn_es.enter_context(nc.sbuf_tensor(f"wu{i}", [128, NCH, 128], BF16)) for i in range(2)]
    wd_sb = ffn_es.enter_context(nc.sbuf_tensor("wd_sb", [128, 6, NCH, 128], BF16))
    sg_sb = ffn_es.enter_context(nc.sbuf_tensor("sg_sb", [128, 2, 512], F32))
    cb_sb = ffn_es.enter_context(nc.sbuf_tensor("cb_sb", [128, 1024], F32))
    xq2 = ffn_es.enter_context(nc.sbuf_tensor("xq2", [128, 2, 512], F32))
    st2 = ffn_es.enter_context(nc.sbuf_tensor("st2", [1, 3 * 512], F32))
    mb2 = ffn_es.enter_context(nc.sbuf_tensor("mb2", [128, 512], F32))
    rb2 = ffn_es.enter_context(nc.sbuf_tensor("rb2", [128, 512], F32))
    x2t = [ffn_es.enter_context(nc.sbuf_tensor(f"x2t{i}", [128, 512], F32)) for i in range(2)]

    stages = [(wsg, wsu, wsd, 0, 0, 0, None), (wsg, wsu, wsd, 11, 0, 0, None)] + \
             [(weg, weu, wed, 0, e * NDE, e * DE, e) for e in range(NE)]
    JGRP = [(0, 6), (6, 11)]
    ffn_pre = {}

    v_y_read_prev = []
    v_x1b_pe_prev = 0
    v_out_last = 0
    v_store = [0, 0]
    v_prev_s2 = 0
    v_prev_norm = 0
    for hf in range(2):
        t0h = hf * 1024
        v_x1b = [0, 0]
        v_ya0 = None
        for ts in range(2):
            g = hf * 2 + ts
            wldx = [("sL", v_spill_tile[g])]
            if v_x1b_pe_prev:
                wldx.append(("pe", v_x1b_pe_prev))
            v_x1b[ts] = sc.dma(
                "gp", "gY", x1bh[:, :, ts * 512:(ts + 1) * 512],
                x1sp[g * 128:(g + 1) * 128, :].rearrange("p (c t) -> p c t", t=512), waits=wldx)
            if hf == 0 and ts == 0:
                for j_ in range(2):
                    ffn_pre[(j_, "g")] = sc.dma(
                        "gp", f"gG{j_}", wg_sb[j_][:, :, :],
                        wsg[j_ * 128:(j_ + 1) * 128, :].rearrange("p (c o) -> p c o", o=128))
                    ffn_pre[(j_, "u")] = sc.dma(
                        "gp", f"gG{j_}", wu_sb[j_][:, :, :],
                        wsu[j_ * 128:(j_ + 1) * 128, :].rearrange("p (c o) -> p c o", o=128))
            wldy = [("sL", v_spill_tile[g])] + v_y_read_prev
            v_ya0 = sc.dma("sp", "sY", y_acc[:, :, ts * 512:(ts + 1) * 512],
                           x1sp[g * 128:(g + 1) * 128, :].rearrange("p (c t) -> p c t", t=512), waits=wldy)

        v_h_read_pe = sc.val("pe")
        v_cb_read = sc.val("dve")
        v_wd_used = sc.val("pe")
        v_wg_used = [sc.val("pe"), sc.val("pe")]
        v_sg_used = [sc.val("dve"), sc.val("dve")]
        v_ya_last = sc.val("dve")
        v_ya_chunk = {}
        for snum, (wgt, wut, wdt, co, rg_, rd_, ex) in enumerate(stages):
            if ex is not None:
                cbank = pb[6]
                v_cb = 0
                for ts in range(2):
                    _mm_group(sc, cbank, [(cbank.t[:, 0:512], sel8_sb[:, ex * 128:(ex + 1) * 128],
                                           comb_sb[:, t0h + ts * 512:t0h + (ts + 1) * 512])],
                              extra_waits=[("act", v_comb_evac), ("gT", v_sel8)])
                    wcb = [cbank.ready, ("dve", v_cb_read)]
                    v_cb = sc.emit("act", lambda e, ts=ts, cbank=cbank: e.copy(
                        cb_sb[:, ts * 512:(ts + 1) * 512], cbank.t[:, 0:512]), wcb, inc="act")
                    cbank.read_done("act", v_cb)

            for g0, g1 in JGRP:
                v_h_w_ts = [0, 0]
                for j in range(g0, g1):
                    jp = j % 2
                    rb = (rg_ + co + j) * 128
                    if hf == 0 and snum == 0 and j < 2:
                        vwg = ffn_pre[(j, "g")]
                        vwu = ffn_pre[(j, "u")]
                    else:
                        wwl = [("pe", v_wg_used[jp])]
                        vwg = sc.dma("gp", f"gG{jp}", wg_sb[jp][:, :, :],
                                     wgt[rb:rb + 128, :].rearrange("p (c o) -> p c o", o=128),
                                     waits=wwl)
                        vwu = sc.dma("gp", f"gG{jp}", wu_sb[jp][:, :, :],
                                     wut[rb:rb + 128, :].rearrange("p (c o) -> p c o", o=128))
                    for ts in range(2):
                        gbank, ubank = pb[0 + ts], pb[2 + ts]
                        wg_ = [(f"gG{jp}", vwu), ("gY", v_x1b[ts])]
                        _mm_group(sc, gbank, [(gbank.t[:, 0:512], wg_sb[jp][:, c, :],
                                               x1bh[:, c, ts * 512:(ts + 1) * 512]) for c in range(NCH)],
                                  extra_waits=wg_)
                        _mm_group(sc, ubank, [(ubank.t[:, 0:512], wu_sb[jp][:, c, :],
                                               x1bh[:, c, ts * 512:(ts + 1) * 512]) for c in range(NCH)])
                        v_wg_used[jp] = sc.val("pe")
                        wsl = [gbank.ready, ("dve", v_sg_used[ts])]
                        vsl = sc.emit("act", lambda e, ts=ts, gbank=gbank: e.activation(
                            sg_sb[:, ts, :], gbank.t[:, 0:512], AF.Silu), wsl, inc="act")
                        gbank.read_done("act", vsl)
                        whh = [("act", vsl), ubank.ready, ("pe", v_h_read_pe)]
                        if ex is not None:
                            vcbf = sc.emit("dve", lambda e, ts=ts: e.tensor_mul(
                                sg_sb[:, ts, :], sg_sb[:, ts, :], cb_sb[:, ts * 512:(ts + 1) * 512]),
                                [("act", vsl), ("act", v_cb), ("dve", v_sg_used[ts])], inc="dve")
                            whh.append(("dve", vcbf))
                        vh = sc.emit("dve", lambda e, j=j, ts=ts, ubank=ubank: e.tensor_mul(
                            h_sb[:, j, ts * 512:(ts + 1) * 512], sg_sb[:, ts, :], ubank.t[:, 0:512]),
                            whh, inc="dve")
                        ubank.read_done("dve", vh)
                        v_sg_used[ts] = vh
                        v_h_w_ts[ts] = vh
                v_wd = 0
                for j in range(g0, g1):
                    wwd = [("pe", v_wd_used)] if j == g0 else []
                    v_wd = sc.dma("gp", "gD", wd_sb[:, j - g0, :, :],
                                  wdt[rd_ + (co + j) * 128:rd_ + (co + j + 1) * 128, :].rearrange("p (c o) -> p c o", o=128),
                                  waits=wwd)
                for ts in range(2):
                    for oc in range(NCH):
                        ybank = (pb[4], pb[5], pb[7])[(ts * NCH + oc) % 3]
                        wdn = [("dve", v_h_w_ts[ts]), ("gD", v_wd)]
                        _mm_group(sc, ybank, [(ybank.t[:, 0:512], wd_sb[:, j2 - g0, oc, :],
                                               h_sb[:, j2, ts * 512:(ts + 1) * 512]) for j2 in range(g0, g1)],
                                  extra_waits=wdn)
                        v_ya_last = sc.emit("dve", lambda e, oc=oc, ts=ts, ybank=ybank: e.tensor_add(
                            y_acc[:, oc, ts * 512:(ts + 1) * 512], ybank.t[:, 0:512],
                            y_acc[:, oc, ts * 512:(ts + 1) * 512]),
                            [ybank.ready, ("dve", v_ya_chunk.get((oc, ts), 0)), ("sY", v_ya0)], inc="dve")
                        v_ya_chunk[(oc, ts)] = v_ya_last
                        ybank.read_done("dve", v_ya_last)
                v_wd_used = sc.val("pe")
            v_h_read_pe = sc.val("pe")
            v_cb_read = sc.val("dve")
        v_x1b_pe_prev = sc.val("pe")

        v_y_read_prev = []
        for ts in range(2):
            sbank, s2bank = pb[6], pb[7]
            v_sq2 = [0] * NCH
            v_m1 = [0] * NCH
            v_m2 = [0] * NCH

            for oc in range(NCH):
                wsq = [("dve", v_ya_last), ("pe", v_m2[oc - 2] if oc >= 2 else v_prev_s2)]
                v_sq2[oc] = sc.emit("act", lambda e, oc=oc, ts=ts: e.activation(
                    xq2[:, oc % 2, :], y_acc[:, oc, ts * 512:(ts + 1) * 512], AF.Square),
                    wsq, inc="act")
                w1 = [("dve", v_ya_last)]
                if oc == 0:
                    w1 += sbank.wait_free() + CST
                v_m1[oc] = sc.emit("pe", lambda e, oc=oc, ts=ts, st=(oc == 0), sp_=(oc == NCH - 1): e.matmul(
                    sbank.t[0:1, 0:512], ones128_sb[:, :], y_acc[:, oc, ts * 512:(ts + 1) * 512],
                    start=st, stop=sp_), w1, inc="pe")
                w2 = [("act", v_sq2[oc])]
                if oc == 0:
                    w2 += s2bank.wait_free()
                v_m2[oc] = sc.emit("pe", lambda e, oc=oc, st=(oc == 0), sp_=(oc == NCH - 1): e.matmul(
                    s2bank.t[0:1, 0:512], ones128_sb[:, :], xq2[:, oc % 2, :],
                    start=st, stop=sp_), w2, inc="pe")
            sbank.ready = ("pe", v_m1[NCH - 1])
            sbank.free = {}
            s2bank.ready = ("pe", v_m2[NCH - 1])
            s2bank.free = {}
            v_prev_s2 = v_m2[NCH - 1]

            va_m = sc.emit("act", lambda e: e.activation(st2[0:1, 0:512], sbank.t[0:1, 0:512],
                                                         AF.Copy, scale=invD[0:1, 0:1]),
                           [sbank.ready], inc="act")
            va_e2 = sc.emit("act", lambda e: e.activation(st2[0:1, 512:1024], s2bank.t[0:1, 0:512],
                                                          AF.Copy, scale=invD[0:1, 0:1]),
                            [s2bank.ready], inc="act")
            sbank.read_done("act", va_m)
            s2bank.read_done("act", va_e2)
            vd_mm = sc.emit("dve", lambda e: e.tensor_mul(st2[0:1, 1024:1536], st2[0:1, 0:512], st2[0:1, 0:512]),
                            [("act", va_m), ("dve", sc.val("dve"))], inc="dve")
            vd_var = sc.emit("dve", lambda e: e.tensor_sub(st2[0:1, 512:1024], st2[0:1, 512:1024], st2[0:1, 1024:1536]),
                             [("act", va_e2), ("dve", vd_mm)], inc="dve")
            va_ln = sc.emit("act", lambda e: e.activation(st2[0:1, 1024:1536], st2[0:1, 512:1024],
                                                          AF.Ln, bias=epsT[0:1, 0:1]),
                            [("dve", vd_var)], inc="act")
            va_rstd = sc.emit("act", lambda e: e.activation(st2[0:1, 512:1024], st2[0:1, 1024:1536],
                                                            AF.Exp, scale=half_neg[0:1, 0:1]),
                              [("act", va_ln)], inc="act")
            mbank, rbank = pb[2], pb[3]
            _mm_group(sc, mbank, [(mbank.t[:, 0:512], ones1_sb[:, :], st2[0:1, 0:512])],
                      extra_waits=[("act", va_m)])
            _mm_group(sc, rbank, [(rbank.t[:, 0:512], ones1_sb[:, :], st2[0:1, 512:1024])],
                      extra_waits=[("act", va_rstd)])
            ve_m = sc.emit("act", lambda e: e.copy(mb2[:, :], mbank.t[:, 0:512]),
                           [mbank.ready, ("dve", v_prev_norm)], inc="act")
            ve_r = sc.emit("act", lambda e: e.copy(rb2[:, :], rbank.t[:, 0:512]),
                           [rbank.ready, ("dve", v_prev_norm)], inc="act")
            mbank.read_done("act", ve_m)
            rbank.read_done("act", ve_r)

            for oc in range(NCH):
                op = oc % 2
                wn = [("act", ve_m), ("act", ve_r), ("dve", sc.val("dve"))]
                if v_store[op]:
                    wn.append((f"sO{op}", v_store[op]))
                vs = sc.emit("dve", lambda e, oc=oc, ts=ts, op=op: e.tensor_sub(
                    x2t[op][:, :], y_acc[:, oc, ts * 512:(ts + 1) * 512], mb2[:, :]), wn, inc="dve")
                vn = sc.emit("dve", lambda e, op=op: e.tensor_mul(
                    x2t[op][:, :], x2t[op][:, :], rb2[:, :]), [("dve", vs)], inc="dve")
                v_out_last = sc.dma("sp", f"sO{op}", outfm[oc * 128:(oc + 1) * 128, t0h + ts * 512:t0h + (ts + 1) * 512],
                                    x2t[op][:, :], waits=[("dve", vn)])
                v_store[op] = v_out_last
                v_y_read_prev.append(("dve", vn))
                v_prev_norm = vn
            v_y_read_prev.append(("pe", sc.val("pe")))
            v_y_read_prev.append(("act", v_sq2[NCH - 1]))

    ffn_es.close()
    _finish(nc, sc, sems, es, [("sO0", sc.val("sO0")), ("sO1", sc.val("sO1"))])
    return nc


def _finish(nc, sc, sems, es, final_waits):
    sc.emit("sp", lambda e: e.nop(), waits=list(final_waits))
    for s in SEMS:
        assert sc.cnt.get(s, 0) < 60000, (s, sc.cnt[s])
    with nc.Block() as block:
        def runner(name):
            def _r(eng):
                hw = {}
                for waits, fn, inc, amt in sc.q[name]:
                    for sem, v in waits:
                        if v and v > hw.get(sem, 0):
                            eng.wait_ge(sems[sem], v)
                            hw[sem] = v
                    ins = fn(eng)
                    if inc is not None:
                        ins.then_inc(sems[inc], amt)
            return _r
        block.tensor(runner("pe"))
        block.scalar(runner("act"))
        block.vector(runner("dve"))
        block.sync(runner("sp"))
        block.gpsimd(runner("gp"))
    es.close()


def _rope_tables():
    idx = np.arange(0, DH, 2, dtype=np.float32)
    inv_freq = (np.float32(10000.0) ** (-idx / np.float32(DH))).astype(np.float32)
    angles = (np.arange(S, dtype=np.float32)[:, None] * inv_freq[None, :]).astype(np.float32)
    cos = np.cos(angles).astype(np.float32)
    sin = np.sin(angles).astype(np.float32)
    cos_e = np.concatenate([cos.T, cos.T], axis=0)
    sin_s = np.concatenate([-sin.T, sin.T], axis=0)
    scl = np.float32(1.0 / np.sqrt(np.float32(DH)))
    tabs = np.stack([cos_e * scl, sin_s * scl, cos_e, sin_s], axis=1)
    pk = tabs.reshape(128, 4, NTT, 512).transpose(0, 2, 1, 3).reshape(128, NTT * 4 * 512)
    return np.ascontiguousarray(pk.astype(ml_dtypes.bfloat16))


def _tile_w(w, nj, od):
    return np.ascontiguousarray(
        np.asarray(w, np.float32).reshape(NCH, 128, nj, od)
        .transpose(2, 1, 0, 3).reshape(nj * 128, NCH * od))


def _stage(inputs):
    x = np.asarray(inputs["L0_attn_x"], np.float32)
    wq = np.asarray(inputs["L0_attn_Wq"], np.float32)
    wk = np.asarray(inputs["L0_attn_Wk"], np.float32)
    wv = np.asarray(inputs["L0_attn_Wv"], np.float32)
    wo = np.asarray(inputs["L0_attn_Wo"], np.float32)

    perm = np.concatenate([np.arange(0, DH, 2), np.arange(1, DH, 2)])
    wqT, wkT, wvT = wq.T, wk.T, wv.T
    tbpk = _rope_tables()

    tri = np.where(np.arange(128)[:, None] <= np.arange(128)[None, :], 0.0, MASKVAL).astype(ml_dtypes.bfloat16)
    identm = np.eye(128, dtype=np.float32)
    swpm = np.zeros((128, 128), dtype=np.float32)
    for i in range(128):
        swpm[(i + 64) % 128, i] = 1.0
    sel8 = np.zeros((8, 8 * 128), dtype=np.float32)
    for e in range(8):
        sel8[e, e * 128:(e + 1) * 128] = 1.0
    ones1 = np.ones((1, 128), dtype=np.float32)

    weg = np.asarray(inputs["L0_ffn_We_gate"], np.float32)
    weu = np.asarray(inputs["L0_ffn_We_up"], np.float32)
    shared = {
        "woT": _tile_w(wo.T, NCH, 128), "tbpk": tbpk,
        "tri": tri, "ident": identm, "identb": np.eye(128, dtype=ml_dtypes.bfloat16),
        "swp": swpm, "sel8": sel8, "ones1": ones1,
        "wrouter": _tile_w(inputs["L0_ffn_W_router"], 1, NE),
        "wsg": _tile_w(inputs["L0_ffn_Ws_gate"], NDS, 128),
        "wsu": _tile_w(inputs["L0_ffn_Ws_up"], NDS, 128),
        "wsd": np.ascontiguousarray(inputs["L0_ffn_Ws_down"], dtype=np.float32),
        "weg": np.ascontiguousarray(np.concatenate(
            [_tile_w(weg[e], NDE, 128) for e in range(NE)], axis=0)),
        "weu": np.ascontiguousarray(np.concatenate(
            [_tile_w(weu[e], NDE, 128) for e in range(NE)], axis=0)),
        "wed": np.ascontiguousarray(np.asarray(inputs["L0_ffn_We_down"], np.float32).reshape(NE * DE, D)),
    }

    xTs = [np.ascontiguousarray(x[b].T) for b in range(B)]
    xTts = [_tile_w(xT, NTT, 512) for xT in xTs]
    in_maps = []
    for c in range(8):
        b, r = c // 4, c % 4
        heads = range(4 * r, 4 * r + 4)
        m = dict(shared)
        m["xTt"] = xTts[b]
        m["xres"] = np.ascontiguousarray(xTs[b][:, r * T:(r + 1) * T])
        m["wqT"] = _tile_w(np.concatenate(
            [wqT[:, h * DH:(h + 1) * DH][:, perm] for h in heads], axis=1), HPC, DH)
        m["wkT"] = _tile_w(np.concatenate(
            [wkT[:, h * DH:(h + 1) * DH][:, perm] for h in heads], axis=1), HPC, DH)
        m["wvT"] = _tile_w(np.concatenate(
            [wvT[:, h * DH:(h + 1) * DH] for h in heads], axis=1), HPC, DH)
        m["roff"] = np.array([[(r // 2) * 2 * S * DH + (r % 2) * T * DH]], dtype=np.int32)
        m["roff3"] = np.array([[r * 4 * T * DH]], dtype=np.int32)
        in_maps.append(m)
    return in_maps


def kernel(**inputs):
    nc = _build("full")
    in_maps = _stage(inputs)
    res = run_bass_kernel_spmd(nc, in_maps, core_ids=list(range(8)))
    out = np.empty((B, S, D), dtype=np.float32)
    for c in range(8):
        b, r = c // 4, c % 4
        out[b, r * T:(r + 1) * T, :] = np.asarray(res.results[c]["outfm"]).T
    return out
